# version 03
# baseline (speedup 1.0000x reference)
import numpy as np
import ml_dtypes
import concourse.bass as bass
import concourse.mybir as mybir
from concourse.ap import AP
from concourse.bass_utils import run_bass_kernel_spmd

F32 = mybir.dt.float32
BF16 = mybir.dt.bfloat16
AFT = mybir.ActivationFunctionType
ALU = mybir.AluOpType
AXX = mybir.AxisListType.X

NCORES = 8
SEQ_PER_CORE = 2
T = 2048
D = 1024
NCH = 8
NEG = -3.0e38



def bcast_free(ap, n):
    return AP(ap.tensor, ap.offset, [list(a) for a in ap.ap] + [[0, n]])


def seg_view(ap, d):
    return ap.rearrange("p (h d) -> p h d", d=d)


def col_unit(ap):
    return AP(ap.tensor, ap.offset, [list(a) for a in ap.ap] + [[1, 1]])



_ENG_CSEM = {"tensor": "t", "vector": "v", "scalar": "s", "gpsimd": "g", "sync": None}
_ENG_DSEM = {"sync": "sd", "gpsimd": "gd", "scalar": "scd"}


class Prog:
    def __init__(self, nc):
        self.nc = nc
        self.sems = {k: nc.alloc_semaphore("sem_" + k)
                     for k in ("t", "v", "s", "g", "sd", "gd", "scd")}
        self.cnt = {k: 0 for k in self.sems}
        self.steps = []

    def c(self, eng, fn):
        self.steps.append((eng, fn, "c"))

    def d(self, eng, pairs):
        self.steps.append((eng, pairs, "d"))

    def w(self, eng):
        self.steps.append((eng, None, "w"))

    def emit(self):
        nc = self.nc
        last_seen = {e: {k: 0 for k in self.cnt} for e in _ENG_CSEM}
        with nc.Block() as block:
            for eng, payload, kind in self.steps:
                self._emit_one(block, eng, payload, kind, last_seen)
        return nc

    def _emit_one(self, block, eng, payload, kind, last_seen):
        sems, cnt = self.sems, self.cnt
        own_c = _ENG_CSEM[eng]
        pre = {k: cnt[k] for k in cnt}

        def body(E):
            for k, c in pre.items():
                if k == own_c:
                    continue
                if c > last_seen[eng][k]:
                    E.wait_ge(sems[k], c)
                    last_seen[eng][k] = c
            if kind == "c":
                ret = payload(E)
                assert ret is not None, "compute step fn must return last inst"
                ret.then_inc(sems[own_c], 1)
                cnt[own_c] += 1
            elif kind == "d":
                dq = _ENG_DSEM[eng]
                prs = payload(E) if callable(payload) else payload
                for out_ap, in_ap in prs:
                    E.dma_start(out=out_ap, in_=in_ap).then_inc(sems[dq], 16)
                    cnt[dq] += 16

        getattr(block, eng)(body)




class Ctx:

    def __init__(self, nc):
        self.nc = nc
        self.prog = Prog(nc)
        self.sb = {}
        self.ps = {}
        self.inputs = {}
        self.outputs = {}

    def din(self, name, shape, dtype=F32):
        t = self.nc.dram_tensor(name, list(shape), dtype, kind="ExternalInput").ap()
        self.inputs[name] = t
        return t

    def dout(self, name, shape, dtype=F32):
        t = self.nc.dram_tensor(name, list(shape), dtype, kind="ExternalOutput").ap()
        self.outputs[name] = t
        return t

    def S(self, name, shape, dtype):
        h = self.nc.alloc_sbuf_tensor(name, list(shape), dtype)
        self.sb[name] = h
        return h

    def P(self, name, shape, dtype=F32):
        h = self.nc.alloc_psum_tensor(name, list(shape), dtype)
        self.ps[name] = h
        return h


def setup_common(ctx):
    c = ctx
    c.din("ident_f", [128, 128], F32)
    c.din("ident_b", [128, 128], BF16)
    c.din("ones_f", [128, 128], F32)
    c.din("ones_b", [128, 128], BF16)
    idf = c.S("identf", [128, 128], F32)
    idb = c.S("identb", [128, 128], BF16)
    onf = c.S("onesf", [128, 128], F32)
    onb = c.S("onesb", [128, 128], BF16)
    c.prog.d("sync", [(idf[:, :], c.inputs["ident_f"][:, :]),
                      (idb[:, :], c.inputs["ident_b"][:, :]),
                      (onf[:, :], c.inputs["ones_f"][:, :]),
                      (onb[:, :], c.inputs["ones_b"][:, :])])


def emit_layernorm_fm(ctx, src_tiles, tc_width=512, out_cb=None,
                      gamma=None, beta=None, scratch_pref="ln"):
    c = ctx
    ntc = T // tc_width
    nm = scratch_pref

    def get(suffix, shape, dt):
        key = nm + suffix
        if key not in c.sb:
            c.S(key, shape, dt)
        return c.sb[key]

    xbf = get("_xbf", [128, tc_width], BF16)
    sqb = get("_sqb", [128, tc_width], BF16)
    mb = get("_mb", [128, tc_width], F32)
    vb = get("_vb", [128, tc_width], F32)
    rb = get("_rb", [128, tc_width], F32)
    mrb = get("_mrb", [128, tc_width], F32)
    t2 = get("_t2", [128, tc_width], F32)
    pss = c.ps["bank0"]
    psq = c.ps["bank1"]
    onb = c.sb["onesb"]

    for tc in range(ntc):
        t0 = tc * tc_width
        for ch in range(NCH):
            src = src_tiles(ch)
            sl = src[:, t0:t0 + tc_width]

            def cast_fn(E, sl=sl):
                return E.tensor_copy(out=xbf[:, :], in_=sl)

            def sq_fn(E, sl=sl):
                return E.activation(out=sqb[:, :], in_=sl, func=AFT.Square)

            c.prog.c("vector", cast_fn)
            c.prog.c("scalar", sq_fn)

            def mm_fn(E, ch=ch):
                E.matmul(pss[:, 0:tc_width], onb[:, :], xbf[:, :],
                         start=(ch == 0), stop=(ch == NCH - 1))
                return E.matmul(psq[:, 0:tc_width], onb[:, :], sqb[:, :],
                                start=(ch == 0), stop=(ch == NCH - 1))

            c.prog.c("tensor", mm_fn)

        c.prog.c("vector", lambda E: E.tensor_single_scalar(
            out=mb[:, :], in_=pss[:, 0:tc_width], scalar=1.0 / D, op=ALU.mult))
        c.prog.c("vector", lambda E: E.tensor_scalar(
            out=vb[:, :], in0=psq[:, 0:tc_width], scalar1=1.0 / D, scalar2=1e-5,
            op0=ALU.mult, op1=ALU.add))
        c.prog.c("vector", lambda E: E.tensor_mul(mrb[:, :], mb[:, :], mb[:, :]))
        c.prog.c("vector", lambda E: E.tensor_sub(vb[:, :], vb[:, :], mrb[:, :]))
        c.prog.c("scalar", lambda E: E.activation(
            out=rb[:, :], in_=vb[:, :], func=AFT.Sqrt))
        c.prog.c("vector", lambda E: E.reciprocal(out=rb[:, :], in_=rb[:, :]))
        c.prog.c("vector", lambda E: E.tensor_mul(mrb[:, :], mb[:, :], rb[:, :]))

        for ch in range(NCH):
            src = src_tiles(ch)
            sl = src[:, t0:t0 + tc_width]

            def ap1(E, sl=sl):
                return E.tensor_mul(t2[:, :], sl, rb[:, :])

            c.prog.c("vector", ap1)
            c.prog.c("vector", lambda E: E.tensor_sub(t2[:, :], t2[:, :], mrb[:, :]))
            out_cb(ch, tc, t2)


def stream_weight(ctx, qname, dst_sb, dram_ap):
    ctx.prog.d("gpsimd", [(dst_sb, dram_ap)])




def build_l1_ffn(ctx, xin, xout):
    c = ctx
    E = 8
    wg = c.din("f_Wg", [8, 128, 8], F32)
    bgb = c.din("f_bg_b", [128, 8], F32)
    w1 = c.din("f_W1", [E, D, D], F32)
    w2 = c.din("f_W2", [E, D, D], F32)
    b1t = c.din("f_b1T", [E, 128, 8], F32)
    b2 = c.din("f_b2", [E, D], F32)

    xmod = c.S("f_xmod", [128, NCH * T], F32)
    xnbf = c.S("f_xn", [128, NCH * T], BF16)
    wgsb = c.S("f_wg", [128, NCH * 8], F32)
    bgsb = c.S("f_bgb", [128, 8], F32)
    b1sb = c.S("f_b1", [128, E * 8], F32)
    b2sb = c.S("f_b2sb", [1, 8 * D], BF16)
    lgs = c.S("f_lgs", [128, 8], F32)
    rmax = c.S("f_rmax", [128, 1], F32)
    nmax = c.S("f_nmax", [128, 1], F32)
    ssum = c.S("f_ssum", [128, 1], F32)
    maskt = c.S("f_mask", [128, 8], F32)
    wselall = c.S("f_wselall", [128, 128], F32)
    wsrow = c.S("f_wsrow", [1, T], F32)
    wsrowb = c.S("f_wsrowb", [1, T], BF16)
    wbe = c.S("f_wbe", [128, 512], F32)
    gl = c.S("f_gl", [128, 512], F32)
    hs = c.S("f_hs", [128, 8 * 512], BF16)
    wa = c.S("f_wa", [128, NCH * 1024], BF16)
    wb = c.S("f_wb", [128, NCH * 1024], BF16)

    bank = [c.ps[f"bank{i}"] for i in range(8)]

    c.prog.d("sync", [
        (wgsb[:, :].rearrange("p (c e) -> p c e", e=8), wg.rearrange("c p e -> p c e")),
        (bgsb[:, :], bgb[:, :]),
        (b1sb[:, :].rearrange("p (e k) -> p e k", k=8), b1t.rearrange("e p k -> p e k"))])
    b2flat = AP(b2.tensor, b2.offset, [[8 * D, 1], [1, 8 * D]])
    c.prog.d("gpsimd", [(b2sb[0:1, :], b2flat)])

    for s in range(SEQ_PER_CORE):
        c.prog.d("sync", [(xmod[:, ch * T:(ch + 1) * T], xin[s, ch])
                          for ch in range(NCH)])

        def consume(ch, tc, t2, s=s):
            def gmm(E_, ch=ch, tc=tc):
                last = None
                for sub in range(4):
                    last = E_.matmul(
                        bank[2 + sub][:, 0:8],
                        t2[:, sub * 128:(sub + 1) * 128],
                        wgsb[:, ch * 8:(ch + 1) * 8],
                        start=(ch == 0), stop=(ch == NCH - 1))
                return last

            c.prog.c("tensor", gmm)

            def xcast(E_, ch=ch, tc=tc):
                return E_.activation(
                    out=xnbf[:, ch * T + tc * 512: ch * T + tc * 512 + 512],
                    in_=t2[:, :], func=AFT.Copy)

            c.prog.c("scalar", xcast)
            if ch == NCH - 1:
                emit_gating(c, tc, bank, bgsb, lgs, rmax, nmax, ssum, maskt,
                            wselall)

        emit_layernorm_fm(ctx, lambda ch: xmod[:, ch * T:(ch + 1) * T],
                          out_cb=consume, scratch_pref="f_ln")

        NTC = 4
        TCW = 512
        for e in range(E):
            c.prog.d("gpsimd", [
                (wa[:, :].rearrange("p (c j) -> p c j", j=1024),
                 w1[e].rearrange("(c p) j -> p c j", p=128)),
                (wb[:, :].rearrange("p (c j) -> p c j", j=1024),
                 w2[e].rearrange("(c p) j -> p c j", p=128))])
            for rnd in range(4):
                for tj in range(4):
                    ti = rnd * 4 + tj

                    def trw(E_, e=e, ti=ti, tj=tj):
                        return E_.transpose(bank[4][0:1, tj * 128:(tj + 1) * 128],
                                            wselall[:, ti * 8 + e: ti * 8 + e + 1],
                                            c.sb["identf"][:, :])

                    c.prog.c("tensor", trw)

                def evr(E_, rnd=rnd):
                    return E_.copy(out=wsrow[0:1, rnd * 512:(rnd + 1) * 512],
                                   in_=bank[4][0:1, 0:512])

                c.prog.c("scalar", evr)

                def evrb(E_, rnd=rnd):
                    return E_.tensor_copy(out=wsrowb[0:1, rnd * 512:(rnd + 1) * 512],
                                          in_=bank[4][0:1, 0:512])

                c.prog.c("vector", evrb)
            for tc in range(NTC):
                t0 = tc * TCW
                def wbmm(E_, e=e, t0=t0):
                    return E_.matmul(bank[6][:, 0:TCW], c.sb["onesf"][0:1, :],
                                     wsrow[0:1, t0:t0 + TCW],
                                     start=True, stop=True)

                c.prog.c("tensor", wbmm)
                c.prog.c("vector", lambda E_: E_.tensor_copy(
                    out=wbe[:, :], in_=bank[6][:, 0:TCW]))

                for hm in range(8):
                    pb = bank[hm % 2]

                    def hmm(E_, e=e, hm=hm, t0=t0, pb=pb):
                        last = None
                        for ch in range(NCH):
                            last = E_.matmul(
                                pb[:, 0:TCW],
                                wa[:, ch * 1024 + hm * 128: ch * 1024 + hm * 128 + 128],
                                xnbf[:, ch * T + t0: ch * T + t0 + TCW],
                                start=(ch == 0), stop=(ch == NCH - 1))
                        return last

                    c.prog.c("tensor", hmm)

                    def gelu_fn(E_, e=e, hm=hm, pb=pb):
                        return E_.activation(out=gl[:, :], in_=pb[:, 0:TCW],
                                             func=AFT.Gelu,
                                             bias=b1sb[:, e * 8 + hm: e * 8 + hm + 1])

                    c.prog.c("scalar", gelu_fn)

                    def hscale(E_, hm=hm):
                        return E_.tensor_mul(hs[:, hm * TCW:(hm + 1) * TCW],
                                             gl[:, :], wbe[:, :])

                    c.prog.c("vector", hscale)

                for co in range(8):
                    pb = bank[2 + co % 2]

                    def w2mm(E_, co=co, pb=pb, e=e, t0=t0):
                        for hm in range(8):
                            E_.matmul(
                                pb[:, 0:TCW],
                                wb[:, hm * 1024 + co * 128: hm * 1024 + co * 128 + 128],
                                hs[:, hm * TCW:(hm + 1) * TCW],
                                start=(hm == 0), stop=False)
                        return E_.matmul(
                            pb[:, 0:TCW],
                            b2sb[0:1, e * D + co * 128: e * D + co * 128 + 128],
                            wsrowb[0:1, t0:t0 + TCW],
                            start=False, stop=True)

                    c.prog.c("tensor", w2mm)
                    dst = xmod[:, co * T + t0: co * T + t0 + TCW]

                    def evf(E_, dst=dst, pb=pb):
                        return E_.tensor_add(dst, dst, pb[:, 0:TCW])

                    c.prog.c("vector", evf)

        c.prog.d("sync", [(xout[s, ch], xmod[:, ch * T:(ch + 1) * T])
                          for ch in range(NCH)])
    c.prog.w("sync")


def emit_gating(c, tc, bank, bgsb, lgs, rmax, nmax, ssum, maskt, wselall):
    for sub in range(4):
        pg = bank[2 + sub][:, 0:8]
        c.prog.c("vector", lambda E_, pg=pg: E_.tensor_add(lgs[:, :], pg, bgsb[:, :]))
        c.prog.c("vector", lambda E_: E_.reduce_max(rmax[:, :], lgs[:, :], axis=AXX))
        c.prog.c("vector", lambda E_: E_.tensor_single_scalar(
            out=nmax[:, :], in_=rmax[:, :], scalar=-1.0, op=ALU.mult))
        c.prog.c("scalar", lambda E_: E_.activation(
            out=lgs[:, :], in_=lgs[:, :], func=AFT.Exp, bias=nmax[:, :]))
        c.prog.c("vector", lambda E_: E_.reduce_sum(ssum[:, :], lgs[:, :], axis=AXX))
        c.prog.c("vector", lambda E_: E_.reciprocal(out=ssum[:, :], in_=ssum[:, :]))
        c.prog.c("vector", lambda E_: E_.tensor_single_scalar(
            out=lgs[:, :], in_=lgs[:, :], scalar=ssum[:, :], op=ALU.mult))
        c.prog.c("vector", lambda E_: E_.reduce_max(rmax[:, :], lgs[:, :], axis=AXX))
        c.prog.c("vector", lambda E_: E_.tensor_tensor(
            out=maskt[:, :], in0=lgs[:, :], in1=bcast_free(rmax[:, 0:1], 8),
            op=ALU.is_equal))
        ti = tc * 4 + sub

        def wselw(E_, ti=ti):
            return E_.tensor_mul(wselall[:, ti * 8:(ti + 1) * 8],
                                 maskt[:, :], lgs[:, :])

        c.prog.c("vector", wselw)




def emit_layernorm_hbm(ctx, src_dram, out_cb, pref):
    c = ctx
    bank = [c.ps[f"bank{i}"] for i in range(8)]

    def get(suffix, shape, dt):
        key = pref + suffix
        if key not in c.sb:
            c.S(key, shape, dt)
        return c.sb[key]

    xch = get("_xch", [128, T], F32)
    xbf = get("_xbf", [128, 512], BF16)
    sqb = get("_sqb", [128, 512], BF16)
    mb = get("_mb", [128, 512], F32)
    vb = get("_vb", [128, 512], F32)
    sc2 = get("_sc2", [128, 512], F32)
    t2 = get("_t2", [128, 512], F32)
    onb = c.sb["onesb"]

    for ch in range(NCH):
        c.prog.d("sync", [(xch[:, :], src_dram(ch))])
        for tc in range(4):
            sl = xch[:, tc * 512:(tc + 1) * 512]
            c.prog.c("vector", lambda E, sl=sl: E.tensor_copy(out=xbf[:, :], in_=sl))
            c.prog.c("scalar", lambda E, sl=sl: E.activation(
                out=sqb[:, :], in_=sl, func=AFT.Square))

            def mm(E, ch=ch, tc=tc):
                E.matmul(bank[tc][:, 0:512], onb[:, :], xbf[:, :],
                         start=(ch == 0), stop=(ch == NCH - 1))
                return E.matmul(bank[4 + tc][:, 0:512], onb[:, :], sqb[:, :],
                                start=(ch == 0), stop=(ch == NCH - 1))

            c.prog.c("tensor", mm)
    for tc in range(4):
        def stat1(E, tc=tc):
            E.tensor_single_scalar(out=mb[:, :], in_=bank[tc][:, 0:512],
                                   scalar=1.0 / D, op=ALU.mult)
            E.tensor_scalar(out=vb[:, :], in0=bank[4 + tc][:, 0:512],
                            scalar1=1.0 / D, scalar2=1e-5, op0=ALU.mult, op1=ALU.add)
            E.tensor_mul(sc2[:, :], mb[:, :], mb[:, :])
            return E.tensor_sub(vb[:, :], vb[:, :], sc2[:, :])

        c.prog.c("vector", stat1)
        c.prog.c("scalar", lambda E: E.activation(out=vb[:, :], in_=vb[:, :],
                                                  func=AFT.Sqrt))

        def stat2(E, tc=tc):
            E.reciprocal(out=bank[tc][:, 0:512], in_=vb[:, :])
            return E.tensor_mul(bank[4 + tc][:, 0:512], mb[:, :],
                                bank[tc][:, 0:512])

        c.prog.c("vector", stat2)
    for ch in range(NCH):
        c.prog.d("sync", [(xch[:, :], src_dram(ch))])
        for tc in range(4):
            sl = xch[:, tc * 512:(tc + 1) * 512]

            def ap1(E, sl=sl, tc=tc):
                E.tensor_mul(t2[:, :], sl, bank[tc][:, 0:512])
                return E.tensor_sub(t2[:, :], t2[:, :], bank[4 + tc][:, 0:512])

            c.prog.c("vector", ap1)
            out_cb(ch, tc, t2)




def emit_qk_proc(c, tgt, ssb, ai, l, ti, rn, S1, S2, QP, xpos_sb, QT):
    c.prog.c("scalar", lambda E: E.activation(out=S1, in_=tgt, func=AFT.Square))
    c.prog.c("vector", lambda E: E.reduce_sum(
        out=col_unit(rn[:, :]), in_=seg_view(S1, 64), axis=AXX))
    c.prog.c("scalar", lambda E: E.activation(
        out=rn[:, :], in_=rn[:, :], func=AFT.Sqrt))

    def rcp(E):
        E.tensor_scalar_max(out=rn[:, :], in0=rn[:, :], scalar1=1e-12)
        return E.reciprocal(out=rn[:, :], in_=rn[:, :])

    c.prog.c("vector", rcp)

    def scl(E):
        rv = AP(rn[:, :].tensor, rn[:, :].offset,
                [list(a) for a in rn[:, :].ap] + [[0, 64]])
        E.tensor_tensor(out=seg_view(tgt, 64), in0=seg_view(tgt, 64),
                        in1=rv, op=ALU.mult)
        sap = ssb[:, l * 64:(l + 1) * 64]
        sv = AP(sap.tensor, sap.offset,
                [list(sap.ap[0])] + [[0, 8], [1, 64]])
        return E.tensor_tensor(out=seg_view(tgt, 64),
                               in0=seg_view(tgt, 64), in1=sv, op=ALU.mult)

    c.prog.c("vector", scl)

    def xposf(E):
        tabA = xpos_sb[:, ai * 64: ai * 64 + 64]
        tabB = xpos_sb[:, (ai + 1) * 64: (ai + 1) * 64 + 64]
        av = AP(tabA.tensor, tabA.offset,
                [list(tabA.ap[0])] + [[0, 8], [1, 64]])
        sw = AP(tgt.tensor, tgt.offset + 32,
                [list(tgt.ap[0])] + [[64, 8], [-32, 2], [1, 32]])
        s1v = AP(S1.tensor, S1.offset,
                 [list(S1.ap[0])] + [[64, 8], [32, 2], [1, 32]])
        bv = AP(tabB.tensor, tabB.offset,
                [list(tabB.ap[0])] + [[0, 8], [32, 2], [1, 32]])
        E.tensor_tensor(out=s1v, in0=sw, in1=bv, op=ALU.mult)
        E.tensor_tensor(out=seg_view(S2, 64), in0=seg_view(tgt, 64),
                        in1=av, op=ALU.mult)
        return E.tensor_add(QP, S2, S1)

    c.prog.c("vector", xposf)


def build_l0_ffn(ctx, xin, xout, dbg=None):
    c = ctx
    NL = 2
    INNER = 512
    wqkv = c.din("g_wqkv", [NL, D, 3 * INNER], F32)
    wo = c.din("g_wo", [NL, INNER, D], F32)
    w1p = c.din("g_w1p", [NL, D, 5888], F32)
    w2p = c.din("g_w2p", [NL, 2816, D], F32)
    qsb_in = c.din("g_qs", [NL, 128, 64], F32)
    ksb_in = c.din("g_ks", [NL, 128, 64], F32)
    lgb_in = c.din("g_lgb", [NL, 2, 128, NCH], F32)
    b1p_in = c.din("g_b1p", [NL, 128, 46], F32)
    b2_in = c.din("g_b2", [NL, 128, NCH], F32)
    xpos = c.din("g_xpos", [4, 16, 128, 64], BF16)
    maskS = c.din("g_maskS", [4, 128, 1024], BF16)
    maskT = c.din("g_maskT", [8, 128, 512], BF16)

    TG = T + 1
    xl = c.S("g_xl", [128, NCH * TG], F32)
    xsh = c.S("g_xsh", [128, NCH * T], BF16)
    vaug = c.S("g_vaug", [128, 16 * 520], BF16)
    qf = c.S("g_qf", [128, 4 * T], BF16)
    kf = c.S("g_kf", [128, 4 * T], BF16)
    ws = c.S("g_ws", [128, 16 * 512], BF16)
    fscr = c.S("g_fscr", [128, 3072], F32)
    bscr = c.S("g_bscr", [128, 1024], BF16)
    mskS_sb = c.S("g_mskS", [128, 4 * 1024], BF16)
    mskT_sb = c.S("g_mskT", [128, 8 * 512], BF16)
    xpos_sb = c.S("g_xpossb", [128, 4 * 64], BF16)
    qs_sb = c.S("g_qssb", [128, NL * 64], F32)
    ks_sb = c.S("g_kssb", [128, NL * 64], F32)
    lgb_sb = c.S("g_lgbsb", [128, NL * 2 * NCH], F32)
    b1p_sb = c.S("g_b1psb", [128, NL * 46], F32)
    b2_sb = c.S("g_b2sb", [128, NL * NCH], F32)
    rn = c.S("g_rn", [128, 8], F32)
    negm = c.S("g_negm", [128, 4], F32)
    mrow = fscr[0:1, 2560:3072]
    rrow = fscr[0:1, 2048:2560]
    mbt = fscr[:, 1024:1536]
    sst = fscr[:, 0:1024]
    of32 = fscr[0:64, 1536:2048]
    pt = bscr[:, 0:512]
    c.sb["g_ln_mb"] = fscr[:, 0:512]
    c.sb["g_ln_vb"] = fscr[:, 512:1024]
    c.sb["g_ln_sc2"] = fscr[:, 1024:1536]
    c.sb["g_ln_t2"] = fscr[:, 1536:2048]
    c.sb["g_ln_xbf"] = bscr[:, 0:512]
    c.sb["g_ln_sqb"] = bscr[:, 512:1024]
    QT = fscr[:, 0:512]
    KT = fscr[:, 512:1024]
    S1 = fscr[:, 1024:1536]
    S2 = fscr[:, 1536:2048]
    S3 = fscr[:, 2048:2560]
    QP = fscr[:, 2048:2560]
    T3 = fscr[:, 2560:3072]
    HP = None

    bank = [c.ps[f"bank{i}"] for i in range(8)]

    c.prog.d("sync", [
        (mskS_sb[:, :].rearrange("p (q j) -> p q j", j=1024),
         maskS.rearrange("q p j -> p q j")),
        (mskT_sb[:, :].rearrange("p (k j) -> p k j", j=512),
         maskT.rearrange("k p j -> p k j")),
        (qs_sb[:, :].rearrange("p (l j) -> p l j", j=64), qsb_in.rearrange("l p j -> p l j")),
        (ks_sb[:, :].rearrange("p (l j) -> p l j", j=64), ksb_in.rearrange("l p j -> p l j")),
        (lgb_sb[:, :].rearrange("p (l g k) -> p l g k", g=2, k=NCH),
         lgb_in.rearrange("l g p k -> p l g k")),
        (b1p_sb[:, :].rearrange("p (l k) -> p l k", k=46), b1p_in.rearrange("l p k -> p l k")),
        (b2_sb[:, :].rearrange("p (l k) -> p l k", k=NCH), b2_in.rearrange("l p k -> p l k")),
    ])

    def xg(ch):
        return xl[:, ch * TG + 1: ch * TG + 1 + T]

    def xsh_slice(ch, t0, w):
        off = ch * TG + (1 if ch < 4 else 0)
        return xl[:, off + t0: off + t0 + w]

    def wslab(dram_col_slice, slot, nch_rows=NCH):
        dst = ws[:, slot * 512 * nch_rows:(slot + 1) * 512 * nch_rows]
        c.prog.d("gpsimd", [(dst.rearrange("p (c j) -> p c j", j=512),
                             dram_col_slice.rearrange("(c p) j -> p c j", p=128))])

    for s in range(SEQ_PER_CORE):
        def gz(E):
            a0 = xl[0:128, 0:1]
            apv = AP(a0.tensor, a0.offset, [list(a0.ap[0])] + [[TG, NCH], [1, 1]])
            return E.memset(apv, 0.0)

        c.prog.c("vector", gz)

        def lncb(ch, tc, t2):
            c.prog.c("vector", lambda E, ch=ch, tc=tc: E.tensor_copy(
                out=xg(ch)[:, tc * 512:(tc + 1) * 512], in_=t2[:, :]))

        emit_layernorm_hbm(c, lambda ch, s=s: xin[s, ch], lncb, "g_ln")
        if dbg == ("xl",):
            for co in range(NCH):
                c.prog.d("sync", [(xout[s, co], xg(co))])
            continue

        for l in range(NL):
            for ch in range(NCH):
                c.prog.c("vector", lambda E, ch=ch: E.tensor_copy(
                    out=xsh[:, ch * T:(ch + 1) * T], in_=xsh_slice(ch, 0, T)))
            if dbg is not None and dbg[0] == "xsh" and dbg[1] == l:
                for co in range(NCH):
                    c.prog.c("vector", lambda E, co=co: E.tensor_copy(
                        out=fscr[:, 0:2048], in_=xsh[:, co * T:(co + 1) * T]))
                    c.prog.d("sync", [(xout[s, co], fscr[:, 0:2048])])
                break
            wslab(wqkv[l][:, 1024:1536], 0)
            for ti in range(16):
                def vmm(E, ti=ti):
                    last = None
                    for ci in range(NCH):
                        last = E.matmul(
                            bank[2][:, 0:512],
                            xsh[:, ci * T + ti * 128: ci * T + ti * 128 + 128],
                            ws[:, ci * 512: ci * 512 + 512],
                            start=(ci == 0), stop=(ci == NCH - 1))
                    return last

                c.prog.c("tensor", vmm)

                def vev(E, ti=ti):
                    dst = vaug[:, ti * 520: (ti + 1) * 520]
                    dstv = AP(dst.tensor, dst.offset,
                              [list(dst.ap[0])] + [[65, 8], [1, 64]])
                    return E.tensor_copy(out=dstv,
                                         in_=seg_view(bank[2][:, 0:512], 64))

                c.prog.c("vector", vev)

            for sl in range(2):
                wslab(wqkv[l][:, sl * 512:(sl + 1) * 512], sl)
            if dbg is not None and dbg[0] == "ws" and dbg[1] == l:
                for co in range(8):
                    c.prog.c("vector", lambda E, co=co: E.tensor_copy(
                        out=fscr[:, 0:2048],
                        in_=ws[:, co * 1024:(co + 1) * 1024]))
                    c.prog.d("sync", [(xout[s, co, :, 0:2048], fscr[:, 0:2048])])
                break
            for ti in range(16):
                def qkmm(E, ti=ti):
                    last = None
                    for sl in range(2):
                        for ci in range(NCH):
                            last = E.matmul(
                                bank[sl][:, 0:512],
                                xsh[:, ci * T + ti * 128: ci * T + ti * 128 + 128],
                                ws[:, sl * NCH * 512 + ci * 512: sl * NCH * 512 + ci * 512 + 512],
                                start=(ci == 0), stop=(ci == NCH - 1))
                    return last

                c.prog.c("tensor", qkmm)
                c.prog.d("sync", [(xpos_sb[:, :].rearrange("p (a j) -> p a j", j=64),
                                   xpos[:, ti].rearrange("a p j -> p a j"))])

                for which, bsrc, dst_fm, ssb, ai in ((0, 0, qf, qs_sb, 0),
                                                     (1, 1, kf, ks_sb, 2)):
                    tgt = QT if which == 0 else KT
                    c.prog.c("vector", lambda E, bsrc=bsrc, tgt=tgt: E.tensor_copy(
                        out=tgt, in_=bank[bsrc][:, 0:512]))
                    if dbg is not None and dbg[0] == "qraw":
                        def rawc(E, tgt=tgt):
                            return E.tensor_copy(out=QP, in_=tgt)

                        c.prog.c("vector", rawc)
                        skip_proc = True
                    else:
                        skip_proc = False
                    if not skip_proc:
                        emit_qk_proc(c, tgt, ssb, ai, l, ti, rn, S1, S2, QP,
                                     xpos_sb, QT)
                    for cq in range(4):
                        c.prog.c("tensor", lambda E, cq=cq: E.transpose(
                            bank[6][:, 0:128], QP[:, cq * 128:(cq + 1) * 128],
                            c.sb["identf"][:, :]))
                        c.prog.c("vector", lambda E, cq=cq, ti=ti, dst_fm=dst_fm:
                                 E.tensor_copy(
                                     out=dst_fm[:, cq * T + ti * 128: cq * T + ti * 128 + 128],
                                     in_=bank[6][:, 0:128]))

            def vones(E):
                a0 = vaug[:, 64:65]
                apv = AP(a0.tensor, a0.offset,
                         [list(a0.ap[0])] + [[520, 16], [65, 8], [1, 1]])
                return E.memset(apv, 1.0)

            c.prog.c("vector", vones)

            OF = xsh
            for w in range(4):
                kt_lo = 4 if w == 0 else 0
                key0 = 512 * (w - 1)
                for h in range(8):
                    hch = h // 2
                    hrow = (h % 2) * 64
                    for qt in range(4):
                        def smm(E, w=w, h=h, qt=qt, hch=hch, hrow=hrow):
                            last = None
                            for half in range(2):
                                if w == 0 and half == 0:
                                    continue
                                t0 = 512 * (w - 1) + half * 512
                                last = E.matmul(
                                    bank[half][:, 0:512],
                                    qf[hrow:hrow + 64,
                                       hch * T + w * 512 + qt * 128:
                                       hch * T + w * 512 + qt * 128 + 128],
                                    kf[hrow:hrow + 64, hch * T + t0: hch * T + t0 + 512],
                                    start=True, stop=True)
                            return last

                        c.prog.c("tensor", smm)

                        def sred(E, w=w, qt=qt):
                            for half in range(2):
                                if w == 0 and half == 0:
                                    continue
                                E.tensor_add(
                                    sst[:, half * 512:(half + 1) * 512],
                                    bank[half][:, 0:512],
                                    mskS_sb[:, qt * 1024 + half * 512:
                                            qt * 1024 + half * 512 + 512])
                            if w == 0:
                                E.reduce_max(rn[:, 0:1], sst[:, 512:1024], axis=AXX)
                            else:
                                E.reduce_max(rn[:, 0:1], sst[:, 0:1024], axis=AXX)
                            return E.tensor_single_scalar(
                                out=negm[:, qt:qt + 1], in_=rn[:, 0:1],
                                scalar=-1.0, op=ALU.mult)

                        c.prog.c("vector", sred)

                    def mtr(E):
                        last = None
                        for j in range(4):
                            last = E.transpose(bank[6][0:1, j * 128:(j + 1) * 128],
                                               negm[:, j:j + 1], c.sb["identf"][:, :])
                        return last

                    c.prog.c("tensor", mtr)
                    c.prog.c("scalar", lambda E: E.copy(out=mrow,
                                                        in_=bank[6][0:1, 0:512]))
                    c.prog.c("tensor", lambda E: E.matmul(
                        bank[6][:, 0:512], c.sb["onesf"][0:1, :], mrow,
                        start=True, stop=True))
                    c.prog.c("vector", lambda E: E.tensor_copy(
                        out=mbt, in_=bank[6][:, 0:512]))
                    for kt in range(kt_lo, 8):
                        ktt = 4 * (w - 1) + kt

                        def stmm(E, w=w, kt=kt, hch=hch, hrow=hrow, ktt=ktt):
                            return E.matmul(
                                bank[4][:, 0:512],
                                kf[hrow:hrow + 64, hch * T + ktt * 128: hch * T + ktt * 128 + 128],
                                qf[hrow:hrow + 64, hch * T + w * 512: hch * T + w * 512 + 512],
                                start=True, stop=True)

                        c.prog.c("tensor", stmm)

                        def stsub(E, kt=kt):
                            E.tensor_add(sst[:, 0:512], bank[4][:, 0:512], mbt)
                            return E.tensor_add(
                                sst[:, 0:512], sst[:, 0:512],
                                mskT_sb[:, kt * 512:(kt + 1) * 512])

                        c.prog.c("vector", stsub)
                        c.prog.c("scalar", lambda E: E.activation(
                            out=pt, in_=sst[:, 0:512], func=AFT.Exp))

                        def pvmm(E, kt=kt, ktt=ktt, h=h, kt_lo=kt_lo):
                            return E.matmul(
                                bank[5][0:65, 0:512],
                                vaug[:, ktt * 520 + h * 65: ktt * 520 + h * 65 + 65],
                                pt, start=(kt == kt_lo), stop=(kt == 7))

                        c.prog.c("tensor", pvmm)
                    c.prog.c("vector", lambda E: E.reciprocal(
                        out=rrow, in_=bank[5][64:65, 0:512]))
                    c.prog.c("scalar", lambda E: E.copy(out=of32,
                                                        in_=bank[5][0:64, 0:512]))
                    c.prog.c("tensor", lambda E: E.matmul(
                        bank[6][0:64, 0:512], c.sb["onesf"][0:1, 0:64], rrow,
                        start=True, stop=True))

                    def oev(E, w=w, hch=hch, hrow=hrow):
                        return E.tensor_mul(
                            OF[hrow:hrow + 64, hch * T + w * 512: hch * T + w * 512 + 512],
                            of32, bank[6][0:64, 0:512])

                    c.prog.c("vector", oev)

            if dbg is not None and dbg[0] == "of" and dbg[1] == l:
                for ci in range(4):
                    c.prog.c("vector", lambda E, ci=ci: E.tensor_copy(
                        out=fscr[:, 0:2048], in_=OF[:, ci * T:(ci + 1) * T]))
                    c.prog.d("sync", [(xout[s, ci], fscr[:, 0:2048])])
                break
            if dbg is not None and dbg[0] in ("qf", "qraw") and dbg[1] == l:
                for ci in range(4):
                    c.prog.c("vector", lambda E, ci=ci: E.tensor_copy(
                        out=fscr[:, 0:2048], in_=qf[:, ci * T:(ci + 1) * T]))
                    c.prog.d("sync", [(xout[s, ci], fscr[:, 0:2048])])
                    c.prog.c("vector", lambda E, ci=ci: E.tensor_copy(
                        out=fscr[:, 0:2048], in_=kf[:, ci * T:(ci + 1) * T]))
                    c.prog.d("sync", [(xout[s, 4 + ci], fscr[:, 0:2048])])
                break
            for sl in range(2):
                wslab(wo[l][:, sl * 512:(sl + 1) * 512], sl, nch_rows=4)
                for col in range(4):
                    co = sl * 4 + col
                    for tc in range(4):
                        def omm(E, sl=sl, col=col, tc=tc):
                            last = None
                            for ci in range(4):
                                last = E.matmul(
                                    bank[tc % 2][:, 0:512],
                                    ws[:, sl * 4 * 512 + ci * 512 + col * 128:
                                       sl * 4 * 512 + ci * 512 + col * 128 + 128],
                                    OF[:, ci * T + tc * 512: ci * T + tc * 512 + 512],
                                    start=(ci == 0), stop=(ci == 3))
                            return last

                        c.prog.c("tensor", omm)

                        def oadd(E, co=co, tc=tc):
                            dst = xg(co)[:, tc * 512:(tc + 1) * 512]
                            return E.tensor_add(dst, dst, bank[tc % 2][:, 0:512])

                        c.prog.c("vector", oadd)

            if dbg is not None and dbg[0] == "xl1" and dbg[1] == l:
                for co in range(NCH):
                    c.prog.d("sync", [(xout[s, co], xg(co))])
                break
            xn2 = xsh
            for tc in range(4):
                for ch in range(NCH):
                    sl_ = xsh_slice(ch, tc * 512, 512)
                    c.prog.c("vector", lambda E, sl_=sl_: E.tensor_copy(
                        out=c.sb["g_ln_xbf"], in_=sl_))
                    c.prog.c("scalar", lambda E, sl_=sl_: E.activation(
                        out=c.sb["g_ln_sqb"], in_=sl_, func=AFT.Square))

                    def mm(E, ch=ch, tc=tc):
                        E.matmul(bank[2][:, 0:512], c.sb["onesb"][:, :],
                                 c.sb["g_ln_xbf"], start=(ch == 0), stop=(ch == 7))
                        return E.matmul(bank[3][:, 0:512], c.sb["onesb"][:, :],
                                        c.sb["g_ln_sqb"], start=(ch == 0),
                                        stop=(ch == 7))

                    c.prog.c("tensor", mm)

                def stat1(E):
                    E.tensor_single_scalar(out=c.sb["g_ln_mb"], in_=bank[2][:, 0:512],
                                           scalar=1.0 / D, op=ALU.mult)
                    E.tensor_scalar(out=c.sb["g_ln_vb"], in0=bank[3][:, 0:512],
                                    scalar1=1.0 / D, scalar2=1e-5, op0=ALU.mult,
                                    op1=ALU.add)
                    E.tensor_mul(c.sb["g_ln_sc2"], c.sb["g_ln_mb"], c.sb["g_ln_mb"])
                    return E.tensor_sub(c.sb["g_ln_vb"], c.sb["g_ln_vb"],
                                        c.sb["g_ln_sc2"])

                c.prog.c("vector", stat1)
                c.prog.c("scalar", lambda E: E.activation(
                    out=c.sb["g_ln_vb"], in_=c.sb["g_ln_vb"], func=AFT.Sqrt))

                def stat2(E):
                    E.reciprocal(out=c.sb["g_ln_vb"], in_=c.sb["g_ln_vb"])
                    return E.tensor_mul(c.sb["g_ln_sc2"], c.sb["g_ln_mb"],
                                        c.sb["g_ln_vb"])

                c.prog.c("vector", stat2)
                for ch in range(NCH):
                    sl_ = xsh_slice(ch, tc * 512, 512)

                    def ap1(E, sl_=sl_, ch=ch, tc=tc, l=l):
                        E.tensor_mul(c.sb["g_ln_t2"], sl_, c.sb["g_ln_vb"])
                        E.tensor_sub(c.sb["g_ln_t2"], c.sb["g_ln_t2"],
                                     c.sb["g_ln_sc2"])
                        return E.tensor_scalar(
                            out=xn2[:, ch * T + tc * 512: ch * T + tc * 512 + 512],
                            in0=c.sb["g_ln_t2"],
                            scalar1=lgb_sb[:, (l * 2) * NCH + ch: (l * 2) * NCH + ch + 1],
                            scalar2=lgb_sb[:, (l * 2 + 1) * NCH + ch: (l * 2 + 1) * NCH + ch + 1],
                            op0=ALU.mult, op1=ALU.add)

                    c.prog.c("vector", ap1)

            if dbg is not None and dbg[0] == "xn2" and dbg[1] == l:
                for co in range(NCH):
                    c.prog.c("vector", lambda E, co=co: E.tensor_copy(
                        out=fscr[:, 0:2048], in_=xn2[:, co * T:(co + 1) * T]))
                    c.prog.d("sync", [(xout[s, co], fscr[:, 0:2048])])
                break
            HPn = 22
            for tc in range(4):
                for cp in range(HPn):
                    if cp % 4 == 0:
                        wslab(w1p[l][:, (cp // 4) * 512:(cp // 4) * 512 + 512], 0)
                        wslab(w1p[l][:, 2816 + (cp // 4) * 512: 2816 + (cp // 4) * 512 + 512], 1)

                    def hmm(E, cp=cp, tc=tc):
                        last = None
                        for ci in range(NCH):
                            for ab in range(2):
                                last = E.matmul(
                                    bank[ab][:, 0:512],
                                    ws[:, ab * NCH * 512 + ci * 512 + (cp % 4) * 128:
                                       ab * NCH * 512 + ci * 512 + (cp % 4) * 128 + 128],
                                    xn2[:, ci * T + tc * 512: ci * T + tc * 512 + 512],
                                    start=(ci == 0), stop=(ci == NCH - 1))
                        return last

                    c.prog.c("tensor", hmm)

                    def aev(E, cp=cp, l=l):
                        return E.tensor_scalar_add(
                            out=S1, in0=bank[0][:, 0:512],
                            scalar1=b1p_sb[:, l * 46 + cp: l * 46 + cp + 1])

                    c.prog.c("vector", aev)

                    def gev(E, cp=cp, l=l):
                        return E.activation(
                            out=S2, in_=bank[1][:, 0:512], func=AFT.Gelu,
                            bias=b1p_sb[:, l * 46 + 22 + cp: l * 46 + 22 + cp + 1])

                    c.prog.c("scalar", gev)
                    hp_dst = (qf if cp < 11 else kf)
                    hpo = (cp % 11)

                    def pev(E, hp_dst=hp_dst, hpo=hpo):
                        return E.tensor_mul(hp_dst[:, hpo * 512:(hpo + 1) * 512],
                                            S1, S2)

                    c.prog.c("vector", pev)
                for osl in range(2):
                    for rhalf in range(2):
                        dst = ws[:, 0:11 * 512]
                        c.prog.d("gpsimd", [
                            (dst.rearrange("p (c j) -> p c j", j=512),
                             w2p[l][rhalf * 1408:(rhalf + 1) * 1408,
                                    osl * 512:(osl + 1) * 512].rearrange(
                                 "(c p) j -> p c j", p=128))])
                        for col in range(4):
                            def wmm(E, rhalf=rhalf, col=col, tc=tc):
                                last = None
                                for hc in range(11):
                                    cpg = rhalf * 11 + hc
                                    hp_src = (qf if cpg < 11 else kf)
                                    hpo = cpg % 11
                                    last = E.matmul(
                                        bank[4 + col][:, 0:512],
                                        ws[:, hc * 512 + col * 128: hc * 512 + col * 128 + 128],
                                        hp_src[:, hpo * 512:(hpo + 1) * 512],
                                        start=(rhalf == 0 and hc == 0),
                                        stop=(rhalf == 1 and hc == 10))
                                return last

                            c.prog.c("tensor", wmm)
                    for col in range(4):
                        co = osl * 4 + col

                        def wev(E, col=col, co=co, tc=tc, l=l):
                            E.tensor_scalar_add(
                                out=T3, in0=bank[4 + col][:, 0:512],
                                scalar1=b2_sb[:, l * NCH + co: l * NCH + co + 1])
                            dst = xg(co)[:, tc * 512:(tc + 1) * 512]
                            return E.tensor_add(dst, dst, T3)

                        c.prog.c("vector", wev)

        if dbg is not None and dbg != ("fin",):
            continue
        for co in range(NCH):
            c.prog.d("sync", [(fscr[:, 0:2048], xin[s, co])])

            def fin(E, co=co):
                return E.tensor_add(fscr[:, 0:2048], fscr[:, 0:2048], xg(co))

            c.prog.c("vector", fin)
            c.prog.d("sync", [(xout[s, co], fscr[:, 0:2048])])
    c.prog.w("sync")




def emit_conv3(E, dst, src, t0, W, wtap, bias_col, total=T):
    w0c, w1c, w2c = wtap
    last = E.tensor_scalar(out=dst[:, 0:W], in0=src[:, t0:t0 + W], scalar1=w1c,
                           scalar2=bias_col, op0=ALU.mult, op1=ALU.add)
    if t0 == 0:
        last = E.scalar_tensor_tensor(out=dst[:, 1:W], in0=src[:, 0:W - 1],
                                      scalar=w0c, in1=dst[:, 1:W],
                                      op0=ALU.mult, op1=ALU.add)
    else:
        last = E.scalar_tensor_tensor(out=dst[:, 0:W], in0=src[:, t0 - 1:t0 + W - 1],
                                      scalar=w0c, in1=dst[:, 0:W],
                                      op0=ALU.mult, op1=ALU.add)
    if t0 + W == total:
        last = E.scalar_tensor_tensor(out=dst[:, 0:W - 1], in0=src[:, t0 + 1:t0 + W],
                                      scalar=w2c, in1=dst[:, 0:W - 1],
                                      op0=ALU.mult, op1=ALU.add)
    else:
        last = E.scalar_tensor_tensor(out=dst[:, 0:W], in0=src[:, t0 + 1:t0 + W + 1],
                                      scalar=w2c, in1=dst[:, 0:W],
                                      op0=ALU.mult, op1=ALU.add)
    return last


def build_l0_attn(ctx, xin, xout):
    c = ctx
    inw = c.din("m_inw", [D, D], F32)
    actw = c.din("m_actp", [D, D], F32)
    qkw = c.din("m_qkw", [D, 2 * D], F32)
    outw = c.din("m_outw", [D, D], F32)
    fc1 = c.din("m_fc1", [D, 4 * D], F32)
    fc2 = c.din("m_fc2", [4 * D, D], F32)
    convw = c.din("m_convw", [4, 3, 128, NCH], F32)
    convb = c.din("m_convb", [4, 128, NCH], F32)
    norms = c.din("m_norms", [4, 128, NCH], F32)
    pbias = c.din("m_pbias", [128, 64], F32)
    qkbrow = c.din("m_qkb_row", [1, 2 * D], F32)
    ctab = c.din("m_costab", [2, 128, 512], BF16)

    xn = c.S("m_xn", [128, NCH * T], BF16)
    xibf = c.S("m_xibf", [128, NCH * T], BF16)
    attn = c.S("m_attn", [128, NCH * T], BF16)
    ws = c.S("m_ws", [128, 4 * NCH * 512], BF16)
    xip = c.S("m_xip", [128, T], F32)
    tmpf = c.S("m_tmpf", [128, T], F32)
    fscr = c.S("m_fscr", [128, 2048], F32)
    bscr = c.S("m_bscr", [128, 5120], BF16)
    c.sb["m_ln_mb"] = fscr[:, 0:512]
    c.sb["m_ln_vb"] = fscr[:, 512:1024]
    c.sb["m_ln_sc2"] = fscr[:, 1024:1536]
    c.sb["m_ln_t2"] = fscr[:, 1536:2048]
    c.sb["m_ln_xbf"] = bscr[:, 0:512]
    c.sb["m_ln_sqb"] = bscr[:, 512:1024]
    ksrow = xip[0:1, 0:D]
    ksb = c.S("m_ksb", [128, D], F32)
    kvbd = c.S("m_kvbd", [128, NCH * 128], BF16)
    kvacc = c.S("m_kvacc", [64, 1024], F32)
    zt = c.S("m_zt", [128, 16], F32)
    qkbias = c.S("m_qkbias", [128, D], BF16)
    tstg = c.S("m_tstg", [128, 128], F32)
    cwsb = c.S("m_cwsb", [128, 4 * 3 * NCH], F32)
    cbsb = c.S("m_cbsb", [128, 4 * NCH], F32)
    nrmsb = c.S("m_nrmsb", [128, 4 * NCH], F32)
    pbsb = c.S("m_pbsb", [128, 64], F32)
    cossb = c.S("m_cossb", [128, 512], BF16)
    sinsb = c.S("m_sinsb", [128, 512], BF16)

    t3 = fscr[:, 1536:2048]
    bank = [c.ps[f"bank{i}"] for i in range(8)]
    xs = c.nc.dram_tensor("m_xs", [2, NCH, 128, T], F32, kind="Internal").ap()

    def cw(conv, tap, ch):
        return cwsb[:, (conv * 3 + tap) * NCH + ch: (conv * 3 + tap) * NCH + ch + 1]

    def cb(conv, ch):
        return cbsb[:, conv * NCH + ch: conv * NCH + ch + 1]

    c.prog.d("sync", [
        (cwsb[:, :].rearrange("p (c t k) -> p c t k", t=3, k=NCH),
         convw.rearrange("c t p k -> p c t k")),
        (cbsb[:, :].rearrange("p (c k) -> p c k", k=NCH),
         convb.rearrange("c p k -> p c k")),
        (nrmsb[:, :].rearrange("p (c k) -> p c k", k=NCH),
         norms.rearrange("c p k -> p c k")),
        (pbsb[:, :], pbias[:, :]),
        (cossb[:, :], ctab[0]),
        (sinsb[:, :], ctab[1]),
    ])

    def wslab(dram_col_slice, dst_off_slabs):
        dst = ws[:, dst_off_slabs * NCH * 512:(dst_off_slabs + 1) * NCH * 512]
        c.prog.d("gpsimd", [(dst.rearrange("p (c j) -> p c j", j=512),
                             dram_col_slice.rearrange("(c p) j -> p c j", p=128))])

    def fm_proj(dst_cb, src_sb, wslot, couts=4, bias=None):
        for col in range(couts):
            for tc in range(4):
                def pmm(E, col=col, tc=tc, wslot=wslot, src_sb=src_sb):
                    last = None
                    for ci in range(NCH):
                        last = E.matmul(
                            bank[tc % 2][:, 0:512],
                            ws[:, wslot * NCH * 512 + ci * 512 + col * 128:
                               wslot * NCH * 512 + ci * 512 + col * 128 + 128],
                            src_sb[:, ci * T + tc * 512: ci * T + tc * 512 + 512],
                            start=(ci == 0), stop=(ci == NCH - 1))
                    return last

                c.prog.c("tensor", pmm)
                dst_cb(col, tc, bank[tc % 2])

    for s in range(SEQ_PER_CORE):
        def lncb(ch, tc, t2):
            c.prog.c("vector", lambda E, ch=ch, tc=tc: E.tensor_copy(
                out=xip[:, tc * 512:(tc + 1) * 512], in_=t2[:, :]))
            if tc == 3:
                def cvf(E, ch=ch):
                    return emit_conv3(E, tmpf, xip, 0, T,
                                      (cw(0, 0, ch), cw(0, 1, ch), cw(0, 2, ch)),
                                      cb(0, ch))

                c.prog.c("vector", cvf)
                c.prog.c("vector", lambda E: E.tensor_add(tmpf[:, :], tmpf[:, :],
                                                          xip[:, :]))
                c.prog.d("sync", [(xs[1, ch], tmpf[:, :])])

        emit_layernorm_hbm(c, lambda ch, s=s: xin[s, ch], lncb, "m_ln")

        def ln1cb(ch, tc, t2):
            def gb(E, ch=ch, tc=tc):
                return E.tensor_scalar(
                    out=xn[:, ch * T + tc * 512: ch * T + tc * 512 + 512],
                    in0=t2[:, :], scalar1=nrmsb[:, ch:ch + 1],
                    scalar2=nrmsb[:, NCH + ch: NCH + ch + 1],
                    op0=ALU.mult, op1=ALU.add)

            c.prog.c("vector", gb)

        emit_layernorm_hbm(c, lambda ch: xs[1, ch], ln1cb, "m_ln")

        for slab in range(2):
            wslab(inw[:, slab * 512:(slab + 1) * 512], 0)

            def consume(col, tc, pb, slab=slab):
                ch = slab * 4 + col

                def ev(E, col=col, tc=tc, pb=pb, ch=ch):
                    return E.tensor_scalar_add(
                        out=xip[:, tc * 512:(tc + 1) * 512], in0=pb[:, 0:512],
                        scalar1=pbsb[:, ch:ch + 1])

                c.prog.c("vector", ev)
                if tc == 3:
                    def cvf(E, ch=ch):
                        return emit_conv3(E, tmpf, xip, 0, T,
                                          (cw(1, 0, ch), cw(1, 1, ch), cw(1, 2, ch)),
                                          cb(1, ch))

                    c.prog.c("vector", cvf)

                    def sil(E, ch=ch):
                        return E.activation(
                            out=xibf[:, ch * T:(ch + 1) * T], in_=tmpf[:, :],
                            func=AFT.Silu)

                    c.prog.c("scalar", sil)

            fm_proj(consume, xn, 0)

        def qkbmm(E, half=1):
            E.matmul(bank[2][:, 0:512], c.sb["onesf"][0:1, :],
                     qkbrow[0:1, 1024:1536] if False else c.sb["m_qkbr"][0:1, 1024:1536],
                     start=True, stop=True)
            return E.matmul(bank[3][:, 0:512], c.sb["onesf"][0:1, :],
                            c.sb["m_qkbr"][0:1, 1536:2048], start=True, stop=True)

        c.sb["m_qkbr"] = xip[0:1, 0:2 * D]
        c.prog.d("sync", [(c.sb["m_qkbr"][0:1, :], qkbrow[0:1, :])])
        c.prog.c("tensor", qkbmm)

        def qkbev(E):
            E.tensor_copy(out=qkbias[:, 0:512], in_=bank[2][:, 0:512])
            return E.tensor_copy(out=qkbias[:, 512:1024], in_=bank[3][:, 0:512])

        c.prog.c("vector", qkbev)
        wslab(qkw[:, 1024:1536], 0)
        wslab(qkw[:, 1536:2048], 1)
        c.prog.c("vector", lambda E: E.memset(kvacc[0:64, :], 0.0))

        KB, KRB, VT, QB, QBF = (bscr[:, 0:1024], bscr[:, 1024:2048],
                                bscr[:, 2048:3072], bscr[:, 3072:4096],
                                bscr[:, 4096:5120])
        QKS = fscr[:, 0:1024]
        SCR2 = fscr[:, 1024:2048]

        for ti in range(16):
            def kmm(E, ti=ti):
                last = None
                for half in range(2):
                    for ci in range(NCH):
                        last = E.matmul(
                            bank[half][:, 0:512],
                            xibf[:, ci * T + ti * 128: ci * T + ti * 128 + 128],
                            ws[:, half * NCH * 512 + ci * 512: half * NCH * 512 + ci * 512 + 512],
                            start=(ci == 0), stop=(ci == NCH - 1))
                return last

            c.prog.c("tensor", kmm)
            emit_elu1(c, bank, qkbias, QKS, SCR2, tmpf)
            c.prog.c("vector", lambda E: E.tensor_copy(out=KB[:, :], in_=QKS[:, :]))

            def ksmm(E, ti=ti):
                E.matmul(bank[2][0:1, 0:512], c.sb["onesb"][:, 0:1], KB[:, 0:512],
                         start=(ti == 0), stop=(ti == 15))
                return E.matmul(bank[3][0:1, 0:512], c.sb["onesb"][:, 0:1],
                                KB[:, 512:1024], start=(ti == 0), stop=(ti == 15))

            c.prog.c("tensor", ksmm)
            emit_rope(c, QKS, KRB, SCR2, cossb, sinsb)
            for ci in range(NCH):
                c.prog.c("vector", lambda E, ci=ci, ti=ti: E.tensor_copy(
                    out=tstg[:, :], in_=xibf[:, ci * T + ti * 128: ci * T + ti * 128 + 128]))
                c.prog.c("tensor", lambda E, ci=ci: E.transpose(
                    bank[6][:, 0:128], tstg[:, :], c.sb["identf"][:, :]))
                c.prog.c("vector", lambda E, ci=ci: E.tensor_copy(
                    out=VT[:, ci * 128:(ci + 1) * 128], in_=bank[6][:, 0:128]))

            for h in range(16):
                pb = bank[4 + h % 2]

                def kvmm(E, h=h, pb=pb):
                    return E.matmul(pb[0:64, 0:64],
                                    KRB[:, h * 64:(h + 1) * 64],
                                    VT[:, h * 64:(h + 1) * 64],
                                    start=True, stop=True)

                c.prog.c("tensor", kvmm)

                def kvacc_f(E, h=h, pb=pb):
                    dst = kvacc[0:64, h * 64:(h + 1) * 64]
                    return E.tensor_add(dst, dst, pb[0:64, 0:64])

                c.prog.c("vector", kvacc_f)

        c.prog.c("vector", lambda E: E.memset(kvbd[:, :], 0.0))

        def kvev(E):
            last = None
            for h in range(16):
                r = (h % 2) * 64
                last = E.tensor_copy(
                    out=kvbd[r:r + 64, (h // 2) * 128 + r: (h // 2) * 128 + r + 64],
                    in_=kvacc[0:64, h * 64:(h + 1) * 64])
            return last

        c.prog.c("vector", kvev)

        def ksev(E):
            E.tensor_copy(out=ksrow[0:1, 0:512], in_=bank[2][0:1, 0:512])
            return E.tensor_copy(out=ksrow[0:1, 512:1024], in_=bank[3][0:1, 0:512])

        c.prog.c("vector", ksev)

        def ksbmm(E):
            E.matmul(bank[0][:, 0:512], c.sb["onesf"][0:1, :], ksrow[0:1, 0:512],
                     start=True, stop=True)
            return E.matmul(bank[1][:, 0:512], c.sb["onesf"][0:1, :],
                            ksrow[0:1, 512:1024], start=True, stop=True)

        c.prog.c("tensor", ksbmm)

        def ksbev(E):
            E.tensor_copy(out=ksb[:, 0:512], in_=bank[0][:, 0:512])
            return E.tensor_copy(out=ksb[:, 512:1024], in_=bank[1][:, 0:512])

        c.prog.c("vector", ksbev)

        c.prog.d("sync", [(c.sb["m_qkbr"][0:1, :], qkbrow[0:1, :])])

        def qkbmm2(E):
            E.matmul(bank[2][:, 0:512], c.sb["onesf"][0:1, :],
                     c.sb["m_qkbr"][0:1, 0:512], start=True, stop=True)
            return E.matmul(bank[3][:, 0:512], c.sb["onesf"][0:1, :],
                            c.sb["m_qkbr"][0:1, 512:1024], start=True, stop=True)

        c.prog.c("tensor", qkbmm2)
        c.prog.c("vector", qkbev)
        wslab(qkw[:, 0:512], 0)
        wslab(qkw[:, 512:1024], 1)

        for ti in range(16):
            def qmm(E, ti=ti):
                last = None
                for half in range(2):
                    for ci in range(NCH):
                        last = E.matmul(
                            bank[half][:, 0:512],
                            xibf[:, ci * T + ti * 128: ci * T + ti * 128 + 128],
                            ws[:, half * NCH * 512 + ci * 512: half * NCH * 512 + ci * 512 + 512],
                            start=(ci == 0), stop=(ci == NCH - 1))
                return last

            c.prog.c("tensor", qmm)
            emit_elu1(c, bank, qkbias, QKS, SCR2, tmpf)
            c.prog.c("vector", lambda E: E.tensor_mul(SCR2[:, :], QKS[:, :], ksb[:, :]))

            def zred(E):
                E.reduce_sum(out=col_unit(zt[:, :]),
                             in_=seg_view(SCR2[:, 0:1024], 64), axis=AXX)
                E.tensor_single_scalar(out=zt[:, :], in_=zt[:, :],
                                       scalar=T * 1e-6, op=ALU.add)
                return E.reciprocal(out=zt[:, :], in_=zt[:, :])

            c.prog.c("vector", zred)
            emit_rope(c, QKS, None, SCR2, cossb, sinsb, out_f32=tmpf)
            def zap(E):
                zv = AP(zt[:, :].tensor, zt[:, :].offset,
                        [list(a) for a in zt[:, :].ap] + [[0, 64]])
                return E.tensor_tensor(out=seg_view(QB[:, :], 64),
                                       in0=seg_view(tmpf[:, 0:1024], 64),
                                       in1=zv, op=ALU.mult)

            c.prog.c("vector", zap)
            for ci in range(NCH):
                c.prog.c("vector", lambda E, ci=ci: E.tensor_copy(
                    out=tstg[:, :], in_=QB[:, ci * 128:(ci + 1) * 128]))
                c.prog.c("tensor", lambda E, ci=ci: E.transpose(
                    bank[6][:, 0:128], tstg[:, :], c.sb["identf"][:, :]))
                c.prog.c("vector", lambda E, ci=ci: E.tensor_copy(
                    out=QBF[:, ci * 128:(ci + 1) * 128], in_=bank[6][:, 0:128]))

            def amm(E, ti=ti):
                last = None
                for ci in range(NCH):
                    pb = bank[2 + ci // 4]
                    last = E.matmul(pb[:, (ci % 4) * 128:(ci % 4) * 128 + 128],
                                    kvbd[:, ci * 128:(ci + 1) * 128],
                                    QBF[:, ci * 128:(ci + 1) * 128],
                                    start=True, stop=True)
                return last

            c.prog.c("tensor", amm)

            def aev(E, ti=ti):
                last = None
                for ci in range(NCH):
                    pb = bank[2 + ci // 4]
                    last = E.tensor_copy(
                        out=attn[:, ci * T + ti * 128: ci * T + ti * 128 + 128],
                        in_=pb[:, (ci % 4) * 128:(ci % 4) * 128 + 128])
                return last

            c.prog.c("vector", aev)

        wslab(actw[:, 0:512], 0)
        wslab(actw[:, 512:1024], 1)
        wslab(outw[:, 0:512], 2)
        wslab(outw[:, 512:1024], 3)
        ACT2 = fscr[:, 0:512]
        LEP = fscr[:, 512:1024]
        TC2 = fscr[:, 1024:1536]
        OWIN = bscr[:, 0:4096]
        for tc in range(4):
            for m in range(NCH):
                def apmm(E, m=m, tc=tc):
                    last = None
                    for ci in range(NCH):
                        last = E.matmul(
                            bank[tc % 2][:, 0:512],
                            ws[:, (m // 4) * NCH * 512 + ci * 512 + (m % 4) * 128:
                               (m // 4) * NCH * 512 + ci * 512 + (m % 4) * 128 + 128],
                            xn[:, ci * T + tc * 512: ci * T + tc * 512 + 512],
                            start=(ci == 0), stop=(ci == NCH - 1))
                    return last

                c.prog.c("tensor", apmm)

                def asil(E, m=m, tc=tc):
                    return E.activation(out=ACT2[:, :], in_=bank[tc % 2][:, 0:512],
                                        func=AFT.Silu, bias=pbsb[:, 8 + m: 8 + m + 1])

                c.prog.c("scalar", asil)

                def lepcv(E, m=m, tc=tc):
                    return emit_conv3(E, LEP, xibf[:, m * T:(m + 1) * T], tc * 512,
                                      512, (cw(2, 0, m), cw(2, 1, m), cw(2, 2, m)),
                                      cb(2, m))

                c.prog.c("vector", lepcv)

                def comb(E, m=m, tc=tc):
                    E.tensor_add(TC2[:, :], LEP[:, :],
                                 attn[:, m * T + tc * 512: m * T + tc * 512 + 512])
                    return E.tensor_mul(OWIN[:, m * 512:(m + 1) * 512],
                                        TC2[:, :], ACT2[:, :])

                c.prog.c("vector", comb)
            for co in range(NCH):
                def omm(E, co=co, tc=tc):
                    last = None
                    for ci in range(NCH):
                        last = E.matmul(
                            bank[2 + tc % 2][:, 0:512],
                            ws[:, (2 + co // 4) * NCH * 512 + ci * 512 + (co % 4) * 128:
                               (2 + co // 4) * NCH * 512 + ci * 512 + (co % 4) * 128 + 128],
                            OWIN[:, ci * 512:(ci + 1) * 512],
                            start=(ci == 0), stop=(ci == NCH - 1))
                    return last

                c.prog.c("tensor", omm)
                c.prog.d("sync", [(t3[:, :], xs[1, co, :, tc * 512:(tc + 1) * 512])])

                def oev(E, co=co, tc=tc):
                    E.tensor_scalar_add(out=TC2[:, :], in0=bank[2 + tc % 2][:, 0:512],
                                        scalar1=pbsb[:, 16 + co: 16 + co + 1])
                    return E.tensor_add(TC2[:, :], TC2[:, :], t3[:, :])

                c.prog.c("vector", oev)
                c.prog.d("sync", [(xs[1, co, :, tc * 512:(tc + 1) * 512], TC2[:, :])])

        for ch in range(NCH):
            c.prog.d("sync", [(c.sb["m_ln_xch"][:, :], xs[1, ch])])

            def cvf(E, ch=ch):
                return emit_conv3(E, tmpf, c.sb["m_ln_xch"], 0, T,
                                  (cw(3, 0, ch), cw(3, 1, ch), cw(3, 2, ch)),
                                  cb(3, ch))

            c.prog.c("vector", cvf)
            c.prog.c("vector", lambda E: E.tensor_add(tmpf[:, :], tmpf[:, :],
                                                      c.sb["m_ln_xch"][:, :]))
            c.prog.d("sync", [(xs[0, ch], tmpf[:, :])])

        def ln2cb(ch, tc, t2):
            def gb(E, ch=ch, tc=tc):
                return E.tensor_scalar(
                    out=xn[:, ch * T + tc * 512: ch * T + tc * 512 + 512],
                    in0=t2[:, :], scalar1=nrmsb[:, 2 * NCH + ch: 2 * NCH + ch + 1],
                    scalar2=nrmsb[:, 3 * NCH + ch: 3 * NCH + ch + 1],
                    op0=ALU.mult, op1=ALU.add)

            c.prog.c("vector", gb)

        emit_layernorm_hbm(c, lambda ch: xs[0, ch], ln2cb, "m_ln")

        H = xibf
        for tc in range(4):
            for fslab in range(8):
                wslab(fc1[:, fslab * 512:(fslab + 1) * 512], fslab % 2)
                for col in range(4):
                    hm = fslab * 4 + col

                    def hmm(E, col=col, tc=tc, fslab=fslab):
                        last = None
                        for ci in range(NCH):
                            last = E.matmul(
                                bank[col % 2][:, 0:512],
                                ws[:, (fslab % 2) * NCH * 512 + ci * 512 + col * 128:
                                   (fslab % 2) * NCH * 512 + ci * 512 + col * 128 + 128],
                                xn[:, ci * T + tc * 512: ci * T + tc * 512 + 512],
                                start=(ci == 0), stop=(ci == NCH - 1))
                        return last

                    c.prog.c("tensor", hmm)

                    def gev(E, hm=hm, col=col):
                        return E.activation(
                            out=H[:, hm * 512:(hm + 1) * 512],
                            in_=bank[col % 2][:, 0:512], func=AFT.Gelu,
                            bias=pbsb[:, 32 + hm: 32 + hm + 1])

                    c.prog.c("scalar", gev)
            for oslab in range(2):
                dst = ws[:, 0:4 * NCH * 512]
                c.prog.d("gpsimd", [
                    (dst.rearrange("p (c j) -> p c j", j=512),
                     fc2[:, oslab * 512:(oslab + 1) * 512].rearrange(
                         "(c p) j -> p c j", p=128))])
                for col in range(4):
                    co = oslab * 4 + col

                    def fmm(E, col=col, tc=tc):
                        last = None
                        for hc in range(32):
                            last = E.matmul(
                                bank[4 + col % 2][:, 0:512],
                                ws[:, hc * 512 + col * 128: hc * 512 + col * 128 + 128],
                                H[:, hc * 512:(hc + 1) * 512],
                                start=(hc == 0), stop=(hc == 31))
                        return last

                    c.prog.c("tensor", fmm)
                    c.prog.d("sync", [
                        (t3[:, :], xs[0, co, :, tc * 512:(tc + 1) * 512]),
                        (xip[:, 0:512], xin[s, co, :, tc * 512:(tc + 1) * 512])])

                    def fev(E, co=co, tc=tc, col=col):
                        E.tensor_scalar_add(out=TC2[:, :],
                                            in0=bank[4 + col % 2][:, 0:512],
                                            scalar1=pbsb[:, 24 + co: 24 + co + 1])
                        E.tensor_add(TC2[:, :], TC2[:, :], t3[:, :])
                        return E.tensor_add(TC2[:, :], TC2[:, :], xip[:, 0:512])

                    c.prog.c("vector", fev)
                    c.prog.d("sync", [(xout[s, co, :, tc * 512:(tc + 1) * 512],
                                       TC2[:, :])])
    c.prog.w("sync")


def emit_elu1(c, bank, qkbias, QKS, SCR2, tmpf):
    def e1(E):
        E.tensor_add(QKS[:, 0:512], bank[0][:, 0:512], qkbias[:, 0:512])
        return E.tensor_add(QKS[:, 512:1024], bank[1][:, 0:512], qkbias[:, 512:1024])

    c.prog.c("vector", e1)
    c.prog.c("vector", lambda E: E.tensor_scalar_min(
        out=tmpf[:, 0:1024], in0=QKS[:, :], scalar1=0.0))
    c.prog.c("scalar", lambda E: E.activation(out=tmpf[:, 0:1024],
                                              in_=tmpf[:, 0:1024], func=AFT.Exp))
    c.prog.c("vector", lambda E: E.tensor_relu(out=SCR2[:, :], in_=QKS[:, :]))
    c.prog.c("vector", lambda E: E.tensor_add(QKS[:, :], tmpf[:, 0:1024],
                                              SCR2[:, :]))


def emit_rope(c, QKS, KRB, SCR2, cossb, sinsb, out_f32=None):
    xe = QKS[:, 0:1024:2]
    xo = QKS[:, 1:1024:2]
    if out_f32 is None:
        oe = KRB[:, 0:1024:2]
        oo = KRB[:, 1:1024:2]
    else:
        oe = out_f32[:, 0:1024:2]
        oo = out_f32[:, 1:1024:2]

    def r1(E):
        E.tensor_mul(SCR2[:, 0:512], xe, cossb[:, :])
        E.tensor_mul(SCR2[:, 512:1024], xo, sinsb[:, :])
        return E.tensor_sub(oe, SCR2[:, 0:512], SCR2[:, 512:1024])

    c.prog.c("vector", r1)

    def r2(E):
        E.tensor_mul(SCR2[:, 0:512], xe, sinsb[:, :])
        E.tensor_mul(SCR2[:, 512:1024], xo, cossb[:, :])
        return E.tensor_add(oo, SCR2[:, 0:512], SCR2[:, 512:1024])

    c.prog.c("vector", r2)




def build_l1_attn(ctx, xin, xout):
    c = ctx
    NH = 16
    HD = 64
    wqkv = c.din("a_wqkvT", [D, 3 * D], F32)
    wproj = c.din("a_wprojT", [D, D], F32)
    bqkvT = c.din("a_bqkvT", [128, 24], F32)
    bprojT = c.din("a_bprojT", [128, 8], F32)
    bvrow = c.din("a_bv_row", [1, D], F32)
    qaug = c.din("a_qaug", [NH, 64, T], BF16)
    kaug = c.din("a_kaug", [NH, 64, T], BF16)
    masks = c.din("a_masks", [2, 128, 128], BF16)
    posi = c.din("a_posi", [128, 4], F32)
    ind8 = c.din("a_ind8", [128, 8], F32)

    qfm = c.S("a_qfm", [128, NCH * T], BF16)
    kfm = c.S("a_kfm", [128, NCH * T], BF16)
    vaug = c.S("a_vaug", [128, 16 * 1040], BF16)
    xnbf = c.S("a_xn", [128, NCH * T], BF16)
    qa = c.S("a_qa", [128, T], BF16)
    ka = c.S("a_ka", [128, T], BF16)
    wsl = c.S("a_wsl", [128, NCH * 512], BF16)
    msk = c.S("a_msk", [128, 2 * 128], BF16)
    bq8 = c.S("a_bq8", [128, 8], F32)
    bqkv_sb = c.S("a_bqkv", [128, 24], F32)
    bproj_sb = c.S("a_bproj", [128, 8], F32)
    bvb = c.S("a_bvb", [128, D], F32)
    stt = c.S("a_stt", [128, 512], F32)
    pt = c.S("a_pt", [128, 512], BF16)
    negm = c.S("a_negm", [128, 16], F32)
    rmq = c.S("a_rmq", [128, 8], F32)
    mrow = c.S("a_mrow", [1, 512], F32)
    mbt = c.S("a_mbt", [128, 512], F32)
    of32 = c.S("a_of32", [64, 512], F32)
    rrow = c.S("a_rrow", [1, 512], F32)
    vsc = c.S("a_vsc", [128, 512], F32)
    t3 = c.S("a_t3", [128, 512], F32)
    ofm = xnbf

    bank = [c.ps[f"bank{i}"] for i in range(8)]

    posisb = c.S("a_posisb", [128, 4], F32)
    ind8sb = c.S("a_ind8sb", [128, 8], F32)
    chsb = c.S("a_chsb", [128, 8], F32)
    c.prog.d("sync", [(msk[:, :].rearrange("p (m j) -> p m j", j=128),
                   masks.rearrange("m p j -> p m j")),
                      (bqkv_sb[:, :], bqkvT[:, :]),
                      (bproj_sb[:, :], bprojT[:, :]),
                      (posisb[:, :], posi[:, :]),
                      (ind8sb[:, :], ind8[:, :])])
    c.prog.c("vector", lambda E: E.tensor_single_scalar(
        out=bq8[:, :], in_=bqkv_sb[:, 0:8], scalar=0.125, op=ALU.mult))
    def chprep(E):
        E.tensor_tensor(out=chsb[:, 0:2], in0=posisb[:, 0:4:2],
                        in1=posisb[:, 1:4:2], op=ALU.add)
        psv = AP(chsb[:, 0:2].tensor, chsb[:, 0:2].offset,
                 [list(chsb[:, 0:2].ap[0])] + [[1, 2], [0, 4]])
        return E.tensor_tensor(out=stt[:, 0:8].rearrange("p (a g) -> p a g", g=4),
                               in0=ind8sb[:, 0:8].rearrange("p (a g) -> p a g", g=4),
                               in1=psv, op=ALU.mult)

    c.prog.c("vector", chprep)
    c.prog.c("tensor", lambda E: E.matmul(
        c.ps["bank0"][:, 0:8], c.sb["onesf"][:, :], stt[:, 0:8],
        start=True, stop=True))
    c.prog.c("vector", lambda E: E.tensor_copy(out=chsb[:, :],
                                               in_=c.ps["bank0"][:, 0:8]))
    bvr = c.S("a_bvr", [1, D], F32)
    c.prog.d("sync", [(bvr[0:1, :], bvrow[0:1, :])])

    for s in range(SEQ_PER_CORE):
        def consume(ch, tc, t2):
            def xcast(E, ch=ch, tc=tc):
                return E.activation(
                    out=xnbf[:, ch * T + tc * 512: ch * T + tc * 512 + 512],
                    in_=t2[:, :], func=AFT.Copy)

            c.prog.c("scalar", xcast)

        emit_layernorm_hbm(c, lambda ch, s=s: xin[s, ch], consume, "a_ln")

        def bvmm(E):
            E.matmul(bank[0][:, 0:512], c.sb["onesf"][0:1, :],
                     c.sb["a_bvr"][0:1, 0:512], start=True, stop=True)
            return E.matmul(bank[1][:, 0:512], c.sb["onesf"][0:1, :],
                            c.sb["a_bvr"][0:1, 512:1024], start=True, stop=True)

        c.prog.c("tensor", bvmm)

        def bvev(E):
            E.tensor_copy(out=bvb[:, 0:512], in_=bank[0][:, 0:512])
            return E.tensor_copy(out=bvb[:, 512:1024], in_=bank[1][:, 0:512])

        c.prog.c("vector", bvev)

        for slab in range(6):
            c0 = slab * 512
            c.prog.d("gpsimd", [
                (wsl[:, :].rearrange("p (c j) -> p c j", j=512),
                 wqkv[:, c0:c0 + 512].rearrange("(c p) j -> p c j", p=128))])
            if slab < 4:
                for col in range(4):
                    co = (c0 + col * 128) // 128
                    dstbase = (qfm if slab < 2 else kfm)
                    och = co % NCH
                    for tc in range(4):
                        def pmm(E, col=col, tc=tc):
                            last = None
                            for ci in range(NCH):
                                last = E.matmul(
                                    bank[tc % 2][:, 0:512],
                                    wsl[:, ci * 512 + col * 128: ci * 512 + col * 128 + 128],
                                    xnbf[:, ci * T + tc * 512: ci * T + tc * 512 + 512],
                                    start=(ci == 0), stop=(ci == NCH - 1))
                            return last

                        c.prog.c("tensor", pmm)
                        if slab < 2:
                            bias = bq8[:, och:och + 1]
                            scale = 0.125
                        else:
                            bias = bqkv_sb[:, 8 + och: 8 + och + 1]
                            scale = 1.0

                        def pev(E, dstbase=dstbase, och=och, tc=tc, bias=bias,
                                scale=scale):
                            return E.tensor_scalar(
                                out=dstbase[:, och * T + tc * 512: och * T + tc * 512 + 512],
                                in0=bank[tc % 2][:, 0:512], scalar1=scale,
                                scalar2=bias, op0=ALU.mult, op1=ALU.add)

                        c.prog.c("vector", pev)
            else:
                half = slab - 4
                for ti in range(16):
                    def vmm(E, ti=ti):
                        last = None
                        for ci in range(NCH):
                            last = E.matmul(
                                bank[ti % 2][:, 0:512],
                                xnbf[:, ci * T + ti * 128: ci * T + ti * 128 + 128],
                                wsl[:, ci * 512: ci * 512 + 512],
                                start=(ci == 0), stop=(ci == NCH - 1))
                        return last

                    c.prog.c("tensor", vmm)

                    def vev(E, ti=ti, half=half):
                        E.tensor_add(vsc[:, :], bank[ti % 2][:, 0:512],
                                     bvb[:, half * 512: half * 512 + 512])
                        dst = vaug[:, ti * 1040 + half * 520: (ti + 1) * 1040]
                        dstv = AP(dst.tensor, dst.offset,
                                  [list(dst.ap[0])] + [[65, 8], [1, 64]])
                        srcv = vsc[:, :].rearrange("p (h d) -> p h d", d=64)
                        return E.tensor_copy(out=dstv, in_=srcv)

                    c.prog.c("vector", vev)

        def vones(E):
            a0 = vaug[:, 64:65]
            apv = AP(a0.tensor, a0.offset,
                     [list(a0.ap[0])] + [[1040, 16], [65, 16], [1, 1]])
            return E.memset(apv, 1.0)

        c.prog.c("vector", vones)

        for h in range(16):
            hch = h // 2
            hrow = (h % 2) * 64
            c.prog.c("vector", lambda E, hch=hch, hrow=hrow: E.tensor_copy(
                out=qa[0:64, :], in_=qfm[hrow:hrow + 64, hch * T:(hch + 1) * T]))
            c.prog.c("vector", lambda E, hch=hch, hrow=hrow: E.tensor_copy(
                out=ka[0:64, :], in_=kfm[hrow:hrow + 64, hch * T:(hch + 1) * T]))
            c.prog.d("sync", [(qa[64:128, :], qaug[h]), (ka[64:128, :], kaug[h])])

            for qt in range(16):
                nkeys = (qt + 1) * 128
                nfull = (qt * 128) // 512
                rem = (qt * 128) % 512

                def smm(E, qt=qt, nkeys=nkeys):
                    last = None
                    for kb in range((nkeys + 511) // 512):
                        w = min(512, nkeys - kb * 512)
                        last = E.matmul(
                            bank[kb][:, 0:w], qa[:, qt * 128:(qt + 1) * 128],
                            ka[:, kb * 512: kb * 512 + w],
                            start=True, stop=True)
                    return last

                c.prog.c("tensor", smm)

                def sred(E, qt=qt, nfull=nfull, rem=rem):
                    db = (qt * 128) // 512
                    doff = (qt * 128) % 512
                    E.tensor_add(stt[:, 0:128], bank[db][:, doff:doff + 128],
                                 msk[:, 0:128])
                    np_ = 0
                    for kb in range(nfull):
                        E.reduce_max(rmq[:, np_:np_ + 1], bank[kb][:, 0:512],
                                     axis=AXX)
                        np_ += 1
                    if rem > 0:
                        E.reduce_max(rmq[:, np_:np_ + 1], bank[nfull][:, 0:rem],
                                     axis=AXX)
                        np_ += 1
                    E.reduce_max(rmq[:, np_:np_ + 1], stt[:, 0:128], axis=AXX)
                    np_ += 1
                    E.reduce_max(rmq[:, 7:8], rmq[:, 0:np_], axis=AXX)
                    return E.tensor_single_scalar(
                        out=negm[:, qt:qt + 1], in_=rmq[:, 7:8], scalar=-1.0,
                        op=ALU.mult)

                c.prog.c("vector", sred)

            for g in range(4):
                def mtr(E, g=g):
                    last = None
                    for j in range(4):
                        last = E.transpose(bank[6][0:1, j * 128:(j + 1) * 128],
                                           negm[:, 4 * g + j: 4 * g + j + 1],
                                           c.sb["identf"][:, :])
                    return last

                c.prog.c("tensor", mtr)
                c.prog.c("scalar", lambda E: E.copy(out=mrow[0:1, :],
                                                    in_=bank[6][0:1, 0:512]))
                c.prog.c("tensor", lambda E: E.matmul(
                    bank[6][:, 0:512], c.sb["onesf"][0:1, :], mrow[0:1, :],
                    start=True, stop=True))
                c.prog.c("vector", lambda E: E.tensor_copy(out=mbt[:, :],
                                                           in_=bank[6][:, 0:512]))
                if h >= 8:
                    c.prog.c("vector", lambda E, h=h: E.tensor_scalar_add(
                        out=mbt[:, :], in0=mbt[:, :],
                        scalar1=chsb[:, h - 8: h - 7]))
                for kt in range(4 * g + 4):
                    def stmm(E, kt=kt, g=g):
                        return E.matmul(
                            bank[4][:, 0:512], ka[:, kt * 128:(kt + 1) * 128],
                            qa[:, g * 512:(g + 1) * 512], start=True, stop=True)

                    c.prog.c("tensor", stmm)

                    def stsub(E, kt=kt, g=g):
                        last = E.tensor_add(stt[:, :], bank[4][:, 0:512], mbt[:, :])
                        qsub = kt - 4 * g
                        if 0 <= qsub <= 3:
                            last = E.tensor_add(
                                stt[:, qsub * 128:(qsub + 1) * 128],
                                stt[:, qsub * 128:(qsub + 1) * 128],
                                msk[:, 128:256])
                        return last

                    c.prog.c("vector", stsub)
                    c.prog.c("scalar", lambda E: E.activation(
                        out=pt[:, :], in_=stt[:, :], func=AFT.Exp))
                    nmask = min(4, kt - 4 * g)
                    if nmask > 0:
                        c.prog.c("vector", lambda E, nmask=nmask: E.memset(
                            pt[:, 0:nmask * 128], 0.0))

                    def pvmm(E, kt=kt, g=g, h=h):
                        return E.matmul(
                            bank[5][0:65, 0:512],
                            vaug[:, kt * 1040 + h * 65: kt * 1040 + h * 65 + 65],
                            pt[:, :], start=(kt == 0), stop=(kt == 4 * g + 3))

                    c.prog.c("tensor", pvmm)
                c.prog.c("vector", lambda E: E.reciprocal(
                    out=rrow[0:1, :], in_=bank[5][64:65, 0:512]))
                c.prog.c("scalar", lambda E: E.copy(out=of32[:, :],
                                                    in_=bank[5][0:64, 0:512]))
                c.prog.c("tensor", lambda E: E.matmul(
                    bank[6][0:64, 0:512], c.sb["onesf"][0:1, 0:64], rrow[0:1, :],
                    start=True, stop=True))

                def oev(E, g=g, hch=hch, hrow=hrow):
                    return E.tensor_mul(
                        ofm[hrow:hrow + 64, hch * T + g * 512: hch * T + g * 512 + 512],
                        of32[:, :], bank[6][0:64, 0:512])

                c.prog.c("vector", oev)

        for slab in range(2):
            c0 = slab * 512
            c.prog.d("gpsimd", [
                (wsl[:, :].rearrange("p (c j) -> p c j", j=512),
                 wproj[:, c0:c0 + 512].rearrange("(c p) j -> p c j", p=128))])
            for col in range(4):
                co = slab * 4 + col
                c.prog.d("sync", [(c.sb["a_ln_xch"][:, :], xin[s, co])])
                for tc in range(4):
                    def omm(E, col=col, tc=tc):
                        last = None
                        for ci in range(NCH):
                            last = E.matmul(
                                bank[tc % 2][:, 0:512],
                                wsl[:, ci * 512 + col * 128: ci * 512 + col * 128 + 128],
                                ofm[:, ci * T + tc * 512: ci * T + tc * 512 + 512],
                                start=(ci == 0), stop=(ci == NCH - 1))
                        return last

                    c.prog.c("tensor", omm)

                    def oev2(E, co=co, tc=tc):
                        return E.tensor_scalar_add(out=t3[:, :],
                                                   in0=bank[tc % 2][:, 0:512],
                                                   scalar1=bproj_sb[:, co:co + 1])

                    c.prog.c("vector", oev2)

                    def oadd(E, tc=tc):
                        xs = c.sb["a_ln_xch"][:, tc * 512:(tc + 1) * 512]
                        return E.tensor_add(t3[:, :], t3[:, :], xs)

                    c.prog.c("vector", oadd)
                    c.prog.d("sync", [(xout[s, co, :, tc * 512:(tc + 1) * 512],
                                       t3[:, :])])
    c.prog.w("sync")




def bT(v):
    v = np.asarray(v, np.float32)
    return np.ascontiguousarray(v.reshape(-1, 128).T)


def common_consts():
    idf = np.eye(128, dtype=np.float32)
    return {
        "ident_f": idf,
        "ident_b": idf.astype(ml_dtypes.bfloat16),
        "ones_f": np.ones((128, 128), np.float32),
        "ones_b": np.ones((128, 128), ml_dtypes.bfloat16),
    }


def l0_ffn_host_inputs(inp):
    g = lambda k: np.asarray(inp["L0_ffn_" + k], np.float32)
    NL = 2
    w1 = g("w1")
    w2 = g("w2")
    b1 = g("b1")
    w1p = np.zeros((NL, D, 5888), np.float32)
    w1p[:, :, 0:2730] = w1[:, :, 0:2730]
    w1p[:, :, 2816:2816 + 2730] = w1[:, :, 2730:5460]
    w2p = np.zeros((NL, 2816, D), np.float32)
    w2p[:, 0:2730, :] = w2
    b1pad = np.zeros((NL, 5888), np.float32)
    b1pad[:, 0:2730] = b1[:, 0:2730]
    b1pad[:, 2816:2816 + 2730] = b1[:, 2730:5460]
    half_idx = np.arange(0, 64, 2, dtype=np.float64)
    inv_freq = 1.0 / 10000.0 ** (half_idx / 64.0)
    t = np.arange(T, dtype=np.float64)
    fr = t[:, None] * inv_freq[None, :]
    cos = np.cos(fr)
    sin = np.sin(fr)
    base = (half_idx + 0.4 * 64) / (1.4 * 64)
    power = (t - T // 2) / 256.0
    scale = base[None, :] ** power[:, None]
    sgn = np.concatenate([-np.ones(32), np.ones(32)])
    cc = np.concatenate([cos, cos], axis=1)
    ss = np.concatenate([sin, sin], axis=1)
    sc = np.concatenate([scale, scale], axis=1)
    Aq = 8.0 * cc * sc
    BqS = 8.0 * ss * sc * sgn[None, :]
    Ak = cc / sc
    BkS = ss / sc * sgn[None, :]
    xpos = np.stack([a.reshape(16, 128, 64) for a in (Aq, BqS, Ak, BkS)])
    irel = (np.arange(512))
    maskS = np.full((4, 128, 1024), NEG, np.float32)
    for qt in range(4):
        i = 128 * qt + np.arange(128)
        j = np.arange(1024)
        ok = (i[:, None] <= j[None, :]) & (j[None, :] <= i[:, None] + 512)
        maskS[qt][ok] = 0.0
    maskT = np.full((8, 128, 512), NEG, np.float32)
    for kt in range(8):
        jj = 128 * kt + np.arange(128)
        q = np.arange(512)
        ok = (q[None, :] <= jj[:, None]) & (jj[:, None] <= q[None, :] + 512)
        maskT[kt][ok] = 0.0
    return {
        "g_wqkv": g("wqkv"),
        "g_wo": g("wo"),
        "g_w1p": w1p,
        "g_w2p": w2p,
        "g_qs": np.ascontiguousarray(np.broadcast_to(
            g("q_scale")[:, None, :], (NL, 128, 64))),
        "g_ks": np.ascontiguousarray(np.broadcast_to(
            g("k_scale")[:, None, :], (NL, 128, 64))),
        "g_lgb": np.stack([np.stack([bT(g("ln_g")[l]), bT(g("ln_b")[l])])
                           for l in range(NL)]),
        "g_b1p": np.stack([bT(b1pad[l]) for l in range(NL)]),
        "g_b2": np.stack([bT(g("b2")[l]) for l in range(NL)]),
        "g_xpos": xpos.astype(ml_dtypes.bfloat16),
        "g_maskS": maskS.astype(ml_dtypes.bfloat16),
        "g_maskT": maskT.astype(ml_dtypes.bfloat16),
    }


def l0_attn_host_inputs(inp):
    g = lambda k: np.asarray(inp["L0_attn_" + k], np.float32)

    def convT(w):
        w = np.asarray(w, np.float32)[:, 0, :]
        return np.stack([bT(w[:, t]) for t in range(3)])

    theta = 1.0 / 10000.0 ** (np.arange(512, dtype=np.float64) / 512.0)
    cosb = np.broadcast_to(np.cos(theta).astype(np.float32), (128, 512))
    sinb = np.broadcast_to(np.sin(theta).astype(np.float32), (128, 512))
    pbias = np.zeros((128, 64), np.float32)
    pbias[:, 0:8] = bT(g("in_b"))
    pbias[:, 8:16] = bT(g("actp_b"))
    pbias[:, 16:24] = bT(g("out_b"))
    pbias[:, 24:32] = bT(g("fc2_b"))
    pbias[:, 32:64] = bT(g("fc1_b"))
    norms = np.stack([bT(g("norm1_g")), bT(g("norm1_b")),
                      bT(g("norm2_g")), bT(g("norm2_b"))])
    return {
        "m_inw": g("in_w"), "m_actp": g("actp_w"), "m_qkw": g("qk_w"),
        "m_outw": g("out_w"), "m_fc1": g("fc1_w"), "m_fc2": g("fc2_w"),
        "m_convw": np.stack([convT(g("cpe1_w")), convT(g("dwc_w")),
                             convT(g("lepe_w")), convT(g("cpe2_w"))]),
        "m_convb": np.stack([bT(g("cpe1_b")), bT(g("dwc_b")),
                             bT(g("lepe_b")), bT(g("cpe2_b"))]),
        "m_norms": norms,
        "m_pbias": pbias,
        "m_qkb_row": np.ascontiguousarray(g("qk_b").reshape(1, 2 * D)),
        "m_costab": np.stack([cosb, sinb]).astype(ml_dtypes.bfloat16),
    }


def l1_attn_host_inputs(inp):
    wqkv = np.asarray(inp["L1_attn_w_qkv"], np.float32)
    wproj = np.asarray(inp["L1_attn_w_proj"], np.float32)
    bqkv = np.asarray(inp["L1_attn_b_qkv"], np.float32)
    bproj = np.asarray(inp["L1_attn_b_proj"], np.float32)
    k = np.arange(256, dtype=np.float64)
    freqs = 1.0 / 10000.0 ** (k / 512.0)
    i = np.arange(T, dtype=np.float64)
    th = i[None, :] * freqs[:, None]
    cos = np.cos(th)
    sin = np.sin(th)
    qaug = np.zeros((16, 64, T), np.float32)
    kaug = np.zeros((16, 64, T), np.float32)
    for h in range(8):
        qaug[h, 0:32] = cos[32 * h:32 * h + 32]
        qaug[h, 32:64] = sin[32 * h:32 * h + 32]
        kaug[h, 0:32] = (cos - sin)[32 * h:32 * h + 32]
        kaug[h, 32:64] = (cos + sin)[32 * h:32 * h + 32]
    ii = np.arange(128)
    maskS = np.where(ii[None, :] > ii[:, None], NEG, 0.0).astype(np.float32)
    masks = np.stack([maskS, maskS.T])
    posi = np.asarray(inp["L1_attn_pos_independent"], np.float32)
    posi4 = np.ascontiguousarray(posi.reshape(2, 128, 2).transpose(1, 0, 2)
                                 .reshape(128, 4))
    r = np.arange(128)
    ind8 = np.zeros((128, 8), np.float32)
    for h in range(8):
        ind8[:, h] = (r // 32 == h % 4).astype(np.float32)
    return {
        "a_posi": posi4,
        "a_ind8": ind8,
        "a_wqkvT": np.ascontiguousarray(wqkv.T),
        "a_wprojT": np.ascontiguousarray(wproj.T),
        "a_bqkvT": bT(bqkv),
        "a_bprojT": bT(bproj),
        "a_bv_row": np.ascontiguousarray(bqkv[2048:].reshape(1, D)),
        "a_qaug": qaug.astype(ml_dtypes.bfloat16),
        "a_kaug": kaug.astype(ml_dtypes.bfloat16),
        "a_masks": masks.astype(ml_dtypes.bfloat16),
    }


def l1_ffn_host_inputs(inp):
    return {
        "f_Wg": np.ascontiguousarray(
            np.asarray(inp["L1_ffn_Wg"], np.float32).reshape(8, 128, 8)),
        "f_bg_b": np.ascontiguousarray(
            np.broadcast_to(np.asarray(inp["L1_ffn_bg"], np.float32), (128, 8))),
        "f_W1": np.asarray(inp["L1_ffn_W1"], np.float32),
        "f_W2": np.asarray(inp["L1_ffn_W2"], np.float32),
        "f_b1T": np.stack([bT(b) for b in np.asarray(inp["L1_ffn_b1"], np.float32)]),
        "f_b2": np.asarray(inp["L1_ffn_b2"], np.float32),
    }


def fm_of(x_seqs):
    s, t, d = x_seqs.shape
    xt = np.ascontiguousarray(x_seqs.transpose(0, 2, 1))
    return np.ascontiguousarray(xt.reshape(s, NCH, 128, t))


def fm_back(xfm):
    s = xfm.shape[0]
    return np.ascontiguousarray(
        xfm.reshape(s, D, T).transpose(0, 2, 1))




def build_program_l0ffn(dbg=None):
    nc = bass.Bass("TRN2", target_bir_lowering=False, detect_race_conditions=False)
    ctx = Ctx(nc)
    for i in range(8):
        ctx.P(f"bank{i}", [128, 512], F32)
    xin = ctx.din("xin", [SEQ_PER_CORE, NCH, 128, T], F32)
    xout = ctx.dout("xout", [SEQ_PER_CORE, NCH, 128, T], F32)
    setup_common(ctx)
    build_l0_ffn(ctx, xin, xout, dbg=dbg)
    ctx.prog.emit()
    return ctx


def run_l0_ffn(x_full, inp, n_cores=NCORES, dbg=None):
    ctx = build_program_l0ffn(dbg=dbg)
    consts = common_consts()
    winp = l0_ffn_host_inputs(inp)
    in_maps = []
    for cix in range(n_cores):
        xs = x_full[cix * SEQ_PER_CORE:(cix + 1) * SEQ_PER_CORE]
        m = {"xin": fm_of(xs)}
        m.update(consts)
        m.update(winp)
        in_maps.append(m)
    res = run_bass_kernel_spmd(ctx.nc, in_maps, core_ids=list(range(n_cores)))
    outs = [fm_back(r["xout"]) for r in res.results]
    return np.concatenate(outs, axis=0)


def build_program_l0attn():
    nc = bass.Bass("TRN2", target_bir_lowering=False, detect_race_conditions=False)
    ctx = Ctx(nc)
    for i in range(8):
        ctx.P(f"bank{i}", [128, 512], F32)
    xin = ctx.din("xin", [SEQ_PER_CORE, NCH, 128, T], F32)
    xout = ctx.dout("xout", [SEQ_PER_CORE, NCH, 128, T], F32)
    setup_common(ctx)
    build_l0_attn(ctx, xin, xout)
    ctx.prog.emit()
    return ctx


def run_l0_attn(x_full, inp, n_cores=NCORES):
    ctx = build_program_l0attn()
    consts = common_consts()
    winp = l0_attn_host_inputs(inp)
    in_maps = []
    for cix in range(n_cores):
        xs = x_full[cix * SEQ_PER_CORE:(cix + 1) * SEQ_PER_CORE]
        m = {"xin": fm_of(xs)}
        m.update(consts)
        m.update(winp)
        in_maps.append(m)
    res = run_bass_kernel_spmd(ctx.nc, in_maps, core_ids=list(range(n_cores)))
    outs = [fm_back(r["xout"]) for r in res.results]
    return np.concatenate(outs, axis=0)


def build_program_l1attn():
    nc = bass.Bass("TRN2", target_bir_lowering=False, detect_race_conditions=False)
    ctx = Ctx(nc)
    for i in range(8):
        ctx.P(f"bank{i}", [128, 512], F32)
    xin = ctx.din("xin", [SEQ_PER_CORE, NCH, 128, T], F32)
    xout = ctx.dout("xout", [SEQ_PER_CORE, NCH, 128, T], F32)
    setup_common(ctx)
    build_l1_attn(ctx, xin, xout)
    ctx.prog.emit()
    return ctx


def run_l1_attn(x_full, inp, n_cores=NCORES):
    ctx = build_program_l1attn()
    consts = common_consts()
    winp = l1_attn_host_inputs(inp)
    in_maps = []
    for cix in range(n_cores):
        xs = x_full[cix * SEQ_PER_CORE:(cix + 1) * SEQ_PER_CORE]
        m = {"xin": fm_of(xs)}
        m.update(consts)
        m.update(winp)
        in_maps.append(m)
    res = run_bass_kernel_spmd(ctx.nc, in_maps, core_ids=list(range(n_cores)))
    outs = [fm_back(r["xout"]) for r in res.results]
    return np.concatenate(outs, axis=0)


def build_program_l1ffn():
    nc = bass.Bass("TRN2", target_bir_lowering=False, detect_race_conditions=False)
    ctx = Ctx(nc)
    for i in range(8):
        ctx.P(f"bank{i}", [128, 512], F32)
    xin = ctx.din("xin", [SEQ_PER_CORE, NCH, 128, T], F32)
    xout = ctx.dout("xout", [SEQ_PER_CORE, NCH, 128, T], F32)
    setup_common(ctx)
    build_l1_ffn(ctx, xin, xout)
    ctx.prog.emit()
    return ctx


def run_l1_ffn(x_full, inp, n_cores=NCORES):
    ctx = build_program_l1ffn()
    consts = common_consts()
    winp = l1_ffn_host_inputs(inp)
    in_maps = []
    for cix in range(n_cores):
        xs = x_full[cix * SEQ_PER_CORE:(cix + 1) * SEQ_PER_CORE]
        m = {"xin": fm_of(xs)}
        m.update(consts)
        m.update(winp)
        in_maps.append(m)
    res = run_bass_kernel_spmd(ctx.nc, in_maps, core_ids=list(range(n_cores)))
    outs = [fm_back(r["xout"]) for r in res.results]
    return np.concatenate(outs, axis=0)


_CTX_CACHE = {}


def _get_ctx(name, builder):
    if name not in _CTX_CACHE:
        _CTX_CACHE[name] = builder()
    return _CTX_CACHE[name]


def _run_block(name, builder, host_inputs, x_full, n_cores=NCORES):
    ctx = _get_ctx(name, builder)
    consts = common_consts()
    in_maps = []
    for cix in range(n_cores):
        xs = x_full[cix * SEQ_PER_CORE:(cix + 1) * SEQ_PER_CORE]
        m = {"xin": fm_of(xs)}
        m.update(consts)
        m.update(host_inputs)
        in_maps.append(m)
    res = run_bass_kernel_spmd(ctx.nc, in_maps, core_ids=list(range(n_cores)))
    outs = [fm_back(r["xout"]) for r in res.results]
    return np.concatenate(outs, axis=0)


def kernel(**inputs):
    x = np.ascontiguousarray(np.asarray(inputs["L0_attn_x"], np.float32))
    x = _run_block("b1", build_program_l0attn, l0_attn_host_inputs(inputs), x)
    x = _run_block("b2", build_program_l0ffn, l0_ffn_host_inputs(inputs), x)
    x = _run_block("b3", build_program_l1attn, l1_attn_host_inputs(inputs), x)
    x = _run_block("b4", build_program_l1ffn, l1_ffn_host_inputs(inputs), x)
    return x


# version 09
# speedup vs baseline: 1.4835x; 1.4835x over previous
import numpy as np
import ml_dtypes
import concourse.bass as bass
import concourse.mybir as mybir
from concourse.ap import AP
from concourse.bass_utils import run_bass_kernel_spmd

F32 = mybir.dt.float32
BF16 = mybir.dt.bfloat16
AFT = mybir.ActivationFunctionType
ALU = mybir.AluOpType
AXX = mybir.AxisListType.X

NCORES = 8
SEQ_PER_CORE = 2
T = 2048
D = 1024
NCH = 8
NEG = -3.0e38



def bcast_free(ap, n):
    return AP(ap.tensor, ap.offset, [list(a) for a in ap.ap] + [[0, n]])


def seg_view(ap, d):
    return ap.rearrange("p (h d) -> p h d", d=d)


def col_unit(ap):
    return AP(ap.tensor, ap.offset, [list(a) for a in ap.ap] + [[1, 1]])



_ENG_CSEM = {"tensor": "t", "vector": "v", "scalar": "s", "gpsimd": "g", "sync": None}
_ENG_DSEM = {"sync": "sd", "gpsimd": "gd", "scalar": "scd"}


class Prog:
    def __init__(self, nc):
        self.nc = nc
        self.sems = {k: nc.alloc_semaphore("sem_" + k)
                     for k in ("t", "v", "s", "g", "sd", "gd", "scd")}
        self.cnt = {k: 0 for k in self.sems}
        self.steps = []

    def c(self, eng, fn, upto=None):
        self.steps.append((eng, fn, "c", upto))

    def d(self, eng, pairs, upto=None):
        self.steps.append((eng, pairs, "d", upto))

    def w(self, eng):
        self.steps.append((eng, None, "w", None))

    def mark(self):
        return len(self.steps)

    def emit(self):
        nc = self.nc
        sems, cnt = self.sems, self.cnt
        last_seen = {e: {k: 0 for k in self.cnt} for e in _ENG_CSEM}
        per_eng = {e: [] for e in _ENG_CSEM}
        snaps = []
        for eng, payload, kind, upto in self.steps:
            snaps.append(dict(cnt))
            own_c = _ENG_CSEM[eng]
            tgt = cnt if upto is None else snaps[upto]
            waits = []
            for k in cnt:
                c = tgt[k]
                if k == own_c:
                    continue
                if c > last_seen[eng][k]:
                    waits.append((k, c))
                    last_seen[eng][k] = c
            per_eng[eng].append((waits, payload, kind))
            if kind == "c":
                cnt[own_c] += 1
            elif kind == "d":
                dq = _ENG_DSEM[eng]
                prs = payload
                assert not callable(prs)
                cnt[dq] += 16 * len(prs)

        import os
        chunk_default = int(os.environ.get("PROG_BODY_CHUNK", "1000000"))
        chunk_eng = os.environ.get("PROG_BODY_CHUNK_ENGS", "vector:1")
        chunk_map = {}
        for kv in chunk_eng.split(","):
            if ":" in kv:
                k, v = kv.split(":")
                chunk_map[k] = int(v)

        def mk_body(eng, items):
            def body(E):
                own_c = _ENG_CSEM[eng]
                for waits, payload, kind in items:
                    for k, v in waits:
                        E.wait_ge(sems[k], v)
                    if kind == "c":
                        ret = payload(E)
                        assert ret is not None, "compute step must return inst"
                        ret.then_inc(sems[own_c], 1)
                    elif kind == "d":
                        dq = _ENG_DSEM[eng]
                        for out_ap, in_ap in payload:
                            E.dma_start(out=out_ap, in_=in_ap).then_inc(
                                sems[dq], 16)
            return body

        with nc.Block() as block:
            for eng in per_eng:
                items = per_eng[eng]
                chunk = chunk_map.get(eng, chunk_default)
                for i0 in range(0, len(items), chunk):
                    getattr(block, eng)(mk_body(eng, items[i0:i0 + chunk]))
        return nc


class Pipe:

    def __init__(self, prog):
        self.prog = prog
        self.marks = [prog.mark()]

    def up(self, depth=2):
        return self.marks[max(0, len(self.marks) - depth)]

    def bump(self):
        self.marks.append(self.prog.mark())




class Ctx:

    def __init__(self, nc, pool_kb=None):
        self.nc = nc
        self.prog = Prog(nc)
        self.sb = {}
        self.ps = {}
        self.inputs = {}
        self.outputs = {}
        self.pool = None
        if pool_kb is not None:
            self.pool_bytes = pool_kb * 1024
            ph = nc.alloc_sbuf_tensor("pool", [128, self.pool_bytes // 2], BF16)
            self.pool = {BF16: ph.ap() if hasattr(ph, "ap") else ph[:, :],
                         F32: ph.bitcast(F32)[:, :] if hasattr(ph, "bitcast")
                         else None}
            if self.pool[F32] is None:
                self.pool[F32] = ph[:, :].bitcast(F32)
            self.pool_off = 0
            self.pool_hwm = 0

    def pool_reset(self):
        self.pool_hwm = max(self.pool_hwm, self.pool_off)
        self.pool_off = 0

    def din(self, name, shape, dtype=F32):
        t = self.nc.dram_tensor(name, list(shape), dtype, kind="ExternalInput").ap()
        self.inputs[name] = t
        return t

    def dout(self, name, shape, dtype=F32):
        t = self.nc.dram_tensor(name, list(shape), dtype, kind="ExternalOutput").ap()
        self.outputs[name] = t
        return t

    def S(self, name, shape, dtype, pool=True):
        if self.pool is None or not pool:
            h = self.nc.alloc_sbuf_tensor(name, list(shape), dtype)
            self.sb[name] = h
            return h
        isz = 2 if dtype == BF16 else 4
        cols = shape[1]
        off = (self.pool_off + 511) // 512 * 512
        need = cols * isz
        assert off + need <= self.pool_bytes, (name, off, need, self.pool_bytes)
        self.pool_off = off + need
        base = self.pool[dtype]
        sl = base[:, off // isz: off // isz + cols]
        if shape[0] != 128:
            sl = sl[0:shape[0], :]
        self.sb[name] = sl
        return sl

    def P(self, name, shape, dtype=F32):
        h = self.nc.alloc_psum_tensor(name, list(shape), dtype)
        self.ps[name] = h
        return h


def setup_common(ctx):
    c = ctx
    c.din("ident_f", [128, 128], F32)
    c.din("ident_b", [128, 128], BF16)
    c.din("ones_f", [128, 128], F32)
    c.din("ones_b", [128, 128], BF16)
    idf = c.S("identf", [128, 128], F32, pool=False)
    idb = c.S("identb", [128, 128], BF16, pool=False)
    onf = c.S("onesf", [128, 128], F32, pool=False)
    onb = c.S("onesb", [128, 128], BF16, pool=False)
    c.prog.d("sync", [(idf[:, :], c.inputs["ident_f"][:, :]),
                      (idb[:, :], c.inputs["ident_b"][:, :]),
                      (onf[:, :], c.inputs["ones_f"][:, :]),
                      (onb[:, :], c.inputs["ones_b"][:, :])])


def emit_layernorm_fm(ctx, src_tiles, tc_width=512, out_cb=None,
                      gamma=None, beta=None, scratch_pref="ln"):
    c = ctx
    ntc = T // tc_width
    nm = scratch_pref

    def get(suffix, shape, dt):
        key = nm + suffix
        if key not in c.sb:
            c.S(key, shape, dt)
        return c.sb[key]

    xbf = get("_xbf", [128, tc_width], BF16)
    sqb = get("_sqb", [128, tc_width], BF16)
    mb = get("_mb", [128, tc_width], F32)
    vb = get("_vb", [128, tc_width], F32)
    rb = get("_rb", [128, tc_width], F32)
    mrb = get("_mrb", [128, tc_width], F32)
    t2 = get("_t2", [128, tc_width], F32)
    pss = c.ps["bank0"]
    psq = c.ps["bank1"]
    onb = c.sb["onesb"]

    for tc in range(ntc):
        t0 = tc * tc_width
        for ch in range(NCH):
            src = src_tiles(ch)
            sl = src[:, t0:t0 + tc_width]

            def cast_fn(E, sl=sl):
                return E.tensor_copy(out=xbf[:, :], in_=sl)

            def sq_fn(E, sl=sl):
                return E.activation(out=sqb[:, :], in_=sl, func=AFT.Square)

            c.prog.c("vector", cast_fn)
            c.prog.c("scalar", sq_fn)

            def mm_fn(E, ch=ch):
                E.matmul(pss[:, 0:tc_width], onb[:, :], xbf[:, :],
                         start=(ch == 0), stop=(ch == NCH - 1))
                return E.matmul(psq[:, 0:tc_width], onb[:, :], sqb[:, :],
                                start=(ch == 0), stop=(ch == NCH - 1))

            c.prog.c("tensor", mm_fn)

        c.prog.c("vector", lambda E: E.tensor_single_scalar(
            out=mb[:, :], in_=pss[:, 0:tc_width], scalar=1.0 / D, op=ALU.mult))
        c.prog.c("vector", lambda E: E.tensor_scalar(
            out=vb[:, :], in0=psq[:, 0:tc_width], scalar1=1.0 / D, scalar2=1e-5,
            op0=ALU.mult, op1=ALU.add))
        c.prog.c("vector", lambda E: E.tensor_mul(mrb[:, :], mb[:, :], mb[:, :]))
        c.prog.c("vector", lambda E: E.tensor_sub(vb[:, :], vb[:, :], mrb[:, :]))
        c.prog.c("scalar", lambda E: E.activation(
            out=rb[:, :], in_=vb[:, :], func=AFT.Sqrt))
        c.prog.c("vector", lambda E: E.reciprocal(out=rb[:, :], in_=rb[:, :]))
        c.prog.c("vector", lambda E: E.tensor_mul(mrb[:, :], mb[:, :], rb[:, :]))

        for ch in range(NCH):
            src = src_tiles(ch)
            sl = src[:, t0:t0 + tc_width]

            def ap1(E, sl=sl):
                return E.tensor_mul(t2[:, :], sl, rb[:, :])

            c.prog.c("vector", ap1)
            c.prog.c("vector", lambda E: E.tensor_sub(t2[:, :], t2[:, :], mrb[:, :]))
            out_cb(ch, tc, t2)


def stream_weight(ctx, qname, dst_sb, dram_ap):
    ctx.prog.d("gpsimd", [(dst_sb, dram_ap)])




def build_l1_ffn(ctx, xin, xout):
    c = ctx
    E = 8
    wg = c.din("f_Wg", [8, 128, 8], F32)
    bgb = c.din("f_bg_b", [128, 8], F32)
    w1 = c.din("f_W1", [E, D, D], F32)
    w2 = c.din("f_W2", [E, D, D], F32)
    b1t = c.din("f_b1T", [E, 128, 8], F32)
    b2 = c.din("f_b2", [E, D], F32)

    xmod = c.S("f_xmod", [128, NCH * T], F32)
    xnbf = c.S("f_xn", [128, NCH * T], BF16)
    wgsb = c.S("f_wg", [128, NCH * 8], F32)
    bgsb = c.S("f_bgb", [128, 8], F32)
    b1sb = c.S("f_b1", [128, E * 8], F32)
    b2sb = c.S("f_b2sb", [1, 8 * D], BF16)
    lgs = c.S("f_lgs", [128, 8], F32)
    rmax = c.S("f_rmax", [128, 1], F32)
    nmax = c.S("f_nmax", [128, 1], F32)
    ssum = c.S("f_ssum", [128, 1], F32)
    maskt = c.S("f_mask", [128, 8], F32)
    wselall = c.S("f_wselall", [128, 128], F32)
    wsrow = c.S("f_wsrow", [1, T], F32)
    wsrowb = c.S("f_wsrowb", [1, T], BF16)
    wbe = c.S("f_wbe", [128, 512], F32)
    gl = c.S("f_gl", [128, 512], F32)
    hs = c.S("f_hs", [128, 8 * 512], BF16)
    wa = c.S("f_wa", [128, NCH * 1024], BF16)
    wb = c.S("f_wb", [128, NCH * 1024], BF16)

    bank = [c.ps[f"bank{i}"] for i in range(8)]

    c.prog.d("sync", [
        (wgsb[:, :].rearrange("p (c e) -> p c e", e=8), wg.rearrange("c p e -> p c e")),
        (bgsb[:, :], bgb[:, :]),
        (b1sb[:, :].rearrange("p (e k) -> p e k", k=8), b1t.rearrange("e p k -> p e k"))])
    b2flat = AP(b2.tensor, b2.offset, [[8 * D, 1], [1, 8 * D]])
    c.prog.d("gpsimd", [(b2sb[0:1, :], b2flat)])

    for s in range(SEQ_PER_CORE):
        c.prog.d("sync", [(xmod[:, ch * T:(ch + 1) * T], xin[s, ch])
                          for ch in range(NCH)])

        def consume(ch, tc, t2, s=s):
            def gmm(E_, ch=ch, tc=tc):
                last = None
                for sub in range(4):
                    last = E_.matmul(
                        bank[2 + sub][:, 0:8],
                        t2[:, sub * 128:(sub + 1) * 128],
                        wgsb[:, ch * 8:(ch + 1) * 8],
                        start=(ch == 0), stop=(ch == NCH - 1))
                return last

            c.prog.c("tensor", gmm)

            def xcast(E_, ch=ch, tc=tc):
                return E_.activation(
                    out=xnbf[:, ch * T + tc * 512: ch * T + tc * 512 + 512],
                    in_=t2[:, :], func=AFT.Copy)

            c.prog.c("scalar", xcast)
            if ch == NCH - 1:
                emit_gating(c, tc, bank, bgsb, lgs, rmax, nmax, ssum, maskt,
                            wselall)

        emit_layernorm_fm(ctx, lambda ch: xmod[:, ch * T:(ch + 1) * T],
                          out_cb=consume, scratch_pref="f_ln")

        NTC = 4
        TCW = 512
        for e in range(E):
            c.prog.d("gpsimd", [
                (wa[:, :].rearrange("p (c j) -> p c j", j=1024),
                 w1[e].rearrange("(c p) j -> p c j", p=128)),
                (wb[:, :].rearrange("p (c j) -> p c j", j=1024),
                 w2[e].rearrange("(c p) j -> p c j", p=128))])
            for rnd in range(4):
                for tj in range(4):
                    ti = rnd * 4 + tj

                    def trw(E_, e=e, ti=ti, tj=tj):
                        return E_.transpose(bank[4][0:1, tj * 128:(tj + 1) * 128],
                                            wselall[:, ti * 8 + e: ti * 8 + e + 1],
                                            c.sb["identf"][:, :])

                    c.prog.c("tensor", trw)

                def evr(E_, rnd=rnd):
                    return E_.copy(out=wsrow[0:1, rnd * 512:(rnd + 1) * 512],
                                   in_=bank[4][0:1, 0:512])

                c.prog.c("scalar", evr)

                def evrb(E_, rnd=rnd):
                    return E_.tensor_copy(out=wsrowb[0:1, rnd * 512:(rnd + 1) * 512],
                                          in_=bank[4][0:1, 0:512])

                c.prog.c("vector", evrb)
            for tc in range(NTC):
                t0 = tc * TCW
                def wbmm(E_, e=e, t0=t0):
                    return E_.matmul(bank[6][:, 0:TCW], c.sb["onesf"][0:1, :],
                                     wsrow[0:1, t0:t0 + TCW],
                                     start=True, stop=True)

                c.prog.c("tensor", wbmm)
                c.prog.c("vector", lambda E_: E_.tensor_copy(
                    out=wbe[:, :], in_=bank[6][:, 0:TCW]))

                plh = Pipe(c.prog)
                for hm in range(8):
                    pb = bank[hm % 2]

                    def hmm(E_, e=e, hm=hm, t0=t0, pb=pb):
                        last = None
                        for ch in range(NCH):
                            last = E_.matmul(
                                pb[:, 0:TCW],
                                wa[:, ch * 1024 + hm * 128: ch * 1024 + hm * 128 + 128],
                                xnbf[:, ch * T + t0: ch * T + t0 + TCW],
                                start=(ch == 0), stop=(ch == NCH - 1))
                        return last

                    c.prog.c("tensor", hmm, upto=plh.up())

                    def gelu_fn(E_, e=e, hm=hm, pb=pb):
                        return E_.activation(out=gl[:, :], in_=pb[:, 0:TCW],
                                             func=AFT.Gelu,
                                             bias=b1sb[:, e * 8 + hm: e * 8 + hm + 1])

                    c.prog.c("scalar", gelu_fn)

                    def hscale(E_, hm=hm):
                        return E_.tensor_mul(hs[:, hm * TCW:(hm + 1) * TCW],
                                             gl[:, :], wbe[:, :])

                    c.prog.c("vector", hscale)
                    plh.bump()

                plw = Pipe(c.prog)
                for co in range(8):
                    pb = bank[2 + co % 2]

                    def w2mm(E_, co=co, pb=pb, e=e, t0=t0):
                        for hm in range(8):
                            E_.matmul(
                                pb[:, 0:TCW],
                                wb[:, hm * 1024 + co * 128: hm * 1024 + co * 128 + 128],
                                hs[:, hm * TCW:(hm + 1) * TCW],
                                start=(hm == 0), stop=False)
                        return E_.matmul(
                            pb[:, 0:TCW],
                            b2sb[0:1, e * D + co * 128: e * D + co * 128 + 128],
                            wsrowb[0:1, t0:t0 + TCW],
                            start=False, stop=True)

                    c.prog.c("tensor", w2mm,
                             upto=(None if co == 0 else plw.up()))
                    dst = xmod[:, co * T + t0: co * T + t0 + TCW]

                    def evf(E_, dst=dst, pb=pb):
                        return E_.tensor_add(dst, dst, pb[:, 0:TCW])

                    c.prog.c("vector", evf)
                    plw.bump()

        c.prog.d("sync", [(xout[s, ch], xmod[:, ch * T:(ch + 1) * T])
                          for ch in range(NCH)])
    c.prog.w("sync")


def emit_gating(c, tc, bank, bgsb, lgs, rmax, nmax, ssum, maskt, wselall):
    for sub in range(4):
        pg = bank[2 + sub][:, 0:8]
        c.prog.c("vector", lambda E_, pg=pg: E_.tensor_add(lgs[:, :], pg, bgsb[:, :]))
        c.prog.c("vector", lambda E_: E_.reduce_max(rmax[:, :], lgs[:, :], axis=AXX))
        c.prog.c("vector", lambda E_: E_.tensor_single_scalar(
            out=nmax[:, :], in_=rmax[:, :], scalar=-1.0, op=ALU.mult))
        c.prog.c("scalar", lambda E_: E_.activation(
            out=lgs[:, :], in_=lgs[:, :], func=AFT.Exp, bias=nmax[:, :]))
        c.prog.c("vector", lambda E_: E_.reduce_sum(ssum[:, :], lgs[:, :], axis=AXX))
        c.prog.c("vector", lambda E_: E_.reciprocal(out=ssum[:, :], in_=ssum[:, :]))
        c.prog.c("vector", lambda E_: E_.tensor_single_scalar(
            out=lgs[:, :], in_=lgs[:, :], scalar=ssum[:, :], op=ALU.mult))
        c.prog.c("vector", lambda E_: E_.reduce_max(rmax[:, :], lgs[:, :], axis=AXX))
        c.prog.c("vector", lambda E_: E_.tensor_tensor(
            out=maskt[:, :], in0=lgs[:, :], in1=bcast_free(rmax[:, 0:1], 8),
            op=ALU.is_equal))
        ti = tc * 4 + sub

        def wselw(E_, ti=ti):
            return E_.tensor_mul(wselall[:, ti * 8:(ti + 1) * 8],
                                 maskt[:, :], lgs[:, :])

        c.prog.c("vector", wselw)




def emit_layernorm_hbm(ctx, src_dram, out_cb, pref):
    c = ctx
    bank = [c.ps[f"bank{i}"] for i in range(8)]

    def get(suffix, shape, dt):
        key = pref + suffix
        if key not in c.sb:
            c.S(key, shape, dt)
        return c.sb[key]

    xch = get("_xch", [128, T], F32)
    xbf = get("_xbf", [128, 512], BF16)
    sqb = get("_sqb", [128, 512], BF16)
    mb = get("_mb", [128, 512], F32)
    vb = get("_vb", [128, 512], F32)
    sc2 = get("_sc2", [128, 512], F32)
    t2 = get("_t2", [128, 512], F32)
    onb = c.sb["onesb"]

    for ch in range(NCH):
        c.prog.d("sync", [(xch[:, :], src_dram(ch))])
        for tc in range(4):
            sl = xch[:, tc * 512:(tc + 1) * 512]
            c.prog.c("vector", lambda E, sl=sl: E.tensor_copy(out=xbf[:, :], in_=sl))
            c.prog.c("scalar", lambda E, sl=sl: E.activation(
                out=sqb[:, :], in_=sl, func=AFT.Square))

            def mm(E, ch=ch, tc=tc):
                E.matmul(bank[tc][:, 0:512], onb[:, :], xbf[:, :],
                         start=(ch == 0), stop=(ch == NCH - 1))
                return E.matmul(bank[4 + tc][:, 0:512], onb[:, :], sqb[:, :],
                                start=(ch == 0), stop=(ch == NCH - 1))

            c.prog.c("tensor", mm)
    for tc in range(4):
        def stat1(E, tc=tc):
            E.tensor_single_scalar(out=mb[:, :], in_=bank[tc][:, 0:512],
                                   scalar=1.0 / D, op=ALU.mult)
            E.tensor_scalar(out=vb[:, :], in0=bank[4 + tc][:, 0:512],
                            scalar1=1.0 / D, scalar2=1e-5, op0=ALU.mult, op1=ALU.add)
            E.tensor_mul(sc2[:, :], mb[:, :], mb[:, :])
            return E.tensor_sub(vb[:, :], vb[:, :], sc2[:, :])

        c.prog.c("vector", stat1)
        c.prog.c("scalar", lambda E: E.activation(out=vb[:, :], in_=vb[:, :],
                                                  func=AFT.Sqrt))

        def stat2(E, tc=tc):
            E.reciprocal(out=bank[tc][:, 0:512], in_=vb[:, :])
            return E.tensor_mul(bank[4 + tc][:, 0:512], mb[:, :],
                                bank[tc][:, 0:512])

        c.prog.c("vector", stat2)
    for ch in range(NCH):
        c.prog.d("sync", [(xch[:, :], src_dram(ch))])
        for tc in range(4):
            sl = xch[:, tc * 512:(tc + 1) * 512]

            def ap1(E, sl=sl, tc=tc):
                E.tensor_mul(t2[:, :], sl, bank[tc][:, 0:512])
                return E.tensor_sub(t2[:, :], t2[:, :], bank[4 + tc][:, 0:512])

            c.prog.c("vector", ap1)
            out_cb(ch, tc, t2)




def emit_qk_proc(c, tgt, ssb, ai, l, ti, rn, S1, S2, QP, xpos_sb, QT):
    c.prog.c("scalar", lambda E: E.activation(out=S1, in_=tgt, func=AFT.Square))
    c.prog.c("vector", lambda E: E.reduce_sum(
        out=col_unit(rn[:, :]), in_=seg_view(S1, 64), axis=AXX))
    c.prog.c("scalar", lambda E: E.activation(
        out=rn[:, :], in_=rn[:, :], func=AFT.Sqrt))

    def rcp(E):
        E.tensor_scalar_max(out=rn[:, :], in0=rn[:, :], scalar1=1e-12)
        return E.reciprocal(out=rn[:, :], in_=rn[:, :])

    c.prog.c("vector", rcp)

    def scl(E):
        rv = AP(rn[:, :].tensor, rn[:, :].offset,
                [list(a) for a in rn[:, :].ap] + [[0, 64]])
        E.tensor_tensor(out=seg_view(tgt, 64), in0=seg_view(tgt, 64),
                        in1=rv, op=ALU.mult)
        sap = ssb[:, l * 64:(l + 1) * 64]
        sv = AP(sap.tensor, sap.offset,
                [list(sap.ap[0])] + [[0, 8], [1, 64]])
        return E.tensor_tensor(out=seg_view(tgt, 64),
                               in0=seg_view(tgt, 64), in1=sv, op=ALU.mult)

    c.prog.c("vector", scl)

    def xposf(E):
        tabA = xpos_sb[:, ai * 64: ai * 64 + 64]
        tabB = xpos_sb[:, (ai + 1) * 64: (ai + 1) * 64 + 64]
        av = AP(tabA.tensor, tabA.offset,
                [list(tabA.ap[0])] + [[0, 8], [1, 64]])
        sw = AP(tgt.tensor, tgt.offset + 32,
                [list(tgt.ap[0])] + [[64, 8], [-32, 2], [1, 32]])
        s1v = AP(S1.tensor, S1.offset,
                 [list(S1.ap[0])] + [[64, 8], [32, 2], [1, 32]])
        bv = AP(tabB.tensor, tabB.offset,
                [list(tabB.ap[0])] + [[0, 8], [32, 2], [1, 32]])
        E.tensor_tensor(out=s1v, in0=sw, in1=bv, op=ALU.mult)
        E.tensor_tensor(out=seg_view(S2, 64), in0=seg_view(tgt, 64),
                        in1=av, op=ALU.mult)
        return E.tensor_add(QP, S2, S1)

    c.prog.c("vector", xposf)


def build_l0_ffn(ctx, xin, xout, dbg=None):
    c = ctx
    NL = 2
    INNER = 512
    wqkv = c.din("g_wqkv", [NL, D, 3 * INNER], F32)
    wo = c.din("g_wo", [NL, INNER, D], F32)
    w1p = c.din("g_w1p", [NL, D, 5888], F32)
    w2p = c.din("g_w2p", [NL, 2816, D], F32)
    qsb_in = c.din("g_qs", [NL, 128, 64], F32)
    ksb_in = c.din("g_ks", [NL, 128, 64], F32)
    lgb_in = c.din("g_lgb", [NL, 2, 128, NCH], F32)
    b1p_in = c.din("g_b1p", [NL, 128, 46], F32)
    b2_in = c.din("g_b2", [NL, 128, NCH], F32)
    xpos = c.din("g_xpos", [4, 16, 128, 64], BF16)
    maskS = c.din("g_maskS", [4, 128, 1024], BF16)
    maskT = c.din("g_maskT", [8, 128, 512], BF16)

    TG = T + 1
    xl = c.S("g_xl", [128, NCH * TG], F32)
    xsh = c.S("g_xsh", [128, NCH * T], BF16)
    vaug = c.S("g_vaug", [128, 16 * 520], BF16)
    qf = c.S("g_qf", [128, 4 * T], BF16)
    kf = c.S("g_kf", [128, 4 * T], BF16)
    ws = c.S("g_ws", [128, 16 * 512], BF16)
    fscr = c.S("g_fscr", [128, 3072], F32)
    bscr = c.S("g_bscr", [128, 1024], BF16)
    mskS_sb = c.S("g_mskS", [128, 4 * 1024], BF16)
    mskT_sb = c.S("g_mskT", [128, 8 * 512], BF16)
    xpos_sb = c.S("g_xpossb", [128, 4 * 64], BF16)
    qs_sb = c.S("g_qssb", [128, NL * 64], F32)
    ks_sb = c.S("g_kssb", [128, NL * 64], F32)
    lgb_sb = c.S("g_lgbsb", [128, NL * 2 * NCH], F32)
    b1p_sb = c.S("g_b1psb", [128, NL * 46], F32)
    b2_sb = c.S("g_b2sb", [128, NL * NCH], F32)
    rn = c.S("g_rn", [128, 8], F32)
    negm = c.S("g_negm", [128, 4], F32)
    mrow = fscr[0:1, 2560:3072]
    rrow = fscr[0:1, 2048:2560]
    mbt = fscr[:, 1024:1536]
    sst = fscr[:, 0:1024]
    of32 = fscr[0:64, 1536:2048]
    pt = bscr[:, 0:512]
    pt2f = bscr[:, 0:1024]
    c.sb["g_ln_mb"] = fscr[:, 0:512]
    c.sb["g_ln_vb"] = fscr[:, 512:1024]
    c.sb["g_ln_sc2"] = fscr[:, 1024:1536]
    c.sb["g_ln_t2"] = fscr[:, 1536:2048]
    c.sb["g_ln_xbf"] = bscr[:, 0:512]
    c.sb["g_ln_sqb"] = bscr[:, 512:1024]
    QT = fscr[:, 0:512]
    KT = fscr[:, 512:1024]
    S1 = fscr[:, 1024:1536]
    S2 = fscr[:, 1536:2048]
    S3 = fscr[:, 2048:2560]
    QP = fscr[:, 2048:2560]
    T3 = fscr[:, 2560:3072]
    HP = None

    bank = [c.ps[f"bank{i}"] for i in range(8)]

    c.prog.d("sync", [
        (mskS_sb[:, :].rearrange("p (q j) -> p q j", j=1024),
         maskS.rearrange("q p j -> p q j")),
        (mskT_sb[:, :].rearrange("p (k j) -> p k j", j=512),
         maskT.rearrange("k p j -> p k j")),
        (qs_sb[:, :].rearrange("p (l j) -> p l j", j=64), qsb_in.rearrange("l p j -> p l j")),
        (ks_sb[:, :].rearrange("p (l j) -> p l j", j=64), ksb_in.rearrange("l p j -> p l j")),
        (lgb_sb[:, :].rearrange("p (l g k) -> p l g k", g=2, k=NCH),
         lgb_in.rearrange("l g p k -> p l g k")),
        (b1p_sb[:, :].rearrange("p (l k) -> p l k", k=46), b1p_in.rearrange("l p k -> p l k")),
        (b2_sb[:, :].rearrange("p (l k) -> p l k", k=NCH), b2_in.rearrange("l p k -> p l k")),
    ])

    def xg(ch):
        return xl[:, ch * TG + 1: ch * TG + 1 + T]

    def xsh_slice(ch, t0, w):
        off = ch * TG + (1 if ch < 4 else 0)
        return xl[:, off + t0: off + t0 + w]

    def wslab(dram_col_slice, slot, nch_rows=NCH):
        dst = ws[:, slot * 512 * nch_rows:(slot + 1) * 512 * nch_rows]
        c.prog.d("gpsimd", [(dst.rearrange("p (c j) -> p c j", j=512),
                             dram_col_slice.rearrange("(c p) j -> p c j", p=128))])

    for s in range(SEQ_PER_CORE):
        def gz(E):
            a0 = xl[0:128, 0:1]
            apv = AP(a0.tensor, a0.offset, [list(a0.ap[0])] + [[TG, NCH], [1, 1]])
            return E.memset(apv, 0.0)

        c.prog.c("vector", gz)

        def lncb(ch, tc, t2):
            c.prog.c("vector", lambda E, ch=ch, tc=tc: E.tensor_copy(
                out=xg(ch)[:, tc * 512:(tc + 1) * 512], in_=t2[:, :]))

        emit_layernorm_hbm(c, lambda ch, s=s: xin[s, ch], lncb, "g_ln")
        if dbg == ("xl",):
            for co in range(NCH):
                c.prog.d("sync", [(xout[s, co], xg(co))])
            continue

        for l in range(NL):
            for ch in range(NCH):
                c.prog.c("vector", lambda E, ch=ch: E.tensor_copy(
                    out=xsh[:, ch * T:(ch + 1) * T], in_=xsh_slice(ch, 0, T)))
            if dbg is not None and dbg[0] == "xsh" and dbg[1] == l:
                for co in range(NCH):
                    c.prog.c("vector", lambda E, co=co: E.tensor_copy(
                        out=fscr[:, 0:2048], in_=xsh[:, co * T:(co + 1) * T]))
                    c.prog.d("sync", [(xout[s, co], fscr[:, 0:2048])])
                break
            wslab(wqkv[l][:, 1024:1536], 0)
            plv = Pipe(c.prog)
            for ti in range(16):
                def vmm(E, ti=ti):
                    last = None
                    for ci in range(NCH):
                        last = E.matmul(
                            bank[2 + ti % 2][:, 0:512],
                            xsh[:, ci * T + ti * 128: ci * T + ti * 128 + 128],
                            ws[:, ci * 512: ci * 512 + 512],
                            start=(ci == 0), stop=(ci == NCH - 1))
                    return last

                c.prog.c("tensor", vmm, upto=plv.up())

                def vev(E, ti=ti):
                    dst = vaug[:, ti * 520: (ti + 1) * 520]
                    dstv = AP(dst.tensor, dst.offset,
                              [list(dst.ap[0])] + [[65, 8], [1, 64]])
                    return E.tensor_copy(out=dstv,
                                         in_=seg_view(bank[2 + ti % 2][:, 0:512], 64))

                c.prog.c("vector", vev)
                plv.bump()

            for sl in range(2):
                wslab(wqkv[l][:, sl * 512:(sl + 1) * 512], sl)
            if dbg is not None and dbg[0] == "ws" and dbg[1] == l:
                for co in range(8):
                    c.prog.c("vector", lambda E, co=co: E.tensor_copy(
                        out=fscr[:, 0:2048],
                        in_=ws[:, co * 1024:(co + 1) * 1024]))
                    c.prog.d("sync", [(xout[s, co, :, 0:2048], fscr[:, 0:2048])])
                break
            for ti in range(16):
                def qkmm(E, ti=ti):
                    last = None
                    for sl in range(2):
                        for ci in range(NCH):
                            last = E.matmul(
                                bank[sl][:, 0:512],
                                xsh[:, ci * T + ti * 128: ci * T + ti * 128 + 128],
                                ws[:, sl * NCH * 512 + ci * 512: sl * NCH * 512 + ci * 512 + 512],
                                start=(ci == 0), stop=(ci == NCH - 1))
                    return last

                c.prog.c("tensor", qkmm)
                c.prog.d("sync", [(xpos_sb[:, :].rearrange("p (a j) -> p a j", j=64),
                                   xpos[:, ti].rearrange("a p j -> p a j"))])

                for which, bsrc, dst_fm, ssb, ai in ((0, 0, qf, qs_sb, 0),
                                                     (1, 1, kf, ks_sb, 2)):
                    tgt = QT if which == 0 else KT
                    c.prog.c("vector", lambda E, bsrc=bsrc, tgt=tgt: E.tensor_copy(
                        out=tgt, in_=bank[bsrc][:, 0:512]))
                    if dbg is not None and dbg[0] == "qraw":
                        def rawc(E, tgt=tgt):
                            return E.tensor_copy(out=QP, in_=tgt)

                        c.prog.c("vector", rawc)
                        skip_proc = True
                    else:
                        skip_proc = False
                    if not skip_proc:
                        emit_qk_proc(c, tgt, ssb, ai, l, ti, rn, S1, S2, QP,
                                     xpos_sb, QT)
                    for cq in range(4):
                        c.prog.c("tensor", lambda E, cq=cq: E.transpose(
                            bank[6][:, 0:128], QP[:, cq * 128:(cq + 1) * 128],
                            c.sb["identf"][:, :]))
                        c.prog.c("vector", lambda E, cq=cq, ti=ti, dst_fm=dst_fm:
                                 E.tensor_copy(
                                     out=dst_fm[:, cq * T + ti * 128: cq * T + ti * 128 + 128],
                                     in_=bank[6][:, 0:128]))

            def vones(E):
                a0 = vaug[:, 64:65]
                apv = AP(a0.tensor, a0.offset,
                         [list(a0.ap[0])] + [[520, 16], [65, 8], [1, 1]])
                return E.memset(apv, 1.0)

            c.prog.c("vector", vones)

            OF = xsh
            for w in range(4):
                kt_lo = 4 if w == 0 else 0
                key0 = 512 * (w - 1)
                for h in range(8):
                    hch = h // 2
                    hrow = (h % 2) * 64
                    smm_marks = {}
                    tail_marks = {}
                    for qt in range(4):
                        b0 = (qt % 2) * 2

                        def smm(E, w=w, h=h, qt=qt, hch=hch, hrow=hrow, b0=b0):
                            last = None
                            for half in range(2):
                                if w == 0 and half == 0:
                                    continue
                                t0 = 512 * (w - 1) + half * 512
                                last = E.matmul(
                                    bank[b0 + half][:, 0:512],
                                    qf[hrow:hrow + 64,
                                       hch * T + w * 512 + qt * 128:
                                       hch * T + w * 512 + qt * 128 + 128],
                                    kf[hrow:hrow + 64, hch * T + t0: hch * T + t0 + 512],
                                    start=True, stop=True)
                            return last

                        c.prog.c("tensor", smm, upto=tail_marks.get(qt - 2))
                        smm_marks[qt] = c.prog.mark()

                        def sred(E, w=w, qt=qt, b0=b0):
                            for half in range(2):
                                if w == 0 and half == 0:
                                    continue
                                E.tensor_add(
                                    sst[:, half * 512:(half + 1) * 512],
                                    bank[b0 + half][:, 0:512],
                                    mskS_sb[:, qt * 1024 + half * 512:
                                            qt * 1024 + half * 512 + 512])
                            if w == 0:
                                E.reduce_max(rn[:, 0:1], sst[:, 512:1024], axis=AXX)
                            else:
                                E.reduce_max(rn[:, 0:1], sst[:, 0:1024], axis=AXX)
                            return E.tensor_single_scalar(
                                out=negm[:, qt:qt + 1], in_=rn[:, 0:1],
                                scalar=-1.0, op=ALU.mult)

                        c.prog.c("vector", sred, upto=smm_marks[qt])
                        tail_marks[qt] = c.prog.mark()

                    def mtr(E):
                        last = None
                        for j in range(4):
                            last = E.transpose(bank[6][0:1, j * 128:(j + 1) * 128],
                                               negm[:, j:j + 1], c.sb["identf"][:, :])
                        return last

                    c.prog.c("tensor", mtr)
                    c.prog.c("scalar", lambda E: E.copy(out=mrow,
                                                        in_=bank[6][0:1, 0:512]))
                    c.prog.c("tensor", lambda E: E.matmul(
                        bank[6][:, 0:512], c.sb["onesf"][0:1, :], mrow,
                        start=True, stop=True))
                    c.prog.c("vector", lambda E: E.tensor_copy(
                        out=mbt, in_=bank[6][:, 0:512]))
                    mbt_mark = c.prog.mark()
                    stm = {}
                    tdone = {}

                    def emit_tail(kt, w=w, h=h, kt_lo=kt_lo):
                        bb = kt % 2
                        sb4 = bank[4 + 2 * bb]
                        ktt = 4 * (w - 1) + kt

                        def stsub(E, kt=kt, bb=bb, sb4=sb4):
                            dst = sst[:, bb * 512: bb * 512 + 512]
                            E.tensor_add(dst, sb4[:, 0:512], mbt)
                            return E.tensor_add(
                                dst, dst, mskT_sb[:, kt * 512:(kt + 1) * 512])

                        c.prog.c("vector", stsub, upto=stm[kt])
                        m1 = c.prog.mark()
                        c.prog.c("scalar", lambda E, bb=bb: E.activation(
                            out=pt2f[:, bb * 512: bb * 512 + 512],
                            in_=sst[:, bb * 512: bb * 512 + 512], func=AFT.Exp),
                            upto=m1)
                        m2 = c.prog.mark()

                        def pvmm(E, kt=kt, ktt=ktt, h=h, kt_lo=kt_lo, bb=bb):
                            return E.matmul(
                                bank[5][0:65, 0:512],
                                vaug[:, ktt * 520 + h * 65: ktt * 520 + h * 65 + 65],
                                pt2f[:, bb * 512: bb * 512 + 512],
                                start=(kt == kt_lo), stop=(kt == 7))

                        c.prog.c("tensor", pvmm, upto=m2)
                        tdone[kt] = c.prog.mark()

                    for kt in range(kt_lo, 8):
                        ktt = 4 * (w - 1) + kt

                        def stmm(E, w=w, kt=kt, hch=hch, hrow=hrow, ktt=ktt):
                            return E.matmul(
                                bank[4 + 2 * (kt % 2)][:, 0:512],
                                kf[hrow:hrow + 64, hch * T + ktt * 128: hch * T + ktt * 128 + 128],
                                qf[hrow:hrow + 64, hch * T + w * 512: hch * T + w * 512 + 512],
                                start=True, stop=True)

                        c.prog.c("tensor", stmm, upto=tdone.get(kt - 2, mbt_mark))
                        stm[kt] = c.prog.mark()
                        if kt > kt_lo:
                            emit_tail(kt - 1)
                    emit_tail(7)
                    c.prog.c("vector", lambda E: E.reciprocal(
                        out=rrow, in_=bank[5][64:65, 0:512]))
                    c.prog.c("scalar", lambda E: E.copy(out=of32,
                                                        in_=bank[5][0:64, 0:512]))
                    c.prog.c("tensor", lambda E: E.matmul(
                        bank[6][0:64, 0:512], c.sb["onesf"][0:1, 0:64], rrow,
                        start=True, stop=True))

                    def oev(E, w=w, hch=hch, hrow=hrow):
                        return E.tensor_mul(
                            OF[hrow:hrow + 64, hch * T + w * 512: hch * T + w * 512 + 512],
                            of32, bank[6][0:64, 0:512])

                    c.prog.c("vector", oev)

            if dbg is not None and dbg[0] == "of" and dbg[1] == l:
                for ci in range(4):
                    c.prog.c("vector", lambda E, ci=ci: E.tensor_copy(
                        out=fscr[:, 0:2048], in_=OF[:, ci * T:(ci + 1) * T]))
                    c.prog.d("sync", [(xout[s, ci], fscr[:, 0:2048])])
                break
            if dbg is not None and dbg[0] in ("qf", "qraw") and dbg[1] == l:
                for ci in range(4):
                    c.prog.c("vector", lambda E, ci=ci: E.tensor_copy(
                        out=fscr[:, 0:2048], in_=qf[:, ci * T:(ci + 1) * T]))
                    c.prog.d("sync", [(xout[s, ci], fscr[:, 0:2048])])
                    c.prog.c("vector", lambda E, ci=ci: E.tensor_copy(
                        out=fscr[:, 0:2048], in_=kf[:, ci * T:(ci + 1) * T]))
                    c.prog.d("sync", [(xout[s, 4 + ci], fscr[:, 0:2048])])
                break
            for sl in range(2):
                wslab(wo[l][:, sl * 512:(sl + 1) * 512], sl, nch_rows=4)
                plo = Pipe(c.prog)
                for col in range(4):
                    co = sl * 4 + col
                    for tc in range(4):
                        def omm(E, sl=sl, col=col, tc=tc):
                            last = None
                            for ci in range(4):
                                last = E.matmul(
                                    bank[tc % 2][:, 0:512],
                                    ws[:, sl * 4 * 512 + ci * 512 + col * 128:
                                       sl * 4 * 512 + ci * 512 + col * 128 + 128],
                                    OF[:, ci * T + tc * 512: ci * T + tc * 512 + 512],
                                    start=(ci == 0), stop=(ci == 3))
                            return last

                        c.prog.c("tensor", omm, upto=plo.up())

                        def oadd(E, co=co, tc=tc):
                            dst = xg(co)[:, tc * 512:(tc + 1) * 512]
                            return E.tensor_add(dst, dst, bank[tc % 2][:, 0:512])

                        c.prog.c("vector", oadd)
                        plo.bump()

            if dbg is not None and dbg[0] == "xl1" and dbg[1] == l:
                for co in range(NCH):
                    c.prog.d("sync", [(xout[s, co], xg(co))])
                break
            xn2 = xsh
            for tc in range(4):
                for ch in range(NCH):
                    sl_ = xsh_slice(ch, tc * 512, 512)
                    c.prog.c("vector", lambda E, sl_=sl_: E.tensor_copy(
                        out=c.sb["g_ln_xbf"], in_=sl_))
                    c.prog.c("scalar", lambda E, sl_=sl_: E.activation(
                        out=c.sb["g_ln_sqb"], in_=sl_, func=AFT.Square))

                    def mm(E, ch=ch, tc=tc):
                        E.matmul(bank[2][:, 0:512], c.sb["onesb"][:, :],
                                 c.sb["g_ln_xbf"], start=(ch == 0), stop=(ch == 7))
                        return E.matmul(bank[3][:, 0:512], c.sb["onesb"][:, :],
                                        c.sb["g_ln_sqb"], start=(ch == 0),
                                        stop=(ch == 7))

                    c.prog.c("tensor", mm)

                def stat1(E):
                    E.tensor_single_scalar(out=c.sb["g_ln_mb"], in_=bank[2][:, 0:512],
                                           scalar=1.0 / D, op=ALU.mult)
                    E.tensor_scalar(out=c.sb["g_ln_vb"], in0=bank[3][:, 0:512],
                                    scalar1=1.0 / D, scalar2=1e-5, op0=ALU.mult,
                                    op1=ALU.add)
                    E.tensor_mul(c.sb["g_ln_sc2"], c.sb["g_ln_mb"], c.sb["g_ln_mb"])
                    return E.tensor_sub(c.sb["g_ln_vb"], c.sb["g_ln_vb"],
                                        c.sb["g_ln_sc2"])

                c.prog.c("vector", stat1)
                c.prog.c("scalar", lambda E: E.activation(
                    out=c.sb["g_ln_vb"], in_=c.sb["g_ln_vb"], func=AFT.Sqrt))

                def stat2(E):
                    E.reciprocal(out=c.sb["g_ln_vb"], in_=c.sb["g_ln_vb"])
                    return E.tensor_mul(c.sb["g_ln_sc2"], c.sb["g_ln_mb"],
                                        c.sb["g_ln_vb"])

                c.prog.c("vector", stat2)
                for ch in range(NCH):
                    sl_ = xsh_slice(ch, tc * 512, 512)

                    def ap1(E, sl_=sl_, ch=ch, tc=tc, l=l):
                        E.tensor_mul(c.sb["g_ln_t2"], sl_, c.sb["g_ln_vb"])
                        E.tensor_sub(c.sb["g_ln_t2"], c.sb["g_ln_t2"],
                                     c.sb["g_ln_sc2"])
                        return E.tensor_scalar(
                            out=xn2[:, ch * T + tc * 512: ch * T + tc * 512 + 512],
                            in0=c.sb["g_ln_t2"],
                            scalar1=lgb_sb[:, (l * 2) * NCH + ch: (l * 2) * NCH + ch + 1],
                            scalar2=lgb_sb[:, (l * 2 + 1) * NCH + ch: (l * 2 + 1) * NCH + ch + 1],
                            op0=ALU.mult, op1=ALU.add)

                    c.prog.c("vector", ap1)

            if dbg is not None and dbg[0] == "xn2" and dbg[1] == l:
                for co in range(NCH):
                    c.prog.c("vector", lambda E, co=co: E.tensor_copy(
                        out=fscr[:, 0:2048], in_=xn2[:, co * T:(co + 1) * T]))
                    c.prog.d("sync", [(xout[s, co], fscr[:, 0:2048])])
                break
            HPn = 22
            for tc in range(4):
                plm = Pipe(c.prog)
                for cp in range(HPn):
                    if cp % 4 == 0:
                        wslab(w1p[l][:, (cp // 4) * 512:(cp // 4) * 512 + 512], 0)
                        wslab(w1p[l][:, 2816 + (cp // 4) * 512: 2816 + (cp // 4) * 512 + 512], 1)
                        slab_mark = c.prog.mark()

                    bb = (cp % 2) * 2

                    def hmm(E, cp=cp, tc=tc, bb=bb):
                        last = None
                        for ci in range(NCH):
                            for ab in range(2):
                                last = E.matmul(
                                    bank[bb + ab][:, 0:512],
                                    ws[:, ab * NCH * 512 + ci * 512 + (cp % 4) * 128:
                                       ab * NCH * 512 + ci * 512 + (cp % 4) * 128 + 128],
                                    xn2[:, ci * T + tc * 512: ci * T + tc * 512 + 512],
                                    start=(ci == 0), stop=(ci == NCH - 1))
                        return last

                    c.prog.c("tensor", hmm,
                             upto=(slab_mark if cp % 4 == 0 else plm.up()))

                    def aev(E, cp=cp, l=l, bb=bb):
                        return E.tensor_scalar_add(
                            out=S1, in0=bank[bb][:, 0:512],
                            scalar1=b1p_sb[:, l * 46 + cp: l * 46 + cp + 1])

                    c.prog.c("vector", aev)

                    def gev(E, cp=cp, l=l, bb=bb):
                        return E.activation(
                            out=S2, in_=bank[bb + 1][:, 0:512], func=AFT.Gelu,
                            bias=b1p_sb[:, l * 46 + 22 + cp: l * 46 + 22 + cp + 1])

                    c.prog.c("scalar", gev)
                    hp_dst = (qf if cp < 11 else kf)
                    hpo = (cp % 11)

                    def pev(E, hp_dst=hp_dst, hpo=hpo):
                        return E.tensor_mul(hp_dst[:, hpo * 512:(hpo + 1) * 512],
                                            S1, S2)

                    c.prog.c("vector", pev)
                    plm.bump()
                for osl in range(2):
                    for rhalf in range(2):
                        dst = ws[:, 0:11 * 512]
                        c.prog.d("gpsimd", [
                            (dst.rearrange("p (c j) -> p c j", j=512),
                             w2p[l][rhalf * 1408:(rhalf + 1) * 1408,
                                    osl * 512:(osl + 1) * 512].rearrange(
                                 "(c p) j -> p c j", p=128))])
                        for col in range(4):
                            def wmm(E, rhalf=rhalf, col=col, tc=tc):
                                last = None
                                for hc in range(11):
                                    cpg = rhalf * 11 + hc
                                    hp_src = (qf if cpg < 11 else kf)
                                    hpo = cpg % 11
                                    last = E.matmul(
                                        bank[4 + col][:, 0:512],
                                        ws[:, hc * 512 + col * 128: hc * 512 + col * 128 + 128],
                                        hp_src[:, hpo * 512:(hpo + 1) * 512],
                                        start=(rhalf == 0 and hc == 0),
                                        stop=(rhalf == 1 and hc == 10))
                                return last

                            c.prog.c("tensor", wmm)
                    for col in range(4):
                        co = osl * 4 + col

                        def wev(E, col=col, co=co, tc=tc, l=l):
                            E.tensor_scalar_add(
                                out=T3, in0=bank[4 + col][:, 0:512],
                                scalar1=b2_sb[:, l * NCH + co: l * NCH + co + 1])
                            dst = xg(co)[:, tc * 512:(tc + 1) * 512]
                            return E.tensor_add(dst, dst, T3)

                        c.prog.c("vector", wev)

        if dbg is not None and dbg != ("fin",):
            continue
        for co in range(NCH):
            c.prog.d("sync", [(fscr[:, 0:2048], xin[s, co])])

            def fin(E, co=co):
                return E.tensor_add(fscr[:, 0:2048], fscr[:, 0:2048], xg(co))

            c.prog.c("vector", fin)
            c.prog.d("sync", [(xout[s, co], fscr[:, 0:2048])])
    c.prog.w("sync")




def emit_conv3(E, dst, src, t0, W, wtap, bias_col, total=T):
    w0c, w1c, w2c = wtap
    last = E.tensor_scalar(out=dst[:, 0:W], in0=src[:, t0:t0 + W], scalar1=w1c,
                           scalar2=bias_col, op0=ALU.mult, op1=ALU.add)
    if t0 == 0:
        last = E.scalar_tensor_tensor(out=dst[:, 1:W], in0=src[:, 0:W - 1],
                                      scalar=w0c, in1=dst[:, 1:W],
                                      op0=ALU.mult, op1=ALU.add)
    else:
        last = E.scalar_tensor_tensor(out=dst[:, 0:W], in0=src[:, t0 - 1:t0 + W - 1],
                                      scalar=w0c, in1=dst[:, 0:W],
                                      op0=ALU.mult, op1=ALU.add)
    if t0 + W == total:
        last = E.scalar_tensor_tensor(out=dst[:, 0:W - 1], in0=src[:, t0 + 1:t0 + W],
                                      scalar=w2c, in1=dst[:, 0:W - 1],
                                      op0=ALU.mult, op1=ALU.add)
    else:
        last = E.scalar_tensor_tensor(out=dst[:, 0:W], in0=src[:, t0 + 1:t0 + W + 1],
                                      scalar=w2c, in1=dst[:, 0:W],
                                      op0=ALU.mult, op1=ALU.add)
    return last


def build_l0_attn(ctx, xin, xout):
    c = ctx
    inw = c.din("m_inw", [D, D], F32)
    actw = c.din("m_actp", [D, D], F32)
    qkw = c.din("m_qkw", [D, 2 * D], F32)
    outw = c.din("m_outw", [D, D], F32)
    fc1 = c.din("m_fc1", [D, 4 * D], F32)
    fc2 = c.din("m_fc2", [4 * D, D], F32)
    convw = c.din("m_convw", [4, 3, 128, NCH], F32)
    convb = c.din("m_convb", [4, 128, NCH], F32)
    norms = c.din("m_norms", [4, 128, NCH], F32)
    pbias = c.din("m_pbias", [128, 64], F32)
    qkbrow = c.din("m_qkb_row", [1, 2 * D], F32)
    ctab = c.din("m_costab", [2, 128, 512], BF16)

    xn = c.S("m_xn", [128, NCH * T], BF16)
    xibf = c.S("m_xibf", [128, NCH * T], BF16)
    attn = c.S("m_attn", [128, NCH * T], BF16)
    ws = c.S("m_ws", [128, 4 * NCH * 512], BF16)
    xip = c.S("m_xip", [128, T], F32)
    tmpf = c.S("m_tmpf", [128, T], F32)
    fscr = c.S("m_fscr", [128, 2048], F32)
    bscr = c.S("m_bscr", [128, 5120], BF16)
    c.sb["m_ln_mb"] = fscr[:, 0:512]
    c.sb["m_ln_vb"] = fscr[:, 512:1024]
    c.sb["m_ln_sc2"] = fscr[:, 1024:1536]
    c.sb["m_ln_t2"] = fscr[:, 1536:2048]
    c.sb["m_ln_xbf"] = bscr[:, 0:512]
    c.sb["m_ln_sqb"] = bscr[:, 512:1024]
    ksrow = xip[0:1, 0:D]
    ksb = c.S("m_ksb", [128, D], F32)
    kvbd = c.S("m_kvbd", [128, NCH * 128], BF16)
    kvacc = c.S("m_kvacc", [64, 1024], F32)
    zt = c.S("m_zt", [128, 16], F32)
    qkbias = c.S("m_qkbias", [128, D], BF16)
    tstg = c.S("m_tstg", [128, 128], F32)
    cwsb = c.S("m_cwsb", [128, 4 * 3 * NCH], F32)
    cbsb = c.S("m_cbsb", [128, 4 * NCH], F32)
    nrmsb = c.S("m_nrmsb", [128, 4 * NCH], F32)
    pbsb = c.S("m_pbsb", [128, 64], F32)
    cossb = c.S("m_cossb", [128, 512], BF16)
    sinsb = c.S("m_sinsb", [128, 512], BF16)

    t3 = fscr[:, 1536:2048]
    bank = [c.ps[f"bank{i}"] for i in range(8)]
    xs = c.nc.dram_tensor("m_xs", [2, NCH, 128, T], F32, kind="Internal").ap()

    def cw(conv, tap, ch):
        return cwsb[:, (conv * 3 + tap) * NCH + ch: (conv * 3 + tap) * NCH + ch + 1]

    def cb(conv, ch):
        return cbsb[:, conv * NCH + ch: conv * NCH + ch + 1]

    c.prog.d("sync", [
        (cwsb[:, :].rearrange("p (c t k) -> p c t k", t=3, k=NCH),
         convw.rearrange("c t p k -> p c t k")),
        (cbsb[:, :].rearrange("p (c k) -> p c k", k=NCH),
         convb.rearrange("c p k -> p c k")),
        (nrmsb[:, :].rearrange("p (c k) -> p c k", k=NCH),
         norms.rearrange("c p k -> p c k")),
        (pbsb[:, :], pbias[:, :]),
        (cossb[:, :], ctab[0]),
        (sinsb[:, :], ctab[1]),
    ])

    def wslab(dram_col_slice, dst_off_slabs, upto=None):
        dst = ws[:, dst_off_slabs * NCH * 512:(dst_off_slabs + 1) * NCH * 512]
        c.prog.d("gpsimd", [(dst.rearrange("p (c j) -> p c j", j=512),
                             dram_col_slice.rearrange("(c p) j -> p c j", p=128))],
                 upto=upto)

    def fm_proj(dst_cb, src_sb, wslot, couts=4, bias=None):
        pl = Pipe(c.prog)
        for col in range(couts):
            for tc in range(4):
                def pmm(E, col=col, tc=tc, wslot=wslot, src_sb=src_sb):
                    last = None
                    for ci in range(NCH):
                        last = E.matmul(
                            bank[tc % 2][:, 0:512],
                            ws[:, wslot * NCH * 512 + ci * 512 + col * 128:
                               wslot * NCH * 512 + ci * 512 + col * 128 + 128],
                            src_sb[:, ci * T + tc * 512: ci * T + tc * 512 + 512],
                            start=(ci == 0), stop=(ci == NCH - 1))
                    return last

                c.prog.c("tensor", pmm, upto=pl.up())
                dst_cb(col, tc, bank[tc % 2])
                pl.bump()

    for s in range(SEQ_PER_CORE):
        def lncb(ch, tc, t2):
            c.prog.c("vector", lambda E, ch=ch, tc=tc: E.tensor_copy(
                out=xip[:, tc * 512:(tc + 1) * 512], in_=t2[:, :]))
            if tc == 3:
                def cvf(E, ch=ch):
                    return emit_conv3(E, tmpf, xip, 0, T,
                                      (cw(0, 0, ch), cw(0, 1, ch), cw(0, 2, ch)),
                                      cb(0, ch))

                c.prog.c("vector", cvf)
                c.prog.c("vector", lambda E: E.tensor_add(tmpf[:, :], tmpf[:, :],
                                                          xip[:, :]))
                c.prog.d("sync", [(xs[1, ch], tmpf[:, :])])

        emit_layernorm_hbm(c, lambda ch, s=s: xin[s, ch], lncb, "m_ln")

        def ln1cb(ch, tc, t2):
            def gb(E, ch=ch, tc=tc):
                return E.tensor_scalar(
                    out=xn[:, ch * T + tc * 512: ch * T + tc * 512 + 512],
                    in0=t2[:, :], scalar1=nrmsb[:, ch:ch + 1],
                    scalar2=nrmsb[:, NCH + ch: NCH + ch + 1],
                    op0=ALU.mult, op1=ALU.add)

            c.prog.c("vector", gb)

        emit_layernorm_hbm(c, lambda ch: xs[1, ch], ln1cb, "m_ln")

        for slab in range(2):
            wslab(inw[:, slab * 512:(slab + 1) * 512], 0)

            def consume(col, tc, pb, slab=slab):
                ch = slab * 4 + col

                def ev(E, col=col, tc=tc, pb=pb, ch=ch):
                    return E.tensor_scalar_add(
                        out=xip[:, tc * 512:(tc + 1) * 512], in0=pb[:, 0:512],
                        scalar1=pbsb[:, ch:ch + 1])

                c.prog.c("vector", ev)
                if tc == 3:
                    def cvf(E, ch=ch):
                        return emit_conv3(E, tmpf, xip, 0, T,
                                          (cw(1, 0, ch), cw(1, 1, ch), cw(1, 2, ch)),
                                          cb(1, ch))

                    c.prog.c("vector", cvf)

                    def sil(E, ch=ch):
                        return E.activation(
                            out=xibf[:, ch * T:(ch + 1) * T], in_=tmpf[:, :],
                            func=AFT.Silu)

                    c.prog.c("scalar", sil)

            fm_proj(consume, xn, 0)

        def qkbmm(E, half=1):
            E.matmul(bank[2][:, 0:512], c.sb["onesf"][0:1, :],
                     qkbrow[0:1, 1024:1536] if False else c.sb["m_qkbr"][0:1, 1024:1536],
                     start=True, stop=True)
            return E.matmul(bank[3][:, 0:512], c.sb["onesf"][0:1, :],
                            c.sb["m_qkbr"][0:1, 1536:2048], start=True, stop=True)

        c.sb["m_qkbr"] = xip[0:1, 0:2 * D]
        c.prog.d("sync", [(c.sb["m_qkbr"][0:1, :], qkbrow[0:1, :])])
        c.prog.c("tensor", qkbmm)

        def qkbev(E):
            E.tensor_copy(out=qkbias[:, 0:512], in_=bank[2][:, 0:512])
            return E.tensor_copy(out=qkbias[:, 512:1024], in_=bank[3][:, 0:512])

        c.prog.c("vector", qkbev)
        wslab(qkw[:, 1024:1536], 0)
        wslab(qkw[:, 1536:2048], 1)
        c.prog.c("vector", lambda E: E.memset(kvacc[0:64, :], 0.0))

        KB, KRB, VT, QB, QBF = (bscr[:, 0:1024], bscr[:, 1024:2048],
                                bscr[:, 2048:3072], bscr[:, 3072:4096],
                                bscr[:, 4096:5120])
        QKS = fscr[:, 0:1024]
        SCR2 = fscr[:, 1024:2048]

        for ti in range(16):
            def kmm(E, ti=ti):
                last = None
                for half in range(2):
                    for ci in range(NCH):
                        last = E.matmul(
                            bank[half][:, 0:512],
                            xibf[:, ci * T + ti * 128: ci * T + ti * 128 + 128],
                            ws[:, half * NCH * 512 + ci * 512: half * NCH * 512 + ci * 512 + 512],
                            start=(ci == 0), stop=(ci == NCH - 1))
                return last

            c.prog.c("tensor", kmm)
            emit_elu1(c, bank, qkbias, QKS, SCR2, tmpf)
            c.prog.c("vector", lambda E: E.tensor_copy(out=KB[:, :], in_=QKS[:, :]))

            def ksmm(E, ti=ti):
                E.matmul(bank[2][0:1, 0:512], c.sb["onesb"][:, 0:1], KB[:, 0:512],
                         start=(ti == 0), stop=(ti == 15))
                return E.matmul(bank[3][0:1, 0:512], c.sb["onesb"][:, 0:1],
                                KB[:, 512:1024], start=(ti == 0), stop=(ti == 15))

            c.prog.c("tensor", ksmm)
            emit_rope(c, QKS, KRB, SCR2, cossb, sinsb)
            for ci in range(NCH):
                c.prog.c("vector", lambda E, ci=ci, ti=ti: E.tensor_copy(
                    out=tstg[:, :], in_=xibf[:, ci * T + ti * 128: ci * T + ti * 128 + 128]))
                c.prog.c("tensor", lambda E, ci=ci: E.transpose(
                    bank[6][:, 0:128], tstg[:, :], c.sb["identf"][:, :]))
                c.prog.c("vector", lambda E, ci=ci: E.tensor_copy(
                    out=VT[:, ci * 128:(ci + 1) * 128], in_=bank[6][:, 0:128]))

            for h in range(16):
                pb = bank[4 + h % 2]

                def kvmm(E, h=h, pb=pb):
                    return E.matmul(pb[0:64, 0:64],
                                    KRB[:, h * 64:(h + 1) * 64],
                                    VT[:, h * 64:(h + 1) * 64],
                                    start=True, stop=True)

                c.prog.c("tensor", kvmm)

                def kvacc_f(E, h=h, pb=pb):
                    dst = kvacc[0:64, h * 64:(h + 1) * 64]
                    return E.tensor_add(dst, dst, pb[0:64, 0:64])

                c.prog.c("vector", kvacc_f)

        c.prog.c("vector", lambda E: E.memset(kvbd[:, :], 0.0))

        def kvev(E):
            last = None
            for h in range(16):
                r = (h % 2) * 64
                last = E.tensor_copy(
                    out=kvbd[r:r + 64, (h // 2) * 128 + r: (h // 2) * 128 + r + 64],
                    in_=kvacc[0:64, h * 64:(h + 1) * 64])
            return last

        c.prog.c("vector", kvev)

        def ksev(E):
            E.tensor_copy(out=ksrow[0:1, 0:512], in_=bank[2][0:1, 0:512])
            return E.tensor_copy(out=ksrow[0:1, 512:1024], in_=bank[3][0:1, 0:512])

        c.prog.c("vector", ksev)

        def ksbmm(E):
            E.matmul(bank[0][:, 0:512], c.sb["onesf"][0:1, :], ksrow[0:1, 0:512],
                     start=True, stop=True)
            return E.matmul(bank[1][:, 0:512], c.sb["onesf"][0:1, :],
                            ksrow[0:1, 512:1024], start=True, stop=True)

        c.prog.c("tensor", ksbmm)

        def ksbev(E):
            E.tensor_copy(out=ksb[:, 0:512], in_=bank[0][:, 0:512])
            return E.tensor_copy(out=ksb[:, 512:1024], in_=bank[1][:, 0:512])

        c.prog.c("vector", ksbev)

        c.prog.d("sync", [(c.sb["m_qkbr"][0:1, :], qkbrow[0:1, :])])

        def qkbmm2(E):
            E.matmul(bank[2][:, 0:512], c.sb["onesf"][0:1, :],
                     c.sb["m_qkbr"][0:1, 0:512], start=True, stop=True)
            return E.matmul(bank[3][:, 0:512], c.sb["onesf"][0:1, :],
                            c.sb["m_qkbr"][0:1, 512:1024], start=True, stop=True)

        c.prog.c("tensor", qkbmm2)
        c.prog.c("vector", qkbev)
        wslab(qkw[:, 0:512], 0)
        wslab(qkw[:, 512:1024], 1)

        for ti in range(16):
            def qmm(E, ti=ti):
                last = None
                for half in range(2):
                    for ci in range(NCH):
                        last = E.matmul(
                            bank[half][:, 0:512],
                            xibf[:, ci * T + ti * 128: ci * T + ti * 128 + 128],
                            ws[:, half * NCH * 512 + ci * 512: half * NCH * 512 + ci * 512 + 512],
                            start=(ci == 0), stop=(ci == NCH - 1))
                return last

            c.prog.c("tensor", qmm)
            emit_elu1(c, bank, qkbias, QKS, SCR2, tmpf)
            c.prog.c("vector", lambda E: E.tensor_mul(SCR2[:, :], QKS[:, :], ksb[:, :]))

            def zred(E):
                E.reduce_sum(out=col_unit(zt[:, :]),
                             in_=seg_view(SCR2[:, 0:1024], 64), axis=AXX)
                E.tensor_single_scalar(out=zt[:, :], in_=zt[:, :],
                                       scalar=T * 1e-6, op=ALU.add)
                return E.reciprocal(out=zt[:, :], in_=zt[:, :])

            c.prog.c("vector", zred)
            emit_rope(c, QKS, None, SCR2, cossb, sinsb, out_f32=tmpf)
            def zap(E):
                zv = AP(zt[:, :].tensor, zt[:, :].offset,
                        [list(a) for a in zt[:, :].ap] + [[0, 64]])
                return E.tensor_tensor(out=seg_view(QB[:, :], 64),
                                       in0=seg_view(tmpf[:, 0:1024], 64),
                                       in1=zv, op=ALU.mult)

            c.prog.c("vector", zap)
            for ci in range(NCH):
                c.prog.c("vector", lambda E, ci=ci: E.tensor_copy(
                    out=tstg[:, :], in_=QB[:, ci * 128:(ci + 1) * 128]))
                c.prog.c("tensor", lambda E, ci=ci: E.transpose(
                    bank[6][:, 0:128], tstg[:, :], c.sb["identf"][:, :]))
                c.prog.c("vector", lambda E, ci=ci: E.tensor_copy(
                    out=QBF[:, ci * 128:(ci + 1) * 128], in_=bank[6][:, 0:128]))

            def amm(E, ti=ti):
                last = None
                for ci in range(NCH):
                    pb = bank[2 + ci // 4]
                    last = E.matmul(pb[:, (ci % 4) * 128:(ci % 4) * 128 + 128],
                                    kvbd[:, ci * 128:(ci + 1) * 128],
                                    QBF[:, ci * 128:(ci + 1) * 128],
                                    start=True, stop=True)
                return last

            c.prog.c("tensor", amm)

            def aev(E, ti=ti):
                last = None
                for ci in range(NCH):
                    pb = bank[2 + ci // 4]
                    last = E.tensor_copy(
                        out=attn[:, ci * T + ti * 128: ci * T + ti * 128 + 128],
                        in_=pb[:, (ci % 4) * 128:(ci % 4) * 128 + 128])
                return last

            c.prog.c("vector", aev)

        wslab(actw[:, 0:512], 0)
        wslab(actw[:, 512:1024], 1)
        wslab(outw[:, 0:512], 2)
        wslab(outw[:, 512:1024], 3)
        ACT2 = fscr[:, 0:512]
        LEP = fscr[:, 512:1024]
        TC2 = fscr[:, 1024:1536]
        OWIN = bscr[:, 0:4096]
        pl6 = Pipe(c.prog)
        for tc in range(4):
            for m in range(NCH):
                def apmm(E, m=m, tc=tc):
                    last = None
                    for ci in range(NCH):
                        last = E.matmul(
                            bank[m % 2][:, 0:512],
                            ws[:, (m // 4) * NCH * 512 + ci * 512 + (m % 4) * 128:
                               (m // 4) * NCH * 512 + ci * 512 + (m % 4) * 128 + 128],
                            xn[:, ci * T + tc * 512: ci * T + tc * 512 + 512],
                            start=(ci == 0), stop=(ci == NCH - 1))
                    return last

                c.prog.c("tensor", apmm, upto=pl6.up())

                def asil(E, m=m, tc=tc):
                    return E.activation(out=ACT2[:, :], in_=bank[m % 2][:, 0:512],
                                        func=AFT.Silu, bias=pbsb[:, 8 + m: 8 + m + 1])

                c.prog.c("scalar", asil)

                def lepcv(E, m=m, tc=tc):
                    return emit_conv3(E, LEP, xibf[:, m * T:(m + 1) * T], tc * 512,
                                      512, (cw(2, 0, m), cw(2, 1, m), cw(2, 2, m)),
                                      cb(2, m))

                c.prog.c("vector", lepcv)

                def comb(E, m=m, tc=tc):
                    E.tensor_add(TC2[:, :], LEP[:, :],
                                 attn[:, m * T + tc * 512: m * T + tc * 512 + 512])
                    return E.tensor_mul(OWIN[:, m * 512:(m + 1) * 512],
                                        TC2[:, :], ACT2[:, :])

                c.prog.c("vector", comb)
                pl6.bump()
            for co in range(NCH):
                def omm(E, co=co, tc=tc):
                    last = None
                    for ci in range(NCH):
                        last = E.matmul(
                            bank[2 + co % 2][:, 0:512],
                            ws[:, (2 + co // 4) * NCH * 512 + ci * 512 + (co % 4) * 128:
                               (2 + co // 4) * NCH * 512 + ci * 512 + (co % 4) * 128 + 128],
                            OWIN[:, ci * 512:(ci + 1) * 512],
                            start=(ci == 0), stop=(ci == NCH - 1))
                    return last

                c.prog.c("tensor", omm, upto=(None if co == 0 else pl6.up()))
                c.prog.d("sync", [(t3[:, :], xs[1, co, :, tc * 512:(tc + 1) * 512])],
                         upto=pl6.up(1))

                def oev(E, co=co, tc=tc):
                    E.tensor_scalar_add(out=TC2[:, :], in0=bank[2 + co % 2][:, 0:512],
                                        scalar1=pbsb[:, 16 + co: 16 + co + 1])
                    return E.tensor_add(TC2[:, :], TC2[:, :], t3[:, :])

                c.prog.c("vector", oev)
                c.prog.d("sync", [(xs[1, co, :, tc * 512:(tc + 1) * 512], TC2[:, :])])
                pl6.bump()

        for ch in range(NCH):
            c.prog.d("sync", [(c.sb["m_ln_xch"][:, :], xs[1, ch])])

            def cvf(E, ch=ch):
                return emit_conv3(E, tmpf, c.sb["m_ln_xch"], 0, T,
                                  (cw(3, 0, ch), cw(3, 1, ch), cw(3, 2, ch)),
                                  cb(3, ch))

            c.prog.c("vector", cvf)
            c.prog.c("vector", lambda E: E.tensor_add(tmpf[:, :], tmpf[:, :],
                                                      c.sb["m_ln_xch"][:, :]))
            c.prog.d("sync", [(xs[0, ch], tmpf[:, :])])

        def ln2cb(ch, tc, t2):
            def gb(E, ch=ch, tc=tc):
                return E.tensor_scalar(
                    out=xn[:, ch * T + tc * 512: ch * T + tc * 512 + 512],
                    in0=t2[:, :], scalar1=nrmsb[:, 2 * NCH + ch: 2 * NCH + ch + 1],
                    scalar2=nrmsb[:, 3 * NCH + ch: 3 * NCH + ch + 1],
                    op0=ALU.mult, op1=ALU.add)

            c.prog.c("vector", gb)

        emit_layernorm_hbm(c, lambda ch: xs[0, ch], ln2cb, "m_ln")

        H = xibf
        for tc in range(4):
            pl8 = Pipe(c.prog)
            prev_slab_mark = None
            for fslab in range(8):
                wslab(fc1[:, fslab * 512:(fslab + 1) * 512], fslab % 2,
                      upto=prev_slab_mark)
                slab_mark = c.prog.mark()
                prev_slab_mark = slab_mark
                for col in range(4):
                    hm = fslab * 4 + col

                    def hmm(E, col=col, tc=tc, fslab=fslab):
                        last = None
                        for ci in range(NCH):
                            last = E.matmul(
                                bank[col % 2][:, 0:512],
                                ws[:, (fslab % 2) * NCH * 512 + ci * 512 + col * 128:
                                   (fslab % 2) * NCH * 512 + ci * 512 + col * 128 + 128],
                                xn[:, ci * T + tc * 512: ci * T + tc * 512 + 512],
                                start=(ci == 0), stop=(ci == NCH - 1))
                        return last

                    up = slab_mark if col < 2 else pl8.up(2)
                    c.prog.c("tensor", hmm, upto=up)

                    def gev(E, hm=hm, col=col):
                        return E.activation(
                            out=H[:, hm * 512:(hm + 1) * 512],
                            in_=bank[col % 2][:, 0:512], func=AFT.Gelu,
                            bias=pbsb[:, 32 + hm: 32 + hm + 1])

                    c.prog.c("scalar", gev)
                    pl8.bump()
            for oslab in range(2):
                dst = ws[:, 0:4 * NCH * 512]
                c.prog.d("gpsimd", [
                    (dst.rearrange("p (c j) -> p c j", j=512),
                     fc2[:, oslab * 512:(oslab + 1) * 512].rearrange(
                         "(c p) j -> p c j", p=128))])
                plf = Pipe(c.prog)
                for col in range(4):
                    co = oslab * 4 + col

                    def fmm(E, col=col, tc=tc):
                        last = None
                        for hc in range(32):
                            last = E.matmul(
                                bank[4 + col % 2][:, 0:512],
                                ws[:, hc * 512 + col * 128: hc * 512 + col * 128 + 128],
                                H[:, hc * 512:(hc + 1) * 512],
                                start=(hc == 0), stop=(hc == 31))
                        return last

                    c.prog.c("tensor", fmm, upto=(None if col == 0 else plf.up()))
                    c.prog.d("sync", [
                        (t3[:, :], xs[0, co, :, tc * 512:(tc + 1) * 512]),
                        (xip[:, 0:512], xin[s, co, :, tc * 512:(tc + 1) * 512])])

                    def fev(E, co=co, tc=tc, col=col):
                        E.tensor_scalar_add(out=TC2[:, :],
                                            in0=bank[4 + col % 2][:, 0:512],
                                            scalar1=pbsb[:, 24 + co: 24 + co + 1])
                        E.tensor_add(TC2[:, :], TC2[:, :], t3[:, :])
                        return E.tensor_add(TC2[:, :], TC2[:, :], xip[:, 0:512])

                    c.prog.c("vector", fev)
                    c.prog.d("sync", [(xout[s, co, :, tc * 512:(tc + 1) * 512],
                                       TC2[:, :])])
                    plf.bump()
    c.prog.w("sync")


def emit_elu1(c, bank, qkbias, QKS, SCR2, tmpf):
    def e1(E):
        E.tensor_add(QKS[:, 0:512], bank[0][:, 0:512], qkbias[:, 0:512])
        return E.tensor_add(QKS[:, 512:1024], bank[1][:, 0:512], qkbias[:, 512:1024])

    c.prog.c("vector", e1)
    c.prog.c("vector", lambda E: E.tensor_scalar_min(
        out=tmpf[:, 0:1024], in0=QKS[:, :], scalar1=0.0))
    c.prog.c("scalar", lambda E: E.activation(out=tmpf[:, 0:1024],
                                              in_=tmpf[:, 0:1024], func=AFT.Exp))
    c.prog.c("vector", lambda E: E.tensor_relu(out=SCR2[:, :], in_=QKS[:, :]))
    c.prog.c("vector", lambda E: E.tensor_add(QKS[:, :], tmpf[:, 0:1024],
                                              SCR2[:, :]))


def emit_rope(c, QKS, KRB, SCR2, cossb, sinsb, out_f32=None):
    xe = QKS[:, 0:1024:2]
    xo = QKS[:, 1:1024:2]
    if out_f32 is None:
        oe = KRB[:, 0:1024:2]
        oo = KRB[:, 1:1024:2]
    else:
        oe = out_f32[:, 0:1024:2]
        oo = out_f32[:, 1:1024:2]

    def r1(E):
        E.tensor_mul(SCR2[:, 0:512], xe, cossb[:, :])
        E.tensor_mul(SCR2[:, 512:1024], xo, sinsb[:, :])
        return E.tensor_sub(oe, SCR2[:, 0:512], SCR2[:, 512:1024])

    c.prog.c("vector", r1)

    def r2(E):
        E.tensor_mul(SCR2[:, 0:512], xe, sinsb[:, :])
        E.tensor_mul(SCR2[:, 512:1024], xo, cossb[:, :])
        return E.tensor_add(oo, SCR2[:, 0:512], SCR2[:, 512:1024])

    c.prog.c("vector", r2)




def build_l1_attn(ctx, xin, xout):
    c = ctx
    NH = 16
    HD = 64
    wqkv = c.din("a_wqkvT", [D, 3 * D], F32)
    wproj = c.din("a_wprojT", [D, D], F32)
    bqkvT = c.din("a_bqkvT", [128, 24], F32)
    bprojT = c.din("a_bprojT", [128, 8], F32)
    bvrow = c.din("a_bv_row", [1, D], F32)
    qaug = c.din("a_qaug", [NH, 64, T], BF16)
    kaug = c.din("a_kaug", [NH, 64, T], BF16)
    masks = c.din("a_masks", [2, 128, 128], BF16)
    posi = c.din("a_posi", [128, 4], F32)
    ind8 = c.din("a_ind8", [128, 8], F32)

    qfm = c.S("a_qfm", [128, NCH * T], BF16)
    kfm = c.S("a_kfm", [128, NCH * T], BF16)
    vaug = c.S("a_vaug", [128, 16 * 1040], BF16)
    xnbf = c.S("a_xn", [128, NCH * T], BF16)
    qa = c.S("a_qa", [128, T], BF16)
    ka = c.S("a_ka", [128, T], BF16)
    wsl = c.S("a_wsl", [128, NCH * 512], BF16)
    msk = c.S("a_msk", [128, 2 * 128], BF16)
    bq8 = c.S("a_bq8", [128, 8], F32)
    bqkv_sb = c.S("a_bqkv", [128, 24], F32)
    bproj_sb = c.S("a_bproj", [128, 8], F32)
    bvb = c.S("a_bvb", [128, D], F32)
    stt = c.S("a_stt", [128, 512], F32)
    stt2 = c.S("a_stt2", [128, 1024], F32)
    pt2 = c.S("a_pt2", [128, 1024], BF16)
    negm = c.S("a_negm", [128, 16], F32)
    rmq = c.S("a_rmq", [128, 8], F32)
    mrow = c.S("a_mrow", [1, 512], F32)
    mbt = c.S("a_mbt", [128, 512], F32)
    of32 = c.S("a_of32", [64, 512], F32)
    rrow = c.S("a_rrow", [1, 512], F32)
    vsc = c.S("a_vsc", [128, 512], F32)
    t3 = c.S("a_t3", [128, 512], F32)
    ofm = xnbf

    bank = [c.ps[f"bank{i}"] for i in range(8)]

    posisb = c.S("a_posisb", [128, 4], F32)
    ind8sb = c.S("a_ind8sb", [128, 8], F32)
    chsb = c.S("a_chsb", [128, 8], F32)
    c.prog.d("sync", [(msk[:, :].rearrange("p (m j) -> p m j", j=128),
                   masks.rearrange("m p j -> p m j")),
                      (bqkv_sb[:, :], bqkvT[:, :]),
                      (bproj_sb[:, :], bprojT[:, :]),
                      (posisb[:, :], posi[:, :]),
                      (ind8sb[:, :], ind8[:, :])])
    c.prog.c("vector", lambda E: E.tensor_single_scalar(
        out=bq8[:, :], in_=bqkv_sb[:, 0:8], scalar=0.125, op=ALU.mult))
    def chprep(E):
        E.tensor_tensor(out=chsb[:, 0:2], in0=posisb[:, 0:4:2],
                        in1=posisb[:, 1:4:2], op=ALU.add)
        psv = AP(chsb[:, 0:2].tensor, chsb[:, 0:2].offset,
                 [list(chsb[:, 0:2].ap[0])] + [[1, 2], [0, 4]])
        return E.tensor_tensor(out=stt[:, 0:8].rearrange("p (a g) -> p a g", g=4),
                               in0=ind8sb[:, 0:8].rearrange("p (a g) -> p a g", g=4),
                               in1=psv, op=ALU.mult)

    c.prog.c("vector", chprep)
    c.prog.c("tensor", lambda E: E.matmul(
        c.ps["bank0"][:, 0:8], c.sb["onesf"][:, :], stt[:, 0:8],
        start=True, stop=True))
    c.prog.c("vector", lambda E: E.tensor_copy(out=chsb[:, :],
                                               in_=c.ps["bank0"][:, 0:8]))
    bvr = c.S("a_bvr", [1, D], F32)
    c.prog.d("sync", [(bvr[0:1, :], bvrow[0:1, :])])

    for s in range(SEQ_PER_CORE):
        def consume(ch, tc, t2):
            def xcast(E, ch=ch, tc=tc):
                return E.activation(
                    out=xnbf[:, ch * T + tc * 512: ch * T + tc * 512 + 512],
                    in_=t2[:, :], func=AFT.Copy)

            c.prog.c("scalar", xcast)

        emit_layernorm_hbm(c, lambda ch, s=s: xin[s, ch], consume, "a_ln")

        def bvmm(E):
            E.matmul(bank[0][:, 0:512], c.sb["onesf"][0:1, :],
                     c.sb["a_bvr"][0:1, 0:512], start=True, stop=True)
            return E.matmul(bank[1][:, 0:512], c.sb["onesf"][0:1, :],
                            c.sb["a_bvr"][0:1, 512:1024], start=True, stop=True)

        c.prog.c("tensor", bvmm)

        def bvev(E):
            E.tensor_copy(out=bvb[:, 0:512], in_=bank[0][:, 0:512])
            return E.tensor_copy(out=bvb[:, 512:1024], in_=bank[1][:, 0:512])

        c.prog.c("vector", bvev)

        for slab in range(6):
            c0 = slab * 512
            c.prog.d("gpsimd", [
                (wsl[:, :].rearrange("p (c j) -> p c j", j=512),
                 wqkv[:, c0:c0 + 512].rearrange("(c p) j -> p c j", p=128))])
            if slab < 4:
                plq = Pipe(c.prog)
                for col in range(4):
                    co = (c0 + col * 128) // 128
                    dstbase = (qfm if slab < 2 else kfm)
                    och = co % NCH
                    for tc in range(4):
                        def pmm(E, col=col, tc=tc):
                            last = None
                            for ci in range(NCH):
                                last = E.matmul(
                                    bank[tc % 2][:, 0:512],
                                    wsl[:, ci * 512 + col * 128: ci * 512 + col * 128 + 128],
                                    xnbf[:, ci * T + tc * 512: ci * T + tc * 512 + 512],
                                    start=(ci == 0), stop=(ci == NCH - 1))
                            return last

                        c.prog.c("tensor", pmm, upto=plq.up())
                        if slab < 2:
                            bias = bq8[:, och:och + 1]
                            scale = 0.125
                        else:
                            bias = bqkv_sb[:, 8 + och: 8 + och + 1]
                            scale = 1.0

                        def pev(E, dstbase=dstbase, och=och, tc=tc, bias=bias,
                                scale=scale):
                            return E.tensor_scalar(
                                out=dstbase[:, och * T + tc * 512: och * T + tc * 512 + 512],
                                in0=bank[tc % 2][:, 0:512], scalar1=scale,
                                scalar2=bias, op0=ALU.mult, op1=ALU.add)

                        c.prog.c("vector", pev)
                        plq.bump()
            else:
                half = slab - 4
                plv = Pipe(c.prog)
                for ti in range(16):
                    def vmm(E, ti=ti):
                        last = None
                        for ci in range(NCH):
                            last = E.matmul(
                                bank[ti % 2][:, 0:512],
                                xnbf[:, ci * T + ti * 128: ci * T + ti * 128 + 128],
                                wsl[:, ci * 512: ci * 512 + 512],
                                start=(ci == 0), stop=(ci == NCH - 1))
                        return last

                    c.prog.c("tensor", vmm, upto=plv.up())

                    def vev(E, ti=ti, half=half):
                        E.tensor_add(vsc[:, :], bank[ti % 2][:, 0:512],
                                     bvb[:, half * 512: half * 512 + 512])
                        dst = vaug[:, ti * 1040 + half * 520: (ti + 1) * 1040]
                        dstv = AP(dst.tensor, dst.offset,
                                  [list(dst.ap[0])] + [[65, 8], [1, 64]])
                        srcv = vsc[:, :].rearrange("p (h d) -> p h d", d=64)
                        return E.tensor_copy(out=dstv, in_=srcv)

                    c.prog.c("vector", vev)
                    plv.bump()

        def vones(E):
            a0 = vaug[:, 64:65]
            apv = AP(a0.tensor, a0.offset,
                     [list(a0.ap[0])] + [[1040, 16], [65, 16], [1, 1]])
            return E.memset(apv, 1.0)

        c.prog.c("vector", vones)

        for h in range(16):
            hch = h // 2
            hrow = (h % 2) * 64
            c.prog.c("vector", lambda E, hch=hch, hrow=hrow: E.tensor_copy(
                out=qa[0:64, :], in_=qfm[hrow:hrow + 64, hch * T:(hch + 1) * T]))
            c.prog.c("vector", lambda E, hch=hch, hrow=hrow: E.tensor_copy(
                out=ka[0:64, :], in_=kfm[hrow:hrow + 64, hch * T:(hch + 1) * T]))
            c.prog.d("sync", [(qa[64:128, :], qaug[h]), (ka[64:128, :], kaug[h])])

            smm_marks = {}
            tail_marks = {}
            for qt in range(16):
                nkeys = (qt + 1) * 128
                nfull = (qt * 128) // 512
                rem = (qt * 128) % 512
                b0 = (qt % 2) * 4

                def smm(E, qt=qt, nkeys=nkeys, b0=b0):
                    last = None
                    for kb in range((nkeys + 511) // 512):
                        w = min(512, nkeys - kb * 512)
                        last = E.matmul(
                            bank[b0 + kb][:, 0:w], qa[:, qt * 128:(qt + 1) * 128],
                            ka[:, kb * 512: kb * 512 + w],
                            start=True, stop=True)
                    return last

                c.prog.c("tensor", smm, upto=tail_marks.get(qt - 2))
                smm_marks[qt] = c.prog.mark()

                def sred(E, qt=qt, nfull=nfull, rem=rem, b0=b0):
                    db = (qt * 128) // 512
                    doff = (qt * 128) % 512
                    buf = stt[:, (qt % 2) * 128:(qt % 2) * 128 + 128]
                    E.tensor_add(buf, bank[b0 + db][:, doff:doff + 128],
                                 msk[:, 0:128])
                    np_ = 0
                    for kb in range(nfull):
                        E.reduce_max(rmq[:, np_:np_ + 1], bank[b0 + kb][:, 0:512],
                                     axis=AXX)
                        np_ += 1
                    if rem > 0:
                        E.reduce_max(rmq[:, np_:np_ + 1], bank[b0 + nfull][:, 0:rem],
                                     axis=AXX)
                        np_ += 1
                    E.reduce_max(rmq[:, np_:np_ + 1], buf, axis=AXX)
                    np_ += 1
                    E.reduce_max(rmq[:, 7:8], rmq[:, 0:np_], axis=AXX)
                    return E.tensor_single_scalar(
                        out=negm[:, qt:qt + 1], in_=rmq[:, 7:8], scalar=-1.0,
                        op=ALU.mult)

                c.prog.c("vector", sred, upto=smm_marks[qt])
                tail_marks[qt] = c.prog.mark()

            for g in range(4):
                def mtr(E, g=g):
                    last = None
                    for j in range(4):
                        last = E.transpose(bank[6][0:1, j * 128:(j + 1) * 128],
                                           negm[:, 4 * g + j: 4 * g + j + 1],
                                           c.sb["identf"][:, :])
                    return last

                c.prog.c("tensor", mtr)
                c.prog.c("scalar", lambda E: E.copy(out=mrow[0:1, :],
                                                    in_=bank[6][0:1, 0:512]))
                c.prog.c("tensor", lambda E: E.matmul(
                    bank[6][:, 0:512], c.sb["onesf"][0:1, :], mrow[0:1, :],
                    start=True, stop=True))
                c.prog.c("vector", lambda E: E.tensor_copy(out=mbt[:, :],
                                                           in_=bank[6][:, 0:512]))
                mbt_mark = c.prog.mark()
                if h >= 8:
                    c.prog.c("vector", lambda E, h=h: E.tensor_scalar_add(
                        out=mbt[:, :], in0=mbt[:, :],
                        scalar1=chsb[:, h - 8: h - 7]))
                nkt = 4 * g + 4
                stm = {}
                tdone = {}

                def emit_tail(kt, g=g, h=h, nkt=nkt):
                    bb = kt % 2
                    sbuf = stt[:, bb * 512: bb * 512 + 512] if False else None

                    def stsub(E, kt=kt, g=g, bb=bb):
                        dst = stt2[:, bb * 512: bb * 512 + 512]
                        last = E.tensor_add(dst, bank[bb][:, 0:512], mbt[:, :])
                        qsub = kt - 4 * g
                        if 0 <= qsub <= 3:
                            last = E.tensor_add(
                                dst[:, qsub * 128:(qsub + 1) * 128],
                                dst[:, qsub * 128:(qsub + 1) * 128],
                                msk[:, 128:256])
                        return last

                    c.prog.c("vector", stsub, upto=stm[kt])
                    m1 = c.prog.mark()
                    c.prog.c("scalar", lambda E, bb=bb: E.activation(
                        out=pt2[:, bb * 512: bb * 512 + 512],
                        in_=stt2[:, bb * 512: bb * 512 + 512], func=AFT.Exp),
                        upto=m1)
                    m2 = c.prog.mark()
                    nmask = min(4, kt - 4 * g)
                    if nmask > 0:
                        c.prog.c("vector", lambda E, nmask=nmask, bb=bb: E.memset(
                            pt2[:, bb * 512: bb * 512 + nmask * 128], 0.0),
                            upto=m2)
                    m3 = c.prog.mark()

                    def pvmm(E, kt=kt, g=g, h=h, bb=bb):
                        return E.matmul(
                            bank[5][0:65, 0:512],
                            vaug[:, kt * 1040 + h * 65: kt * 1040 + h * 65 + 65],
                            pt2[:, bb * 512: bb * 512 + 512],
                            start=(kt == 0), stop=(kt == nkt - 1))

                    c.prog.c("tensor", pvmm, upto=m3)
                    tdone[kt] = c.prog.mark()

                for kt in range(nkt):
                    def stmm(E, kt=kt, g=g):
                        return E.matmul(
                            bank[kt % 2][:, 0:512], ka[:, kt * 128:(kt + 1) * 128],
                            qa[:, g * 512:(g + 1) * 512], start=True, stop=True)

                    c.prog.c("tensor", stmm, upto=tdone.get(kt - 2, mbt_mark))
                    stm[kt] = c.prog.mark()
                    if kt >= 1:
                        emit_tail(kt - 1)
                emit_tail(nkt - 1)
                c.prog.c("vector", lambda E: E.reciprocal(
                    out=rrow[0:1, :], in_=bank[5][64:65, 0:512]))
                c.prog.c("scalar", lambda E: E.copy(out=of32[:, :],
                                                    in_=bank[5][0:64, 0:512]))
                c.prog.c("tensor", lambda E: E.matmul(
                    bank[6][0:64, 0:512], c.sb["onesf"][0:1, 0:64], rrow[0:1, :],
                    start=True, stop=True))

                def oev(E, g=g, hch=hch, hrow=hrow):
                    return E.tensor_mul(
                        ofm[hrow:hrow + 64, hch * T + g * 512: hch * T + g * 512 + 512],
                        of32[:, :], bank[6][0:64, 0:512])

                c.prog.c("vector", oev)

        for slab in range(2):
            c0 = slab * 512
            c.prog.d("gpsimd", [
                (wsl[:, :].rearrange("p (c j) -> p c j", j=512),
                 wproj[:, c0:c0 + 512].rearrange("(c p) j -> p c j", p=128))])
            plo = Pipe(c.prog)
            for col in range(4):
                co = slab * 4 + col
                c.prog.d("sync", [(c.sb["a_ln_xch"][:, :], xin[s, co])])
                for tc in range(4):
                    def omm(E, col=col, tc=tc):
                        last = None
                        for ci in range(NCH):
                            last = E.matmul(
                                bank[tc % 2][:, 0:512],
                                wsl[:, ci * 512 + col * 128: ci * 512 + col * 128 + 128],
                                ofm[:, ci * T + tc * 512: ci * T + tc * 512 + 512],
                                start=(ci == 0), stop=(ci == NCH - 1))
                        return last

                    c.prog.c("tensor", omm, upto=plo.up())

                    def oev2(E, co=co, tc=tc):
                        return E.tensor_scalar_add(out=t3[:, :],
                                                   in0=bank[tc % 2][:, 0:512],
                                                   scalar1=bproj_sb[:, co:co + 1])

                    c.prog.c("vector", oev2)

                    def oadd(E, tc=tc):
                        xs = c.sb["a_ln_xch"][:, tc * 512:(tc + 1) * 512]
                        return E.tensor_add(t3[:, :], t3[:, :], xs)

                    c.prog.c("vector", oadd)
                    c.prog.d("sync", [(xout[s, co, :, tc * 512:(tc + 1) * 512],
                                       t3[:, :])])
                    plo.bump()
    c.prog.w("sync")




def bT(v):
    v = np.asarray(v, np.float32)
    return np.ascontiguousarray(v.reshape(-1, 128).T)


def common_consts():
    idf = np.eye(128, dtype=np.float32)
    return {
        "ident_f": idf,
        "ident_b": idf.astype(ml_dtypes.bfloat16),
        "ones_f": np.ones((128, 128), np.float32),
        "ones_b": np.ones((128, 128), ml_dtypes.bfloat16),
    }


def l0_ffn_host_inputs(inp):
    g = lambda k: np.asarray(inp["L0_ffn_" + k], np.float32)
    NL = 2
    w1 = g("w1")
    w2 = g("w2")
    b1 = g("b1")
    w1p = np.zeros((NL, D, 5888), np.float32)
    w1p[:, :, 0:2730] = w1[:, :, 0:2730]
    w1p[:, :, 2816:2816 + 2730] = w1[:, :, 2730:5460]
    w2p = np.zeros((NL, 2816, D), np.float32)
    w2p[:, 0:2730, :] = w2
    b1pad = np.zeros((NL, 5888), np.float32)
    b1pad[:, 0:2730] = b1[:, 0:2730]
    b1pad[:, 2816:2816 + 2730] = b1[:, 2730:5460]
    half_idx = np.arange(0, 64, 2, dtype=np.float64)
    inv_freq = 1.0 / 10000.0 ** (half_idx / 64.0)
    t = np.arange(T, dtype=np.float64)
    fr = t[:, None] * inv_freq[None, :]
    cos = np.cos(fr)
    sin = np.sin(fr)
    base = (half_idx + 0.4 * 64) / (1.4 * 64)
    power = (t - T // 2) / 256.0
    scale = base[None, :] ** power[:, None]
    sgn = np.concatenate([-np.ones(32), np.ones(32)])
    cc = np.concatenate([cos, cos], axis=1)
    ss = np.concatenate([sin, sin], axis=1)
    sc = np.concatenate([scale, scale], axis=1)
    Aq = 8.0 * cc * sc
    BqS = 8.0 * ss * sc * sgn[None, :]
    Ak = cc / sc
    BkS = ss / sc * sgn[None, :]
    xpos = np.stack([a.reshape(16, 128, 64) for a in (Aq, BqS, Ak, BkS)])
    irel = (np.arange(512))
    maskS = np.full((4, 128, 1024), NEG, np.float32)
    for qt in range(4):
        i = 128 * qt + np.arange(128)
        j = np.arange(1024)
        ok = (i[:, None] <= j[None, :]) & (j[None, :] <= i[:, None] + 512)
        maskS[qt][ok] = 0.0
    maskT = np.full((8, 128, 512), NEG, np.float32)
    for kt in range(8):
        jj = 128 * kt + np.arange(128)
        q = np.arange(512)
        ok = (q[None, :] <= jj[:, None]) & (jj[:, None] <= q[None, :] + 512)
        maskT[kt][ok] = 0.0
    return {
        "g_wqkv": g("wqkv"),
        "g_wo": g("wo"),
        "g_w1p": w1p,
        "g_w2p": w2p,
        "g_qs": np.ascontiguousarray(np.broadcast_to(
            g("q_scale")[:, None, :], (NL, 128, 64))),
        "g_ks": np.ascontiguousarray(np.broadcast_to(
            g("k_scale")[:, None, :], (NL, 128, 64))),
        "g_lgb": np.stack([np.stack([bT(g("ln_g")[l]), bT(g("ln_b")[l])])
                           for l in range(NL)]),
        "g_b1p": np.stack([bT(b1pad[l]) for l in range(NL)]),
        "g_b2": np.stack([bT(g("b2")[l]) for l in range(NL)]),
        "g_xpos": xpos.astype(ml_dtypes.bfloat16),
        "g_maskS": maskS.astype(ml_dtypes.bfloat16),
        "g_maskT": maskT.astype(ml_dtypes.bfloat16),
    }


def l0_attn_host_inputs(inp):
    g = lambda k: np.asarray(inp["L0_attn_" + k], np.float32)

    def convT(w):
        w = np.asarray(w, np.float32)[:, 0, :]
        return np.stack([bT(w[:, t]) for t in range(3)])

    theta = 1.0 / 10000.0 ** (np.arange(512, dtype=np.float64) / 512.0)
    cosb = np.broadcast_to(np.cos(theta).astype(np.float32), (128, 512))
    sinb = np.broadcast_to(np.sin(theta).astype(np.float32), (128, 512))
    pbias = np.zeros((128, 64), np.float32)
    pbias[:, 0:8] = bT(g("in_b"))
    pbias[:, 8:16] = bT(g("actp_b"))
    pbias[:, 16:24] = bT(g("out_b"))
    pbias[:, 24:32] = bT(g("fc2_b"))
    pbias[:, 32:64] = bT(g("fc1_b"))
    norms = np.stack([bT(g("norm1_g")), bT(g("norm1_b")),
                      bT(g("norm2_g")), bT(g("norm2_b"))])
    return {
        "m_inw": g("in_w"), "m_actp": g("actp_w"), "m_qkw": g("qk_w"),
        "m_outw": g("out_w"), "m_fc1": g("fc1_w"), "m_fc2": g("fc2_w"),
        "m_convw": np.stack([convT(g("cpe1_w")), convT(g("dwc_w")),
                             convT(g("lepe_w")), convT(g("cpe2_w"))]),
        "m_convb": np.stack([bT(g("cpe1_b")), bT(g("dwc_b")),
                             bT(g("lepe_b")), bT(g("cpe2_b"))]),
        "m_norms": norms,
        "m_pbias": pbias,
        "m_qkb_row": np.ascontiguousarray(g("qk_b").reshape(1, 2 * D)),
        "m_costab": np.stack([cosb, sinb]).astype(ml_dtypes.bfloat16),
    }


def l1_attn_host_inputs(inp):
    wqkv = np.asarray(inp["L1_attn_w_qkv"], np.float32)
    wproj = np.asarray(inp["L1_attn_w_proj"], np.float32)
    bqkv = np.asarray(inp["L1_attn_b_qkv"], np.float32)
    bproj = np.asarray(inp["L1_attn_b_proj"], np.float32)
    k = np.arange(256, dtype=np.float64)
    freqs = 1.0 / 10000.0 ** (k / 512.0)
    i = np.arange(T, dtype=np.float64)
    th = i[None, :] * freqs[:, None]
    cos = np.cos(th)
    sin = np.sin(th)
    qaug = np.zeros((16, 64, T), np.float32)
    kaug = np.zeros((16, 64, T), np.float32)
    for h in range(8):
        qaug[h, 0:32] = cos[32 * h:32 * h + 32]
        qaug[h, 32:64] = sin[32 * h:32 * h + 32]
        kaug[h, 0:32] = (cos - sin)[32 * h:32 * h + 32]
        kaug[h, 32:64] = (cos + sin)[32 * h:32 * h + 32]
    ii = np.arange(128)
    maskS = np.where(ii[None, :] > ii[:, None], NEG, 0.0).astype(np.float32)
    masks = np.stack([maskS, maskS.T])
    posi = np.asarray(inp["L1_attn_pos_independent"], np.float32)
    posi4 = np.ascontiguousarray(posi.reshape(2, 128, 2).transpose(1, 0, 2)
                                 .reshape(128, 4))
    r = np.arange(128)
    ind8 = np.zeros((128, 8), np.float32)
    for h in range(8):
        ind8[:, h] = (r // 32 == h % 4).astype(np.float32)
    return {
        "a_posi": posi4,
        "a_ind8": ind8,
        "a_wqkvT": np.ascontiguousarray(wqkv.T),
        "a_wprojT": np.ascontiguousarray(wproj.T),
        "a_bqkvT": bT(bqkv),
        "a_bprojT": bT(bproj),
        "a_bv_row": np.ascontiguousarray(bqkv[2048:].reshape(1, D)),
        "a_qaug": qaug.astype(ml_dtypes.bfloat16),
        "a_kaug": kaug.astype(ml_dtypes.bfloat16),
        "a_masks": masks.astype(ml_dtypes.bfloat16),
    }


def l1_ffn_host_inputs(inp):
    return {
        "f_Wg": np.ascontiguousarray(
            np.asarray(inp["L1_ffn_Wg"], np.float32).reshape(8, 128, 8)),
        "f_bg_b": np.ascontiguousarray(
            np.broadcast_to(np.asarray(inp["L1_ffn_bg"], np.float32), (128, 8))),
        "f_W1": np.asarray(inp["L1_ffn_W1"], np.float32),
        "f_W2": np.asarray(inp["L1_ffn_W2"], np.float32),
        "f_b1T": np.stack([bT(b) for b in np.asarray(inp["L1_ffn_b1"], np.float32)]),
        "f_b2": np.asarray(inp["L1_ffn_b2"], np.float32),
    }


def fm_of(x_seqs):
    s, t, d = x_seqs.shape
    xt = np.ascontiguousarray(x_seqs.transpose(0, 2, 1))
    return np.ascontiguousarray(xt.reshape(s, NCH, 128, t))


def fm_back(xfm):
    s = xfm.shape[0]
    return np.ascontiguousarray(
        xfm.reshape(s, D, T).transpose(0, 2, 1))




def build_program_l0ffn(dbg=None):
    nc = bass.Bass("TRN2", target_bir_lowering=False, detect_race_conditions=False)
    ctx = Ctx(nc)
    for i in range(8):
        ctx.P(f"bank{i}", [128, 512], F32)
    xin = ctx.din("xin", [SEQ_PER_CORE, NCH, 128, T], F32)
    xout = ctx.dout("xout", [SEQ_PER_CORE, NCH, 128, T], F32)
    setup_common(ctx)
    build_l0_ffn(ctx, xin, xout, dbg=dbg)
    ctx.prog.emit()
    return ctx


def run_l0_ffn(x_full, inp, n_cores=NCORES, dbg=None):
    ctx = build_program_l0ffn(dbg=dbg)
    consts = common_consts()
    winp = l0_ffn_host_inputs(inp)
    in_maps = []
    for cix in range(n_cores):
        xs = x_full[cix * SEQ_PER_CORE:(cix + 1) * SEQ_PER_CORE]
        m = {"xin": fm_of(xs)}
        m.update(consts)
        m.update(winp)
        in_maps.append(m)
    res = run_bass_kernel_spmd(ctx.nc, in_maps, core_ids=list(range(n_cores)))
    outs = [fm_back(r["xout"]) for r in res.results]
    return np.concatenate(outs, axis=0)


def build_program_l0attn():
    nc = bass.Bass("TRN2", target_bir_lowering=False, detect_race_conditions=False)
    ctx = Ctx(nc)
    for i in range(8):
        ctx.P(f"bank{i}", [128, 512], F32)
    xin = ctx.din("xin", [SEQ_PER_CORE, NCH, 128, T], F32)
    xout = ctx.dout("xout", [SEQ_PER_CORE, NCH, 128, T], F32)
    setup_common(ctx)
    build_l0_attn(ctx, xin, xout)
    ctx.prog.emit()
    return ctx


def run_l0_attn(x_full, inp, n_cores=NCORES):
    ctx = build_program_l0attn()
    consts = common_consts()
    winp = l0_attn_host_inputs(inp)
    in_maps = []
    for cix in range(n_cores):
        xs = x_full[cix * SEQ_PER_CORE:(cix + 1) * SEQ_PER_CORE]
        m = {"xin": fm_of(xs)}
        m.update(consts)
        m.update(winp)
        in_maps.append(m)
    res = run_bass_kernel_spmd(ctx.nc, in_maps, core_ids=list(range(n_cores)))
    outs = [fm_back(r["xout"]) for r in res.results]
    return np.concatenate(outs, axis=0)


def build_program_l1attn():
    nc = bass.Bass("TRN2", target_bir_lowering=False, detect_race_conditions=False)
    ctx = Ctx(nc)
    for i in range(8):
        ctx.P(f"bank{i}", [128, 512], F32)
    xin = ctx.din("xin", [SEQ_PER_CORE, NCH, 128, T], F32)
    xout = ctx.dout("xout", [SEQ_PER_CORE, NCH, 128, T], F32)
    setup_common(ctx)
    build_l1_attn(ctx, xin, xout)
    ctx.prog.emit()
    return ctx


def run_l1_attn(x_full, inp, n_cores=NCORES):
    ctx = build_program_l1attn()
    consts = common_consts()
    winp = l1_attn_host_inputs(inp)
    in_maps = []
    for cix in range(n_cores):
        xs = x_full[cix * SEQ_PER_CORE:(cix + 1) * SEQ_PER_CORE]
        m = {"xin": fm_of(xs)}
        m.update(consts)
        m.update(winp)
        in_maps.append(m)
    res = run_bass_kernel_spmd(ctx.nc, in_maps, core_ids=list(range(n_cores)))
    outs = [fm_back(r["xout"]) for r in res.results]
    return np.concatenate(outs, axis=0)


def build_program_l1ffn():
    nc = bass.Bass("TRN2", target_bir_lowering=False, detect_race_conditions=False)
    ctx = Ctx(nc)
    for i in range(8):
        ctx.P(f"bank{i}", [128, 512], F32)
    xin = ctx.din("xin", [SEQ_PER_CORE, NCH, 128, T], F32)
    xout = ctx.dout("xout", [SEQ_PER_CORE, NCH, 128, T], F32)
    setup_common(ctx)
    build_l1_ffn(ctx, xin, xout)
    ctx.prog.emit()
    return ctx


def run_l1_ffn(x_full, inp, n_cores=NCORES):
    ctx = build_program_l1ffn()
    consts = common_consts()
    winp = l1_ffn_host_inputs(inp)
    in_maps = []
    for cix in range(n_cores):
        xs = x_full[cix * SEQ_PER_CORE:(cix + 1) * SEQ_PER_CORE]
        m = {"xin": fm_of(xs)}
        m.update(consts)
        m.update(winp)
        in_maps.append(m)
    res = run_bass_kernel_spmd(ctx.nc, in_maps, core_ids=list(range(n_cores)))
    outs = [fm_back(r["xout"]) for r in res.results]
    return np.concatenate(outs, axis=0)


def build_program_fused():
    nc = bass.Bass("TRN2", target_bir_lowering=False, detect_race_conditions=False)
    ctx = Ctx(nc, pool_kb=205)
    for i in range(8):
        ctx.P(f"bank{i}", [128, 512], F32)
    xin = ctx.din("xin", [SEQ_PER_CORE, NCH, 128, T], F32)
    xout = ctx.dout("xout", [SEQ_PER_CORE, NCH, 128, T], F32)
    xa = nc.dram_tensor("xbounceA", [SEQ_PER_CORE, NCH, 128, T], F32,
                        kind="Internal").ap()
    xb = nc.dram_tensor("xbounceB", [SEQ_PER_CORE, NCH, 128, T], F32,
                        kind="Internal").ap()
    setup_common(ctx)
    ctx.pool_reset()
    build_l0_attn(ctx, xin, xa)
    ctx.pool_reset()
    build_l0_ffn(ctx, xa, xb)
    ctx.pool_reset()
    build_l1_attn(ctx, xb, xa)
    ctx.pool_reset()
    build_l1_ffn(ctx, xa, xout)
    ctx.pool_reset()
    ctx.prog.emit()
    return ctx


def run_fused(inputs, n_cores=NCORES):
    ctx = _get_ctx("fused", build_program_fused)
    x = np.ascontiguousarray(np.asarray(inputs["L0_attn_x"], np.float32))
    host = {}
    host.update(common_consts())
    host.update(l0_attn_host_inputs(inputs))
    host.update(l0_ffn_host_inputs(inputs))
    host.update(l1_attn_host_inputs(inputs))
    host.update(l1_ffn_host_inputs(inputs))
    in_maps = []
    for cix in range(n_cores):
        xs = x[cix * SEQ_PER_CORE:(cix + 1) * SEQ_PER_CORE]
        m = {"xin": fm_of(xs)}
        m.update(host)
        in_maps.append(m)
    res = run_bass_kernel_spmd(ctx.nc, in_maps, core_ids=list(range(n_cores)))
    outs = [fm_back(r["xout"]) for r in res.results]
    return np.concatenate(outs, axis=0)


_CTX_CACHE = {}


def _get_ctx(name, builder):
    if name not in _CTX_CACHE:
        _CTX_CACHE[name] = builder()
    return _CTX_CACHE[name]


def _run_block(name, builder, host_inputs, x_full, n_cores=NCORES):
    ctx = _get_ctx(name, builder)
    consts = common_consts()
    in_maps = []
    for cix in range(n_cores):
        xs = x_full[cix * SEQ_PER_CORE:(cix + 1) * SEQ_PER_CORE]
        m = {"xin": fm_of(xs)}
        m.update(consts)
        m.update(host_inputs)
        in_maps.append(m)
    res = run_bass_kernel_spmd(ctx.nc, in_maps, core_ids=list(range(n_cores)))
    outs = [fm_back(r["xout"]) for r in res.results]
    return np.concatenate(outs, axis=0)


def kernel(**inputs):
    return run_fused(inputs)


def kernel_unfused(**inputs):
    x = np.ascontiguousarray(np.asarray(inputs["L0_attn_x"], np.float32))
    x = _run_block("b1", build_program_l0attn, l0_attn_host_inputs(inputs), x)
    x = _run_block("b2", build_program_l0ffn, l0_ffn_host_inputs(inputs), x)
    x = _run_block("b3", build_program_l1attn, l1_attn_host_inputs(inputs), x)
    x = _run_block("b4", build_program_l1ffn, l1_ffn_host_inputs(inputs), x)
    return x


# version 11
# speedup vs baseline: 1.5390x; 1.0374x over previous
import numpy as np
import ml_dtypes
import concourse.bass as bass
import concourse.mybir as mybir
from concourse.ap import AP
from concourse.bass_utils import run_bass_kernel_spmd

F32 = mybir.dt.float32
BF16 = mybir.dt.bfloat16
AFT = mybir.ActivationFunctionType
ALU = mybir.AluOpType
AXX = mybir.AxisListType.X

NCORES = 8
SEQ_PER_CORE = 2
T = 2048
D = 1024
NCH = 8
NEG = -3.0e38



def bcast_free(ap, n):
    return AP(ap.tensor, ap.offset, [list(a) for a in ap.ap] + [[0, n]])


def seg_view(ap, d):
    return ap.rearrange("p (h d) -> p h d", d=d)


def col_unit(ap):
    return AP(ap.tensor, ap.offset, [list(a) for a in ap.ap] + [[1, 1]])



_ENG_CSEM = {"tensor": "t", "vector": "v", "scalar": "s", "gpsimd": "g", "sync": None}
_ENG_DSEM = {"sync": "sd", "gpsimd": "gd", "scalar": "scd"}


class Prog:
    def __init__(self, nc):
        self.nc = nc
        self.sems = {k: nc.alloc_semaphore("sem_" + k)
                     for k in ("t", "v", "s", "g", "sd", "gd", "scd")}
        self.cnt = {k: 0 for k in self.sems}
        self.steps = []

    def c(self, eng, fn, upto=None):
        self.steps.append((eng, fn, "c", upto))

    def d(self, eng, pairs, upto=None):
        self.steps.append((eng, pairs, "d", upto))

    def w(self, eng):
        self.steps.append((eng, None, "w", None))

    def mark(self):
        return len(self.steps)

    def emit(self):
        nc = self.nc
        sems, cnt = self.sems, self.cnt
        last_seen = {e: {k: 0 for k in self.cnt} for e in _ENG_CSEM}
        per_eng = {e: [] for e in _ENG_CSEM}
        snaps = []
        for eng, payload, kind, upto in self.steps:
            snaps.append(dict(cnt))
            own_c = _ENG_CSEM[eng]
            tgt = cnt if upto is None else snaps[upto]
            waits = []
            for k in cnt:
                c = tgt[k]
                if k == own_c:
                    continue
                if c > last_seen[eng][k]:
                    waits.append((k, c))
                    last_seen[eng][k] = c
            per_eng[eng].append((waits, payload, kind))
            if kind == "c":
                cnt[own_c] += 1
            elif kind == "d":
                dq = _ENG_DSEM[eng]
                prs = payload
                assert not callable(prs)
                cnt[dq] += 16 * len(prs)

        import os
        chunk_default = int(os.environ.get("PROG_BODY_CHUNK", "1000000"))
        chunk_eng = os.environ.get("PROG_BODY_CHUNK_ENGS", "vector:1")
        chunk_map = {}
        for kv in chunk_eng.split(","):
            if ":" in kv:
                k, v = kv.split(":")
                chunk_map[k] = int(v)

        def mk_body(eng, items):
            def body(E):
                own_c = _ENG_CSEM[eng]
                for waits, payload, kind in items:
                    for k, v in waits:
                        E.wait_ge(sems[k], v)
                    if kind == "c":
                        ret = payload(E)
                        assert ret is not None, "compute step must return inst"
                        ret.then_inc(sems[own_c], 1)
                    elif kind == "d":
                        dq = _ENG_DSEM[eng]
                        for out_ap, in_ap in payload:
                            E.dma_start(out=out_ap, in_=in_ap).then_inc(
                                sems[dq], 16)
            return body

        with nc.Block() as block:
            for eng in per_eng:
                items = per_eng[eng]
                chunk = chunk_map.get(eng, chunk_default)
                for i0 in range(0, len(items), chunk):
                    getattr(block, eng)(mk_body(eng, items[i0:i0 + chunk]))
        return nc


class Pipe:

    def __init__(self, prog):
        self.prog = prog
        self.marks = [prog.mark()]

    def up(self, depth=2):
        return self.marks[max(0, len(self.marks) - depth)]

    def bump(self):
        self.marks.append(self.prog.mark())




class Ctx:

    def __init__(self, nc, pool_kb=None):
        self.nc = nc
        self.prog = Prog(nc)
        self.sb = {}
        self.ps = {}
        self.inputs = {}
        self.outputs = {}
        self.pool = None
        if pool_kb is not None:
            self.pool_bytes = pool_kb * 1024
            ph = nc.alloc_sbuf_tensor("pool", [128, self.pool_bytes // 2], BF16)
            self.pool = {BF16: ph.ap() if hasattr(ph, "ap") else ph[:, :],
                         F32: ph.bitcast(F32)[:, :] if hasattr(ph, "bitcast")
                         else None}
            if self.pool[F32] is None:
                self.pool[F32] = ph[:, :].bitcast(F32)
            self.pool_off = 0
            self.pool_hwm = 0

    def pool_reset(self):
        self.pool_hwm = max(self.pool_hwm, self.pool_off)
        self.pool_off = 0

    def din(self, name, shape, dtype=F32):
        t = self.nc.dram_tensor(name, list(shape), dtype, kind="ExternalInput").ap()
        self.inputs[name] = t
        return t

    def dout(self, name, shape, dtype=F32):
        t = self.nc.dram_tensor(name, list(shape), dtype, kind="ExternalOutput").ap()
        self.outputs[name] = t
        return t

    def S(self, name, shape, dtype, pool=True):
        if self.pool is None or not pool:
            h = self.nc.alloc_sbuf_tensor(name, list(shape), dtype)
            self.sb[name] = h
            return h
        isz = 2 if dtype == BF16 else 4
        cols = shape[1]
        off = (self.pool_off + 511) // 512 * 512
        need = cols * isz
        assert off + need <= self.pool_bytes, (name, off, need, self.pool_bytes)
        self.pool_off = off + need
        base = self.pool[dtype]
        sl = base[:, off // isz: off // isz + cols]
        if shape[0] != 128:
            sl = sl[0:shape[0], :]
        self.sb[name] = sl
        return sl

    def P(self, name, shape, dtype=F32):
        h = self.nc.alloc_psum_tensor(name, list(shape), dtype)
        self.ps[name] = h
        return h


def setup_common(ctx):
    c = ctx
    c.din("ident_f", [128, 128], F32)
    c.din("ident_b", [128, 128], BF16)
    c.din("ones_f", [128, 128], F32)
    c.din("ones_b", [128, 128], BF16)
    idf = c.S("identf", [128, 128], F32, pool=False)
    idb = c.S("identb", [128, 128], BF16, pool=False)
    onf = c.S("onesf", [128, 128], F32, pool=False)
    onb = c.S("onesb", [128, 128], BF16, pool=False)
    c.prog.d("sync", [(idf[:, :], c.inputs["ident_f"][:, :]),
                      (idb[:, :], c.inputs["ident_b"][:, :]),
                      (onf[:, :], c.inputs["ones_f"][:, :]),
                      (onb[:, :], c.inputs["ones_b"][:, :])])


def emit_layernorm_fm(ctx, src_tiles, tc_width=512, out_cb=None,
                      gamma=None, beta=None, scratch_pref="ln"):
    c = ctx
    ntc = T // tc_width
    nm = scratch_pref

    def get(suffix, shape, dt):
        key = nm + suffix
        if key not in c.sb:
            c.S(key, shape, dt)
        return c.sb[key]

    xbf = get("_xbf", [128, tc_width], BF16)
    sqb = get("_sqb", [128, tc_width], BF16)
    mb = get("_mb", [128, tc_width], F32)
    vb = get("_vb", [128, tc_width], F32)
    rb = get("_rb", [128, tc_width], F32)
    mrb = get("_mrb", [128, tc_width], F32)
    t2 = get("_t2", [128, tc_width], F32)
    pss = c.ps["bank0"]
    psq = c.ps["bank1"]
    onb = c.sb["onesb"]

    for tc in range(ntc):
        t0 = tc * tc_width
        for ch in range(NCH):
            src = src_tiles(ch)
            sl = src[:, t0:t0 + tc_width]

            def cast_fn(E, sl=sl):
                return E.tensor_copy(out=xbf[:, :], in_=sl)

            def sq_fn(E, sl=sl):
                return E.activation(out=sqb[:, :], in_=sl, func=AFT.Square)

            c.prog.c("vector", cast_fn)
            c.prog.c("scalar", sq_fn)

            def mm_fn(E, ch=ch):
                E.matmul(pss[:, 0:tc_width], onb[:, :], xbf[:, :],
                         start=(ch == 0), stop=(ch == NCH - 1))
                return E.matmul(psq[:, 0:tc_width], onb[:, :], sqb[:, :],
                                start=(ch == 0), stop=(ch == NCH - 1))

            c.prog.c("tensor", mm_fn)

        c.prog.c("vector", lambda E: E.tensor_single_scalar(
            out=mb[:, :], in_=pss[:, 0:tc_width], scalar=1.0 / D, op=ALU.mult))
        c.prog.c("vector", lambda E: E.tensor_scalar(
            out=vb[:, :], in0=psq[:, 0:tc_width], scalar1=1.0 / D, scalar2=1e-5,
            op0=ALU.mult, op1=ALU.add))
        c.prog.c("vector", lambda E: E.tensor_mul(mrb[:, :], mb[:, :], mb[:, :]))
        c.prog.c("vector", lambda E: E.tensor_sub(vb[:, :], vb[:, :], mrb[:, :]))
        c.prog.c("scalar", lambda E: E.activation(
            out=rb[:, :], in_=vb[:, :], func=AFT.Sqrt))
        c.prog.c("vector", lambda E: E.reciprocal(out=rb[:, :], in_=rb[:, :]))
        c.prog.c("vector", lambda E: E.tensor_mul(mrb[:, :], mb[:, :], rb[:, :]))

        for ch in range(NCH):
            src = src_tiles(ch)
            sl = src[:, t0:t0 + tc_width]

            def ap1(E, sl=sl):
                return E.tensor_mul(t2[:, :], sl, rb[:, :])

            c.prog.c("vector", ap1)
            c.prog.c("vector", lambda E: E.tensor_sub(t2[:, :], t2[:, :], mrb[:, :]))
            out_cb(ch, tc, t2)


def stream_weight(ctx, qname, dst_sb, dram_ap):
    ctx.prog.d("gpsimd", [(dst_sb, dram_ap)])




def build_l1_ffn(ctx, xin, xout):
    c = ctx
    E = 8
    wg = c.din("f_Wg", [8, 128, 8], F32)
    bgb = c.din("f_bg_b", [128, 8], F32)
    w1 = c.din("f_W1", [E, D, D], F32)
    w2 = c.din("f_W2", [E, D, D], F32)
    b1t = c.din("f_b1T", [E, 128, 8], F32)
    b2 = c.din("f_b2", [E, D], F32)

    xmod = c.S("f_xmod", [128, NCH * T], F32)
    xnbf = c.S("f_xn", [128, NCH * T], BF16)
    wgsb = c.S("f_wg", [128, NCH * 8], F32)
    bgsb = c.S("f_bgb", [128, 8], F32)
    b1sb = c.S("f_b1", [128, E * 8], F32)
    b2sb = c.S("f_b2sb", [1, 8 * D], BF16)
    lgs = c.S("f_lgs", [128, 8], F32)
    rmax = c.S("f_rmax", [128, 1], F32)
    nmax = c.S("f_nmax", [128, 1], F32)
    ssum = c.S("f_ssum", [128, 1], F32)
    maskt = c.S("f_mask", [128, 8], F32)
    wselall = c.S("f_wselall", [128, 128], F32)
    wsrow = c.S("f_wsrow", [1, T], F32)
    wsrowb = c.S("f_wsrowb", [1, T], BF16)
    wbe = c.S("f_wbe", [128, 512], F32)
    gl = c.S("f_gl", [128, 512], F32)
    hs = c.S("f_hs", [128, 8 * 512], BF16)
    wa = c.S("f_wa", [128, NCH * 1024], BF16)
    wb = c.S("f_wb", [128, NCH * 1024], BF16)

    bank = [c.ps[f"bank{i}"] for i in range(8)]

    c.prog.d("sync", [
        (wgsb[:, :].rearrange("p (c e) -> p c e", e=8), wg.rearrange("c p e -> p c e")),
        (bgsb[:, :], bgb[:, :]),
        (b1sb[:, :].rearrange("p (e k) -> p e k", k=8), b1t.rearrange("e p k -> p e k"))])
    b2flat = AP(b2.tensor, b2.offset, [[8 * D, 1], [1, 8 * D]])
    c.prog.d("gpsimd", [(b2sb[0:1, :], b2flat)])

    for s in range(SEQ_PER_CORE):
        c.prog.d("sync", [(xmod[:, ch * T:(ch + 1) * T], xin[s, ch])
                          for ch in range(NCH)])

        def consume(ch, tc, t2, s=s):
            def gmm(E_, ch=ch, tc=tc):
                last = None
                for sub in range(4):
                    last = E_.matmul(
                        bank[2 + sub][:, 0:8],
                        t2[:, sub * 128:(sub + 1) * 128],
                        wgsb[:, ch * 8:(ch + 1) * 8],
                        start=(ch == 0), stop=(ch == NCH - 1))
                return last

            c.prog.c("tensor", gmm)

            def xcast(E_, ch=ch, tc=tc):
                return E_.activation(
                    out=xnbf[:, ch * T + tc * 512: ch * T + tc * 512 + 512],
                    in_=t2[:, :], func=AFT.Copy)

            c.prog.c("scalar", xcast)
            if ch == NCH - 1:
                emit_gating(c, tc, bank, bgsb, lgs, rmax, nmax, ssum, maskt,
                            wselall)

        emit_layernorm_fm(ctx, lambda ch: xmod[:, ch * T:(ch + 1) * T],
                          out_cb=consume, scratch_pref="f_ln")

        NTC = 4
        TCW = 512
        for e in range(E):
            c.prog.d("gpsimd", [
                (wa[:, :].rearrange("p (c j) -> p c j", j=1024),
                 w1[e].rearrange("(c p) j -> p c j", p=128)),
                (wb[:, :].rearrange("p (c j) -> p c j", j=1024),
                 w2[e].rearrange("(c p) j -> p c j", p=128))])
            for rnd in range(4):
                for tj in range(4):
                    ti = rnd * 4 + tj

                    def trw(E_, e=e, ti=ti, tj=tj):
                        return E_.transpose(bank[4][0:1, tj * 128:(tj + 1) * 128],
                                            wselall[:, ti * 8 + e: ti * 8 + e + 1],
                                            c.sb["identf"][:, :])

                    c.prog.c("tensor", trw)

                def evr(E_, rnd=rnd):
                    return E_.copy(out=wsrow[0:1, rnd * 512:(rnd + 1) * 512],
                                   in_=bank[4][0:1, 0:512])

                c.prog.c("scalar", evr)

                def evrb(E_, rnd=rnd):
                    return E_.tensor_copy(out=wsrowb[0:1, rnd * 512:(rnd + 1) * 512],
                                          in_=bank[4][0:1, 0:512])

                c.prog.c("vector", evrb)
            for tc in range(NTC):
                t0 = tc * TCW
                def wbmm(E_, e=e, t0=t0):
                    return E_.matmul(bank[6][:, 0:TCW], c.sb["onesf"][0:1, :],
                                     wsrow[0:1, t0:t0 + TCW],
                                     start=True, stop=True)

                c.prog.c("tensor", wbmm)
                c.prog.c("vector", lambda E_: E_.tensor_copy(
                    out=wbe[:, :], in_=bank[6][:, 0:TCW]))

                plh = Pipe(c.prog)
                for hm in range(8):
                    pb = bank[hm % 2]

                    def hmm(E_, e=e, hm=hm, t0=t0, pb=pb):
                        last = None
                        for ch in range(NCH):
                            last = E_.matmul(
                                pb[:, 0:TCW],
                                wa[:, ch * 1024 + hm * 128: ch * 1024 + hm * 128 + 128],
                                xnbf[:, ch * T + t0: ch * T + t0 + TCW],
                                start=(ch == 0), stop=(ch == NCH - 1))
                        return last

                    c.prog.c("tensor", hmm, upto=plh.up())

                    def gelu_fn(E_, e=e, hm=hm, pb=pb):
                        return E_.activation(out=gl[:, :], in_=pb[:, 0:TCW],
                                             func=AFT.Gelu,
                                             bias=b1sb[:, e * 8 + hm: e * 8 + hm + 1])

                    c.prog.c("scalar", gelu_fn)

                    def hscale(E_, hm=hm):
                        return E_.tensor_mul(hs[:, hm * TCW:(hm + 1) * TCW],
                                             gl[:, :], wbe[:, :])

                    c.prog.c("vector", hscale)
                    plh.bump()

                plw = Pipe(c.prog)
                for co in range(8):
                    pb = bank[2 + co % 2]

                    def w2mm(E_, co=co, pb=pb, e=e, t0=t0):
                        for hm in range(8):
                            E_.matmul(
                                pb[:, 0:TCW],
                                wb[:, hm * 1024 + co * 128: hm * 1024 + co * 128 + 128],
                                hs[:, hm * TCW:(hm + 1) * TCW],
                                start=(hm == 0), stop=False)
                        return E_.matmul(
                            pb[:, 0:TCW],
                            b2sb[0:1, e * D + co * 128: e * D + co * 128 + 128],
                            wsrowb[0:1, t0:t0 + TCW],
                            start=False, stop=True)

                    c.prog.c("tensor", w2mm,
                             upto=(None if co == 0 else plw.up()))
                    dst = xmod[:, co * T + t0: co * T + t0 + TCW]

                    def evf(E_, dst=dst, pb=pb):
                        return E_.tensor_add(dst, dst, pb[:, 0:TCW])

                    c.prog.c("vector", evf)
                    plw.bump()

        c.prog.d("sync", [(xout[s, ch], xmod[:, ch * T:(ch + 1) * T])
                          for ch in range(NCH)])
    c.prog.w("sync")


def emit_gating(c, tc, bank, bgsb, lgs, rmax, nmax, ssum, maskt, wselall):
    for sub in range(4):
        pg = bank[2 + sub][:, 0:8]
        c.prog.c("vector", lambda E_, pg=pg: E_.tensor_add(lgs[:, :], pg, bgsb[:, :]))
        c.prog.c("vector", lambda E_: E_.reduce_max(rmax[:, :], lgs[:, :], axis=AXX))
        c.prog.c("vector", lambda E_: E_.tensor_single_scalar(
            out=nmax[:, :], in_=rmax[:, :], scalar=-1.0, op=ALU.mult))
        c.prog.c("scalar", lambda E_: E_.activation(
            out=lgs[:, :], in_=lgs[:, :], func=AFT.Exp, bias=nmax[:, :]))
        c.prog.c("vector", lambda E_: E_.reduce_sum(ssum[:, :], lgs[:, :], axis=AXX))
        c.prog.c("vector", lambda E_: E_.reciprocal(out=ssum[:, :], in_=ssum[:, :]))
        c.prog.c("vector", lambda E_: E_.tensor_single_scalar(
            out=lgs[:, :], in_=lgs[:, :], scalar=ssum[:, :], op=ALU.mult))
        c.prog.c("vector", lambda E_: E_.reduce_max(rmax[:, :], lgs[:, :], axis=AXX))
        c.prog.c("vector", lambda E_: E_.tensor_tensor(
            out=maskt[:, :], in0=lgs[:, :], in1=bcast_free(rmax[:, 0:1], 8),
            op=ALU.is_equal))
        ti = tc * 4 + sub

        def wselw(E_, ti=ti):
            return E_.tensor_mul(wselall[:, ti * 8:(ti + 1) * 8],
                                 maskt[:, :], lgs[:, :])

        c.prog.c("vector", wselw)




def emit_layernorm_hbm(ctx, src_dram, out_cb, pref, dbuf=True):
    c = ctx
    bank = [c.ps[f"bank{i}"] for i in range(8)]

    def get(suffix, shape, dt):
        key = pref + suffix
        if key not in c.sb:
            c.S(key, shape, dt)
        return c.sb[key]

    nx = 2 if dbuf else 1
    xch = get("_xch2" if dbuf else "_xch", [128, nx * T], F32)
    xbf = get("_xbf2" if dbuf else "_xbf", [128, nx * 512], BF16)
    sqb = get("_sqb2" if dbuf else "_sqb", [128, nx * 512], BF16)
    mb = get("_mb", [128, 512], F32)
    vb = get("_vb", [128, 512], F32)
    sc2 = get("_sc2", [128, 512], F32)
    t2 = get("_t2", [128, 512], F32)
    onb = c.sb["onesb"]

    def xsl(ch):
        return xch[:, (ch % nx) * T:(ch % nx) * T + T]

    ch_marks = {}
    it_marks = {}
    for ch in range(NCH):
        c.prog.d("sync", [(xsl(ch), src_dram(ch))], upto=ch_marks.get(ch - nx))
        dma_mark = c.prog.mark()
        for tc in range(4):
            i = ch * 4 + tc
            par = (i % 2) if dbuf else 0
            sl = xsl(ch)[:, tc * 512:(tc + 1) * 512]
            xb = xbf[:, par * 512: par * 512 + 512]
            sq = sqb[:, par * 512: par * 512 + 512]
            upc = max(it_marks.get(i - (2 if dbuf else 1), 0), dma_mark)
            c.prog.c("vector", lambda E, sl=sl, xb=xb: E.tensor_copy(
                out=xb, in_=sl), upto=upc)
            c.prog.c("scalar", lambda E, sl=sl, sq=sq: E.activation(
                out=sq, in_=sl, func=AFT.Square), upto=upc)
            m1 = c.prog.mark()

            def mm(E, ch=ch, tc=tc, xb=xb, sq=sq):
                E.matmul(bank[tc][:, 0:512], onb[:, :], xb,
                         start=(ch == 0), stop=(ch == NCH - 1))
                return E.matmul(bank[4 + tc][:, 0:512], onb[:, :], sq,
                                start=(ch == 0), stop=(ch == NCH - 1))

            c.prog.c("tensor", mm, upto=m1)
            it_marks[i] = c.prog.mark()
        ch_marks[ch] = c.prog.mark()
    for tc in range(4):
        def stat1(E, tc=tc):
            E.tensor_single_scalar(out=mb[:, :], in_=bank[tc][:, 0:512],
                                   scalar=1.0 / D, op=ALU.mult)
            E.tensor_scalar(out=vb[:, :], in0=bank[4 + tc][:, 0:512],
                            scalar1=1.0 / D, scalar2=1e-5, op0=ALU.mult, op1=ALU.add)
            E.tensor_mul(sc2[:, :], mb[:, :], mb[:, :])
            return E.tensor_sub(vb[:, :], vb[:, :], sc2[:, :])

        c.prog.c("vector", stat1)
        c.prog.c("scalar", lambda E: E.activation(out=vb[:, :], in_=vb[:, :],
                                                  func=AFT.Ln))
        c.prog.c("scalar", lambda E, tc=tc: E.activation(
            out=bank[tc][:, 0:512], in_=vb[:, :], func=AFT.Exp, scale=-0.5))
        c.prog.c("vector", lambda E, tc=tc: E.tensor_mul(
            bank[4 + tc][:, 0:512], mb[:, :], bank[tc][:, 0:512]))
    ch_marks = {}
    for ch in range(NCH):
        c.prog.d("sync", [(xsl(ch), src_dram(ch))], upto=ch_marks.get(ch - nx))
        dma_mark = c.prog.mark()
        for tc in range(4):
            sl = xsl(ch)[:, tc * 512:(tc + 1) * 512]

            def ap1(E, sl=sl, tc=tc):
                E.tensor_mul(t2[:, :], sl, bank[tc][:, 0:512])
                return E.tensor_sub(t2[:, :], t2[:, :], bank[4 + tc][:, 0:512])

            c.prog.c("vector", ap1, upto=(dma_mark if tc == 0 else None))
            out_cb(ch, tc, t2)
        ch_marks[ch] = c.prog.mark()




def emit_qk_proc(c, tgt, ssb, ai, l, ti, rn, S1, S2, QP, xpos_sb, QT):
    c.prog.c("scalar", lambda E: E.activation(out=S1, in_=tgt, func=AFT.Square))
    c.prog.c("vector", lambda E: E.reduce_sum(
        out=col_unit(rn[:, :]), in_=seg_view(S1, 64), axis=AXX))
    c.prog.c("scalar", lambda E: E.activation(
        out=rn[:, :], in_=rn[:, :], func=AFT.Sqrt))

    def rcp(E):
        E.tensor_scalar_max(out=rn[:, :], in0=rn[:, :], scalar1=1e-12)
        return E.reciprocal(out=rn[:, :], in_=rn[:, :])

    c.prog.c("vector", rcp)

    def scl(E):
        rv = AP(rn[:, :].tensor, rn[:, :].offset,
                [list(a) for a in rn[:, :].ap] + [[0, 64]])
        E.tensor_tensor(out=seg_view(tgt, 64), in0=seg_view(tgt, 64),
                        in1=rv, op=ALU.mult)
        sap = ssb[:, l * 64:(l + 1) * 64]
        sv = AP(sap.tensor, sap.offset,
                [list(sap.ap[0])] + [[0, 8], [1, 64]])
        return E.tensor_tensor(out=seg_view(tgt, 64),
                               in0=seg_view(tgt, 64), in1=sv, op=ALU.mult)

    c.prog.c("vector", scl)

    def xposf(E):
        tabA = xpos_sb[:, ai * 64: ai * 64 + 64]
        tabB = xpos_sb[:, (ai + 1) * 64: (ai + 1) * 64 + 64]
        av = AP(tabA.tensor, tabA.offset,
                [list(tabA.ap[0])] + [[0, 8], [1, 64]])
        sw = AP(tgt.tensor, tgt.offset + 32,
                [list(tgt.ap[0])] + [[64, 8], [-32, 2], [1, 32]])
        s1v = AP(S1.tensor, S1.offset,
                 [list(S1.ap[0])] + [[64, 8], [32, 2], [1, 32]])
        bv = AP(tabB.tensor, tabB.offset,
                [list(tabB.ap[0])] + [[0, 8], [32, 2], [1, 32]])
        E.tensor_tensor(out=s1v, in0=sw, in1=bv, op=ALU.mult)
        E.tensor_tensor(out=seg_view(S2, 64), in0=seg_view(tgt, 64),
                        in1=av, op=ALU.mult)
        return E.tensor_add(QP, S2, S1)

    c.prog.c("vector", xposf)


def build_l0_ffn(ctx, xin, xout, dbg=None):
    c = ctx
    NL = 2
    INNER = 512
    wqkv = c.din("g_wqkv", [NL, D, 3 * INNER], F32)
    wo = c.din("g_wo", [NL, INNER, D], F32)
    w1p = c.din("g_w1p", [NL, D, 5888], F32)
    w2p = c.din("g_w2p", [NL, 2816, D], F32)
    qsb_in = c.din("g_qs", [NL, 128, 64], F32)
    ksb_in = c.din("g_ks", [NL, 128, 64], F32)
    lgb_in = c.din("g_lgb", [NL, 2, 128, NCH], F32)
    b1p_in = c.din("g_b1p", [NL, 128, 46], F32)
    b2_in = c.din("g_b2", [NL, 128, NCH], F32)
    xpos = c.din("g_xpos", [4, 16, 128, 64], BF16)
    maskS = c.din("g_maskS", [4, 128, 1024], BF16)
    maskT = c.din("g_maskT", [8, 128, 512], BF16)

    TG = T + 1
    xl = c.S("g_xl", [128, NCH * TG], F32)
    xsh = c.S("g_xsh", [128, NCH * T], BF16)
    vaug = c.S("g_vaug", [128, 16 * 520], BF16)
    qf = c.S("g_qf", [128, 4 * T], BF16)
    kf = c.S("g_kf", [128, 4 * T], BF16)
    ws = c.S("g_ws", [128, 16 * 512], BF16)
    fscr = c.S("g_fscr", [128, 3072], F32)
    bscr = c.S("g_bscr", [128, 1024], BF16)
    mskS_sb = c.S("g_mskS", [128, 4 * 1024], BF16)
    mskT_sb = c.S("g_mskT", [128, 8 * 512], BF16)
    xpos_sb = c.S("g_xpossb", [128, 4 * 64], BF16)
    qs_sb = c.S("g_qssb", [128, NL * 64], F32)
    ks_sb = c.S("g_kssb", [128, NL * 64], F32)
    lgb_sb = c.S("g_lgbsb", [128, NL * 2 * NCH], F32)
    b1p_sb = c.S("g_b1psb", [128, NL * 46], F32)
    b2_sb = c.S("g_b2sb", [128, NL * NCH], F32)
    rn = c.S("g_rn", [128, 8], F32)
    negm = c.S("g_negm", [128, 4], F32)
    mrow = fscr[0:1, 2560:3072]
    rrow = fscr[0:1, 2048:2560]
    mbt = fscr[:, 1024:1536]
    sst = fscr[:, 0:1024]
    of32 = fscr[0:64, 1536:2048]
    pt = bscr[:, 0:512]
    pt2f = bscr[:, 0:1024]
    c.sb["g_ln_mb"] = fscr[:, 0:512]
    c.sb["g_ln_vb"] = fscr[:, 512:1024]
    c.sb["g_ln_sc2"] = fscr[:, 1024:1536]
    c.sb["g_ln_t2"] = fscr[:, 1536:2048]
    c.sb["g_ln_xbf"] = bscr[:, 0:512]
    c.sb["g_ln_sqb"] = bscr[:, 512:1024]
    QT = fscr[:, 0:512]
    KT = fscr[:, 512:1024]
    S1 = fscr[:, 1024:1536]
    S2 = fscr[:, 1536:2048]
    S3 = fscr[:, 2048:2560]
    QP = fscr[:, 2048:2560]
    T3 = fscr[:, 2560:3072]
    HP = None

    bank = [c.ps[f"bank{i}"] for i in range(8)]

    c.prog.d("sync", [
        (mskS_sb[:, :].rearrange("p (q j) -> p q j", j=1024),
         maskS.rearrange("q p j -> p q j")),
        (mskT_sb[:, :].rearrange("p (k j) -> p k j", j=512),
         maskT.rearrange("k p j -> p k j")),
        (qs_sb[:, :].rearrange("p (l j) -> p l j", j=64), qsb_in.rearrange("l p j -> p l j")),
        (ks_sb[:, :].rearrange("p (l j) -> p l j", j=64), ksb_in.rearrange("l p j -> p l j")),
        (lgb_sb[:, :].rearrange("p (l g k) -> p l g k", g=2, k=NCH),
         lgb_in.rearrange("l g p k -> p l g k")),
        (b1p_sb[:, :].rearrange("p (l k) -> p l k", k=46), b1p_in.rearrange("l p k -> p l k")),
        (b2_sb[:, :].rearrange("p (l k) -> p l k", k=NCH), b2_in.rearrange("l p k -> p l k")),
    ])

    def xg(ch):
        return xl[:, ch * TG + 1: ch * TG + 1 + T]

    def xsh_slice(ch, t0, w):
        off = ch * TG + (1 if ch < 4 else 0)
        return xl[:, off + t0: off + t0 + w]

    def wslab(dram_col_slice, slot, nch_rows=NCH):
        dst = ws[:, slot * 512 * nch_rows:(slot + 1) * 512 * nch_rows]
        c.prog.d("gpsimd", [(dst.rearrange("p (c j) -> p c j", j=512),
                             dram_col_slice.rearrange("(c p) j -> p c j", p=128))])

    for s in range(SEQ_PER_CORE):
        def gz(E):
            a0 = xl[0:128, 0:1]
            apv = AP(a0.tensor, a0.offset, [list(a0.ap[0])] + [[TG, NCH], [1, 1]])
            return E.memset(apv, 0.0)

        c.prog.c("vector", gz)

        def lncb(ch, tc, t2):
            c.prog.c("vector", lambda E, ch=ch, tc=tc: E.tensor_copy(
                out=xg(ch)[:, tc * 512:(tc + 1) * 512], in_=t2[:, :]))

        emit_layernorm_hbm(c, lambda ch, s=s: xin[s, ch], lncb, "g_ln", dbuf=False)
        if dbg == ("xl",):
            for co in range(NCH):
                c.prog.d("sync", [(xout[s, co], xg(co))])
            continue

        for l in range(NL):
            for ch in range(NCH):
                c.prog.c("vector", lambda E, ch=ch: E.tensor_copy(
                    out=xsh[:, ch * T:(ch + 1) * T], in_=xsh_slice(ch, 0, T)))
            if dbg is not None and dbg[0] == "xsh" and dbg[1] == l:
                for co in range(NCH):
                    c.prog.c("vector", lambda E, co=co: E.tensor_copy(
                        out=fscr[:, 0:2048], in_=xsh[:, co * T:(co + 1) * T]))
                    c.prog.d("sync", [(xout[s, co], fscr[:, 0:2048])])
                break
            wslab(wqkv[l][:, 1024:1536], 0)
            plv = Pipe(c.prog)
            for ti in range(16):
                def vmm(E, ti=ti):
                    last = None
                    for ci in range(NCH):
                        last = E.matmul(
                            bank[2 + ti % 2][:, 0:512],
                            xsh[:, ci * T + ti * 128: ci * T + ti * 128 + 128],
                            ws[:, ci * 512: ci * 512 + 512],
                            start=(ci == 0), stop=(ci == NCH - 1))
                    return last

                c.prog.c("tensor", vmm, upto=plv.up())

                def vev(E, ti=ti):
                    dst = vaug[:, ti * 520: (ti + 1) * 520]
                    dstv = AP(dst.tensor, dst.offset,
                              [list(dst.ap[0])] + [[65, 8], [1, 64]])
                    return E.tensor_copy(out=dstv,
                                         in_=seg_view(bank[2 + ti % 2][:, 0:512], 64))

                c.prog.c("vector", vev)
                plv.bump()

            for sl in range(2):
                wslab(wqkv[l][:, sl * 512:(sl + 1) * 512], sl)
            if dbg is not None and dbg[0] == "ws" and dbg[1] == l:
                for co in range(8):
                    c.prog.c("vector", lambda E, co=co: E.tensor_copy(
                        out=fscr[:, 0:2048],
                        in_=ws[:, co * 1024:(co + 1) * 1024]))
                    c.prog.d("sync", [(xout[s, co, :, 0:2048], fscr[:, 0:2048])])
                break
            for ti in range(16):
                def qkmm(E, ti=ti):
                    last = None
                    for sl in range(2):
                        for ci in range(NCH):
                            last = E.matmul(
                                bank[sl][:, 0:512],
                                xsh[:, ci * T + ti * 128: ci * T + ti * 128 + 128],
                                ws[:, sl * NCH * 512 + ci * 512: sl * NCH * 512 + ci * 512 + 512],
                                start=(ci == 0), stop=(ci == NCH - 1))
                    return last

                c.prog.c("tensor", qkmm)
                c.prog.d("sync", [(xpos_sb[:, :].rearrange("p (a j) -> p a j", j=64),
                                   xpos[:, ti].rearrange("a p j -> p a j"))])

                for which, bsrc, dst_fm, ssb, ai in ((0, 0, qf, qs_sb, 0),
                                                     (1, 1, kf, ks_sb, 2)):
                    tgt = QT if which == 0 else KT
                    c.prog.c("vector", lambda E, bsrc=bsrc, tgt=tgt: E.tensor_copy(
                        out=tgt, in_=bank[bsrc][:, 0:512]))
                    if dbg is not None and dbg[0] == "qraw":
                        def rawc(E, tgt=tgt):
                            return E.tensor_copy(out=QP, in_=tgt)

                        c.prog.c("vector", rawc)
                        skip_proc = True
                    else:
                        skip_proc = False
                    if not skip_proc:
                        emit_qk_proc(c, tgt, ssb, ai, l, ti, rn, S1, S2, QP,
                                     xpos_sb, QT)
                    for cq in range(4):
                        c.prog.c("tensor", lambda E, cq=cq: E.transpose(
                            bank[6][:, 0:128], QP[:, cq * 128:(cq + 1) * 128],
                            c.sb["identf"][:, :]))
                        c.prog.c("vector", lambda E, cq=cq, ti=ti, dst_fm=dst_fm:
                                 E.tensor_copy(
                                     out=dst_fm[:, cq * T + ti * 128: cq * T + ti * 128 + 128],
                                     in_=bank[6][:, 0:128]))

            def vones(E):
                a0 = vaug[:, 64:65]
                apv = AP(a0.tensor, a0.offset,
                         [list(a0.ap[0])] + [[520, 16], [65, 8], [1, 1]])
                return E.memset(apv, 1.0)

            c.prog.c("vector", vones)

            OF = xsh
            for w in range(4):
                kt_lo = 4 if w == 0 else 0
                key0 = 512 * (w - 1)
                for h in range(8):
                    hch = h // 2
                    hrow = (h % 2) * 64
                    smm_marks = {}
                    tail_marks = {}
                    for qt in range(4):
                        b0 = (qt % 2) * 2

                        def smm(E, w=w, h=h, qt=qt, hch=hch, hrow=hrow, b0=b0):
                            last = None
                            for half in range(2):
                                if w == 0 and half == 0:
                                    continue
                                t0 = 512 * (w - 1) + half * 512
                                last = E.matmul(
                                    bank[b0 + half][:, 0:512],
                                    qf[hrow:hrow + 64,
                                       hch * T + w * 512 + qt * 128:
                                       hch * T + w * 512 + qt * 128 + 128],
                                    kf[hrow:hrow + 64, hch * T + t0: hch * T + t0 + 512],
                                    start=True, stop=True)
                            return last

                        c.prog.c("tensor", smm, upto=tail_marks.get(qt - 2))
                        smm_marks[qt] = c.prog.mark()

                        def sred(E, w=w, qt=qt, b0=b0):
                            for half in range(2):
                                if w == 0 and half == 0:
                                    continue
                                E.tensor_add(
                                    sst[:, half * 512:(half + 1) * 512],
                                    bank[b0 + half][:, 0:512],
                                    mskS_sb[:, qt * 1024 + half * 512:
                                            qt * 1024 + half * 512 + 512])
                            if w == 0:
                                E.reduce_max(rn[:, 0:1], sst[:, 512:1024], axis=AXX)
                            else:
                                E.reduce_max(rn[:, 0:1], sst[:, 0:1024], axis=AXX)
                            return E.tensor_single_scalar(
                                out=negm[:, qt:qt + 1], in_=rn[:, 0:1],
                                scalar=-1.0, op=ALU.mult)

                        c.prog.c("vector", sred, upto=smm_marks[qt])
                        tail_marks[qt] = c.prog.mark()

                    def mtr(E):
                        last = None
                        for j in range(4):
                            last = E.transpose(bank[6][0:1, j * 128:(j + 1) * 128],
                                               negm[:, j:j + 1], c.sb["identf"][:, :])
                        return last

                    c.prog.c("tensor", mtr)
                    c.prog.c("scalar", lambda E: E.copy(out=mrow,
                                                        in_=bank[6][0:1, 0:512]))
                    c.prog.c("tensor", lambda E: E.matmul(
                        bank[6][:, 0:512], c.sb["onesf"][0:1, :], mrow,
                        start=True, stop=True))
                    c.prog.c("vector", lambda E: E.tensor_copy(
                        out=mbt, in_=bank[6][:, 0:512]))
                    mbt_mark = c.prog.mark()
                    stm = {}
                    tdone = {}

                    def emit_tail(kt, w=w, h=h, kt_lo=kt_lo):
                        bb = kt % 2
                        sb4 = bank[4 + 2 * bb]
                        ktt = 4 * (w - 1) + kt

                        def stsub(E, kt=kt, bb=bb, sb4=sb4):
                            dst = sst[:, bb * 512: bb * 512 + 512]
                            E.tensor_add(dst, sb4[:, 0:512], mbt)
                            return E.tensor_add(
                                dst, dst, mskT_sb[:, kt * 512:(kt + 1) * 512])

                        c.prog.c("vector", stsub, upto=stm[kt])
                        m1 = c.prog.mark()
                        c.prog.c("scalar", lambda E, bb=bb: E.activation(
                            out=pt2f[:, bb * 512: bb * 512 + 512],
                            in_=sst[:, bb * 512: bb * 512 + 512], func=AFT.Exp),
                            upto=m1)
                        m2 = c.prog.mark()

                        def pvmm(E, kt=kt, ktt=ktt, h=h, kt_lo=kt_lo, bb=bb):
                            return E.matmul(
                                bank[5][0:65, 0:512],
                                vaug[:, ktt * 520 + h * 65: ktt * 520 + h * 65 + 65],
                                pt2f[:, bb * 512: bb * 512 + 512],
                                start=(kt == kt_lo), stop=(kt == 7))

                        c.prog.c("tensor", pvmm, upto=m2)
                        tdone[kt] = c.prog.mark()

                    for kt in range(kt_lo, 8):
                        ktt = 4 * (w - 1) + kt

                        def stmm(E, w=w, kt=kt, hch=hch, hrow=hrow, ktt=ktt):
                            return E.matmul(
                                bank[4 + 2 * (kt % 2)][:, 0:512],
                                kf[hrow:hrow + 64, hch * T + ktt * 128: hch * T + ktt * 128 + 128],
                                qf[hrow:hrow + 64, hch * T + w * 512: hch * T + w * 512 + 512],
                                start=True, stop=True)

                        c.prog.c("tensor", stmm, upto=tdone.get(kt - 2, mbt_mark))
                        stm[kt] = c.prog.mark()
                        if kt > kt_lo:
                            emit_tail(kt - 1)
                    emit_tail(7)
                    c.prog.c("vector", lambda E: E.reciprocal(
                        out=rrow, in_=bank[5][64:65, 0:512]))
                    c.prog.c("scalar", lambda E: E.copy(out=of32,
                                                        in_=bank[5][0:64, 0:512]))
                    c.prog.c("tensor", lambda E: E.matmul(
                        bank[6][0:64, 0:512], c.sb["onesf"][0:1, 0:64], rrow,
                        start=True, stop=True))

                    def oev(E, w=w, hch=hch, hrow=hrow):
                        return E.tensor_mul(
                            OF[hrow:hrow + 64, hch * T + w * 512: hch * T + w * 512 + 512],
                            of32, bank[6][0:64, 0:512])

                    c.prog.c("vector", oev)

            if dbg is not None and dbg[0] == "of" and dbg[1] == l:
                for ci in range(4):
                    c.prog.c("vector", lambda E, ci=ci: E.tensor_copy(
                        out=fscr[:, 0:2048], in_=OF[:, ci * T:(ci + 1) * T]))
                    c.prog.d("sync", [(xout[s, ci], fscr[:, 0:2048])])
                break
            if dbg is not None and dbg[0] in ("qf", "qraw") and dbg[1] == l:
                for ci in range(4):
                    c.prog.c("vector", lambda E, ci=ci: E.tensor_copy(
                        out=fscr[:, 0:2048], in_=qf[:, ci * T:(ci + 1) * T]))
                    c.prog.d("sync", [(xout[s, ci], fscr[:, 0:2048])])
                    c.prog.c("vector", lambda E, ci=ci: E.tensor_copy(
                        out=fscr[:, 0:2048], in_=kf[:, ci * T:(ci + 1) * T]))
                    c.prog.d("sync", [(xout[s, 4 + ci], fscr[:, 0:2048])])
                break
            for sl in range(2):
                wslab(wo[l][:, sl * 512:(sl + 1) * 512], sl, nch_rows=4)
                plo = Pipe(c.prog)
                for col in range(4):
                    co = sl * 4 + col
                    for tc in range(4):
                        def omm(E, sl=sl, col=col, tc=tc):
                            last = None
                            for ci in range(4):
                                last = E.matmul(
                                    bank[tc % 2][:, 0:512],
                                    ws[:, sl * 4 * 512 + ci * 512 + col * 128:
                                       sl * 4 * 512 + ci * 512 + col * 128 + 128],
                                    OF[:, ci * T + tc * 512: ci * T + tc * 512 + 512],
                                    start=(ci == 0), stop=(ci == 3))
                            return last

                        c.prog.c("tensor", omm, upto=plo.up())

                        def oadd(E, co=co, tc=tc):
                            dst = xg(co)[:, tc * 512:(tc + 1) * 512]
                            return E.tensor_add(dst, dst, bank[tc % 2][:, 0:512])

                        c.prog.c("vector", oadd)
                        plo.bump()

            if dbg is not None and dbg[0] == "xl1" and dbg[1] == l:
                for co in range(NCH):
                    c.prog.d("sync", [(xout[s, co], xg(co))])
                break
            xn2 = xsh
            for tc in range(4):
                for ch in range(NCH):
                    sl_ = xsh_slice(ch, tc * 512, 512)
                    c.prog.c("vector", lambda E, sl_=sl_: E.tensor_copy(
                        out=c.sb["g_ln_xbf"], in_=sl_))
                    c.prog.c("scalar", lambda E, sl_=sl_: E.activation(
                        out=c.sb["g_ln_sqb"], in_=sl_, func=AFT.Square))

                    def mm(E, ch=ch, tc=tc):
                        E.matmul(bank[2][:, 0:512], c.sb["onesb"][:, :],
                                 c.sb["g_ln_xbf"], start=(ch == 0), stop=(ch == 7))
                        return E.matmul(bank[3][:, 0:512], c.sb["onesb"][:, :],
                                        c.sb["g_ln_sqb"], start=(ch == 0),
                                        stop=(ch == 7))

                    c.prog.c("tensor", mm)

                def stat1(E):
                    E.tensor_single_scalar(out=c.sb["g_ln_mb"], in_=bank[2][:, 0:512],
                                           scalar=1.0 / D, op=ALU.mult)
                    E.tensor_scalar(out=c.sb["g_ln_vb"], in0=bank[3][:, 0:512],
                                    scalar1=1.0 / D, scalar2=1e-5, op0=ALU.mult,
                                    op1=ALU.add)
                    E.tensor_mul(c.sb["g_ln_sc2"], c.sb["g_ln_mb"], c.sb["g_ln_mb"])
                    return E.tensor_sub(c.sb["g_ln_vb"], c.sb["g_ln_vb"],
                                        c.sb["g_ln_sc2"])

                c.prog.c("vector", stat1)
                c.prog.c("scalar", lambda E: E.activation(
                    out=c.sb["g_ln_vb"], in_=c.sb["g_ln_vb"], func=AFT.Sqrt))

                def stat2(E):
                    E.reciprocal(out=c.sb["g_ln_vb"], in_=c.sb["g_ln_vb"])
                    return E.tensor_mul(c.sb["g_ln_sc2"], c.sb["g_ln_mb"],
                                        c.sb["g_ln_vb"])

                c.prog.c("vector", stat2)
                for ch in range(NCH):
                    sl_ = xsh_slice(ch, tc * 512, 512)

                    def ap1(E, sl_=sl_, ch=ch, tc=tc, l=l):
                        E.tensor_mul(c.sb["g_ln_t2"], sl_, c.sb["g_ln_vb"])
                        E.tensor_sub(c.sb["g_ln_t2"], c.sb["g_ln_t2"],
                                     c.sb["g_ln_sc2"])
                        return E.tensor_scalar(
                            out=xn2[:, ch * T + tc * 512: ch * T + tc * 512 + 512],
                            in0=c.sb["g_ln_t2"],
                            scalar1=lgb_sb[:, (l * 2) * NCH + ch: (l * 2) * NCH + ch + 1],
                            scalar2=lgb_sb[:, (l * 2 + 1) * NCH + ch: (l * 2 + 1) * NCH + ch + 1],
                            op0=ALU.mult, op1=ALU.add)

                    c.prog.c("vector", ap1)

            if dbg is not None and dbg[0] == "xn2" and dbg[1] == l:
                for co in range(NCH):
                    c.prog.c("vector", lambda E, co=co: E.tensor_copy(
                        out=fscr[:, 0:2048], in_=xn2[:, co * T:(co + 1) * T]))
                    c.prog.d("sync", [(xout[s, co], fscr[:, 0:2048])])
                break
            HPn = 22
            for tc in range(4):
                plm = Pipe(c.prog)
                for cp in range(HPn):
                    if cp % 4 == 0:
                        wslab(w1p[l][:, (cp // 4) * 512:(cp // 4) * 512 + 512], 0)
                        wslab(w1p[l][:, 2816 + (cp // 4) * 512: 2816 + (cp // 4) * 512 + 512], 1)
                        slab_mark = c.prog.mark()

                    bb = (cp % 2) * 2

                    def hmm(E, cp=cp, tc=tc, bb=bb):
                        last = None
                        for ci in range(NCH):
                            for ab in range(2):
                                last = E.matmul(
                                    bank[bb + ab][:, 0:512],
                                    ws[:, ab * NCH * 512 + ci * 512 + (cp % 4) * 128:
                                       ab * NCH * 512 + ci * 512 + (cp % 4) * 128 + 128],
                                    xn2[:, ci * T + tc * 512: ci * T + tc * 512 + 512],
                                    start=(ci == 0), stop=(ci == NCH - 1))
                        return last

                    c.prog.c("tensor", hmm,
                             upto=(slab_mark if cp % 4 == 0 else plm.up()))

                    def aev(E, cp=cp, l=l, bb=bb):
                        return E.tensor_scalar_add(
                            out=S1, in0=bank[bb][:, 0:512],
                            scalar1=b1p_sb[:, l * 46 + cp: l * 46 + cp + 1])

                    c.prog.c("vector", aev)

                    def gev(E, cp=cp, l=l, bb=bb):
                        return E.activation(
                            out=S2, in_=bank[bb + 1][:, 0:512], func=AFT.Gelu,
                            bias=b1p_sb[:, l * 46 + 22 + cp: l * 46 + 22 + cp + 1])

                    c.prog.c("scalar", gev)
                    hp_dst = (qf if cp < 11 else kf)
                    hpo = (cp % 11)

                    def pev(E, hp_dst=hp_dst, hpo=hpo):
                        return E.tensor_mul(hp_dst[:, hpo * 512:(hpo + 1) * 512],
                                            S1, S2)

                    c.prog.c("vector", pev)
                    plm.bump()
                for osl in range(2):
                    for rhalf in range(2):
                        dst = ws[:, 0:11 * 512]
                        c.prog.d("gpsimd", [
                            (dst.rearrange("p (c j) -> p c j", j=512),
                             w2p[l][rhalf * 1408:(rhalf + 1) * 1408,
                                    osl * 512:(osl + 1) * 512].rearrange(
                                 "(c p) j -> p c j", p=128))])
                        for col in range(4):
                            def wmm(E, rhalf=rhalf, col=col, tc=tc):
                                last = None
                                for hc in range(11):
                                    cpg = rhalf * 11 + hc
                                    hp_src = (qf if cpg < 11 else kf)
                                    hpo = cpg % 11
                                    last = E.matmul(
                                        bank[4 + col][:, 0:512],
                                        ws[:, hc * 512 + col * 128: hc * 512 + col * 128 + 128],
                                        hp_src[:, hpo * 512:(hpo + 1) * 512],
                                        start=(rhalf == 0 and hc == 0),
                                        stop=(rhalf == 1 and hc == 10))
                                return last

                            c.prog.c("tensor", wmm)
                    for col in range(4):
                        co = osl * 4 + col

                        def wev(E, col=col, co=co, tc=tc, l=l):
                            E.tensor_scalar_add(
                                out=T3, in0=bank[4 + col][:, 0:512],
                                scalar1=b2_sb[:, l * NCH + co: l * NCH + co + 1])
                            dst = xg(co)[:, tc * 512:(tc + 1) * 512]
                            return E.tensor_add(dst, dst, T3)

                        c.prog.c("vector", wev)

        if dbg is not None and dbg != ("fin",):
            continue
        for co in range(NCH):
            c.prog.d("sync", [(fscr[:, 0:2048], xin[s, co])])

            def fin(E, co=co):
                return E.tensor_add(fscr[:, 0:2048], fscr[:, 0:2048], xg(co))

            c.prog.c("vector", fin)
            c.prog.d("sync", [(xout[s, co], fscr[:, 0:2048])])
    c.prog.w("sync")




def emit_conv3(E, dst, src, t0, W, wtap, bias_col, total=T):
    w0c, w1c, w2c = wtap
    last = E.tensor_scalar(out=dst[:, 0:W], in0=src[:, t0:t0 + W], scalar1=w1c,
                           scalar2=bias_col, op0=ALU.mult, op1=ALU.add)
    if t0 == 0:
        last = E.scalar_tensor_tensor(out=dst[:, 1:W], in0=src[:, 0:W - 1],
                                      scalar=w0c, in1=dst[:, 1:W],
                                      op0=ALU.mult, op1=ALU.add)
    else:
        last = E.scalar_tensor_tensor(out=dst[:, 0:W], in0=src[:, t0 - 1:t0 + W - 1],
                                      scalar=w0c, in1=dst[:, 0:W],
                                      op0=ALU.mult, op1=ALU.add)
    if t0 + W == total:
        last = E.scalar_tensor_tensor(out=dst[:, 0:W - 1], in0=src[:, t0 + 1:t0 + W],
                                      scalar=w2c, in1=dst[:, 0:W - 1],
                                      op0=ALU.mult, op1=ALU.add)
    else:
        last = E.scalar_tensor_tensor(out=dst[:, 0:W], in0=src[:, t0 + 1:t0 + W + 1],
                                      scalar=w2c, in1=dst[:, 0:W],
                                      op0=ALU.mult, op1=ALU.add)
    return last


def build_l0_attn(ctx, xin, xout):
    c = ctx
    inw = c.din("m_inw", [D, D], F32)
    actw = c.din("m_actp", [D, D], F32)
    qkw = c.din("m_qkw", [D, 2 * D], F32)
    outw = c.din("m_outw", [D, D], F32)
    fc1 = c.din("m_fc1", [D, 4 * D], F32)
    fc2 = c.din("m_fc2", [4 * D, D], F32)
    convw = c.din("m_convw", [4, 3, 128, NCH], F32)
    convb = c.din("m_convb", [4, 128, NCH], F32)
    norms = c.din("m_norms", [4, 128, NCH], F32)
    pbias = c.din("m_pbias", [128, 64], F32)
    qkbrow = c.din("m_qkb_row", [1, 2 * D], F32)
    ctab = c.din("m_costab", [2, 128, 512], BF16)

    xn = c.S("m_xn", [128, NCH * T], BF16)
    xibf = c.S("m_xibf", [128, NCH * T], BF16)
    attn = c.S("m_attn", [128, NCH * T], BF16)
    ws = c.S("m_ws", [128, 4 * NCH * 512], BF16)
    xip = c.S("m_xip", [128, T], F32)
    tmpf = c.S("m_tmpf", [128, T], F32)
    fscr = c.S("m_fscr", [128, 2048], F32)
    bscr = c.S("m_bscr", [128, 5120], BF16)
    c.sb["m_ln_mb"] = fscr[:, 0:512]
    c.sb["m_ln_vb"] = fscr[:, 512:1024]
    c.sb["m_ln_sc2"] = fscr[:, 1024:1536]
    c.sb["m_ln_t2"] = fscr[:, 1536:2048]
    c.sb["m_ln_xbf"] = bscr[:, 0:512]
    c.sb["m_ln_sqb"] = bscr[:, 512:1024]
    ksrow = xip[0:1, 0:D]
    ksb = c.S("m_ksb", [128, D], F32)
    kvbd = c.S("m_kvbd", [128, NCH * 128], BF16)
    kvacc = c.S("m_kvacc", [64, 1024], F32)
    zt = c.S("m_zt", [128, 16], F32)
    qkbias = c.S("m_qkbias", [128, D], BF16)
    tstg = c.S("m_tstg", [128, 128], F32)
    cwsb = c.S("m_cwsb", [128, 4 * 3 * NCH], F32)
    cbsb = c.S("m_cbsb", [128, 4 * NCH], F32)
    nrmsb = c.S("m_nrmsb", [128, 4 * NCH], F32)
    pbsb = c.S("m_pbsb", [128, 64], F32)
    cossb = c.S("m_cossb", [128, 512], BF16)
    sinsb = c.S("m_sinsb", [128, 512], BF16)

    t3 = fscr[:, 1536:2048]
    bank = [c.ps[f"bank{i}"] for i in range(8)]
    xs = c.nc.dram_tensor("m_xs", [2, NCH, 128, T], F32, kind="Internal").ap()

    def cw(conv, tap, ch):
        return cwsb[:, (conv * 3 + tap) * NCH + ch: (conv * 3 + tap) * NCH + ch + 1]

    def cb(conv, ch):
        return cbsb[:, conv * NCH + ch: conv * NCH + ch + 1]

    c.prog.d("sync", [
        (cwsb[:, :].rearrange("p (c t k) -> p c t k", t=3, k=NCH),
         convw.rearrange("c t p k -> p c t k")),
        (cbsb[:, :].rearrange("p (c k) -> p c k", k=NCH),
         convb.rearrange("c p k -> p c k")),
        (nrmsb[:, :].rearrange("p (c k) -> p c k", k=NCH),
         norms.rearrange("c p k -> p c k")),
        (pbsb[:, :], pbias[:, :]),
        (cossb[:, :], ctab[0]),
        (sinsb[:, :], ctab[1]),
    ])

    def wslab(dram_col_slice, dst_off_slabs, upto=None):
        dst = ws[:, dst_off_slabs * NCH * 512:(dst_off_slabs + 1) * NCH * 512]
        c.prog.d("gpsimd", [(dst.rearrange("p (c j) -> p c j", j=512),
                             dram_col_slice.rearrange("(c p) j -> p c j", p=128))],
                 upto=upto)

    def fm_proj(dst_cb, src_sb, wslot, couts=4, bias=None):
        pl = Pipe(c.prog)
        for col in range(couts):
            for tc in range(4):
                def pmm(E, col=col, tc=tc, wslot=wslot, src_sb=src_sb):
                    last = None
                    for ci in range(NCH):
                        last = E.matmul(
                            bank[tc % 2][:, 0:512],
                            ws[:, wslot * NCH * 512 + ci * 512 + col * 128:
                               wslot * NCH * 512 + ci * 512 + col * 128 + 128],
                            src_sb[:, ci * T + tc * 512: ci * T + tc * 512 + 512],
                            start=(ci == 0), stop=(ci == NCH - 1))
                    return last

                c.prog.c("tensor", pmm, upto=pl.up())
                dst_cb(col, tc, bank[tc % 2])
                pl.bump()

    for s in range(SEQ_PER_CORE):
        def lncb(ch, tc, t2):
            c.prog.c("vector", lambda E, ch=ch, tc=tc: E.tensor_copy(
                out=xip[:, tc * 512:(tc + 1) * 512], in_=t2[:, :]))
            if tc == 3:
                def cvf(E, ch=ch):
                    return emit_conv3(E, tmpf, xip, 0, T,
                                      (cw(0, 0, ch), cw(0, 1, ch), cw(0, 2, ch)),
                                      cb(0, ch))

                c.prog.c("vector", cvf)
                c.prog.c("vector", lambda E: E.tensor_add(tmpf[:, :], tmpf[:, :],
                                                          xip[:, :]))
                c.prog.d("sync", [(xs[1, ch], tmpf[:, :])])

        emit_layernorm_hbm(c, lambda ch, s=s: xin[s, ch], lncb, "m_ln")
        c.sb["m_ln_xch"] = c.sb["m_ln_xch2"][:, 0:T]

        def ln1cb(ch, tc, t2):
            def gb(E, ch=ch, tc=tc):
                return E.tensor_scalar(
                    out=xn[:, ch * T + tc * 512: ch * T + tc * 512 + 512],
                    in0=t2[:, :], scalar1=nrmsb[:, ch:ch + 1],
                    scalar2=nrmsb[:, NCH + ch: NCH + ch + 1],
                    op0=ALU.mult, op1=ALU.add)

            c.prog.c("vector", gb)

        emit_layernorm_hbm(c, lambda ch: xs[1, ch], ln1cb, "m_ln")

        for slab in range(2):
            wslab(inw[:, slab * 512:(slab + 1) * 512], 0)

            def consume(col, tc, pb, slab=slab):
                ch = slab * 4 + col

                def ev(E, col=col, tc=tc, pb=pb, ch=ch):
                    return E.tensor_scalar_add(
                        out=xip[:, tc * 512:(tc + 1) * 512], in0=pb[:, 0:512],
                        scalar1=pbsb[:, ch:ch + 1])

                c.prog.c("vector", ev)
                if tc == 3:
                    def cvf(E, ch=ch):
                        return emit_conv3(E, tmpf, xip, 0, T,
                                          (cw(1, 0, ch), cw(1, 1, ch), cw(1, 2, ch)),
                                          cb(1, ch))

                    c.prog.c("vector", cvf)

                    def sil(E, ch=ch):
                        return E.activation(
                            out=xibf[:, ch * T:(ch + 1) * T], in_=tmpf[:, :],
                            func=AFT.Silu)

                    c.prog.c("scalar", sil)

            fm_proj(consume, xn, 0)

        def qkbmm(E, half=1):
            E.matmul(bank[2][:, 0:512], c.sb["onesf"][0:1, :],
                     qkbrow[0:1, 1024:1536] if False else c.sb["m_qkbr"][0:1, 1024:1536],
                     start=True, stop=True)
            return E.matmul(bank[3][:, 0:512], c.sb["onesf"][0:1, :],
                            c.sb["m_qkbr"][0:1, 1536:2048], start=True, stop=True)

        c.sb["m_qkbr"] = xip[0:1, 0:2 * D]
        c.prog.d("sync", [(c.sb["m_qkbr"][0:1, :], qkbrow[0:1, :])])
        c.prog.c("tensor", qkbmm)

        def qkbev(E):
            E.tensor_copy(out=qkbias[:, 0:512], in_=bank[2][:, 0:512])
            return E.tensor_copy(out=qkbias[:, 512:1024], in_=bank[3][:, 0:512])

        c.prog.c("vector", qkbev)
        wslab(qkw[:, 1024:1536], 0)
        wslab(qkw[:, 1536:2048], 1)
        c.prog.c("vector", lambda E: E.memset(kvacc[0:64, :], 0.0))

        KB, KRB, VT, QB, QBF = (bscr[:, 0:1024], bscr[:, 1024:2048],
                                bscr[:, 2048:3072], bscr[:, 3072:4096],
                                bscr[:, 4096:5120])
        QKS = fscr[:, 0:1024]
        SCR2 = fscr[:, 1024:2048]

        for ti in range(16):
            def kmm(E, ti=ti):
                last = None
                for half in range(2):
                    for ci in range(NCH):
                        last = E.matmul(
                            bank[half][:, 0:512],
                            xibf[:, ci * T + ti * 128: ci * T + ti * 128 + 128],
                            ws[:, half * NCH * 512 + ci * 512: half * NCH * 512 + ci * 512 + 512],
                            start=(ci == 0), stop=(ci == NCH - 1))
                return last

            c.prog.c("tensor", kmm)
            emit_elu1(c, bank, qkbias, QKS, SCR2, tmpf)
            c.prog.c("vector", lambda E: E.tensor_copy(out=KB[:, :], in_=QKS[:, :]))

            def ksmm(E, ti=ti):
                E.matmul(bank[2][0:1, 0:512], c.sb["onesb"][:, 0:1], KB[:, 0:512],
                         start=(ti == 0), stop=(ti == 15))
                return E.matmul(bank[3][0:1, 0:512], c.sb["onesb"][:, 0:1],
                                KB[:, 512:1024], start=(ti == 0), stop=(ti == 15))

            c.prog.c("tensor", ksmm)
            emit_rope(c, QKS, KRB, SCR2, cossb, sinsb)
            for ci in range(NCH):
                c.prog.c("vector", lambda E, ci=ci, ti=ti: E.tensor_copy(
                    out=tstg[:, :], in_=xibf[:, ci * T + ti * 128: ci * T + ti * 128 + 128]))
                c.prog.c("tensor", lambda E, ci=ci: E.transpose(
                    bank[6][:, 0:128], tstg[:, :], c.sb["identf"][:, :]))
                c.prog.c("vector", lambda E, ci=ci: E.tensor_copy(
                    out=VT[:, ci * 128:(ci + 1) * 128], in_=bank[6][:, 0:128]))

            for h in range(16):
                pb = bank[4 + h % 2]

                def kvmm(E, h=h, pb=pb):
                    return E.matmul(pb[0:64, 0:64],
                                    KRB[:, h * 64:(h + 1) * 64],
                                    VT[:, h * 64:(h + 1) * 64],
                                    start=True, stop=True)

                c.prog.c("tensor", kvmm)

                def kvacc_f(E, h=h, pb=pb):
                    dst = kvacc[0:64, h * 64:(h + 1) * 64]
                    return E.tensor_add(dst, dst, pb[0:64, 0:64])

                c.prog.c("vector", kvacc_f)

        c.prog.c("vector", lambda E: E.memset(kvbd[:, :], 0.0))

        def kvev(E):
            last = None
            for h in range(16):
                r = (h % 2) * 64
                last = E.tensor_copy(
                    out=kvbd[r:r + 64, (h // 2) * 128 + r: (h // 2) * 128 + r + 64],
                    in_=kvacc[0:64, h * 64:(h + 1) * 64])
            return last

        c.prog.c("vector", kvev)

        def ksev(E):
            E.tensor_copy(out=ksrow[0:1, 0:512], in_=bank[2][0:1, 0:512])
            return E.tensor_copy(out=ksrow[0:1, 512:1024], in_=bank[3][0:1, 0:512])

        c.prog.c("vector", ksev)

        def ksbmm(E):
            E.matmul(bank[0][:, 0:512], c.sb["onesf"][0:1, :], ksrow[0:1, 0:512],
                     start=True, stop=True)
            return E.matmul(bank[1][:, 0:512], c.sb["onesf"][0:1, :],
                            ksrow[0:1, 512:1024], start=True, stop=True)

        c.prog.c("tensor", ksbmm)

        def ksbev(E):
            E.tensor_copy(out=ksb[:, 0:512], in_=bank[0][:, 0:512])
            return E.tensor_copy(out=ksb[:, 512:1024], in_=bank[1][:, 0:512])

        c.prog.c("vector", ksbev)

        c.prog.d("sync", [(c.sb["m_qkbr"][0:1, :], qkbrow[0:1, :])])

        def qkbmm2(E):
            E.matmul(bank[2][:, 0:512], c.sb["onesf"][0:1, :],
                     c.sb["m_qkbr"][0:1, 0:512], start=True, stop=True)
            return E.matmul(bank[3][:, 0:512], c.sb["onesf"][0:1, :],
                            c.sb["m_qkbr"][0:1, 512:1024], start=True, stop=True)

        c.prog.c("tensor", qkbmm2)
        c.prog.c("vector", qkbev)
        wslab(qkw[:, 0:512], 0)
        wslab(qkw[:, 512:1024], 1)

        for ti in range(16):
            def qmm(E, ti=ti):
                last = None
                for half in range(2):
                    for ci in range(NCH):
                        last = E.matmul(
                            bank[half][:, 0:512],
                            xibf[:, ci * T + ti * 128: ci * T + ti * 128 + 128],
                            ws[:, half * NCH * 512 + ci * 512: half * NCH * 512 + ci * 512 + 512],
                            start=(ci == 0), stop=(ci == NCH - 1))
                return last

            c.prog.c("tensor", qmm)
            emit_elu1(c, bank, qkbias, QKS, SCR2, tmpf)
            c.prog.c("vector", lambda E: E.tensor_mul(SCR2[:, :], QKS[:, :], ksb[:, :]))

            def zred(E):
                E.reduce_sum(out=col_unit(zt[:, :]),
                             in_=seg_view(SCR2[:, 0:1024], 64), axis=AXX)
                E.tensor_single_scalar(out=zt[:, :], in_=zt[:, :],
                                       scalar=T * 1e-6, op=ALU.add)
                return E.reciprocal(out=zt[:, :], in_=zt[:, :])

            c.prog.c("vector", zred)
            emit_rope(c, QKS, None, SCR2, cossb, sinsb, out_f32=tmpf)
            def zap(E):
                zv = AP(zt[:, :].tensor, zt[:, :].offset,
                        [list(a) for a in zt[:, :].ap] + [[0, 64]])
                return E.tensor_tensor(out=seg_view(QB[:, :], 64),
                                       in0=seg_view(tmpf[:, 0:1024], 64),
                                       in1=zv, op=ALU.mult)

            c.prog.c("vector", zap)
            for ci in range(NCH):
                c.prog.c("vector", lambda E, ci=ci: E.tensor_copy(
                    out=tstg[:, :], in_=QB[:, ci * 128:(ci + 1) * 128]))
                c.prog.c("tensor", lambda E, ci=ci: E.transpose(
                    bank[6][:, 0:128], tstg[:, :], c.sb["identf"][:, :]))
                c.prog.c("vector", lambda E, ci=ci: E.tensor_copy(
                    out=QBF[:, ci * 128:(ci + 1) * 128], in_=bank[6][:, 0:128]))

            def amm(E, ti=ti):
                last = None
                for ci in range(NCH):
                    pb = bank[2 + ci // 4]
                    last = E.matmul(pb[:, (ci % 4) * 128:(ci % 4) * 128 + 128],
                                    kvbd[:, ci * 128:(ci + 1) * 128],
                                    QBF[:, ci * 128:(ci + 1) * 128],
                                    start=True, stop=True)
                return last

            c.prog.c("tensor", amm)

            def aev(E, ti=ti):
                last = None
                for ci in range(NCH):
                    pb = bank[2 + ci // 4]
                    last = E.tensor_copy(
                        out=attn[:, ci * T + ti * 128: ci * T + ti * 128 + 128],
                        in_=pb[:, (ci % 4) * 128:(ci % 4) * 128 + 128])
                return last

            c.prog.c("vector", aev)

        wslab(actw[:, 0:512], 0)
        wslab(actw[:, 512:1024], 1)
        wslab(outw[:, 0:512], 2)
        wslab(outw[:, 512:1024], 3)
        ACT2 = fscr[:, 0:512]
        LEP = fscr[:, 512:1024]
        TC2 = fscr[:, 1024:1536]
        OWIN = bscr[:, 0:4096]
        pl6 = Pipe(c.prog)
        for tc in range(4):
            for m in range(NCH):
                def apmm(E, m=m, tc=tc):
                    last = None
                    for ci in range(NCH):
                        last = E.matmul(
                            bank[m % 2][:, 0:512],
                            ws[:, (m // 4) * NCH * 512 + ci * 512 + (m % 4) * 128:
                               (m // 4) * NCH * 512 + ci * 512 + (m % 4) * 128 + 128],
                            xn[:, ci * T + tc * 512: ci * T + tc * 512 + 512],
                            start=(ci == 0), stop=(ci == NCH - 1))
                    return last

                c.prog.c("tensor", apmm, upto=pl6.up())

                def asil(E, m=m, tc=tc):
                    return E.activation(out=ACT2[:, :], in_=bank[m % 2][:, 0:512],
                                        func=AFT.Silu, bias=pbsb[:, 8 + m: 8 + m + 1])

                c.prog.c("scalar", asil)

                def lepcv(E, m=m, tc=tc):
                    return emit_conv3(E, LEP, xibf[:, m * T:(m + 1) * T], tc * 512,
                                      512, (cw(2, 0, m), cw(2, 1, m), cw(2, 2, m)),
                                      cb(2, m))

                c.prog.c("vector", lepcv)

                def comb(E, m=m, tc=tc):
                    E.tensor_add(TC2[:, :], LEP[:, :],
                                 attn[:, m * T + tc * 512: m * T + tc * 512 + 512])
                    return E.tensor_mul(OWIN[:, m * 512:(m + 1) * 512],
                                        TC2[:, :], ACT2[:, :])

                c.prog.c("vector", comb)
                pl6.bump()
            for co in range(NCH):
                def omm(E, co=co, tc=tc):
                    last = None
                    for ci in range(NCH):
                        last = E.matmul(
                            bank[2 + co % 2][:, 0:512],
                            ws[:, (2 + co // 4) * NCH * 512 + ci * 512 + (co % 4) * 128:
                               (2 + co // 4) * NCH * 512 + ci * 512 + (co % 4) * 128 + 128],
                            OWIN[:, ci * 512:(ci + 1) * 512],
                            start=(ci == 0), stop=(ci == NCH - 1))
                    return last

                c.prog.c("tensor", omm, upto=(None if co == 0 else pl6.up()))
                c.prog.d("sync", [(t3[:, :], xs[1, co, :, tc * 512:(tc + 1) * 512])],
                         upto=pl6.up(1))

                def oev(E, co=co, tc=tc):
                    E.tensor_scalar_add(out=TC2[:, :], in0=bank[2 + co % 2][:, 0:512],
                                        scalar1=pbsb[:, 16 + co: 16 + co + 1])
                    return E.tensor_add(TC2[:, :], TC2[:, :], t3[:, :])

                c.prog.c("vector", oev)
                c.prog.d("sync", [(xs[1, co, :, tc * 512:(tc + 1) * 512], TC2[:, :])])
                pl6.bump()

        for ch in range(NCH):
            c.prog.d("sync", [(c.sb["m_ln_xch"][:, :], xs[1, ch])])

            def cvf(E, ch=ch):
                return emit_conv3(E, tmpf, c.sb["m_ln_xch"], 0, T,
                                  (cw(3, 0, ch), cw(3, 1, ch), cw(3, 2, ch)),
                                  cb(3, ch))

            c.prog.c("vector", cvf)
            c.prog.c("vector", lambda E: E.tensor_add(tmpf[:, :], tmpf[:, :],
                                                      c.sb["m_ln_xch"][:, :]))
            c.prog.d("sync", [(xs[0, ch], tmpf[:, :])])

        def ln2cb(ch, tc, t2):
            def gb(E, ch=ch, tc=tc):
                return E.tensor_scalar(
                    out=xn[:, ch * T + tc * 512: ch * T + tc * 512 + 512],
                    in0=t2[:, :], scalar1=nrmsb[:, 2 * NCH + ch: 2 * NCH + ch + 1],
                    scalar2=nrmsb[:, 3 * NCH + ch: 3 * NCH + ch + 1],
                    op0=ALU.mult, op1=ALU.add)

            c.prog.c("vector", gb)

        emit_layernorm_hbm(c, lambda ch: xs[0, ch], ln2cb, "m_ln")

        H = xibf
        for tc in range(4):
            pl8 = Pipe(c.prog)
            prev_slab_mark = None
            for fslab in range(8):
                wslab(fc1[:, fslab * 512:(fslab + 1) * 512], fslab % 2,
                      upto=prev_slab_mark)
                slab_mark = c.prog.mark()
                prev_slab_mark = slab_mark
                for col in range(4):
                    hm = fslab * 4 + col

                    def hmm(E, col=col, tc=tc, fslab=fslab):
                        last = None
                        for ci in range(NCH):
                            last = E.matmul(
                                bank[col % 2][:, 0:512],
                                ws[:, (fslab % 2) * NCH * 512 + ci * 512 + col * 128:
                                   (fslab % 2) * NCH * 512 + ci * 512 + col * 128 + 128],
                                xn[:, ci * T + tc * 512: ci * T + tc * 512 + 512],
                                start=(ci == 0), stop=(ci == NCH - 1))
                        return last

                    up = slab_mark if col < 2 else pl8.up(2)
                    c.prog.c("tensor", hmm, upto=up)

                    def gev(E, hm=hm, col=col):
                        return E.activation(
                            out=H[:, hm * 512:(hm + 1) * 512],
                            in_=bank[col % 2][:, 0:512], func=AFT.Gelu,
                            bias=pbsb[:, 32 + hm: 32 + hm + 1])

                    c.prog.c("scalar", gev)
                    pl8.bump()
            for oslab in range(2):
                dst = ws[:, 0:4 * NCH * 512]
                c.prog.d("gpsimd", [
                    (dst.rearrange("p (c j) -> p c j", j=512),
                     fc2[:, oslab * 512:(oslab + 1) * 512].rearrange(
                         "(c p) j -> p c j", p=128))])
                plf = Pipe(c.prog)
                for col in range(4):
                    co = oslab * 4 + col

                    def fmm(E, col=col, tc=tc):
                        last = None
                        for hc in range(32):
                            last = E.matmul(
                                bank[4 + col % 2][:, 0:512],
                                ws[:, hc * 512 + col * 128: hc * 512 + col * 128 + 128],
                                H[:, hc * 512:(hc + 1) * 512],
                                start=(hc == 0), stop=(hc == 31))
                        return last

                    c.prog.c("tensor", fmm, upto=(None if col == 0 else plf.up()))
                    c.prog.d("sync", [
                        (t3[:, :], xs[0, co, :, tc * 512:(tc + 1) * 512]),
                        (xip[:, 0:512], xin[s, co, :, tc * 512:(tc + 1) * 512])])

                    def fev(E, co=co, tc=tc, col=col):
                        E.tensor_scalar_add(out=TC2[:, :],
                                            in0=bank[4 + col % 2][:, 0:512],
                                            scalar1=pbsb[:, 24 + co: 24 + co + 1])
                        E.tensor_add(TC2[:, :], TC2[:, :], t3[:, :])
                        return E.tensor_add(TC2[:, :], TC2[:, :], xip[:, 0:512])

                    c.prog.c("vector", fev)
                    c.prog.d("sync", [(xout[s, co, :, tc * 512:(tc + 1) * 512],
                                       TC2[:, :])])
                    plf.bump()
    c.prog.w("sync")


def emit_elu1(c, bank, qkbias, QKS, SCR2, tmpf):
    def e1(E):
        E.tensor_add(QKS[:, 0:512], bank[0][:, 0:512], qkbias[:, 0:512])
        return E.tensor_add(QKS[:, 512:1024], bank[1][:, 0:512], qkbias[:, 512:1024])

    c.prog.c("vector", e1)
    c.prog.c("vector", lambda E: E.tensor_scalar_min(
        out=tmpf[:, 0:1024], in0=QKS[:, :], scalar1=0.0))
    c.prog.c("scalar", lambda E: E.activation(out=tmpf[:, 0:1024],
                                              in_=tmpf[:, 0:1024], func=AFT.Exp))
    c.prog.c("vector", lambda E: E.tensor_relu(out=SCR2[:, :], in_=QKS[:, :]))
    c.prog.c("vector", lambda E: E.tensor_add(QKS[:, :], tmpf[:, 0:1024],
                                              SCR2[:, :]))


def emit_rope(c, QKS, KRB, SCR2, cossb, sinsb, out_f32=None):
    xe = QKS[:, 0:1024:2]
    xo = QKS[:, 1:1024:2]
    if out_f32 is None:
        oe = KRB[:, 0:1024:2]
        oo = KRB[:, 1:1024:2]
    else:
        oe = out_f32[:, 0:1024:2]
        oo = out_f32[:, 1:1024:2]

    def r1(E):
        E.tensor_mul(SCR2[:, 0:512], xe, cossb[:, :])
        E.tensor_mul(SCR2[:, 512:1024], xo, sinsb[:, :])
        return E.tensor_sub(oe, SCR2[:, 0:512], SCR2[:, 512:1024])

    c.prog.c("vector", r1)

    def r2(E):
        E.tensor_mul(SCR2[:, 0:512], xe, sinsb[:, :])
        E.tensor_mul(SCR2[:, 512:1024], xo, cossb[:, :])
        return E.tensor_add(oo, SCR2[:, 0:512], SCR2[:, 512:1024])

    c.prog.c("vector", r2)




def build_l1_attn(ctx, xin, xout):
    c = ctx
    NH = 16
    HD = 64
    wqkv = c.din("a_wqkvT", [D, 3 * D], F32)
    wproj = c.din("a_wprojT", [D, D], F32)
    bqkvT = c.din("a_bqkvT", [128, 24], F32)
    bprojT = c.din("a_bprojT", [128, 8], F32)
    bvrow = c.din("a_bv_row", [1, D], F32)
    qaug = c.din("a_qaug", [NH, 64, T], BF16)
    kaug = c.din("a_kaug", [NH, 64, T], BF16)
    masks = c.din("a_masks", [2, 128, 128], BF16)
    posi = c.din("a_posi", [128, 4], F32)
    ind8 = c.din("a_ind8", [128, 8], F32)

    qfm = c.S("a_qfm", [128, NCH * T], BF16)
    kfm = c.S("a_kfm", [128, NCH * T], BF16)
    vaug = c.S("a_vaug", [128, 16 * 1040], BF16)
    xnbf = c.S("a_xn", [128, NCH * T], BF16)
    qa = c.S("a_qa", [128, T], BF16)
    ka = c.S("a_ka", [128, T], BF16)
    wsl = c.S("a_wsl", [128, NCH * 512], BF16)
    msk = c.S("a_msk", [128, 2 * 128], BF16)
    bq8 = c.S("a_bq8", [128, 8], F32)
    bqkv_sb = c.S("a_bqkv", [128, 24], F32)
    bproj_sb = c.S("a_bproj", [128, 8], F32)
    bvb = c.S("a_bvb", [128, D], F32)
    stt = c.S("a_stt", [128, 512], F32)
    stt2 = c.S("a_stt2", [128, 1024], F32)
    pt2 = c.S("a_pt2", [128, 1024], BF16)
    negm = c.S("a_negm", [128, 16], F32)
    rmq = c.S("a_rmq", [128, 8], F32)
    mrow = c.S("a_mrow", [1, 512], F32)
    mbt = c.S("a_mbt", [128, 512], F32)
    of32 = c.S("a_of32", [64, 512], F32)
    rrow = c.S("a_rrow", [1, 512], F32)
    vsc = c.S("a_vsc", [128, 512], F32)
    t3 = c.S("a_t3", [128, 512], F32)
    ofm = xnbf

    bank = [c.ps[f"bank{i}"] for i in range(8)]

    posisb = c.S("a_posisb", [128, 4], F32)
    ind8sb = c.S("a_ind8sb", [128, 8], F32)
    chsb = c.S("a_chsb", [128, 8], F32)
    c.prog.d("sync", [(msk[:, :].rearrange("p (m j) -> p m j", j=128),
                   masks.rearrange("m p j -> p m j")),
                      (bqkv_sb[:, :], bqkvT[:, :]),
                      (bproj_sb[:, :], bprojT[:, :]),
                      (posisb[:, :], posi[:, :]),
                      (ind8sb[:, :], ind8[:, :])])
    c.prog.c("vector", lambda E: E.tensor_single_scalar(
        out=bq8[:, :], in_=bqkv_sb[:, 0:8], scalar=0.125, op=ALU.mult))
    def chprep(E):
        E.tensor_tensor(out=chsb[:, 0:2], in0=posisb[:, 0:4:2],
                        in1=posisb[:, 1:4:2], op=ALU.add)
        psv = AP(chsb[:, 0:2].tensor, chsb[:, 0:2].offset,
                 [list(chsb[:, 0:2].ap[0])] + [[1, 2], [0, 4]])
        return E.tensor_tensor(out=stt[:, 0:8].rearrange("p (a g) -> p a g", g=4),
                               in0=ind8sb[:, 0:8].rearrange("p (a g) -> p a g", g=4),
                               in1=psv, op=ALU.mult)

    c.prog.c("vector", chprep)
    c.prog.c("tensor", lambda E: E.matmul(
        c.ps["bank0"][:, 0:8], c.sb["onesf"][:, :], stt[:, 0:8],
        start=True, stop=True))
    c.prog.c("vector", lambda E: E.tensor_copy(out=chsb[:, :],
                                               in_=c.ps["bank0"][:, 0:8]))
    bvr = c.S("a_bvr", [1, D], F32)
    c.prog.d("sync", [(bvr[0:1, :], bvrow[0:1, :])])

    for s in range(SEQ_PER_CORE):
        def consume(ch, tc, t2):
            def xcast(E, ch=ch, tc=tc):
                return E.activation(
                    out=xnbf[:, ch * T + tc * 512: ch * T + tc * 512 + 512],
                    in_=t2[:, :], func=AFT.Copy)

            c.prog.c("scalar", xcast)

        emit_layernorm_hbm(c, lambda ch, s=s: xin[s, ch], consume, "a_ln")
        c.sb["a_ln_xch"] = c.sb["a_ln_xch2"][:, 0:T]

        def bvmm(E):
            E.matmul(bank[0][:, 0:512], c.sb["onesf"][0:1, :],
                     c.sb["a_bvr"][0:1, 0:512], start=True, stop=True)
            return E.matmul(bank[1][:, 0:512], c.sb["onesf"][0:1, :],
                            c.sb["a_bvr"][0:1, 512:1024], start=True, stop=True)

        c.prog.c("tensor", bvmm)

        def bvev(E):
            E.tensor_copy(out=bvb[:, 0:512], in_=bank[0][:, 0:512])
            return E.tensor_copy(out=bvb[:, 512:1024], in_=bank[1][:, 0:512])

        c.prog.c("vector", bvev)

        for slab in range(6):
            c0 = slab * 512
            c.prog.d("gpsimd", [
                (wsl[:, :].rearrange("p (c j) -> p c j", j=512),
                 wqkv[:, c0:c0 + 512].rearrange("(c p) j -> p c j", p=128))])
            if slab < 4:
                plq = Pipe(c.prog)
                for col in range(4):
                    co = (c0 + col * 128) // 128
                    dstbase = (qfm if slab < 2 else kfm)
                    och = co % NCH
                    for tc in range(4):
                        def pmm(E, col=col, tc=tc):
                            last = None
                            for ci in range(NCH):
                                last = E.matmul(
                                    bank[tc % 2][:, 0:512],
                                    wsl[:, ci * 512 + col * 128: ci * 512 + col * 128 + 128],
                                    xnbf[:, ci * T + tc * 512: ci * T + tc * 512 + 512],
                                    start=(ci == 0), stop=(ci == NCH - 1))
                            return last

                        c.prog.c("tensor", pmm, upto=plq.up())
                        if slab < 2:
                            bias = bq8[:, och:och + 1]
                            scale = 0.125
                        else:
                            bias = bqkv_sb[:, 8 + och: 8 + och + 1]
                            scale = 1.0

                        def pev(E, dstbase=dstbase, och=och, tc=tc, bias=bias,
                                scale=scale):
                            return E.tensor_scalar(
                                out=dstbase[:, och * T + tc * 512: och * T + tc * 512 + 512],
                                in0=bank[tc % 2][:, 0:512], scalar1=scale,
                                scalar2=bias, op0=ALU.mult, op1=ALU.add)

                        c.prog.c("vector", pev)
                        plq.bump()
            else:
                half = slab - 4
                plv = Pipe(c.prog)
                for ti in range(16):
                    def vmm(E, ti=ti):
                        last = None
                        for ci in range(NCH):
                            last = E.matmul(
                                bank[ti % 2][:, 0:512],
                                xnbf[:, ci * T + ti * 128: ci * T + ti * 128 + 128],
                                wsl[:, ci * 512: ci * 512 + 512],
                                start=(ci == 0), stop=(ci == NCH - 1))
                        return last

                    c.prog.c("tensor", vmm, upto=plv.up())

                    def vev(E, ti=ti, half=half):
                        E.tensor_add(vsc[:, :], bank[ti % 2][:, 0:512],
                                     bvb[:, half * 512: half * 512 + 512])
                        dst = vaug[:, ti * 1040 + half * 520: (ti + 1) * 1040]
                        dstv = AP(dst.tensor, dst.offset,
                                  [list(dst.ap[0])] + [[65, 8], [1, 64]])
                        srcv = vsc[:, :].rearrange("p (h d) -> p h d", d=64)
                        return E.tensor_copy(out=dstv, in_=srcv)

                    c.prog.c("vector", vev)
                    plv.bump()

        def vones(E):
            a0 = vaug[:, 64:65]
            apv = AP(a0.tensor, a0.offset,
                     [list(a0.ap[0])] + [[1040, 16], [65, 16], [1, 1]])
            return E.memset(apv, 1.0)

        c.prog.c("vector", vones)

        for h in range(16):
            hch = h // 2
            hrow = (h % 2) * 64
            c.prog.c("vector", lambda E, hch=hch, hrow=hrow: E.tensor_copy(
                out=qa[0:64, :], in_=qfm[hrow:hrow + 64, hch * T:(hch + 1) * T]))
            c.prog.c("vector", lambda E, hch=hch, hrow=hrow: E.tensor_copy(
                out=ka[0:64, :], in_=kfm[hrow:hrow + 64, hch * T:(hch + 1) * T]))
            c.prog.d("sync", [(qa[64:128, :], qaug[h]), (ka[64:128, :], kaug[h])])

            smm_marks = {}
            tail_marks = {}
            for qt in range(16):
                nkeys = (qt + 1) * 128
                nfull = (qt * 128) // 512
                rem = (qt * 128) % 512
                b0 = (qt % 2) * 4

                def smm(E, qt=qt, nkeys=nkeys, b0=b0):
                    last = None
                    for kb in range((nkeys + 511) // 512):
                        w = min(512, nkeys - kb * 512)
                        last = E.matmul(
                            bank[b0 + kb][:, 0:w], qa[:, qt * 128:(qt + 1) * 128],
                            ka[:, kb * 512: kb * 512 + w],
                            start=True, stop=True)
                    return last

                c.prog.c("tensor", smm, upto=tail_marks.get(qt - 2))
                smm_marks[qt] = c.prog.mark()

                def sred(E, qt=qt, nfull=nfull, rem=rem, b0=b0):
                    db = (qt * 128) // 512
                    doff = (qt * 128) % 512
                    buf = stt[:, (qt % 2) * 128:(qt % 2) * 128 + 128]
                    E.tensor_add(buf, bank[b0 + db][:, doff:doff + 128],
                                 msk[:, 0:128])
                    np_ = 0
                    for kb in range(nfull):
                        E.reduce_max(rmq[:, np_:np_ + 1], bank[b0 + kb][:, 0:512],
                                     axis=AXX)
                        np_ += 1
                    if rem > 0:
                        E.reduce_max(rmq[:, np_:np_ + 1], bank[b0 + nfull][:, 0:rem],
                                     axis=AXX)
                        np_ += 1
                    E.reduce_max(rmq[:, np_:np_ + 1], buf, axis=AXX)
                    np_ += 1
                    E.reduce_max(rmq[:, 7:8], rmq[:, 0:np_], axis=AXX)
                    return E.tensor_single_scalar(
                        out=negm[:, qt:qt + 1], in_=rmq[:, 7:8], scalar=-1.0,
                        op=ALU.mult)

                c.prog.c("vector", sred, upto=smm_marks[qt])
                tail_marks[qt] = c.prog.mark()

            for g in range(4):
                def mtr(E, g=g):
                    last = None
                    for j in range(4):
                        last = E.transpose(bank[6][0:1, j * 128:(j + 1) * 128],
                                           negm[:, 4 * g + j: 4 * g + j + 1],
                                           c.sb["identf"][:, :])
                    return last

                c.prog.c("tensor", mtr)
                c.prog.c("scalar", lambda E: E.copy(out=mrow[0:1, :],
                                                    in_=bank[6][0:1, 0:512]))
                c.prog.c("tensor", lambda E: E.matmul(
                    bank[6][:, 0:512], c.sb["onesf"][0:1, :], mrow[0:1, :],
                    start=True, stop=True))
                c.prog.c("vector", lambda E: E.tensor_copy(out=mbt[:, :],
                                                           in_=bank[6][:, 0:512]))
                mbt_mark = c.prog.mark()
                if h >= 8:
                    c.prog.c("vector", lambda E, h=h: E.tensor_scalar_add(
                        out=mbt[:, :], in0=mbt[:, :],
                        scalar1=chsb[:, h - 8: h - 7]))
                nkt = 4 * g + 4
                stm = {}
                tdone = {}

                def emit_tail(kt, g=g, h=h, nkt=nkt):
                    bb = kt % 2
                    sbuf = stt[:, bb * 512: bb * 512 + 512] if False else None

                    def stsub(E, kt=kt, g=g, bb=bb):
                        dst = stt2[:, bb * 512: bb * 512 + 512]
                        last = E.tensor_add(dst, bank[bb][:, 0:512], mbt[:, :])
                        qsub = kt - 4 * g
                        if 0 <= qsub <= 3:
                            last = E.tensor_add(
                                dst[:, qsub * 128:(qsub + 1) * 128],
                                dst[:, qsub * 128:(qsub + 1) * 128],
                                msk[:, 128:256])
                        return last

                    c.prog.c("vector", stsub, upto=stm[kt])
                    m1 = c.prog.mark()
                    c.prog.c("scalar", lambda E, bb=bb: E.activation(
                        out=pt2[:, bb * 512: bb * 512 + 512],
                        in_=stt2[:, bb * 512: bb * 512 + 512], func=AFT.Exp),
                        upto=m1)
                    m2 = c.prog.mark()
                    nmask = min(4, kt - 4 * g)
                    if nmask > 0:
                        c.prog.c("vector", lambda E, nmask=nmask, bb=bb: E.memset(
                            pt2[:, bb * 512: bb * 512 + nmask * 128], 0.0),
                            upto=m2)
                    m3 = c.prog.mark()

                    def pvmm(E, kt=kt, g=g, h=h, bb=bb):
                        return E.matmul(
                            bank[5][0:65, 0:512],
                            vaug[:, kt * 1040 + h * 65: kt * 1040 + h * 65 + 65],
                            pt2[:, bb * 512: bb * 512 + 512],
                            start=(kt == 0), stop=(kt == nkt - 1))

                    c.prog.c("tensor", pvmm, upto=m3)
                    tdone[kt] = c.prog.mark()

                for kt in range(nkt):
                    def stmm(E, kt=kt, g=g):
                        return E.matmul(
                            bank[kt % 2][:, 0:512], ka[:, kt * 128:(kt + 1) * 128],
                            qa[:, g * 512:(g + 1) * 512], start=True, stop=True)

                    c.prog.c("tensor", stmm, upto=tdone.get(kt - 2, mbt_mark))
                    stm[kt] = c.prog.mark()
                    if kt >= 1:
                        emit_tail(kt - 1)
                emit_tail(nkt - 1)
                c.prog.c("vector", lambda E: E.reciprocal(
                    out=rrow[0:1, :], in_=bank[5][64:65, 0:512]))
                c.prog.c("scalar", lambda E: E.copy(out=of32[:, :],
                                                    in_=bank[5][0:64, 0:512]))
                c.prog.c("tensor", lambda E: E.matmul(
                    bank[6][0:64, 0:512], c.sb["onesf"][0:1, 0:64], rrow[0:1, :],
                    start=True, stop=True))

                def oev(E, g=g, hch=hch, hrow=hrow):
                    return E.tensor_mul(
                        ofm[hrow:hrow + 64, hch * T + g * 512: hch * T + g * 512 + 512],
                        of32[:, :], bank[6][0:64, 0:512])

                c.prog.c("vector", oev)

        for slab in range(2):
            c0 = slab * 512
            c.prog.d("gpsimd", [
                (wsl[:, :].rearrange("p (c j) -> p c j", j=512),
                 wproj[:, c0:c0 + 512].rearrange("(c p) j -> p c j", p=128))])
            plo = Pipe(c.prog)
            for col in range(4):
                co = slab * 4 + col
                c.prog.d("sync", [(c.sb["a_ln_xch"][:, :], xin[s, co])])
                for tc in range(4):
                    def omm(E, col=col, tc=tc):
                        last = None
                        for ci in range(NCH):
                            last = E.matmul(
                                bank[tc % 2][:, 0:512],
                                wsl[:, ci * 512 + col * 128: ci * 512 + col * 128 + 128],
                                ofm[:, ci * T + tc * 512: ci * T + tc * 512 + 512],
                                start=(ci == 0), stop=(ci == NCH - 1))
                        return last

                    c.prog.c("tensor", omm, upto=plo.up())

                    def oev2(E, co=co, tc=tc):
                        return E.tensor_scalar_add(out=t3[:, :],
                                                   in0=bank[tc % 2][:, 0:512],
                                                   scalar1=bproj_sb[:, co:co + 1])

                    c.prog.c("vector", oev2)

                    def oadd(E, tc=tc):
                        xs = c.sb["a_ln_xch"][:, tc * 512:(tc + 1) * 512]
                        return E.tensor_add(t3[:, :], t3[:, :], xs)

                    c.prog.c("vector", oadd)
                    c.prog.d("sync", [(xout[s, co, :, tc * 512:(tc + 1) * 512],
                                       t3[:, :])])
                    plo.bump()
    c.prog.w("sync")




def bT(v):
    v = np.asarray(v, np.float32)
    return np.ascontiguousarray(v.reshape(-1, 128).T)


def common_consts():
    idf = np.eye(128, dtype=np.float32)
    return {
        "ident_f": idf,
        "ident_b": idf.astype(ml_dtypes.bfloat16),
        "ones_f": np.ones((128, 128), np.float32),
        "ones_b": np.ones((128, 128), ml_dtypes.bfloat16),
    }


def l0_ffn_host_inputs(inp):
    g = lambda k: np.asarray(inp["L0_ffn_" + k], np.float32)
    NL = 2
    w1 = g("w1")
    w2 = g("w2")
    b1 = g("b1")
    w1p = np.zeros((NL, D, 5888), np.float32)
    w1p[:, :, 0:2730] = w1[:, :, 0:2730]
    w1p[:, :, 2816:2816 + 2730] = w1[:, :, 2730:5460]
    w2p = np.zeros((NL, 2816, D), np.float32)
    w2p[:, 0:2730, :] = w2
    b1pad = np.zeros((NL, 5888), np.float32)
    b1pad[:, 0:2730] = b1[:, 0:2730]
    b1pad[:, 2816:2816 + 2730] = b1[:, 2730:5460]
    half_idx = np.arange(0, 64, 2, dtype=np.float64)
    inv_freq = 1.0 / 10000.0 ** (half_idx / 64.0)
    t = np.arange(T, dtype=np.float64)
    fr = t[:, None] * inv_freq[None, :]
    cos = np.cos(fr)
    sin = np.sin(fr)
    base = (half_idx + 0.4 * 64) / (1.4 * 64)
    power = (t - T // 2) / 256.0
    scale = base[None, :] ** power[:, None]
    sgn = np.concatenate([-np.ones(32), np.ones(32)])
    cc = np.concatenate([cos, cos], axis=1)
    ss = np.concatenate([sin, sin], axis=1)
    sc = np.concatenate([scale, scale], axis=1)
    Aq = 8.0 * cc * sc
    BqS = 8.0 * ss * sc * sgn[None, :]
    Ak = cc / sc
    BkS = ss / sc * sgn[None, :]
    xpos = np.stack([a.reshape(16, 128, 64) for a in (Aq, BqS, Ak, BkS)])
    irel = (np.arange(512))
    maskS = np.full((4, 128, 1024), NEG, np.float32)
    for qt in range(4):
        i = 128 * qt + np.arange(128)
        j = np.arange(1024)
        ok = (i[:, None] <= j[None, :]) & (j[None, :] <= i[:, None] + 512)
        maskS[qt][ok] = 0.0
    maskT = np.full((8, 128, 512), NEG, np.float32)
    for kt in range(8):
        jj = 128 * kt + np.arange(128)
        q = np.arange(512)
        ok = (q[None, :] <= jj[:, None]) & (jj[:, None] <= q[None, :] + 512)
        maskT[kt][ok] = 0.0
    return {
        "g_wqkv": g("wqkv"),
        "g_wo": g("wo"),
        "g_w1p": w1p,
        "g_w2p": w2p,
        "g_qs": np.ascontiguousarray(np.broadcast_to(
            g("q_scale")[:, None, :], (NL, 128, 64))),
        "g_ks": np.ascontiguousarray(np.broadcast_to(
            g("k_scale")[:, None, :], (NL, 128, 64))),
        "g_lgb": np.stack([np.stack([bT(g("ln_g")[l]), bT(g("ln_b")[l])])
                           for l in range(NL)]),
        "g_b1p": np.stack([bT(b1pad[l]) for l in range(NL)]),
        "g_b2": np.stack([bT(g("b2")[l]) for l in range(NL)]),
        "g_xpos": xpos.astype(ml_dtypes.bfloat16),
        "g_maskS": maskS.astype(ml_dtypes.bfloat16),
        "g_maskT": maskT.astype(ml_dtypes.bfloat16),
    }


def l0_attn_host_inputs(inp):
    g = lambda k: np.asarray(inp["L0_attn_" + k], np.float32)

    def convT(w):
        w = np.asarray(w, np.float32)[:, 0, :]
        return np.stack([bT(w[:, t]) for t in range(3)])

    theta = 1.0 / 10000.0 ** (np.arange(512, dtype=np.float64) / 512.0)
    cosb = np.broadcast_to(np.cos(theta).astype(np.float32), (128, 512))
    sinb = np.broadcast_to(np.sin(theta).astype(np.float32), (128, 512))
    pbias = np.zeros((128, 64), np.float32)
    pbias[:, 0:8] = bT(g("in_b"))
    pbias[:, 8:16] = bT(g("actp_b"))
    pbias[:, 16:24] = bT(g("out_b"))
    pbias[:, 24:32] = bT(g("fc2_b"))
    pbias[:, 32:64] = bT(g("fc1_b"))
    norms = np.stack([bT(g("norm1_g")), bT(g("norm1_b")),
                      bT(g("norm2_g")), bT(g("norm2_b"))])
    return {
        "m_inw": g("in_w"), "m_actp": g("actp_w"), "m_qkw": g("qk_w"),
        "m_outw": g("out_w"), "m_fc1": g("fc1_w"), "m_fc2": g("fc2_w"),
        "m_convw": np.stack([convT(g("cpe1_w")), convT(g("dwc_w")),
                             convT(g("lepe_w")), convT(g("cpe2_w"))]),
        "m_convb": np.stack([bT(g("cpe1_b")), bT(g("dwc_b")),
                             bT(g("lepe_b")), bT(g("cpe2_b"))]),
        "m_norms": norms,
        "m_pbias": pbias,
        "m_qkb_row": np.ascontiguousarray(g("qk_b").reshape(1, 2 * D)),
        "m_costab": np.stack([cosb, sinb]).astype(ml_dtypes.bfloat16),
    }


def l1_attn_host_inputs(inp):
    wqkv = np.asarray(inp["L1_attn_w_qkv"], np.float32)
    wproj = np.asarray(inp["L1_attn_w_proj"], np.float32)
    bqkv = np.asarray(inp["L1_attn_b_qkv"], np.float32)
    bproj = np.asarray(inp["L1_attn_b_proj"], np.float32)
    k = np.arange(256, dtype=np.float64)
    freqs = 1.0 / 10000.0 ** (k / 512.0)
    i = np.arange(T, dtype=np.float64)
    th = i[None, :] * freqs[:, None]
    cos = np.cos(th)
    sin = np.sin(th)
    qaug = np.zeros((16, 64, T), np.float32)
    kaug = np.zeros((16, 64, T), np.float32)
    for h in range(8):
        qaug[h, 0:32] = cos[32 * h:32 * h + 32]
        qaug[h, 32:64] = sin[32 * h:32 * h + 32]
        kaug[h, 0:32] = (cos - sin)[32 * h:32 * h + 32]
        kaug[h, 32:64] = (cos + sin)[32 * h:32 * h + 32]
    ii = np.arange(128)
    maskS = np.where(ii[None, :] > ii[:, None], NEG, 0.0).astype(np.float32)
    masks = np.stack([maskS, maskS.T])
    posi = np.asarray(inp["L1_attn_pos_independent"], np.float32)
    posi4 = np.ascontiguousarray(posi.reshape(2, 128, 2).transpose(1, 0, 2)
                                 .reshape(128, 4))
    r = np.arange(128)
    ind8 = np.zeros((128, 8), np.float32)
    for h in range(8):
        ind8[:, h] = (r // 32 == h % 4).astype(np.float32)
    return {
        "a_posi": posi4,
        "a_ind8": ind8,
        "a_wqkvT": np.ascontiguousarray(wqkv.T),
        "a_wprojT": np.ascontiguousarray(wproj.T),
        "a_bqkvT": bT(bqkv),
        "a_bprojT": bT(bproj),
        "a_bv_row": np.ascontiguousarray(bqkv[2048:].reshape(1, D)),
        "a_qaug": qaug.astype(ml_dtypes.bfloat16),
        "a_kaug": kaug.astype(ml_dtypes.bfloat16),
        "a_masks": masks.astype(ml_dtypes.bfloat16),
    }


def l1_ffn_host_inputs(inp):
    return {
        "f_Wg": np.ascontiguousarray(
            np.asarray(inp["L1_ffn_Wg"], np.float32).reshape(8, 128, 8)),
        "f_bg_b": np.ascontiguousarray(
            np.broadcast_to(np.asarray(inp["L1_ffn_bg"], np.float32), (128, 8))),
        "f_W1": np.asarray(inp["L1_ffn_W1"], np.float32),
        "f_W2": np.asarray(inp["L1_ffn_W2"], np.float32),
        "f_b1T": np.stack([bT(b) for b in np.asarray(inp["L1_ffn_b1"], np.float32)]),
        "f_b2": np.asarray(inp["L1_ffn_b2"], np.float32),
    }


def fm_of(x_seqs):
    s, t, d = x_seqs.shape
    xt = np.ascontiguousarray(x_seqs.transpose(0, 2, 1))
    return np.ascontiguousarray(xt.reshape(s, NCH, 128, t))


def fm_back(xfm):
    s = xfm.shape[0]
    return np.ascontiguousarray(
        xfm.reshape(s, D, T).transpose(0, 2, 1))




def build_program_l0ffn(dbg=None):
    nc = bass.Bass("TRN2", target_bir_lowering=False, detect_race_conditions=False)
    ctx = Ctx(nc)
    for i in range(8):
        ctx.P(f"bank{i}", [128, 512], F32)
    xin = ctx.din("xin", [SEQ_PER_CORE, NCH, 128, T], F32)
    xout = ctx.dout("xout", [SEQ_PER_CORE, NCH, 128, T], F32)
    setup_common(ctx)
    build_l0_ffn(ctx, xin, xout, dbg=dbg)
    ctx.prog.emit()
    return ctx


def run_l0_ffn(x_full, inp, n_cores=NCORES, dbg=None):
    ctx = build_program_l0ffn(dbg=dbg)
    consts = common_consts()
    winp = l0_ffn_host_inputs(inp)
    in_maps = []
    for cix in range(n_cores):
        xs = x_full[cix * SEQ_PER_CORE:(cix + 1) * SEQ_PER_CORE]
        m = {"xin": fm_of(xs)}
        m.update(consts)
        m.update(winp)
        in_maps.append(m)
    res = run_bass_kernel_spmd(ctx.nc, in_maps, core_ids=list(range(n_cores)))
    outs = [fm_back(r["xout"]) for r in res.results]
    return np.concatenate(outs, axis=0)


def build_program_l0attn():
    nc = bass.Bass("TRN2", target_bir_lowering=False, detect_race_conditions=False)
    ctx = Ctx(nc)
    for i in range(8):
        ctx.P(f"bank{i}", [128, 512], F32)
    xin = ctx.din("xin", [SEQ_PER_CORE, NCH, 128, T], F32)
    xout = ctx.dout("xout", [SEQ_PER_CORE, NCH, 128, T], F32)
    setup_common(ctx)
    build_l0_attn(ctx, xin, xout)
    ctx.prog.emit()
    return ctx


def run_l0_attn(x_full, inp, n_cores=NCORES):
    ctx = build_program_l0attn()
    consts = common_consts()
    winp = l0_attn_host_inputs(inp)
    in_maps = []
    for cix in range(n_cores):
        xs = x_full[cix * SEQ_PER_CORE:(cix + 1) * SEQ_PER_CORE]
        m = {"xin": fm_of(xs)}
        m.update(consts)
        m.update(winp)
        in_maps.append(m)
    res = run_bass_kernel_spmd(ctx.nc, in_maps, core_ids=list(range(n_cores)))
    outs = [fm_back(r["xout"]) for r in res.results]
    return np.concatenate(outs, axis=0)


def build_program_l1attn():
    nc = bass.Bass("TRN2", target_bir_lowering=False, detect_race_conditions=False)
    ctx = Ctx(nc)
    for i in range(8):
        ctx.P(f"bank{i}", [128, 512], F32)
    xin = ctx.din("xin", [SEQ_PER_CORE, NCH, 128, T], F32)
    xout = ctx.dout("xout", [SEQ_PER_CORE, NCH, 128, T], F32)
    setup_common(ctx)
    build_l1_attn(ctx, xin, xout)
    ctx.prog.emit()
    return ctx


def run_l1_attn(x_full, inp, n_cores=NCORES):
    ctx = build_program_l1attn()
    consts = common_consts()
    winp = l1_attn_host_inputs(inp)
    in_maps = []
    for cix in range(n_cores):
        xs = x_full[cix * SEQ_PER_CORE:(cix + 1) * SEQ_PER_CORE]
        m = {"xin": fm_of(xs)}
        m.update(consts)
        m.update(winp)
        in_maps.append(m)
    res = run_bass_kernel_spmd(ctx.nc, in_maps, core_ids=list(range(n_cores)))
    outs = [fm_back(r["xout"]) for r in res.results]
    return np.concatenate(outs, axis=0)


def build_program_l1ffn():
    nc = bass.Bass("TRN2", target_bir_lowering=False, detect_race_conditions=False)
    ctx = Ctx(nc)
    for i in range(8):
        ctx.P(f"bank{i}", [128, 512], F32)
    xin = ctx.din("xin", [SEQ_PER_CORE, NCH, 128, T], F32)
    xout = ctx.dout("xout", [SEQ_PER_CORE, NCH, 128, T], F32)
    setup_common(ctx)
    build_l1_ffn(ctx, xin, xout)
    ctx.prog.emit()
    return ctx


def run_l1_ffn(x_full, inp, n_cores=NCORES):
    ctx = build_program_l1ffn()
    consts = common_consts()
    winp = l1_ffn_host_inputs(inp)
    in_maps = []
    for cix in range(n_cores):
        xs = x_full[cix * SEQ_PER_CORE:(cix + 1) * SEQ_PER_CORE]
        m = {"xin": fm_of(xs)}
        m.update(consts)
        m.update(winp)
        in_maps.append(m)
    res = run_bass_kernel_spmd(ctx.nc, in_maps, core_ids=list(range(n_cores)))
    outs = [fm_back(r["xout"]) for r in res.results]
    return np.concatenate(outs, axis=0)


def build_program_fused():
    nc = bass.Bass("TRN2", target_bir_lowering=False, detect_race_conditions=False)
    ctx = Ctx(nc, pool_kb=205)
    for i in range(8):
        ctx.P(f"bank{i}", [128, 512], F32)
    xin = ctx.din("xin", [SEQ_PER_CORE, NCH, 128, T], F32)
    xout = ctx.dout("xout", [SEQ_PER_CORE, NCH, 128, T], F32)
    xa = nc.dram_tensor("xbounceA", [SEQ_PER_CORE, NCH, 128, T], F32,
                        kind="Internal").ap()
    xb = nc.dram_tensor("xbounceB", [SEQ_PER_CORE, NCH, 128, T], F32,
                        kind="Internal").ap()
    setup_common(ctx)
    ctx.pool_reset()
    build_l0_attn(ctx, xin, xa)
    ctx.pool_reset()
    build_l0_ffn(ctx, xa, xb)
    ctx.pool_reset()
    build_l1_attn(ctx, xb, xa)
    ctx.pool_reset()
    build_l1_ffn(ctx, xa, xout)
    ctx.pool_reset()
    ctx.prog.emit()
    return ctx


def run_fused(inputs, n_cores=NCORES):
    ctx = _get_ctx("fused", build_program_fused)
    x = np.ascontiguousarray(np.asarray(inputs["L0_attn_x"], np.float32))
    host = {}
    host.update(common_consts())
    host.update(l0_attn_host_inputs(inputs))
    host.update(l0_ffn_host_inputs(inputs))
    host.update(l1_attn_host_inputs(inputs))
    host.update(l1_ffn_host_inputs(inputs))
    in_maps = []
    for cix in range(n_cores):
        xs = x[cix * SEQ_PER_CORE:(cix + 1) * SEQ_PER_CORE]
        m = {"xin": fm_of(xs)}
        m.update(host)
        in_maps.append(m)
    res = run_bass_kernel_spmd(ctx.nc, in_maps, core_ids=list(range(n_cores)))
    outs = [fm_back(r["xout"]) for r in res.results]
    return np.concatenate(outs, axis=0)


_CTX_CACHE = {}


def _get_ctx(name, builder):
    if name not in _CTX_CACHE:
        _CTX_CACHE[name] = builder()
    return _CTX_CACHE[name]


def _run_block(name, builder, host_inputs, x_full, n_cores=NCORES):
    ctx = _get_ctx(name, builder)
    consts = common_consts()
    in_maps = []
    for cix in range(n_cores):
        xs = x_full[cix * SEQ_PER_CORE:(cix + 1) * SEQ_PER_CORE]
        m = {"xin": fm_of(xs)}
        m.update(consts)
        m.update(host_inputs)
        in_maps.append(m)
    res = run_bass_kernel_spmd(ctx.nc, in_maps, core_ids=list(range(n_cores)))
    outs = [fm_back(r["xout"]) for r in res.results]
    return np.concatenate(outs, axis=0)


def kernel(**inputs):
    return run_fused(inputs)


def kernel_unfused(**inputs):
    x = np.ascontiguousarray(np.asarray(inputs["L0_attn_x"], np.float32))
    x = _run_block("b1", build_program_l0attn, l0_attn_host_inputs(inputs), x)
    x = _run_block("b2", build_program_l0ffn, l0_ffn_host_inputs(inputs), x)
    x = _run_block("b3", build_program_l1attn, l1_attn_host_inputs(inputs), x)
    x = _run_block("b4", build_program_l1ffn, l1_ffn_host_inputs(inputs), x)
    return x


# version 12
# speedup vs baseline: 1.5471x; 1.0053x over previous
import numpy as np
import ml_dtypes
import concourse.bass as bass
import concourse.mybir as mybir
from concourse.ap import AP
from concourse.bass_utils import run_bass_kernel_spmd

F32 = mybir.dt.float32
BF16 = mybir.dt.bfloat16
AFT = mybir.ActivationFunctionType
ALU = mybir.AluOpType
AXX = mybir.AxisListType.X

NCORES = 8
SEQ_PER_CORE = 2
T = 2048
D = 1024
NCH = 8
NEG = -3.0e38



def bcast_free(ap, n):
    return AP(ap.tensor, ap.offset, [list(a) for a in ap.ap] + [[0, n]])


def seg_view(ap, d):
    return ap.rearrange("p (h d) -> p h d", d=d)


def col_unit(ap):
    return AP(ap.tensor, ap.offset, [list(a) for a in ap.ap] + [[1, 1]])



_ENG_CSEM = {"tensor": "t", "vector": "v", "scalar": "s", "gpsimd": "g", "sync": None}
_ENG_DSEM = {"sync": "sd", "gpsimd": "gd", "scalar": "scd"}


class Prog:
    def __init__(self, nc):
        self.nc = nc
        self.sems = {k: nc.alloc_semaphore("sem_" + k)
                     for k in ("t", "v", "s", "g", "sd", "gd", "scd")}
        self.cnt = {k: 0 for k in self.sems}
        self.steps = []

    def c(self, eng, fn, upto=None):
        self.steps.append((eng, fn, "c", upto))

    def d(self, eng, pairs, upto=None):
        self.steps.append((eng, pairs, "d", upto))

    def w(self, eng):
        self.steps.append((eng, None, "w", None))

    def mark(self):
        return len(self.steps)

    def emit(self):
        nc = self.nc
        sems, cnt = self.sems, self.cnt
        last_seen = {e: {k: 0 for k in self.cnt} for e in _ENG_CSEM}
        per_eng = {e: [] for e in _ENG_CSEM}
        snaps = []
        for eng, payload, kind, upto in self.steps:
            snaps.append(dict(cnt))
            own_c = _ENG_CSEM[eng]
            tgt = cnt if upto is None else snaps[upto]
            waits = []
            for k in cnt:
                c = tgt[k]
                if k == own_c:
                    continue
                if c > last_seen[eng][k]:
                    waits.append((k, c))
                    last_seen[eng][k] = c
            per_eng[eng].append((waits, payload, kind))
            if kind == "c":
                cnt[own_c] += 1
            elif kind == "d":
                dq = _ENG_DSEM[eng]
                prs = payload
                assert not callable(prs)
                cnt[dq] += 16 * len(prs)

        import os
        chunk_default = int(os.environ.get("PROG_BODY_CHUNK", "1000000"))
        chunk_eng = os.environ.get("PROG_BODY_CHUNK_ENGS", "vector:1")
        chunk_map = {}
        for kv in chunk_eng.split(","):
            if ":" in kv:
                k, v = kv.split(":")
                chunk_map[k] = int(v)

        def mk_body(eng, items):
            def body(E):
                own_c = _ENG_CSEM[eng]
                for waits, payload, kind in items:
                    for k, v in waits:
                        E.wait_ge(sems[k], v)
                    if kind == "c":
                        ret = payload(E)
                        assert ret is not None, "compute step must return inst"
                        ret.then_inc(sems[own_c], 1)
                    elif kind == "d":
                        dq = _ENG_DSEM[eng]
                        for out_ap, in_ap in payload:
                            E.dma_start(out=out_ap, in_=in_ap).then_inc(
                                sems[dq], 16)
            return body

        with nc.Block() as block:
            for eng in per_eng:
                items = per_eng[eng]
                chunk = chunk_map.get(eng, chunk_default)
                for i0 in range(0, len(items), chunk):
                    getattr(block, eng)(mk_body(eng, items[i0:i0 + chunk]))
        return nc


class Pipe:

    def __init__(self, prog):
        self.prog = prog
        self.marks = [prog.mark()]

    def up(self, depth=2):
        return self.marks[max(0, len(self.marks) - depth)]

    def bump(self):
        self.marks.append(self.prog.mark())




class Ctx:

    def __init__(self, nc, pool_kb=None):
        self.nc = nc
        self.prog = Prog(nc)
        self.sb = {}
        self.ps = {}
        self.inputs = {}
        self.outputs = {}
        self.pool = None
        if pool_kb is not None:
            self.pool_bytes = pool_kb * 1024
            ph = nc.alloc_sbuf_tensor("pool", [128, self.pool_bytes // 2], BF16)
            self.pool = {BF16: ph.ap() if hasattr(ph, "ap") else ph[:, :],
                         F32: ph.bitcast(F32)[:, :] if hasattr(ph, "bitcast")
                         else None}
            if self.pool[F32] is None:
                self.pool[F32] = ph[:, :].bitcast(F32)
            self.pool_off = 0
            self.pool_hwm = 0

    def pool_reset(self):
        self.pool_hwm = max(self.pool_hwm, self.pool_off)
        self.pool_off = 0

    def din(self, name, shape, dtype=F32):
        t = self.nc.dram_tensor(name, list(shape), dtype, kind="ExternalInput").ap()
        self.inputs[name] = t
        return t

    def dout(self, name, shape, dtype=F32):
        t = self.nc.dram_tensor(name, list(shape), dtype, kind="ExternalOutput").ap()
        self.outputs[name] = t
        return t

    def S(self, name, shape, dtype, pool=True):
        if self.pool is None or not pool:
            h = self.nc.alloc_sbuf_tensor(name, list(shape), dtype)
            self.sb[name] = h
            return h
        isz = 2 if dtype == BF16 else 4
        cols = shape[1]
        off = (self.pool_off + 511) // 512 * 512
        need = cols * isz
        assert off + need <= self.pool_bytes, (name, off, need, self.pool_bytes)
        self.pool_off = off + need
        base = self.pool[dtype]
        sl = base[:, off // isz: off // isz + cols]
        if shape[0] != 128:
            sl = sl[0:shape[0], :]
        self.sb[name] = sl
        return sl

    def P(self, name, shape, dtype=F32):
        h = self.nc.alloc_psum_tensor(name, list(shape), dtype)
        self.ps[name] = h
        return h


def setup_common(ctx):
    c = ctx
    c.din("ident_f", [128, 128], F32)
    c.din("ident_b", [128, 128], BF16)
    c.din("ones_f", [128, 128], F32)
    c.din("ones_b", [128, 128], BF16)
    idf = c.S("identf", [128, 128], F32, pool=False)
    idb = c.S("identb", [128, 128], BF16, pool=False)
    onf = c.S("onesf", [128, 128], F32, pool=False)
    onb = c.S("onesb", [128, 128], BF16, pool=False)
    c.prog.d("sync", [(idf[:, :], c.inputs["ident_f"][:, :]),
                      (idb[:, :], c.inputs["ident_b"][:, :]),
                      (onf[:, :], c.inputs["ones_f"][:, :]),
                      (onb[:, :], c.inputs["ones_b"][:, :])])


def emit_layernorm_fm(ctx, src_tiles, tc_width=512, out_cb=None,
                      gamma=None, beta=None, scratch_pref="ln"):
    c = ctx
    ntc = T // tc_width
    nm = scratch_pref

    def get(suffix, shape, dt):
        key = nm + suffix
        if key not in c.sb:
            c.S(key, shape, dt)
        return c.sb[key]

    xbf = get("_xbf", [128, tc_width], BF16)
    sqb = get("_sqb", [128, tc_width], BF16)
    mb = get("_mb", [128, tc_width], F32)
    vb = get("_vb", [128, tc_width], F32)
    rb = get("_rb", [128, tc_width], F32)
    mrb = get("_mrb", [128, tc_width], F32)
    t2 = get("_t2", [128, tc_width], F32)
    pss = c.ps["bank0"]
    psq = c.ps["bank1"]
    onb = c.sb["onesb"]

    for tc in range(ntc):
        t0 = tc * tc_width
        for ch in range(NCH):
            src = src_tiles(ch)
            sl = src[:, t0:t0 + tc_width]

            def cast_fn(E, sl=sl):
                return E.tensor_copy(out=xbf[:, :], in_=sl)

            def sq_fn(E, sl=sl):
                return E.activation(out=sqb[:, :], in_=sl, func=AFT.Square)

            c.prog.c("vector", cast_fn)
            c.prog.c("scalar", sq_fn)

            def mm_fn(E, ch=ch):
                E.matmul(pss[:, 0:tc_width], onb[:, :], xbf[:, :],
                         start=(ch == 0), stop=(ch == NCH - 1))
                return E.matmul(psq[:, 0:tc_width], onb[:, :], sqb[:, :],
                                start=(ch == 0), stop=(ch == NCH - 1))

            c.prog.c("tensor", mm_fn)

        c.prog.c("vector", lambda E: E.tensor_single_scalar(
            out=mb[:, :], in_=pss[:, 0:tc_width], scalar=1.0 / D, op=ALU.mult))
        c.prog.c("vector", lambda E: E.tensor_scalar(
            out=vb[:, :], in0=psq[:, 0:tc_width], scalar1=1.0 / D, scalar2=1e-5,
            op0=ALU.mult, op1=ALU.add))
        c.prog.c("vector", lambda E: E.tensor_mul(mrb[:, :], mb[:, :], mb[:, :]))
        c.prog.c("vector", lambda E: E.tensor_sub(vb[:, :], vb[:, :], mrb[:, :]))
        c.prog.c("scalar", lambda E: E.activation(
            out=rb[:, :], in_=vb[:, :], func=AFT.Sqrt))
        c.prog.c("vector", lambda E: E.reciprocal(out=rb[:, :], in_=rb[:, :]))
        c.prog.c("vector", lambda E: E.tensor_mul(mrb[:, :], mb[:, :], rb[:, :]))

        for ch in range(NCH):
            src = src_tiles(ch)
            sl = src[:, t0:t0 + tc_width]

            def ap1(E, sl=sl):
                return E.tensor_mul(t2[:, :], sl, rb[:, :])

            c.prog.c("vector", ap1)
            c.prog.c("vector", lambda E: E.tensor_sub(t2[:, :], t2[:, :], mrb[:, :]))
            out_cb(ch, tc, t2)


def stream_weight(ctx, qname, dst_sb, dram_ap):
    ctx.prog.d("gpsimd", [(dst_sb, dram_ap)])




def build_l1_ffn(ctx, xin, xout):
    c = ctx
    E = 8
    wg = c.din("f_Wg", [8, 128, 8], F32)
    bgb = c.din("f_bg_b", [128, 8], F32)
    w1 = c.din("f_W1", [E, D, D], F32)
    w2 = c.din("f_W2", [E, D, D], F32)
    b1t = c.din("f_b1T", [E, 128, 8], F32)
    b2 = c.din("f_b2", [E, D], F32)

    xmod = c.S("f_xmod", [128, NCH * T], F32)
    xnbf = c.S("f_xn", [128, NCH * T], BF16)
    wgsb = c.S("f_wg", [128, NCH * 8], F32)
    bgsb = c.S("f_bgb", [128, 8], F32)
    b1sb = c.S("f_b1", [128, E * 8], F32)
    b2sb = c.S("f_b2sb", [1, 8 * D], BF16)
    lgs = c.S("f_lgs", [128, 8], F32)
    rmax = c.S("f_rmax", [128, 1], F32)
    nmax = c.S("f_nmax", [128, 1], F32)
    ssum = c.S("f_ssum", [128, 1], F32)
    maskt = c.S("f_mask", [128, 8], F32)
    wselall = c.S("f_wselall", [128, 128], F32)
    wsrow = c.S("f_wsrow", [1, T], F32)
    wsrowb = c.S("f_wsrowb", [1, T], BF16)
    wbe = c.S("f_wbe", [128, 512], F32)
    gl = c.S("f_gl", [128, 512], F32)
    hs = c.S("f_hs", [128, 8 * 512], BF16)
    wa = c.S("f_wa", [128, NCH * 1024], BF16)
    wb = c.S("f_wb", [128, NCH * 1024], BF16)

    bank = [c.ps[f"bank{i}"] for i in range(8)]

    c.prog.d("sync", [
        (wgsb[:, :].rearrange("p (c e) -> p c e", e=8), wg.rearrange("c p e -> p c e")),
        (bgsb[:, :], bgb[:, :]),
        (b1sb[:, :].rearrange("p (e k) -> p e k", k=8), b1t.rearrange("e p k -> p e k"))])
    b2flat = AP(b2.tensor, b2.offset, [[8 * D, 1], [1, 8 * D]])
    c.prog.d("gpsimd", [(b2sb[0:1, :], b2flat)])

    for s in range(SEQ_PER_CORE):
        c.prog.d("sync", [(xmod[:, ch * T:(ch + 1) * T], xin[s, ch])
                          for ch in range(NCH)])

        def consume(ch, tc, t2, s=s):
            def gmm(E_, ch=ch, tc=tc):
                last = None
                for sub in range(4):
                    last = E_.matmul(
                        bank[2 + sub][:, 0:8],
                        t2[:, sub * 128:(sub + 1) * 128],
                        wgsb[:, ch * 8:(ch + 1) * 8],
                        start=(ch == 0), stop=(ch == NCH - 1))
                return last

            c.prog.c("tensor", gmm)

            def xcast(E_, ch=ch, tc=tc):
                return E_.activation(
                    out=xnbf[:, ch * T + tc * 512: ch * T + tc * 512 + 512],
                    in_=t2[:, :], func=AFT.Copy)

            c.prog.c("scalar", xcast)
            if ch == NCH - 1:
                emit_gating(c, tc, bank, bgsb, lgs, rmax, nmax, ssum, maskt,
                            wselall)

        emit_layernorm_fm(ctx, lambda ch: xmod[:, ch * T:(ch + 1) * T],
                          out_cb=consume, scratch_pref="f_ln")

        NTC = 4
        TCW = 512
        for e in range(E):
            c.prog.d("gpsimd", [
                (wa[:, :].rearrange("p (c j) -> p c j", j=1024),
                 w1[e].rearrange("(c p) j -> p c j", p=128)),
                (wb[:, :].rearrange("p (c j) -> p c j", j=1024),
                 w2[e].rearrange("(c p) j -> p c j", p=128))])
            for rnd in range(4):
                for tj in range(4):
                    ti = rnd * 4 + tj

                    def trw(E_, e=e, ti=ti, tj=tj):
                        return E_.transpose(bank[4][0:1, tj * 128:(tj + 1) * 128],
                                            wselall[:, ti * 8 + e: ti * 8 + e + 1],
                                            c.sb["identf"][:, :])

                    c.prog.c("tensor", trw)

                def evr(E_, rnd=rnd):
                    return E_.copy(out=wsrow[0:1, rnd * 512:(rnd + 1) * 512],
                                   in_=bank[4][0:1, 0:512])

                c.prog.c("scalar", evr)

                def evrb(E_, rnd=rnd):
                    return E_.tensor_copy(out=wsrowb[0:1, rnd * 512:(rnd + 1) * 512],
                                          in_=bank[4][0:1, 0:512])

                c.prog.c("vector", evrb)
            for tc in range(NTC):
                t0 = tc * TCW
                def wbmm(E_, e=e, t0=t0):
                    return E_.matmul(bank[6][:, 0:TCW], c.sb["onesf"][0:1, :],
                                     wsrow[0:1, t0:t0 + TCW],
                                     start=True, stop=True)

                c.prog.c("tensor", wbmm)
                c.prog.c("vector", lambda E_: E_.tensor_copy(
                    out=wbe[:, :], in_=bank[6][:, 0:TCW]))

                plh = Pipe(c.prog)
                for hm in range(8):
                    pb = bank[hm % 2]

                    def hmm(E_, e=e, hm=hm, t0=t0, pb=pb):
                        last = None
                        for ch in range(NCH):
                            last = E_.matmul(
                                pb[:, 0:TCW],
                                wa[:, ch * 1024 + hm * 128: ch * 1024 + hm * 128 + 128],
                                xnbf[:, ch * T + t0: ch * T + t0 + TCW],
                                start=(ch == 0), stop=(ch == NCH - 1))
                        return last

                    c.prog.c("tensor", hmm, upto=plh.up())

                    def gelu_fn(E_, e=e, hm=hm, pb=pb):
                        return E_.activation(out=gl[:, :], in_=pb[:, 0:TCW],
                                             func=AFT.Gelu,
                                             bias=b1sb[:, e * 8 + hm: e * 8 + hm + 1])

                    c.prog.c("scalar", gelu_fn)

                    def hscale(E_, hm=hm):
                        return E_.tensor_mul(hs[:, hm * TCW:(hm + 1) * TCW],
                                             gl[:, :], wbe[:, :])

                    c.prog.c("vector", hscale)
                    plh.bump()

                plw = Pipe(c.prog)
                for co in range(8):
                    pb = bank[2 + co % 2]

                    def w2mm(E_, co=co, pb=pb, e=e, t0=t0):
                        for hm in range(8):
                            E_.matmul(
                                pb[:, 0:TCW],
                                wb[:, hm * 1024 + co * 128: hm * 1024 + co * 128 + 128],
                                hs[:, hm * TCW:(hm + 1) * TCW],
                                start=(hm == 0), stop=False)
                        return E_.matmul(
                            pb[:, 0:TCW],
                            b2sb[0:1, e * D + co * 128: e * D + co * 128 + 128],
                            wsrowb[0:1, t0:t0 + TCW],
                            start=False, stop=True)

                    c.prog.c("tensor", w2mm,
                             upto=(None if co == 0 else plw.up()))
                    dst = xmod[:, co * T + t0: co * T + t0 + TCW]

                    def evf(E_, dst=dst, pb=pb):
                        return E_.tensor_add(dst, dst, pb[:, 0:TCW])

                    c.prog.c("vector", evf)
                    plw.bump()

        c.prog.d("sync", [(xout[s, ch], xmod[:, ch * T:(ch + 1) * T])
                          for ch in range(NCH)])
    c.prog.w("sync")


def emit_gating(c, tc, bank, bgsb, lgs, rmax, nmax, ssum, maskt, wselall):
    for sub in range(4):
        pg = bank[2 + sub][:, 0:8]
        c.prog.c("vector", lambda E_, pg=pg: E_.tensor_add(lgs[:, :], pg, bgsb[:, :]))
        c.prog.c("vector", lambda E_: E_.reduce_max(rmax[:, :], lgs[:, :], axis=AXX))
        c.prog.c("vector", lambda E_: E_.tensor_single_scalar(
            out=nmax[:, :], in_=rmax[:, :], scalar=-1.0, op=ALU.mult))
        c.prog.c("scalar", lambda E_: E_.activation(
            out=lgs[:, :], in_=lgs[:, :], func=AFT.Exp, bias=nmax[:, :]))
        c.prog.c("vector", lambda E_: E_.reduce_sum(ssum[:, :], lgs[:, :], axis=AXX))
        c.prog.c("vector", lambda E_: E_.reciprocal(out=ssum[:, :], in_=ssum[:, :]))
        c.prog.c("vector", lambda E_: E_.tensor_single_scalar(
            out=lgs[:, :], in_=lgs[:, :], scalar=ssum[:, :], op=ALU.mult))
        c.prog.c("vector", lambda E_: E_.reduce_max(rmax[:, :], lgs[:, :], axis=AXX))
        c.prog.c("vector", lambda E_: E_.tensor_tensor(
            out=maskt[:, :], in0=lgs[:, :], in1=bcast_free(rmax[:, 0:1], 8),
            op=ALU.is_equal))
        ti = tc * 4 + sub

        def wselw(E_, ti=ti):
            return E_.tensor_mul(wselall[:, ti * 8:(ti + 1) * 8],
                                 maskt[:, :], lgs[:, :])

        c.prog.c("vector", wselw)




def emit_layernorm_hbm(ctx, src_dram, out_cb, pref, dbuf=True):
    c = ctx
    bank = [c.ps[f"bank{i}"] for i in range(8)]

    def get(suffix, shape, dt):
        key = pref + suffix
        if key not in c.sb:
            c.S(key, shape, dt)
        return c.sb[key]

    nx = 2 if dbuf else 1
    xch = get("_xch2" if dbuf else "_xch", [128, nx * T], F32)
    xbf = get("_xbf2" if dbuf else "_xbf", [128, nx * 512], BF16)
    sqb = get("_sqb2" if dbuf else "_sqb", [128, nx * 512], BF16)
    mb = get("_mb", [128, 512], F32)
    vb = get("_vb", [128, 512], F32)
    sc2 = get("_sc2", [128, 512], F32)
    t2 = get("_t2", [128, 512], F32)
    onb = c.sb["onesb"]

    def xsl(ch):
        return xch[:, (ch % nx) * T:(ch % nx) * T + T]

    ch_marks = {}
    it_marks = {}
    for ch in range(NCH):
        c.prog.d("sync", [(xsl(ch), src_dram(ch))], upto=ch_marks.get(ch - nx))
        dma_mark = c.prog.mark()
        for tc in range(4):
            i = ch * 4 + tc
            par = (i % 2) if dbuf else 0
            sl = xsl(ch)[:, tc * 512:(tc + 1) * 512]
            xb = xbf[:, par * 512: par * 512 + 512]
            sq = sqb[:, par * 512: par * 512 + 512]
            upc = max(it_marks.get(i - (2 if dbuf else 1), 0), dma_mark)
            c.prog.c("vector", lambda E, sl=sl, xb=xb: E.tensor_copy(
                out=xb, in_=sl), upto=upc)
            c.prog.c("scalar", lambda E, sl=sl, sq=sq: E.activation(
                out=sq, in_=sl, func=AFT.Square), upto=upc)
            m1 = c.prog.mark()

            def mm(E, ch=ch, tc=tc, xb=xb, sq=sq):
                E.matmul(bank[tc][:, 0:512], onb[:, :], xb,
                         start=(ch == 0), stop=(ch == NCH - 1))
                return E.matmul(bank[4 + tc][:, 0:512], onb[:, :], sq,
                                start=(ch == 0), stop=(ch == NCH - 1))

            c.prog.c("tensor", mm, upto=m1)
            it_marks[i] = c.prog.mark()
        ch_marks[ch] = c.prog.mark()
    for tc in range(4):
        def stat1(E, tc=tc):
            E.tensor_single_scalar(out=mb[:, :], in_=bank[tc][:, 0:512],
                                   scalar=1.0 / D, op=ALU.mult)
            E.tensor_scalar(out=vb[:, :], in0=bank[4 + tc][:, 0:512],
                            scalar1=1.0 / D, scalar2=1e-5, op0=ALU.mult, op1=ALU.add)
            E.tensor_mul(sc2[:, :], mb[:, :], mb[:, :])
            return E.tensor_sub(vb[:, :], vb[:, :], sc2[:, :])

        c.prog.c("vector", stat1)
        c.prog.c("scalar", lambda E: E.activation(out=vb[:, :], in_=vb[:, :],
                                                  func=AFT.Ln))
        c.prog.c("scalar", lambda E, tc=tc: E.activation(
            out=bank[tc][:, 0:512], in_=vb[:, :], func=AFT.Exp, scale=-0.5))
        c.prog.c("vector", lambda E, tc=tc: E.tensor_mul(
            bank[4 + tc][:, 0:512], mb[:, :], bank[tc][:, 0:512]))
    ch_marks = {}
    for ch in range(NCH):
        c.prog.d("sync", [(xsl(ch), src_dram(ch))], upto=ch_marks.get(ch - nx))
        dma_mark = c.prog.mark()
        for tc in range(4):
            sl = xsl(ch)[:, tc * 512:(tc + 1) * 512]

            def ap1(E, sl=sl, tc=tc):
                E.tensor_mul(t2[:, :], sl, bank[tc][:, 0:512])
                return E.tensor_sub(t2[:, :], t2[:, :], bank[4 + tc][:, 0:512])

            c.prog.c("vector", ap1, upto=(dma_mark if tc == 0 else None))
            out_cb(ch, tc, t2)
        ch_marks[ch] = c.prog.mark()




def emit_qk_proc(c, tgt, ssb, ai, l, ti, rn, S1, S2, QP, xpos_sb, QT):
    c.prog.c("scalar", lambda E: E.activation(out=S1, in_=tgt, func=AFT.Square))
    c.prog.c("vector", lambda E: E.reduce_sum(
        out=col_unit(rn[:, :]), in_=seg_view(S1, 64), axis=AXX))
    c.prog.c("scalar", lambda E: E.activation(
        out=rn[:, :], in_=rn[:, :], func=AFT.Sqrt))

    def rcp(E):
        E.tensor_scalar_max(out=rn[:, :], in0=rn[:, :], scalar1=1e-12)
        return E.reciprocal(out=rn[:, :], in_=rn[:, :])

    c.prog.c("vector", rcp)

    def scl(E):
        rv = AP(rn[:, :].tensor, rn[:, :].offset,
                [list(a) for a in rn[:, :].ap] + [[0, 64]])
        E.tensor_tensor(out=seg_view(tgt, 64), in0=seg_view(tgt, 64),
                        in1=rv, op=ALU.mult)
        sap = ssb[:, l * 64:(l + 1) * 64]
        sv = AP(sap.tensor, sap.offset,
                [list(sap.ap[0])] + [[0, 8], [1, 64]])
        return E.tensor_tensor(out=seg_view(tgt, 64),
                               in0=seg_view(tgt, 64), in1=sv, op=ALU.mult)

    c.prog.c("vector", scl)

    def xposf(E):
        tabA = xpos_sb[:, ai * 64: ai * 64 + 64]
        tabB = xpos_sb[:, (ai + 1) * 64: (ai + 1) * 64 + 64]
        av = AP(tabA.tensor, tabA.offset,
                [list(tabA.ap[0])] + [[0, 8], [1, 64]])
        sw = AP(tgt.tensor, tgt.offset + 32,
                [list(tgt.ap[0])] + [[64, 8], [-32, 2], [1, 32]])
        s1v = AP(S1.tensor, S1.offset,
                 [list(S1.ap[0])] + [[64, 8], [32, 2], [1, 32]])
        bv = AP(tabB.tensor, tabB.offset,
                [list(tabB.ap[0])] + [[0, 8], [32, 2], [1, 32]])
        E.tensor_tensor(out=s1v, in0=sw, in1=bv, op=ALU.mult)
        E.tensor_tensor(out=seg_view(S2, 64), in0=seg_view(tgt, 64),
                        in1=av, op=ALU.mult)
        return E.tensor_add(QP, S2, S1)

    c.prog.c("vector", xposf)


def build_l0_ffn(ctx, xin, xout, dbg=None):
    c = ctx
    NL = 2
    INNER = 512
    wqkv = c.din("g_wqkv", [NL, D, 3 * INNER], F32)
    wo = c.din("g_wo", [NL, INNER, D], F32)
    w1p = c.din("g_w1p", [NL, D, 5888], F32)
    w2p = c.din("g_w2p", [NL, 2816, D], F32)
    qsb_in = c.din("g_qs", [NL, 128, 64], F32)
    ksb_in = c.din("g_ks", [NL, 128, 64], F32)
    lgb_in = c.din("g_lgb", [NL, 2, 128, NCH], F32)
    b1p_in = c.din("g_b1p", [NL, 128, 46], F32)
    b2_in = c.din("g_b2", [NL, 128, NCH], F32)
    xpos = c.din("g_xpos", [4, 16, 128, 64], BF16)
    maskS = c.din("g_maskS", [4, 128, 1024], BF16)
    maskT = c.din("g_maskT", [8, 128, 512], BF16)

    TG = T + 1
    xl = c.S("g_xl", [128, NCH * TG], F32)
    xsh = c.S("g_xsh", [128, NCH * T], BF16)
    vaug = c.S("g_vaug", [128, 16 * 520], BF16)
    qf = c.S("g_qf", [128, 4 * T], BF16)
    kf = c.S("g_kf", [128, 4 * T], BF16)
    ws = c.S("g_ws", [128, 16 * 512], BF16)
    fscr = c.S("g_fscr", [128, 3072], F32)
    bscr = c.S("g_bscr", [128, 1024], BF16)
    mskS_sb = c.S("g_mskS", [128, 4 * 1024], BF16)
    mskT_sb = c.S("g_mskT", [128, 8 * 512], BF16)
    xpos_sb = c.S("g_xpossb", [128, 4 * 64], BF16)
    qs_sb = c.S("g_qssb", [128, NL * 64], F32)
    ks_sb = c.S("g_kssb", [128, NL * 64], F32)
    lgb_sb = c.S("g_lgbsb", [128, NL * 2 * NCH], F32)
    b1p_sb = c.S("g_b1psb", [128, NL * 46], F32)
    b2_sb = c.S("g_b2sb", [128, NL * NCH], F32)
    rn = c.S("g_rn", [128, 8], F32)
    negm = c.S("g_negm", [128, 4], F32)
    mrow = fscr[0:1, 2560:3072]
    rrow = fscr[0:1, 2048:2560]
    mbt = fscr[:, 1024:1536]
    sst = fscr[:, 0:1024]
    of32 = fscr[0:64, 1536:2048]
    pt = bscr[:, 0:512]
    pt2f = bscr[:, 0:1024]
    c.sb["g_ln_mb"] = fscr[:, 0:512]
    c.sb["g_ln_vb"] = fscr[:, 512:1024]
    c.sb["g_ln_sc2"] = fscr[:, 1024:1536]
    c.sb["g_ln_t2"] = fscr[:, 1536:2048]
    c.sb["g_ln_xbf"] = bscr[:, 0:512]
    c.sb["g_ln_sqb"] = bscr[:, 512:1024]
    QT = fscr[:, 0:512]
    KT = fscr[:, 512:1024]
    S1 = fscr[:, 1024:1536]
    S2 = fscr[:, 1536:2048]
    S3 = fscr[:, 2048:2560]
    QP = fscr[:, 2048:2560]
    T3 = fscr[:, 2560:3072]
    HP = None

    bank = [c.ps[f"bank{i}"] for i in range(8)]

    c.prog.d("sync", [
        (mskS_sb[:, :].rearrange("p (q j) -> p q j", j=1024),
         maskS.rearrange("q p j -> p q j")),
        (mskT_sb[:, :].rearrange("p (k j) -> p k j", j=512),
         maskT.rearrange("k p j -> p k j")),
        (qs_sb[:, :].rearrange("p (l j) -> p l j", j=64), qsb_in.rearrange("l p j -> p l j")),
        (ks_sb[:, :].rearrange("p (l j) -> p l j", j=64), ksb_in.rearrange("l p j -> p l j")),
        (lgb_sb[:, :].rearrange("p (l g k) -> p l g k", g=2, k=NCH),
         lgb_in.rearrange("l g p k -> p l g k")),
        (b1p_sb[:, :].rearrange("p (l k) -> p l k", k=46), b1p_in.rearrange("l p k -> p l k")),
        (b2_sb[:, :].rearrange("p (l k) -> p l k", k=NCH), b2_in.rearrange("l p k -> p l k")),
    ])

    def xg(ch):
        return xl[:, ch * TG + 1: ch * TG + 1 + T]

    def xsh_slice(ch, t0, w):
        off = ch * TG + (1 if ch < 4 else 0)
        return xl[:, off + t0: off + t0 + w]

    def wslab(dram_col_slice, slot, nch_rows=NCH):
        dst = ws[:, slot * 512 * nch_rows:(slot + 1) * 512 * nch_rows]
        c.prog.d("gpsimd", [(dst.rearrange("p (c j) -> p c j", j=512),
                             dram_col_slice.rearrange("(c p) j -> p c j", p=128))])

    for s in range(SEQ_PER_CORE):
        def gz(E):
            a0 = xl[0:128, 0:1]
            apv = AP(a0.tensor, a0.offset, [list(a0.ap[0])] + [[TG, NCH], [1, 1]])
            return E.memset(apv, 0.0)

        c.prog.c("vector", gz)

        def lncb(ch, tc, t2):
            c.prog.c("vector", lambda E, ch=ch, tc=tc: E.tensor_copy(
                out=xg(ch)[:, tc * 512:(tc + 1) * 512], in_=t2[:, :]))

        emit_layernorm_hbm(c, lambda ch, s=s: xin[s, ch], lncb, "g_ln", dbuf=False)
        if dbg == ("xl",):
            for co in range(NCH):
                c.prog.d("sync", [(xout[s, co], xg(co))])
            continue

        for l in range(NL):
            for ch in range(NCH):
                c.prog.c("vector", lambda E, ch=ch: E.tensor_copy(
                    out=xsh[:, ch * T:(ch + 1) * T], in_=xsh_slice(ch, 0, T)))
            if dbg is not None and dbg[0] == "xsh" and dbg[1] == l:
                for co in range(NCH):
                    c.prog.c("vector", lambda E, co=co: E.tensor_copy(
                        out=fscr[:, 0:2048], in_=xsh[:, co * T:(co + 1) * T]))
                    c.prog.d("sync", [(xout[s, co], fscr[:, 0:2048])])
                break
            wslab(wqkv[l][:, 1024:1536], 0)
            plv = Pipe(c.prog)
            for ti in range(16):
                def vmm(E, ti=ti):
                    last = None
                    for ci in range(NCH):
                        last = E.matmul(
                            bank[2 + ti % 2][:, 0:512],
                            xsh[:, ci * T + ti * 128: ci * T + ti * 128 + 128],
                            ws[:, ci * 512: ci * 512 + 512],
                            start=(ci == 0), stop=(ci == NCH - 1))
                    return last

                c.prog.c("tensor", vmm, upto=plv.up())

                def vev(E, ti=ti):
                    dst = vaug[:, ti * 520: (ti + 1) * 520]
                    dstv = AP(dst.tensor, dst.offset,
                              [list(dst.ap[0])] + [[65, 8], [1, 64]])
                    return E.tensor_copy(out=dstv,
                                         in_=seg_view(bank[2 + ti % 2][:, 0:512], 64))

                c.prog.c("vector", vev)
                plv.bump()

            for sl in range(2):
                wslab(wqkv[l][:, sl * 512:(sl + 1) * 512], sl)
            if dbg is not None and dbg[0] == "ws" and dbg[1] == l:
                for co in range(8):
                    c.prog.c("vector", lambda E, co=co: E.tensor_copy(
                        out=fscr[:, 0:2048],
                        in_=ws[:, co * 1024:(co + 1) * 1024]))
                    c.prog.d("sync", [(xout[s, co, :, 0:2048], fscr[:, 0:2048])])
                break
            for ti in range(16):
                def qkmm(E, ti=ti):
                    last = None
                    for sl in range(2):
                        for ci in range(NCH):
                            last = E.matmul(
                                bank[sl][:, 0:512],
                                xsh[:, ci * T + ti * 128: ci * T + ti * 128 + 128],
                                ws[:, sl * NCH * 512 + ci * 512: sl * NCH * 512 + ci * 512 + 512],
                                start=(ci == 0), stop=(ci == NCH - 1))
                    return last

                c.prog.c("tensor", qkmm)
                c.prog.d("sync", [(xpos_sb[:, :].rearrange("p (a j) -> p a j", j=64),
                                   xpos[:, ti].rearrange("a p j -> p a j"))])

                for which, bsrc, dst_fm, ssb, ai in ((0, 0, qf, qs_sb, 0),
                                                     (1, 1, kf, ks_sb, 2)):
                    tgt = QT if which == 0 else KT
                    c.prog.c("vector", lambda E, bsrc=bsrc, tgt=tgt: E.tensor_copy(
                        out=tgt, in_=bank[bsrc][:, 0:512]))
                    if dbg is not None and dbg[0] == "qraw":
                        def rawc(E, tgt=tgt):
                            return E.tensor_copy(out=QP, in_=tgt)

                        c.prog.c("vector", rawc)
                        skip_proc = True
                    else:
                        skip_proc = False
                    if not skip_proc:
                        emit_qk_proc(c, tgt, ssb, ai, l, ti, rn, S1, S2, QP,
                                     xpos_sb, QT)

                    def qtr(E):
                        last = None
                        for cq in range(4):
                            last = E.transpose(bank[6][:, cq * 128:(cq + 1) * 128],
                                               QP[:, cq * 128:(cq + 1) * 128],
                                               c.sb["identf"][:, :])
                        return last

                    c.prog.c("tensor", qtr)

                    def qtev(E, ti=ti, dst_fm=dst_fm):
                        last = None
                        for cq in range(4):
                            last = E.tensor_copy(
                                out=dst_fm[:, cq * T + ti * 128: cq * T + ti * 128 + 128],
                                in_=bank[6][:, cq * 128:(cq + 1) * 128])
                        return last

                    c.prog.c("vector", qtev)

            def vones(E):
                a0 = vaug[:, 64:65]
                apv = AP(a0.tensor, a0.offset,
                         [list(a0.ap[0])] + [[520, 16], [65, 8], [1, 1]])
                return E.memset(apv, 1.0)

            c.prog.c("vector", vones)

            OF = xsh
            for w in range(4):
                kt_lo = 4 if w == 0 else 0
                key0 = 512 * (w - 1)
                for h in range(8):
                    hch = h // 2
                    hrow = (h % 2) * 64
                    smm_marks = {}
                    tail_marks = {}
                    for qt in range(4):
                        b0 = (qt % 2) * 2

                        def smm(E, w=w, h=h, qt=qt, hch=hch, hrow=hrow, b0=b0):
                            last = None
                            for half in range(2):
                                if w == 0 and half == 0:
                                    continue
                                t0 = 512 * (w - 1) + half * 512
                                last = E.matmul(
                                    bank[b0 + half][:, 0:512],
                                    qf[hrow:hrow + 64,
                                       hch * T + w * 512 + qt * 128:
                                       hch * T + w * 512 + qt * 128 + 128],
                                    kf[hrow:hrow + 64, hch * T + t0: hch * T + t0 + 512],
                                    start=True, stop=True)
                            return last

                        c.prog.c("tensor", smm, upto=tail_marks.get(qt - 2))
                        smm_marks[qt] = c.prog.mark()

                        def sred(E, w=w, qt=qt, b0=b0):
                            for half in range(2):
                                if w == 0 and half == 0:
                                    continue
                                E.tensor_add(
                                    sst[:, half * 512:(half + 1) * 512],
                                    bank[b0 + half][:, 0:512],
                                    mskS_sb[:, qt * 1024 + half * 512:
                                            qt * 1024 + half * 512 + 512])
                            if w == 0:
                                E.reduce_max(rn[:, 0:1], sst[:, 512:1024], axis=AXX)
                            else:
                                E.reduce_max(rn[:, 0:1], sst[:, 0:1024], axis=AXX)
                            return E.tensor_single_scalar(
                                out=negm[:, qt:qt + 1], in_=rn[:, 0:1],
                                scalar=-1.0, op=ALU.mult)

                        c.prog.c("vector", sred, upto=smm_marks[qt])
                        tail_marks[qt] = c.prog.mark()

                    def mtr(E):
                        last = None
                        for j in range(4):
                            last = E.transpose(bank[6][0:1, j * 128:(j + 1) * 128],
                                               negm[:, j:j + 1], c.sb["identf"][:, :])
                        return last

                    c.prog.c("tensor", mtr)
                    c.prog.c("scalar", lambda E: E.copy(out=mrow,
                                                        in_=bank[6][0:1, 0:512]))
                    c.prog.c("tensor", lambda E: E.matmul(
                        bank[6][:, 0:512], c.sb["onesf"][0:1, :], mrow,
                        start=True, stop=True))
                    c.prog.c("vector", lambda E: E.tensor_copy(
                        out=mbt, in_=bank[6][:, 0:512]))
                    mbt_mark = c.prog.mark()
                    stm = {}
                    tdone = {}

                    def emit_tail(kt, w=w, h=h, kt_lo=kt_lo):
                        bb = kt % 2
                        sb4 = bank[4 + 2 * bb]
                        ktt = 4 * (w - 1) + kt

                        def stsub(E, kt=kt, bb=bb, sb4=sb4):
                            dst = sst[:, bb * 512: bb * 512 + 512]
                            E.tensor_add(dst, sb4[:, 0:512], mbt)
                            return E.tensor_add(
                                dst, dst, mskT_sb[:, kt * 512:(kt + 1) * 512])

                        c.prog.c("vector", stsub, upto=stm[kt])
                        m1 = c.prog.mark()
                        c.prog.c("scalar", lambda E, bb=bb: E.activation(
                            out=pt2f[:, bb * 512: bb * 512 + 512],
                            in_=sst[:, bb * 512: bb * 512 + 512], func=AFT.Exp),
                            upto=m1)
                        m2 = c.prog.mark()

                        def pvmm(E, kt=kt, ktt=ktt, h=h, kt_lo=kt_lo, bb=bb):
                            return E.matmul(
                                bank[5][0:65, 0:512],
                                vaug[:, ktt * 520 + h * 65: ktt * 520 + h * 65 + 65],
                                pt2f[:, bb * 512: bb * 512 + 512],
                                start=(kt == kt_lo), stop=(kt == 7))

                        c.prog.c("tensor", pvmm, upto=m2)
                        tdone[kt] = c.prog.mark()

                    for kt in range(kt_lo, 8):
                        ktt = 4 * (w - 1) + kt

                        def stmm(E, w=w, kt=kt, hch=hch, hrow=hrow, ktt=ktt):
                            return E.matmul(
                                bank[4 + 2 * (kt % 2)][:, 0:512],
                                kf[hrow:hrow + 64, hch * T + ktt * 128: hch * T + ktt * 128 + 128],
                                qf[hrow:hrow + 64, hch * T + w * 512: hch * T + w * 512 + 512],
                                start=True, stop=True)

                        c.prog.c("tensor", stmm, upto=tdone.get(kt - 2, mbt_mark))
                        stm[kt] = c.prog.mark()
                        if kt > kt_lo:
                            emit_tail(kt - 1)
                    emit_tail(7)
                    c.prog.c("vector", lambda E: E.reciprocal(
                        out=rrow, in_=bank[5][64:65, 0:512]))
                    c.prog.c("scalar", lambda E: E.copy(out=of32,
                                                        in_=bank[5][0:64, 0:512]))
                    c.prog.c("tensor", lambda E: E.matmul(
                        bank[6][0:64, 0:512], c.sb["onesf"][0:1, 0:64], rrow,
                        start=True, stop=True))

                    def oev(E, w=w, hch=hch, hrow=hrow):
                        return E.tensor_mul(
                            OF[hrow:hrow + 64, hch * T + w * 512: hch * T + w * 512 + 512],
                            of32, bank[6][0:64, 0:512])

                    c.prog.c("vector", oev)

            if dbg is not None and dbg[0] == "of" and dbg[1] == l:
                for ci in range(4):
                    c.prog.c("vector", lambda E, ci=ci: E.tensor_copy(
                        out=fscr[:, 0:2048], in_=OF[:, ci * T:(ci + 1) * T]))
                    c.prog.d("sync", [(xout[s, ci], fscr[:, 0:2048])])
                break
            if dbg is not None and dbg[0] in ("qf", "qraw") and dbg[1] == l:
                for ci in range(4):
                    c.prog.c("vector", lambda E, ci=ci: E.tensor_copy(
                        out=fscr[:, 0:2048], in_=qf[:, ci * T:(ci + 1) * T]))
                    c.prog.d("sync", [(xout[s, ci], fscr[:, 0:2048])])
                    c.prog.c("vector", lambda E, ci=ci: E.tensor_copy(
                        out=fscr[:, 0:2048], in_=kf[:, ci * T:(ci + 1) * T]))
                    c.prog.d("sync", [(xout[s, 4 + ci], fscr[:, 0:2048])])
                break
            for sl in range(2):
                wslab(wo[l][:, sl * 512:(sl + 1) * 512], sl, nch_rows=4)
                plo = Pipe(c.prog)
                for col in range(4):
                    co = sl * 4 + col
                    for tc in range(4):
                        def omm(E, sl=sl, col=col, tc=tc):
                            last = None
                            for ci in range(4):
                                last = E.matmul(
                                    bank[tc % 2][:, 0:512],
                                    ws[:, sl * 4 * 512 + ci * 512 + col * 128:
                                       sl * 4 * 512 + ci * 512 + col * 128 + 128],
                                    OF[:, ci * T + tc * 512: ci * T + tc * 512 + 512],
                                    start=(ci == 0), stop=(ci == 3))
                            return last

                        c.prog.c("tensor", omm, upto=plo.up())

                        def oadd(E, co=co, tc=tc):
                            dst = xg(co)[:, tc * 512:(tc + 1) * 512]
                            return E.tensor_add(dst, dst, bank[tc % 2][:, 0:512])

                        c.prog.c("vector", oadd)
                        plo.bump()

            if dbg is not None and dbg[0] == "xl1" and dbg[1] == l:
                for co in range(NCH):
                    c.prog.d("sync", [(xout[s, co], xg(co))])
                break
            xn2 = xsh
            for tc in range(4):
                for ch in range(NCH):
                    sl_ = xsh_slice(ch, tc * 512, 512)
                    c.prog.c("vector", lambda E, sl_=sl_: E.tensor_copy(
                        out=c.sb["g_ln_xbf"], in_=sl_))
                    c.prog.c("scalar", lambda E, sl_=sl_: E.activation(
                        out=c.sb["g_ln_sqb"], in_=sl_, func=AFT.Square))

                    def mm(E, ch=ch, tc=tc):
                        E.matmul(bank[2][:, 0:512], c.sb["onesb"][:, :],
                                 c.sb["g_ln_xbf"], start=(ch == 0), stop=(ch == 7))
                        return E.matmul(bank[3][:, 0:512], c.sb["onesb"][:, :],
                                        c.sb["g_ln_sqb"], start=(ch == 0),
                                        stop=(ch == 7))

                    c.prog.c("tensor", mm)

                def stat1(E):
                    E.tensor_single_scalar(out=c.sb["g_ln_mb"], in_=bank[2][:, 0:512],
                                           scalar=1.0 / D, op=ALU.mult)
                    E.tensor_scalar(out=c.sb["g_ln_vb"], in0=bank[3][:, 0:512],
                                    scalar1=1.0 / D, scalar2=1e-5, op0=ALU.mult,
                                    op1=ALU.add)
                    E.tensor_mul(c.sb["g_ln_sc2"], c.sb["g_ln_mb"], c.sb["g_ln_mb"])
                    return E.tensor_sub(c.sb["g_ln_vb"], c.sb["g_ln_vb"],
                                        c.sb["g_ln_sc2"])

                c.prog.c("vector", stat1)
                c.prog.c("scalar", lambda E: E.activation(
                    out=c.sb["g_ln_vb"], in_=c.sb["g_ln_vb"], func=AFT.Sqrt))

                def stat2(E):
                    E.reciprocal(out=c.sb["g_ln_vb"], in_=c.sb["g_ln_vb"])
                    return E.tensor_mul(c.sb["g_ln_sc2"], c.sb["g_ln_mb"],
                                        c.sb["g_ln_vb"])

                c.prog.c("vector", stat2)
                for ch in range(NCH):
                    sl_ = xsh_slice(ch, tc * 512, 512)

                    def ap1(E, sl_=sl_, ch=ch, tc=tc, l=l):
                        E.tensor_mul(c.sb["g_ln_t2"], sl_, c.sb["g_ln_vb"])
                        E.tensor_sub(c.sb["g_ln_t2"], c.sb["g_ln_t2"],
                                     c.sb["g_ln_sc2"])
                        return E.tensor_scalar(
                            out=xn2[:, ch * T + tc * 512: ch * T + tc * 512 + 512],
                            in0=c.sb["g_ln_t2"],
                            scalar1=lgb_sb[:, (l * 2) * NCH + ch: (l * 2) * NCH + ch + 1],
                            scalar2=lgb_sb[:, (l * 2 + 1) * NCH + ch: (l * 2 + 1) * NCH + ch + 1],
                            op0=ALU.mult, op1=ALU.add)

                    c.prog.c("vector", ap1)

            if dbg is not None and dbg[0] == "xn2" and dbg[1] == l:
                for co in range(NCH):
                    c.prog.c("vector", lambda E, co=co: E.tensor_copy(
                        out=fscr[:, 0:2048], in_=xn2[:, co * T:(co + 1) * T]))
                    c.prog.d("sync", [(xout[s, co], fscr[:, 0:2048])])
                break
            HPn = 22
            for tc in range(4):
                plm = Pipe(c.prog)
                for cp in range(HPn):
                    if cp % 4 == 0:
                        wslab(w1p[l][:, (cp // 4) * 512:(cp // 4) * 512 + 512], 0)
                        wslab(w1p[l][:, 2816 + (cp // 4) * 512: 2816 + (cp // 4) * 512 + 512], 1)
                        slab_mark = c.prog.mark()

                    bb = (cp % 2) * 2

                    def hmm(E, cp=cp, tc=tc, bb=bb):
                        last = None
                        for ci in range(NCH):
                            for ab in range(2):
                                last = E.matmul(
                                    bank[bb + ab][:, 0:512],
                                    ws[:, ab * NCH * 512 + ci * 512 + (cp % 4) * 128:
                                       ab * NCH * 512 + ci * 512 + (cp % 4) * 128 + 128],
                                    xn2[:, ci * T + tc * 512: ci * T + tc * 512 + 512],
                                    start=(ci == 0), stop=(ci == NCH - 1))
                        return last

                    c.prog.c("tensor", hmm,
                             upto=(slab_mark if cp % 4 == 0 else plm.up()))

                    def aev(E, cp=cp, l=l, bb=bb):
                        return E.tensor_scalar_add(
                            out=S1, in0=bank[bb][:, 0:512],
                            scalar1=b1p_sb[:, l * 46 + cp: l * 46 + cp + 1])

                    c.prog.c("vector", aev)

                    def gev(E, cp=cp, l=l, bb=bb):
                        return E.activation(
                            out=S2, in_=bank[bb + 1][:, 0:512], func=AFT.Gelu,
                            bias=b1p_sb[:, l * 46 + 22 + cp: l * 46 + 22 + cp + 1])

                    c.prog.c("scalar", gev)
                    hp_dst = (qf if cp < 11 else kf)
                    hpo = (cp % 11)

                    def pev(E, hp_dst=hp_dst, hpo=hpo):
                        return E.tensor_mul(hp_dst[:, hpo * 512:(hpo + 1) * 512],
                                            S1, S2)

                    c.prog.c("vector", pev)
                    plm.bump()
                for osl in range(2):
                    for rhalf in range(2):
                        dst = ws[:, 0:11 * 512]
                        c.prog.d("gpsimd", [
                            (dst.rearrange("p (c j) -> p c j", j=512),
                             w2p[l][rhalf * 1408:(rhalf + 1) * 1408,
                                    osl * 512:(osl + 1) * 512].rearrange(
                                 "(c p) j -> p c j", p=128))])
                        for col in range(4):
                            def wmm(E, rhalf=rhalf, col=col, tc=tc):
                                last = None
                                for hc in range(11):
                                    cpg = rhalf * 11 + hc
                                    hp_src = (qf if cpg < 11 else kf)
                                    hpo = cpg % 11
                                    last = E.matmul(
                                        bank[4 + col][:, 0:512],
                                        ws[:, hc * 512 + col * 128: hc * 512 + col * 128 + 128],
                                        hp_src[:, hpo * 512:(hpo + 1) * 512],
                                        start=(rhalf == 0 and hc == 0),
                                        stop=(rhalf == 1 and hc == 10))
                                return last

                            c.prog.c("tensor", wmm)
                    for col in range(4):
                        co = osl * 4 + col

                        def wev(E, col=col, co=co, tc=tc, l=l):
                            E.tensor_scalar_add(
                                out=T3, in0=bank[4 + col][:, 0:512],
                                scalar1=b2_sb[:, l * NCH + co: l * NCH + co + 1])
                            dst = xg(co)[:, tc * 512:(tc + 1) * 512]
                            return E.tensor_add(dst, dst, T3)

                        c.prog.c("vector", wev)

        if dbg is not None and dbg != ("fin",):
            continue
        for co in range(NCH):
            c.prog.d("sync", [(fscr[:, 0:2048], xin[s, co])])

            def fin(E, co=co):
                return E.tensor_add(fscr[:, 0:2048], fscr[:, 0:2048], xg(co))

            c.prog.c("vector", fin)
            c.prog.d("sync", [(xout[s, co], fscr[:, 0:2048])])
    c.prog.w("sync")




def emit_conv3(E, dst, src, t0, W, wtap, bias_col, total=T):
    w0c, w1c, w2c = wtap
    last = E.tensor_scalar(out=dst[:, 0:W], in0=src[:, t0:t0 + W], scalar1=w1c,
                           scalar2=bias_col, op0=ALU.mult, op1=ALU.add)
    if t0 == 0:
        last = E.scalar_tensor_tensor(out=dst[:, 1:W], in0=src[:, 0:W - 1],
                                      scalar=w0c, in1=dst[:, 1:W],
                                      op0=ALU.mult, op1=ALU.add)
    else:
        last = E.scalar_tensor_tensor(out=dst[:, 0:W], in0=src[:, t0 - 1:t0 + W - 1],
                                      scalar=w0c, in1=dst[:, 0:W],
                                      op0=ALU.mult, op1=ALU.add)
    if t0 + W == total:
        last = E.scalar_tensor_tensor(out=dst[:, 0:W - 1], in0=src[:, t0 + 1:t0 + W],
                                      scalar=w2c, in1=dst[:, 0:W - 1],
                                      op0=ALU.mult, op1=ALU.add)
    else:
        last = E.scalar_tensor_tensor(out=dst[:, 0:W], in0=src[:, t0 + 1:t0 + W + 1],
                                      scalar=w2c, in1=dst[:, 0:W],
                                      op0=ALU.mult, op1=ALU.add)
    return last


def build_l0_attn(ctx, xin, xout):
    c = ctx
    inw = c.din("m_inw", [D, D], F32)
    actw = c.din("m_actp", [D, D], F32)
    qkw = c.din("m_qkw", [D, 2 * D], F32)
    outw = c.din("m_outw", [D, D], F32)
    fc1 = c.din("m_fc1", [D, 4 * D], F32)
    fc2 = c.din("m_fc2", [4 * D, D], F32)
    convw = c.din("m_convw", [4, 3, 128, NCH], F32)
    convb = c.din("m_convb", [4, 128, NCH], F32)
    norms = c.din("m_norms", [4, 128, NCH], F32)
    pbias = c.din("m_pbias", [128, 64], F32)
    qkbrow = c.din("m_qkb_row", [1, 2 * D], F32)
    ctab = c.din("m_costab", [2, 128, 512], BF16)

    xn = c.S("m_xn", [128, NCH * T], BF16)
    xibf = c.S("m_xibf", [128, NCH * T], BF16)
    attn = c.S("m_attn", [128, NCH * T], BF16)
    ws = c.S("m_ws", [128, 4 * NCH * 512], BF16)
    xip = c.S("m_xip", [128, T], F32)
    tmpf = c.S("m_tmpf", [128, T], F32)
    fscr = c.S("m_fscr", [128, 2048], F32)
    bscr = c.S("m_bscr", [128, 5120], BF16)
    c.sb["m_ln_mb"] = fscr[:, 0:512]
    c.sb["m_ln_vb"] = fscr[:, 512:1024]
    c.sb["m_ln_sc2"] = fscr[:, 1024:1536]
    c.sb["m_ln_t2"] = fscr[:, 1536:2048]
    c.sb["m_ln_xbf"] = bscr[:, 0:512]
    c.sb["m_ln_sqb"] = bscr[:, 512:1024]
    ksrow = xip[0:1, 0:D]
    ksb = c.S("m_ksb", [128, D], F32)
    kvbd = c.S("m_kvbd", [128, NCH * 128], BF16)
    kvacc = c.S("m_kvacc", [64, 1024], F32)
    zt = c.S("m_zt", [128, 16], F32)
    qkbias = c.S("m_qkbias", [128, D], BF16)
    tstg = c.S("m_tstg", [128, 512], F32)
    cwsb = c.S("m_cwsb", [128, 4 * 3 * NCH], F32)
    cbsb = c.S("m_cbsb", [128, 4 * NCH], F32)
    nrmsb = c.S("m_nrmsb", [128, 4 * NCH], F32)
    pbsb = c.S("m_pbsb", [128, 64], F32)
    cossb = c.S("m_cossb", [128, 512], BF16)
    sinsb = c.S("m_sinsb", [128, 512], BF16)

    t3 = fscr[:, 1536:2048]
    bank = [c.ps[f"bank{i}"] for i in range(8)]
    xs = c.nc.dram_tensor("m_xs", [2, NCH, 128, T], F32, kind="Internal").ap()

    def cw(conv, tap, ch):
        return cwsb[:, (conv * 3 + tap) * NCH + ch: (conv * 3 + tap) * NCH + ch + 1]

    def cb(conv, ch):
        return cbsb[:, conv * NCH + ch: conv * NCH + ch + 1]

    c.prog.d("sync", [
        (cwsb[:, :].rearrange("p (c t k) -> p c t k", t=3, k=NCH),
         convw.rearrange("c t p k -> p c t k")),
        (cbsb[:, :].rearrange("p (c k) -> p c k", k=NCH),
         convb.rearrange("c p k -> p c k")),
        (nrmsb[:, :].rearrange("p (c k) -> p c k", k=NCH),
         norms.rearrange("c p k -> p c k")),
        (pbsb[:, :], pbias[:, :]),
        (cossb[:, :], ctab[0]),
        (sinsb[:, :], ctab[1]),
    ])

    def wslab(dram_col_slice, dst_off_slabs, upto=None):
        dst = ws[:, dst_off_slabs * NCH * 512:(dst_off_slabs + 1) * NCH * 512]
        c.prog.d("gpsimd", [(dst.rearrange("p (c j) -> p c j", j=512),
                             dram_col_slice.rearrange("(c p) j -> p c j", p=128))],
                 upto=upto)

    def fm_proj(dst_cb, src_sb, wslot, couts=4, bias=None):
        pl = Pipe(c.prog)
        for col in range(couts):
            for tc in range(4):
                def pmm(E, col=col, tc=tc, wslot=wslot, src_sb=src_sb):
                    last = None
                    for ci in range(NCH):
                        last = E.matmul(
                            bank[tc % 2][:, 0:512],
                            ws[:, wslot * NCH * 512 + ci * 512 + col * 128:
                               wslot * NCH * 512 + ci * 512 + col * 128 + 128],
                            src_sb[:, ci * T + tc * 512: ci * T + tc * 512 + 512],
                            start=(ci == 0), stop=(ci == NCH - 1))
                    return last

                c.prog.c("tensor", pmm, upto=pl.up())
                dst_cb(col, tc, bank[tc % 2])
                pl.bump()

    for s in range(SEQ_PER_CORE):
        def lncb(ch, tc, t2):
            c.prog.c("vector", lambda E, ch=ch, tc=tc: E.tensor_copy(
                out=xip[:, tc * 512:(tc + 1) * 512], in_=t2[:, :]))
            if tc == 3:
                def cvf(E, ch=ch):
                    return emit_conv3(E, tmpf, xip, 0, T,
                                      (cw(0, 0, ch), cw(0, 1, ch), cw(0, 2, ch)),
                                      cb(0, ch))

                c.prog.c("vector", cvf)
                c.prog.c("vector", lambda E: E.tensor_add(tmpf[:, :], tmpf[:, :],
                                                          xip[:, :]))
                c.prog.d("sync", [(xs[1, ch], tmpf[:, :])])

        emit_layernorm_hbm(c, lambda ch, s=s: xin[s, ch], lncb, "m_ln")
        c.sb["m_ln_xch"] = c.sb["m_ln_xch2"][:, 0:T]

        def ln1cb(ch, tc, t2):
            def gb(E, ch=ch, tc=tc):
                return E.tensor_scalar(
                    out=xn[:, ch * T + tc * 512: ch * T + tc * 512 + 512],
                    in0=t2[:, :], scalar1=nrmsb[:, ch:ch + 1],
                    scalar2=nrmsb[:, NCH + ch: NCH + ch + 1],
                    op0=ALU.mult, op1=ALU.add)

            c.prog.c("vector", gb)

        emit_layernorm_hbm(c, lambda ch: xs[1, ch], ln1cb, "m_ln")

        for slab in range(2):
            wslab(inw[:, slab * 512:(slab + 1) * 512], 0)

            def consume(col, tc, pb, slab=slab):
                ch = slab * 4 + col

                def ev(E, col=col, tc=tc, pb=pb, ch=ch):
                    return E.tensor_scalar_add(
                        out=xip[:, tc * 512:(tc + 1) * 512], in0=pb[:, 0:512],
                        scalar1=pbsb[:, ch:ch + 1])

                c.prog.c("vector", ev)
                if tc == 3:
                    def cvf(E, ch=ch):
                        return emit_conv3(E, tmpf, xip, 0, T,
                                          (cw(1, 0, ch), cw(1, 1, ch), cw(1, 2, ch)),
                                          cb(1, ch))

                    c.prog.c("vector", cvf)

                    def sil(E, ch=ch):
                        return E.activation(
                            out=xibf[:, ch * T:(ch + 1) * T], in_=tmpf[:, :],
                            func=AFT.Silu)

                    c.prog.c("scalar", sil)

            fm_proj(consume, xn, 0)

        def qkbmm(E, half=1):
            E.matmul(bank[2][:, 0:512], c.sb["onesf"][0:1, :],
                     qkbrow[0:1, 1024:1536] if False else c.sb["m_qkbr"][0:1, 1024:1536],
                     start=True, stop=True)
            return E.matmul(bank[3][:, 0:512], c.sb["onesf"][0:1, :],
                            c.sb["m_qkbr"][0:1, 1536:2048], start=True, stop=True)

        c.sb["m_qkbr"] = xip[0:1, 0:2 * D]
        c.prog.d("sync", [(c.sb["m_qkbr"][0:1, :], qkbrow[0:1, :])])
        c.prog.c("tensor", qkbmm)

        def qkbev(E):
            E.tensor_copy(out=qkbias[:, 0:512], in_=bank[2][:, 0:512])
            return E.tensor_copy(out=qkbias[:, 512:1024], in_=bank[3][:, 0:512])

        c.prog.c("vector", qkbev)
        wslab(qkw[:, 1024:1536], 0)
        wslab(qkw[:, 1536:2048], 1)
        c.prog.c("vector", lambda E: E.memset(kvacc[0:64, :], 0.0))

        KB, KRB, VT, QB, QBF = (bscr[:, 0:1024], bscr[:, 1024:2048],
                                bscr[:, 2048:3072], bscr[:, 3072:4096],
                                bscr[:, 4096:5120])
        QKS = fscr[:, 0:1024]
        SCR2 = fscr[:, 1024:2048]

        for ti in range(16):
            def kmm(E, ti=ti):
                last = None
                for half in range(2):
                    for ci in range(NCH):
                        last = E.matmul(
                            bank[half][:, 0:512],
                            xibf[:, ci * T + ti * 128: ci * T + ti * 128 + 128],
                            ws[:, half * NCH * 512 + ci * 512: half * NCH * 512 + ci * 512 + 512],
                            start=(ci == 0), stop=(ci == NCH - 1))
                return last

            c.prog.c("tensor", kmm)
            emit_elu1(c, bank, qkbias, QKS, SCR2, tmpf)
            c.prog.c("vector", lambda E: E.tensor_copy(out=KB[:, :], in_=QKS[:, :]))

            def ksmm(E, ti=ti):
                E.matmul(bank[2][0:1, 0:512], c.sb["onesb"][:, 0:1], KB[:, 0:512],
                         start=(ti == 0), stop=(ti == 15))
                return E.matmul(bank[3][0:1, 0:512], c.sb["onesb"][:, 0:1],
                                KB[:, 512:1024], start=(ti == 0), stop=(ti == 15))

            c.prog.c("tensor", ksmm)
            emit_rope(c, QKS, KRB, SCR2, cossb, sinsb)
            for grp in range(2):
                def vcast(E, grp=grp, ti=ti):
                    last = None
                    for j in range(4):
                        ci = grp * 4 + j
                        last = E.tensor_copy(
                            out=tstg[:, j * 128:(j + 1) * 128],
                            in_=xibf[:, ci * T + ti * 128: ci * T + ti * 128 + 128])
                    return last

                c.prog.c("vector", vcast)

                def vtr(E, grp=grp):
                    last = None
                    for j in range(4):
                        last = E.transpose(bank[6][:, j * 128:(j + 1) * 128],
                                           tstg[:, j * 128:(j + 1) * 128],
                                           c.sb["identf"][:, :])
                    return last

                c.prog.c("tensor", vtr)

                def vtev(E, grp=grp):
                    last = None
                    for j in range(4):
                        ci = grp * 4 + j
                        last = E.tensor_copy(
                            out=VT[:, ci * 128:(ci + 1) * 128],
                            in_=bank[6][:, j * 128:(j + 1) * 128])
                    return last

                c.prog.c("vector", vtev)

            for h in range(16):
                pb = bank[4 + h % 2]

                def kvmm(E, h=h, pb=pb):
                    return E.matmul(pb[0:64, 0:64],
                                    KRB[:, h * 64:(h + 1) * 64],
                                    VT[:, h * 64:(h + 1) * 64],
                                    start=True, stop=True)

                c.prog.c("tensor", kvmm)

                def kvacc_f(E, h=h, pb=pb):
                    dst = kvacc[0:64, h * 64:(h + 1) * 64]
                    return E.tensor_add(dst, dst, pb[0:64, 0:64])

                c.prog.c("vector", kvacc_f)

        c.prog.c("vector", lambda E: E.memset(kvbd[:, :], 0.0))

        def kvev(E):
            last = None
            for h in range(16):
                r = (h % 2) * 64
                last = E.tensor_copy(
                    out=kvbd[r:r + 64, (h // 2) * 128 + r: (h // 2) * 128 + r + 64],
                    in_=kvacc[0:64, h * 64:(h + 1) * 64])
            return last

        c.prog.c("vector", kvev)

        def ksev(E):
            E.tensor_copy(out=ksrow[0:1, 0:512], in_=bank[2][0:1, 0:512])
            return E.tensor_copy(out=ksrow[0:1, 512:1024], in_=bank[3][0:1, 0:512])

        c.prog.c("vector", ksev)

        def ksbmm(E):
            E.matmul(bank[0][:, 0:512], c.sb["onesf"][0:1, :], ksrow[0:1, 0:512],
                     start=True, stop=True)
            return E.matmul(bank[1][:, 0:512], c.sb["onesf"][0:1, :],
                            ksrow[0:1, 512:1024], start=True, stop=True)

        c.prog.c("tensor", ksbmm)

        def ksbev(E):
            E.tensor_copy(out=ksb[:, 0:512], in_=bank[0][:, 0:512])
            return E.tensor_copy(out=ksb[:, 512:1024], in_=bank[1][:, 0:512])

        c.prog.c("vector", ksbev)

        c.prog.d("sync", [(c.sb["m_qkbr"][0:1, :], qkbrow[0:1, :])])

        def qkbmm2(E):
            E.matmul(bank[2][:, 0:512], c.sb["onesf"][0:1, :],
                     c.sb["m_qkbr"][0:1, 0:512], start=True, stop=True)
            return E.matmul(bank[3][:, 0:512], c.sb["onesf"][0:1, :],
                            c.sb["m_qkbr"][0:1, 512:1024], start=True, stop=True)

        c.prog.c("tensor", qkbmm2)
        c.prog.c("vector", qkbev)
        wslab(qkw[:, 0:512], 0)
        wslab(qkw[:, 512:1024], 1)

        for ti in range(16):
            def qmm(E, ti=ti):
                last = None
                for half in range(2):
                    for ci in range(NCH):
                        last = E.matmul(
                            bank[half][:, 0:512],
                            xibf[:, ci * T + ti * 128: ci * T + ti * 128 + 128],
                            ws[:, half * NCH * 512 + ci * 512: half * NCH * 512 + ci * 512 + 512],
                            start=(ci == 0), stop=(ci == NCH - 1))
                return last

            c.prog.c("tensor", qmm)
            emit_elu1(c, bank, qkbias, QKS, SCR2, tmpf)
            c.prog.c("vector", lambda E: E.tensor_mul(SCR2[:, :], QKS[:, :], ksb[:, :]))

            def zred(E):
                E.reduce_sum(out=col_unit(zt[:, :]),
                             in_=seg_view(SCR2[:, 0:1024], 64), axis=AXX)
                E.tensor_single_scalar(out=zt[:, :], in_=zt[:, :],
                                       scalar=T * 1e-6, op=ALU.add)
                return E.reciprocal(out=zt[:, :], in_=zt[:, :])

            c.prog.c("vector", zred)
            emit_rope(c, QKS, None, SCR2, cossb, sinsb, out_f32=tmpf)
            def zap(E):
                zv = AP(zt[:, :].tensor, zt[:, :].offset,
                        [list(a) for a in zt[:, :].ap] + [[0, 64]])
                return E.tensor_tensor(out=seg_view(QB[:, :], 64),
                                       in0=seg_view(tmpf[:, 0:1024], 64),
                                       in1=zv, op=ALU.mult)

            c.prog.c("vector", zap)
            for grp in range(2):
                def qcast(E, grp=grp):
                    last = None
                    for j in range(4):
                        ci = grp * 4 + j
                        last = E.tensor_copy(
                            out=tstg[:, j * 128:(j + 1) * 128],
                            in_=QB[:, ci * 128:(ci + 1) * 128])
                    return last

                c.prog.c("vector", qcast)

                def qtr(E, grp=grp):
                    last = None
                    for j in range(4):
                        last = E.transpose(bank[6][:, j * 128:(j + 1) * 128],
                                           tstg[:, j * 128:(j + 1) * 128],
                                           c.sb["identf"][:, :])
                    return last

                c.prog.c("tensor", qtr)

                def qtev(E, grp=grp):
                    last = None
                    for j in range(4):
                        ci = grp * 4 + j
                        last = E.tensor_copy(
                            out=QBF[:, ci * 128:(ci + 1) * 128],
                            in_=bank[6][:, j * 128:(j + 1) * 128])
                    return last

                c.prog.c("vector", qtev)

            def amm(E, ti=ti):
                last = None
                for ci in range(NCH):
                    pb = bank[2 + ci // 4]
                    last = E.matmul(pb[:, (ci % 4) * 128:(ci % 4) * 128 + 128],
                                    kvbd[:, ci * 128:(ci + 1) * 128],
                                    QBF[:, ci * 128:(ci + 1) * 128],
                                    start=True, stop=True)
                return last

            c.prog.c("tensor", amm)

            def aev(E, ti=ti):
                last = None
                for ci in range(NCH):
                    pb = bank[2 + ci // 4]
                    last = E.tensor_copy(
                        out=attn[:, ci * T + ti * 128: ci * T + ti * 128 + 128],
                        in_=pb[:, (ci % 4) * 128:(ci % 4) * 128 + 128])
                return last

            c.prog.c("vector", aev)

        wslab(actw[:, 0:512], 0)
        wslab(actw[:, 512:1024], 1)
        wslab(outw[:, 0:512], 2)
        wslab(outw[:, 512:1024], 3)
        ACT2 = fscr[:, 0:512]
        LEP = fscr[:, 512:1024]
        TC2 = fscr[:, 1024:1536]
        OWIN = bscr[:, 0:4096]
        pl6 = Pipe(c.prog)
        for tc in range(4):
            for m in range(NCH):
                def apmm(E, m=m, tc=tc):
                    last = None
                    for ci in range(NCH):
                        last = E.matmul(
                            bank[m % 2][:, 0:512],
                            ws[:, (m // 4) * NCH * 512 + ci * 512 + (m % 4) * 128:
                               (m // 4) * NCH * 512 + ci * 512 + (m % 4) * 128 + 128],
                            xn[:, ci * T + tc * 512: ci * T + tc * 512 + 512],
                            start=(ci == 0), stop=(ci == NCH - 1))
                    return last

                c.prog.c("tensor", apmm, upto=pl6.up())

                def asil(E, m=m, tc=tc):
                    return E.activation(out=ACT2[:, :], in_=bank[m % 2][:, 0:512],
                                        func=AFT.Silu, bias=pbsb[:, 8 + m: 8 + m + 1])

                c.prog.c("scalar", asil)

                def lepcv(E, m=m, tc=tc):
                    return emit_conv3(E, LEP, xibf[:, m * T:(m + 1) * T], tc * 512,
                                      512, (cw(2, 0, m), cw(2, 1, m), cw(2, 2, m)),
                                      cb(2, m))

                c.prog.c("vector", lepcv)

                def comb(E, m=m, tc=tc):
                    E.tensor_add(TC2[:, :], LEP[:, :],
                                 attn[:, m * T + tc * 512: m * T + tc * 512 + 512])
                    return E.tensor_mul(OWIN[:, m * 512:(m + 1) * 512],
                                        TC2[:, :], ACT2[:, :])

                c.prog.c("vector", comb)
                pl6.bump()
            for co in range(NCH):
                def omm(E, co=co, tc=tc):
                    last = None
                    for ci in range(NCH):
                        last = E.matmul(
                            bank[2 + co % 2][:, 0:512],
                            ws[:, (2 + co // 4) * NCH * 512 + ci * 512 + (co % 4) * 128:
                               (2 + co // 4) * NCH * 512 + ci * 512 + (co % 4) * 128 + 128],
                            OWIN[:, ci * 512:(ci + 1) * 512],
                            start=(ci == 0), stop=(ci == NCH - 1))
                    return last

                c.prog.c("tensor", omm, upto=(None if co == 0 else pl6.up()))
                c.prog.d("sync", [(t3[:, :], xs[1, co, :, tc * 512:(tc + 1) * 512])],
                         upto=pl6.up(1))

                def oev(E, co=co, tc=tc):
                    E.tensor_scalar_add(out=TC2[:, :], in0=bank[2 + co % 2][:, 0:512],
                                        scalar1=pbsb[:, 16 + co: 16 + co + 1])
                    return E.tensor_add(TC2[:, :], TC2[:, :], t3[:, :])

                c.prog.c("vector", oev)
                c.prog.d("sync", [(xs[1, co, :, tc * 512:(tc + 1) * 512], TC2[:, :])])
                pl6.bump()

        for ch in range(NCH):
            c.prog.d("sync", [(c.sb["m_ln_xch"][:, :], xs[1, ch])])

            def cvf(E, ch=ch):
                return emit_conv3(E, tmpf, c.sb["m_ln_xch"], 0, T,
                                  (cw(3, 0, ch), cw(3, 1, ch), cw(3, 2, ch)),
                                  cb(3, ch))

            c.prog.c("vector", cvf)
            c.prog.c("vector", lambda E: E.tensor_add(tmpf[:, :], tmpf[:, :],
                                                      c.sb["m_ln_xch"][:, :]))
            c.prog.d("sync", [(xs[0, ch], tmpf[:, :])])

        def ln2cb(ch, tc, t2):
            def gb(E, ch=ch, tc=tc):
                return E.tensor_scalar(
                    out=xn[:, ch * T + tc * 512: ch * T + tc * 512 + 512],
                    in0=t2[:, :], scalar1=nrmsb[:, 2 * NCH + ch: 2 * NCH + ch + 1],
                    scalar2=nrmsb[:, 3 * NCH + ch: 3 * NCH + ch + 1],
                    op0=ALU.mult, op1=ALU.add)

            c.prog.c("vector", gb)

        emit_layernorm_hbm(c, lambda ch: xs[0, ch], ln2cb, "m_ln")

        H = xibf
        for tc in range(4):
            pl8 = Pipe(c.prog)
            prev_slab_mark = None
            for fslab in range(8):
                wslab(fc1[:, fslab * 512:(fslab + 1) * 512], fslab % 2,
                      upto=prev_slab_mark)
                slab_mark = c.prog.mark()
                prev_slab_mark = slab_mark
                for col in range(4):
                    hm = fslab * 4 + col

                    def hmm(E, col=col, tc=tc, fslab=fslab):
                        last = None
                        for ci in range(NCH):
                            last = E.matmul(
                                bank[col % 2][:, 0:512],
                                ws[:, (fslab % 2) * NCH * 512 + ci * 512 + col * 128:
                                   (fslab % 2) * NCH * 512 + ci * 512 + col * 128 + 128],
                                xn[:, ci * T + tc * 512: ci * T + tc * 512 + 512],
                                start=(ci == 0), stop=(ci == NCH - 1))
                        return last

                    up = slab_mark if col < 2 else pl8.up(2)
                    c.prog.c("tensor", hmm, upto=up)

                    def gev(E, hm=hm, col=col):
                        return E.activation(
                            out=H[:, hm * 512:(hm + 1) * 512],
                            in_=bank[col % 2][:, 0:512], func=AFT.Gelu,
                            bias=pbsb[:, 32 + hm: 32 + hm + 1])

                    c.prog.c("scalar", gev)
                    pl8.bump()
            for oslab in range(2):
                dst = ws[:, 0:4 * NCH * 512]
                c.prog.d("gpsimd", [
                    (dst.rearrange("p (c j) -> p c j", j=512),
                     fc2[:, oslab * 512:(oslab + 1) * 512].rearrange(
                         "(c p) j -> p c j", p=128))])
                plf = Pipe(c.prog)
                for col in range(4):
                    co = oslab * 4 + col

                    def fmm(E, col=col, tc=tc):
                        last = None
                        for hc in range(32):
                            last = E.matmul(
                                bank[4 + col % 2][:, 0:512],
                                ws[:, hc * 512 + col * 128: hc * 512 + col * 128 + 128],
                                H[:, hc * 512:(hc + 1) * 512],
                                start=(hc == 0), stop=(hc == 31))
                        return last

                    c.prog.c("tensor", fmm, upto=(None if col == 0 else plf.up()))
                    c.prog.d("sync", [
                        (t3[:, :], xs[0, co, :, tc * 512:(tc + 1) * 512]),
                        (xip[:, 0:512], xin[s, co, :, tc * 512:(tc + 1) * 512])])

                    def fev(E, co=co, tc=tc, col=col):
                        E.tensor_scalar_add(out=TC2[:, :],
                                            in0=bank[4 + col % 2][:, 0:512],
                                            scalar1=pbsb[:, 24 + co: 24 + co + 1])
                        E.tensor_add(TC2[:, :], TC2[:, :], t3[:, :])
                        return E.tensor_add(TC2[:, :], TC2[:, :], xip[:, 0:512])

                    c.prog.c("vector", fev)
                    c.prog.d("sync", [(xout[s, co, :, tc * 512:(tc + 1) * 512],
                                       TC2[:, :])])
                    plf.bump()
    c.prog.w("sync")


def emit_elu1(c, bank, qkbias, QKS, SCR2, tmpf):
    def e1(E):
        E.tensor_add(QKS[:, 0:512], bank[0][:, 0:512], qkbias[:, 0:512])
        return E.tensor_add(QKS[:, 512:1024], bank[1][:, 0:512], qkbias[:, 512:1024])

    c.prog.c("vector", e1)
    c.prog.c("vector", lambda E: E.tensor_scalar_min(
        out=tmpf[:, 0:1024], in0=QKS[:, :], scalar1=0.0))
    c.prog.c("scalar", lambda E: E.activation(out=tmpf[:, 0:1024],
                                              in_=tmpf[:, 0:1024], func=AFT.Exp))
    c.prog.c("vector", lambda E: E.tensor_relu(out=SCR2[:, :], in_=QKS[:, :]))
    c.prog.c("vector", lambda E: E.tensor_add(QKS[:, :], tmpf[:, 0:1024],
                                              SCR2[:, :]))


def emit_rope(c, QKS, KRB, SCR2, cossb, sinsb, out_f32=None):
    xe = QKS[:, 0:1024:2]
    xo = QKS[:, 1:1024:2]
    if out_f32 is None:
        oe = KRB[:, 0:1024:2]
        oo = KRB[:, 1:1024:2]
    else:
        oe = out_f32[:, 0:1024:2]
        oo = out_f32[:, 1:1024:2]

    def r1(E):
        E.tensor_mul(SCR2[:, 0:512], xe, cossb[:, :])
        E.tensor_mul(SCR2[:, 512:1024], xo, sinsb[:, :])
        return E.tensor_sub(oe, SCR2[:, 0:512], SCR2[:, 512:1024])

    c.prog.c("vector", r1)

    def r2(E):
        E.tensor_mul(SCR2[:, 0:512], xe, sinsb[:, :])
        E.tensor_mul(SCR2[:, 512:1024], xo, cossb[:, :])
        return E.tensor_add(oo, SCR2[:, 0:512], SCR2[:, 512:1024])

    c.prog.c("vector", r2)




def build_l1_attn(ctx, xin, xout):
    c = ctx
    NH = 16
    HD = 64
    wqkv = c.din("a_wqkvT", [D, 3 * D], F32)
    wproj = c.din("a_wprojT", [D, D], F32)
    bqkvT = c.din("a_bqkvT", [128, 24], F32)
    bprojT = c.din("a_bprojT", [128, 8], F32)
    bvrow = c.din("a_bv_row", [1, D], F32)
    qaug = c.din("a_qaug", [NH, 64, T], BF16)
    kaug = c.din("a_kaug", [NH, 64, T], BF16)
    masks = c.din("a_masks", [2, 128, 128], BF16)
    posi = c.din("a_posi", [128, 4], F32)
    ind8 = c.din("a_ind8", [128, 8], F32)

    qfm = c.S("a_qfm", [128, NCH * T], BF16)
    kfm = c.S("a_kfm", [128, NCH * T], BF16)
    vaug = c.S("a_vaug", [128, 16 * 1040], BF16)
    xnbf = c.S("a_xn", [128, NCH * T], BF16)
    qa = c.S("a_qa", [128, T], BF16)
    ka = c.S("a_ka", [128, T], BF16)
    wsl = c.S("a_wsl", [128, NCH * 512], BF16)
    msk = c.S("a_msk", [128, 2 * 128], BF16)
    bq8 = c.S("a_bq8", [128, 8], F32)
    bqkv_sb = c.S("a_bqkv", [128, 24], F32)
    bproj_sb = c.S("a_bproj", [128, 8], F32)
    bvb = c.S("a_bvb", [128, D], F32)
    stt = c.S("a_stt", [128, 512], F32)
    stt2 = c.S("a_stt2", [128, 1024], F32)
    pt2 = c.S("a_pt2", [128, 1024], BF16)
    negm = c.S("a_negm", [128, 16], F32)
    rmq = c.S("a_rmq", [128, 8], F32)
    mrow = c.S("a_mrow", [1, 512], F32)
    mbt = c.S("a_mbt", [128, 512], F32)
    of32 = c.S("a_of32", [64, 512], F32)
    rrow = c.S("a_rrow", [1, 512], F32)
    vsc = c.S("a_vsc", [128, 512], F32)
    t3 = c.S("a_t3", [128, 512], F32)
    ofm = xnbf

    bank = [c.ps[f"bank{i}"] for i in range(8)]

    posisb = c.S("a_posisb", [128, 4], F32)
    ind8sb = c.S("a_ind8sb", [128, 8], F32)
    chsb = c.S("a_chsb", [128, 8], F32)
    c.prog.d("sync", [(msk[:, :].rearrange("p (m j) -> p m j", j=128),
                   masks.rearrange("m p j -> p m j")),
                      (bqkv_sb[:, :], bqkvT[:, :]),
                      (bproj_sb[:, :], bprojT[:, :]),
                      (posisb[:, :], posi[:, :]),
                      (ind8sb[:, :], ind8[:, :])])
    c.prog.c("vector", lambda E: E.tensor_single_scalar(
        out=bq8[:, :], in_=bqkv_sb[:, 0:8], scalar=0.125, op=ALU.mult))
    def chprep(E):
        E.tensor_tensor(out=chsb[:, 0:2], in0=posisb[:, 0:4:2],
                        in1=posisb[:, 1:4:2], op=ALU.add)
        psv = AP(chsb[:, 0:2].tensor, chsb[:, 0:2].offset,
                 [list(chsb[:, 0:2].ap[0])] + [[1, 2], [0, 4]])
        return E.tensor_tensor(out=stt[:, 0:8].rearrange("p (a g) -> p a g", g=4),
                               in0=ind8sb[:, 0:8].rearrange("p (a g) -> p a g", g=4),
                               in1=psv, op=ALU.mult)

    c.prog.c("vector", chprep)
    c.prog.c("tensor", lambda E: E.matmul(
        c.ps["bank0"][:, 0:8], c.sb["onesf"][:, :], stt[:, 0:8],
        start=True, stop=True))
    c.prog.c("vector", lambda E: E.tensor_copy(out=chsb[:, :],
                                               in_=c.ps["bank0"][:, 0:8]))
    bvr = c.S("a_bvr", [1, D], F32)
    c.prog.d("sync", [(bvr[0:1, :], bvrow[0:1, :])])

    for s in range(SEQ_PER_CORE):
        def consume(ch, tc, t2):
            def xcast(E, ch=ch, tc=tc):
                return E.activation(
                    out=xnbf[:, ch * T + tc * 512: ch * T + tc * 512 + 512],
                    in_=t2[:, :], func=AFT.Copy)

            c.prog.c("scalar", xcast)

        emit_layernorm_hbm(c, lambda ch, s=s: xin[s, ch], consume, "a_ln")
        c.sb["a_ln_xch"] = c.sb["a_ln_xch2"][:, 0:T]

        def bvmm(E):
            E.matmul(bank[0][:, 0:512], c.sb["onesf"][0:1, :],
                     c.sb["a_bvr"][0:1, 0:512], start=True, stop=True)
            return E.matmul(bank[1][:, 0:512], c.sb["onesf"][0:1, :],
                            c.sb["a_bvr"][0:1, 512:1024], start=True, stop=True)

        c.prog.c("tensor", bvmm)

        def bvev(E):
            E.tensor_copy(out=bvb[:, 0:512], in_=bank[0][:, 0:512])
            return E.tensor_copy(out=bvb[:, 512:1024], in_=bank[1][:, 0:512])

        c.prog.c("vector", bvev)

        for slab in range(6):
            c0 = slab * 512
            c.prog.d("gpsimd", [
                (wsl[:, :].rearrange("p (c j) -> p c j", j=512),
                 wqkv[:, c0:c0 + 512].rearrange("(c p) j -> p c j", p=128))])
            if slab < 4:
                plq = Pipe(c.prog)
                for col in range(4):
                    co = (c0 + col * 128) // 128
                    dstbase = (qfm if slab < 2 else kfm)
                    och = co % NCH
                    for tc in range(4):
                        def pmm(E, col=col, tc=tc):
                            last = None
                            for ci in range(NCH):
                                last = E.matmul(
                                    bank[tc % 2][:, 0:512],
                                    wsl[:, ci * 512 + col * 128: ci * 512 + col * 128 + 128],
                                    xnbf[:, ci * T + tc * 512: ci * T + tc * 512 + 512],
                                    start=(ci == 0), stop=(ci == NCH - 1))
                            return last

                        c.prog.c("tensor", pmm, upto=plq.up())
                        if slab < 2:
                            bias = bq8[:, och:och + 1]
                            scale = 0.125
                        else:
                            bias = bqkv_sb[:, 8 + och: 8 + och + 1]
                            scale = 1.0

                        def pev(E, dstbase=dstbase, och=och, tc=tc, bias=bias,
                                scale=scale):
                            return E.tensor_scalar(
                                out=dstbase[:, och * T + tc * 512: och * T + tc * 512 + 512],
                                in0=bank[tc % 2][:, 0:512], scalar1=scale,
                                scalar2=bias, op0=ALU.mult, op1=ALU.add)

                        c.prog.c("vector", pev)
                        plq.bump()
            else:
                half = slab - 4
                plv = Pipe(c.prog)
                for ti in range(16):
                    def vmm(E, ti=ti):
                        last = None
                        for ci in range(NCH):
                            last = E.matmul(
                                bank[ti % 2][:, 0:512],
                                xnbf[:, ci * T + ti * 128: ci * T + ti * 128 + 128],
                                wsl[:, ci * 512: ci * 512 + 512],
                                start=(ci == 0), stop=(ci == NCH - 1))
                        return last

                    c.prog.c("tensor", vmm, upto=plv.up())

                    def vev(E, ti=ti, half=half):
                        E.tensor_add(vsc[:, :], bank[ti % 2][:, 0:512],
                                     bvb[:, half * 512: half * 512 + 512])
                        dst = vaug[:, ti * 1040 + half * 520: (ti + 1) * 1040]
                        dstv = AP(dst.tensor, dst.offset,
                                  [list(dst.ap[0])] + [[65, 8], [1, 64]])
                        srcv = vsc[:, :].rearrange("p (h d) -> p h d", d=64)
                        return E.tensor_copy(out=dstv, in_=srcv)

                    c.prog.c("vector", vev)
                    plv.bump()

        def vones(E):
            a0 = vaug[:, 64:65]
            apv = AP(a0.tensor, a0.offset,
                     [list(a0.ap[0])] + [[1040, 16], [65, 16], [1, 1]])
            return E.memset(apv, 1.0)

        c.prog.c("vector", vones)

        for h in range(16):
            hch = h // 2
            hrow = (h % 2) * 64
            c.prog.c("vector", lambda E, hch=hch, hrow=hrow: E.tensor_copy(
                out=qa[0:64, :], in_=qfm[hrow:hrow + 64, hch * T:(hch + 1) * T]))
            c.prog.c("vector", lambda E, hch=hch, hrow=hrow: E.tensor_copy(
                out=ka[0:64, :], in_=kfm[hrow:hrow + 64, hch * T:(hch + 1) * T]))
            c.prog.d("sync", [(qa[64:128, :], qaug[h]), (ka[64:128, :], kaug[h])])

            smm_marks = {}
            tail_marks = {}
            for qt in range(16):
                nkeys = (qt + 1) * 128
                nfull = (qt * 128) // 512
                rem = (qt * 128) % 512
                b0 = (qt % 2) * 4

                def smm(E, qt=qt, nkeys=nkeys, b0=b0):
                    last = None
                    for kb in range((nkeys + 511) // 512):
                        w = min(512, nkeys - kb * 512)
                        last = E.matmul(
                            bank[b0 + kb][:, 0:w], qa[:, qt * 128:(qt + 1) * 128],
                            ka[:, kb * 512: kb * 512 + w],
                            start=True, stop=True)
                    return last

                c.prog.c("tensor", smm, upto=tail_marks.get(qt - 2))
                smm_marks[qt] = c.prog.mark()

                def sred(E, qt=qt, nfull=nfull, rem=rem, b0=b0):
                    db = (qt * 128) // 512
                    doff = (qt * 128) % 512
                    buf = stt[:, (qt % 2) * 128:(qt % 2) * 128 + 128]
                    E.tensor_add(buf, bank[b0 + db][:, doff:doff + 128],
                                 msk[:, 0:128])
                    np_ = 0
                    for kb in range(nfull):
                        E.reduce_max(rmq[:, np_:np_ + 1], bank[b0 + kb][:, 0:512],
                                     axis=AXX)
                        np_ += 1
                    if rem > 0:
                        E.reduce_max(rmq[:, np_:np_ + 1], bank[b0 + nfull][:, 0:rem],
                                     axis=AXX)
                        np_ += 1
                    E.reduce_max(rmq[:, np_:np_ + 1], buf, axis=AXX)
                    np_ += 1
                    E.reduce_max(rmq[:, 7:8], rmq[:, 0:np_], axis=AXX)
                    return E.tensor_single_scalar(
                        out=negm[:, qt:qt + 1], in_=rmq[:, 7:8], scalar=-1.0,
                        op=ALU.mult)

                c.prog.c("vector", sred, upto=smm_marks[qt])
                tail_marks[qt] = c.prog.mark()

            for g in range(4):
                def mtr(E, g=g):
                    last = None
                    for j in range(4):
                        last = E.transpose(bank[6][0:1, j * 128:(j + 1) * 128],
                                           negm[:, 4 * g + j: 4 * g + j + 1],
                                           c.sb["identf"][:, :])
                    return last

                c.prog.c("tensor", mtr)
                c.prog.c("scalar", lambda E: E.copy(out=mrow[0:1, :],
                                                    in_=bank[6][0:1, 0:512]))
                c.prog.c("tensor", lambda E: E.matmul(
                    bank[6][:, 0:512], c.sb["onesf"][0:1, :], mrow[0:1, :],
                    start=True, stop=True))
                c.prog.c("vector", lambda E: E.tensor_copy(out=mbt[:, :],
                                                           in_=bank[6][:, 0:512]))
                mbt_mark = c.prog.mark()
                if h >= 8:
                    c.prog.c("vector", lambda E, h=h: E.tensor_scalar_add(
                        out=mbt[:, :], in0=mbt[:, :],
                        scalar1=chsb[:, h - 8: h - 7]))
                nkt = 4 * g + 4
                stm = {}
                tdone = {}

                def emit_tail(kt, g=g, h=h, nkt=nkt):
                    bb = kt % 2
                    sbuf = stt[:, bb * 512: bb * 512 + 512] if False else None

                    def stsub(E, kt=kt, g=g, bb=bb):
                        dst = stt2[:, bb * 512: bb * 512 + 512]
                        last = E.tensor_add(dst, bank[bb][:, 0:512], mbt[:, :])
                        qsub = kt - 4 * g
                        if 0 <= qsub <= 3:
                            last = E.tensor_add(
                                dst[:, qsub * 128:(qsub + 1) * 128],
                                dst[:, qsub * 128:(qsub + 1) * 128],
                                msk[:, 128:256])
                        return last

                    c.prog.c("vector", stsub, upto=stm[kt])
                    m1 = c.prog.mark()
                    c.prog.c("scalar", lambda E, bb=bb: E.activation(
                        out=pt2[:, bb * 512: bb * 512 + 512],
                        in_=stt2[:, bb * 512: bb * 512 + 512], func=AFT.Exp),
                        upto=m1)
                    m2 = c.prog.mark()
                    nmask = min(4, kt - 4 * g)
                    if nmask > 0:
                        c.prog.c("vector", lambda E, nmask=nmask, bb=bb: E.memset(
                            pt2[:, bb * 512: bb * 512 + nmask * 128], 0.0),
                            upto=m2)
                    m3 = c.prog.mark()

                    def pvmm(E, kt=kt, g=g, h=h, bb=bb):
                        return E.matmul(
                            bank[5][0:65, 0:512],
                            vaug[:, kt * 1040 + h * 65: kt * 1040 + h * 65 + 65],
                            pt2[:, bb * 512: bb * 512 + 512],
                            start=(kt == 0), stop=(kt == nkt - 1))

                    c.prog.c("tensor", pvmm, upto=m3)
                    tdone[kt] = c.prog.mark()

                for kt in range(nkt):
                    def stmm(E, kt=kt, g=g):
                        return E.matmul(
                            bank[kt % 2][:, 0:512], ka[:, kt * 128:(kt + 1) * 128],
                            qa[:, g * 512:(g + 1) * 512], start=True, stop=True)

                    c.prog.c("tensor", stmm, upto=tdone.get(kt - 2, mbt_mark))
                    stm[kt] = c.prog.mark()
                    if kt >= 1:
                        emit_tail(kt - 1)
                emit_tail(nkt - 1)
                c.prog.c("vector", lambda E: E.reciprocal(
                    out=rrow[0:1, :], in_=bank[5][64:65, 0:512]))
                c.prog.c("scalar", lambda E: E.copy(out=of32[:, :],
                                                    in_=bank[5][0:64, 0:512]))
                c.prog.c("tensor", lambda E: E.matmul(
                    bank[6][0:64, 0:512], c.sb["onesf"][0:1, 0:64], rrow[0:1, :],
                    start=True, stop=True))

                def oev(E, g=g, hch=hch, hrow=hrow):
                    return E.tensor_mul(
                        ofm[hrow:hrow + 64, hch * T + g * 512: hch * T + g * 512 + 512],
                        of32[:, :], bank[6][0:64, 0:512])

                c.prog.c("vector", oev)

        for slab in range(2):
            c0 = slab * 512
            c.prog.d("gpsimd", [
                (wsl[:, :].rearrange("p (c j) -> p c j", j=512),
                 wproj[:, c0:c0 + 512].rearrange("(c p) j -> p c j", p=128))])
            plo = Pipe(c.prog)
            for col in range(4):
                co = slab * 4 + col
                c.prog.d("sync", [(c.sb["a_ln_xch"][:, :], xin[s, co])])
                for tc in range(4):
                    def omm(E, col=col, tc=tc):
                        last = None
                        for ci in range(NCH):
                            last = E.matmul(
                                bank[tc % 2][:, 0:512],
                                wsl[:, ci * 512 + col * 128: ci * 512 + col * 128 + 128],
                                ofm[:, ci * T + tc * 512: ci * T + tc * 512 + 512],
                                start=(ci == 0), stop=(ci == NCH - 1))
                        return last

                    c.prog.c("tensor", omm, upto=plo.up())

                    def oev2(E, co=co, tc=tc):
                        return E.tensor_scalar_add(out=t3[:, :],
                                                   in0=bank[tc % 2][:, 0:512],
                                                   scalar1=bproj_sb[:, co:co + 1])

                    c.prog.c("vector", oev2)

                    def oadd(E, tc=tc):
                        xs = c.sb["a_ln_xch"][:, tc * 512:(tc + 1) * 512]
                        return E.tensor_add(t3[:, :], t3[:, :], xs)

                    c.prog.c("vector", oadd)
                    c.prog.d("sync", [(xout[s, co, :, tc * 512:(tc + 1) * 512],
                                       t3[:, :])])
                    plo.bump()
    c.prog.w("sync")




def bT(v):
    v = np.asarray(v, np.float32)
    return np.ascontiguousarray(v.reshape(-1, 128).T)


def common_consts():
    idf = np.eye(128, dtype=np.float32)
    return {
        "ident_f": idf,
        "ident_b": idf.astype(ml_dtypes.bfloat16),
        "ones_f": np.ones((128, 128), np.float32),
        "ones_b": np.ones((128, 128), ml_dtypes.bfloat16),
    }


def l0_ffn_host_inputs(inp):
    g = lambda k: np.asarray(inp["L0_ffn_" + k], np.float32)
    NL = 2
    w1 = g("w1")
    w2 = g("w2")
    b1 = g("b1")
    w1p = np.zeros((NL, D, 5888), np.float32)
    w1p[:, :, 0:2730] = w1[:, :, 0:2730]
    w1p[:, :, 2816:2816 + 2730] = w1[:, :, 2730:5460]
    w2p = np.zeros((NL, 2816, D), np.float32)
    w2p[:, 0:2730, :] = w2
    b1pad = np.zeros((NL, 5888), np.float32)
    b1pad[:, 0:2730] = b1[:, 0:2730]
    b1pad[:, 2816:2816 + 2730] = b1[:, 2730:5460]
    half_idx = np.arange(0, 64, 2, dtype=np.float64)
    inv_freq = 1.0 / 10000.0 ** (half_idx / 64.0)
    t = np.arange(T, dtype=np.float64)
    fr = t[:, None] * inv_freq[None, :]
    cos = np.cos(fr)
    sin = np.sin(fr)
    base = (half_idx + 0.4 * 64) / (1.4 * 64)
    power = (t - T // 2) / 256.0
    scale = base[None, :] ** power[:, None]
    sgn = np.concatenate([-np.ones(32), np.ones(32)])
    cc = np.concatenate([cos, cos], axis=1)
    ss = np.concatenate([sin, sin], axis=1)
    sc = np.concatenate([scale, scale], axis=1)
    Aq = 8.0 * cc * sc
    BqS = 8.0 * ss * sc * sgn[None, :]
    Ak = cc / sc
    BkS = ss / sc * sgn[None, :]
    xpos = np.stack([a.reshape(16, 128, 64) for a in (Aq, BqS, Ak, BkS)])
    irel = (np.arange(512))
    maskS = np.full((4, 128, 1024), NEG, np.float32)
    for qt in range(4):
        i = 128 * qt + np.arange(128)
        j = np.arange(1024)
        ok = (i[:, None] <= j[None, :]) & (j[None, :] <= i[:, None] + 512)
        maskS[qt][ok] = 0.0
    maskT = np.full((8, 128, 512), NEG, np.float32)
    for kt in range(8):
        jj = 128 * kt + np.arange(128)
        q = np.arange(512)
        ok = (q[None, :] <= jj[:, None]) & (jj[:, None] <= q[None, :] + 512)
        maskT[kt][ok] = 0.0
    return {
        "g_wqkv": g("wqkv"),
        "g_wo": g("wo"),
        "g_w1p": w1p,
        "g_w2p": w2p,
        "g_qs": np.ascontiguousarray(np.broadcast_to(
            g("q_scale")[:, None, :], (NL, 128, 64))),
        "g_ks": np.ascontiguousarray(np.broadcast_to(
            g("k_scale")[:, None, :], (NL, 128, 64))),
        "g_lgb": np.stack([np.stack([bT(g("ln_g")[l]), bT(g("ln_b")[l])])
                           for l in range(NL)]),
        "g_b1p": np.stack([bT(b1pad[l]) for l in range(NL)]),
        "g_b2": np.stack([bT(g("b2")[l]) for l in range(NL)]),
        "g_xpos": xpos.astype(ml_dtypes.bfloat16),
        "g_maskS": maskS.astype(ml_dtypes.bfloat16),
        "g_maskT": maskT.astype(ml_dtypes.bfloat16),
    }


def l0_attn_host_inputs(inp):
    g = lambda k: np.asarray(inp["L0_attn_" + k], np.float32)

    def convT(w):
        w = np.asarray(w, np.float32)[:, 0, :]
        return np.stack([bT(w[:, t]) for t in range(3)])

    theta = 1.0 / 10000.0 ** (np.arange(512, dtype=np.float64) / 512.0)
    cosb = np.broadcast_to(np.cos(theta).astype(np.float32), (128, 512))
    sinb = np.broadcast_to(np.sin(theta).astype(np.float32), (128, 512))
    pbias = np.zeros((128, 64), np.float32)
    pbias[:, 0:8] = bT(g("in_b"))
    pbias[:, 8:16] = bT(g("actp_b"))
    pbias[:, 16:24] = bT(g("out_b"))
    pbias[:, 24:32] = bT(g("fc2_b"))
    pbias[:, 32:64] = bT(g("fc1_b"))
    norms = np.stack([bT(g("norm1_g")), bT(g("norm1_b")),
                      bT(g("norm2_g")), bT(g("norm2_b"))])
    return {
        "m_inw": g("in_w"), "m_actp": g("actp_w"), "m_qkw": g("qk_w"),
        "m_outw": g("out_w"), "m_fc1": g("fc1_w"), "m_fc2": g("fc2_w"),
        "m_convw": np.stack([convT(g("cpe1_w")), convT(g("dwc_w")),
                             convT(g("lepe_w")), convT(g("cpe2_w"))]),
        "m_convb": np.stack([bT(g("cpe1_b")), bT(g("dwc_b")),
                             bT(g("lepe_b")), bT(g("cpe2_b"))]),
        "m_norms": norms,
        "m_pbias": pbias,
        "m_qkb_row": np.ascontiguousarray(g("qk_b").reshape(1, 2 * D)),
        "m_costab": np.stack([cosb, sinb]).astype(ml_dtypes.bfloat16),
    }


def l1_attn_host_inputs(inp):
    wqkv = np.asarray(inp["L1_attn_w_qkv"], np.float32)
    wproj = np.asarray(inp["L1_attn_w_proj"], np.float32)
    bqkv = np.asarray(inp["L1_attn_b_qkv"], np.float32)
    bproj = np.asarray(inp["L1_attn_b_proj"], np.float32)
    k = np.arange(256, dtype=np.float64)
    freqs = 1.0 / 10000.0 ** (k / 512.0)
    i = np.arange(T, dtype=np.float64)
    th = i[None, :] * freqs[:, None]
    cos = np.cos(th)
    sin = np.sin(th)
    qaug = np.zeros((16, 64, T), np.float32)
    kaug = np.zeros((16, 64, T), np.float32)
    for h in range(8):
        qaug[h, 0:32] = cos[32 * h:32 * h + 32]
        qaug[h, 32:64] = sin[32 * h:32 * h + 32]
        kaug[h, 0:32] = (cos - sin)[32 * h:32 * h + 32]
        kaug[h, 32:64] = (cos + sin)[32 * h:32 * h + 32]
    ii = np.arange(128)
    maskS = np.where(ii[None, :] > ii[:, None], NEG, 0.0).astype(np.float32)
    masks = np.stack([maskS, maskS.T])
    posi = np.asarray(inp["L1_attn_pos_independent"], np.float32)
    posi4 = np.ascontiguousarray(posi.reshape(2, 128, 2).transpose(1, 0, 2)
                                 .reshape(128, 4))
    r = np.arange(128)
    ind8 = np.zeros((128, 8), np.float32)
    for h in range(8):
        ind8[:, h] = (r // 32 == h % 4).astype(np.float32)
    return {
        "a_posi": posi4,
        "a_ind8": ind8,
        "a_wqkvT": np.ascontiguousarray(wqkv.T),
        "a_wprojT": np.ascontiguousarray(wproj.T),
        "a_bqkvT": bT(bqkv),
        "a_bprojT": bT(bproj),
        "a_bv_row": np.ascontiguousarray(bqkv[2048:].reshape(1, D)),
        "a_qaug": qaug.astype(ml_dtypes.bfloat16),
        "a_kaug": kaug.astype(ml_dtypes.bfloat16),
        "a_masks": masks.astype(ml_dtypes.bfloat16),
    }


def l1_ffn_host_inputs(inp):
    return {
        "f_Wg": np.ascontiguousarray(
            np.asarray(inp["L1_ffn_Wg"], np.float32).reshape(8, 128, 8)),
        "f_bg_b": np.ascontiguousarray(
            np.broadcast_to(np.asarray(inp["L1_ffn_bg"], np.float32), (128, 8))),
        "f_W1": np.asarray(inp["L1_ffn_W1"], np.float32),
        "f_W2": np.asarray(inp["L1_ffn_W2"], np.float32),
        "f_b1T": np.stack([bT(b) for b in np.asarray(inp["L1_ffn_b1"], np.float32)]),
        "f_b2": np.asarray(inp["L1_ffn_b2"], np.float32),
    }


def fm_of(x_seqs):
    s, t, d = x_seqs.shape
    xt = np.ascontiguousarray(x_seqs.transpose(0, 2, 1))
    return np.ascontiguousarray(xt.reshape(s, NCH, 128, t))


def fm_back(xfm):
    s = xfm.shape[0]
    return np.ascontiguousarray(
        xfm.reshape(s, D, T).transpose(0, 2, 1))




def build_program_l0ffn(dbg=None):
    nc = bass.Bass("TRN2", target_bir_lowering=False, detect_race_conditions=False)
    ctx = Ctx(nc)
    for i in range(8):
        ctx.P(f"bank{i}", [128, 512], F32)
    xin = ctx.din("xin", [SEQ_PER_CORE, NCH, 128, T], F32)
    xout = ctx.dout("xout", [SEQ_PER_CORE, NCH, 128, T], F32)
    setup_common(ctx)
    build_l0_ffn(ctx, xin, xout, dbg=dbg)
    ctx.prog.emit()
    return ctx


def run_l0_ffn(x_full, inp, n_cores=NCORES, dbg=None):
    ctx = build_program_l0ffn(dbg=dbg)
    consts = common_consts()
    winp = l0_ffn_host_inputs(inp)
    in_maps = []
    for cix in range(n_cores):
        xs = x_full[cix * SEQ_PER_CORE:(cix + 1) * SEQ_PER_CORE]
        m = {"xin": fm_of(xs)}
        m.update(consts)
        m.update(winp)
        in_maps.append(m)
    res = run_bass_kernel_spmd(ctx.nc, in_maps, core_ids=list(range(n_cores)))
    outs = [fm_back(r["xout"]) for r in res.results]
    return np.concatenate(outs, axis=0)


def build_program_l0attn():
    nc = bass.Bass("TRN2", target_bir_lowering=False, detect_race_conditions=False)
    ctx = Ctx(nc)
    for i in range(8):
        ctx.P(f"bank{i}", [128, 512], F32)
    xin = ctx.din("xin", [SEQ_PER_CORE, NCH, 128, T], F32)
    xout = ctx.dout("xout", [SEQ_PER_CORE, NCH, 128, T], F32)
    setup_common(ctx)
    build_l0_attn(ctx, xin, xout)
    ctx.prog.emit()
    return ctx


def run_l0_attn(x_full, inp, n_cores=NCORES):
    ctx = build_program_l0attn()
    consts = common_consts()
    winp = l0_attn_host_inputs(inp)
    in_maps = []
    for cix in range(n_cores):
        xs = x_full[cix * SEQ_PER_CORE:(cix + 1) * SEQ_PER_CORE]
        m = {"xin": fm_of(xs)}
        m.update(consts)
        m.update(winp)
        in_maps.append(m)
    res = run_bass_kernel_spmd(ctx.nc, in_maps, core_ids=list(range(n_cores)))
    outs = [fm_back(r["xout"]) for r in res.results]
    return np.concatenate(outs, axis=0)


def build_program_l1attn():
    nc = bass.Bass("TRN2", target_bir_lowering=False, detect_race_conditions=False)
    ctx = Ctx(nc)
    for i in range(8):
        ctx.P(f"bank{i}", [128, 512], F32)
    xin = ctx.din("xin", [SEQ_PER_CORE, NCH, 128, T], F32)
    xout = ctx.dout("xout", [SEQ_PER_CORE, NCH, 128, T], F32)
    setup_common(ctx)
    build_l1_attn(ctx, xin, xout)
    ctx.prog.emit()
    return ctx


def run_l1_attn(x_full, inp, n_cores=NCORES):
    ctx = build_program_l1attn()
    consts = common_consts()
    winp = l1_attn_host_inputs(inp)
    in_maps = []
    for cix in range(n_cores):
        xs = x_full[cix * SEQ_PER_CORE:(cix + 1) * SEQ_PER_CORE]
        m = {"xin": fm_of(xs)}
        m.update(consts)
        m.update(winp)
        in_maps.append(m)
    res = run_bass_kernel_spmd(ctx.nc, in_maps, core_ids=list(range(n_cores)))
    outs = [fm_back(r["xout"]) for r in res.results]
    return np.concatenate(outs, axis=0)


def build_program_l1ffn():
    nc = bass.Bass("TRN2", target_bir_lowering=False, detect_race_conditions=False)
    ctx = Ctx(nc)
    for i in range(8):
        ctx.P(f"bank{i}", [128, 512], F32)
    xin = ctx.din("xin", [SEQ_PER_CORE, NCH, 128, T], F32)
    xout = ctx.dout("xout", [SEQ_PER_CORE, NCH, 128, T], F32)
    setup_common(ctx)
    build_l1_ffn(ctx, xin, xout)
    ctx.prog.emit()
    return ctx


def run_l1_ffn(x_full, inp, n_cores=NCORES):
    ctx = build_program_l1ffn()
    consts = common_consts()
    winp = l1_ffn_host_inputs(inp)
    in_maps = []
    for cix in range(n_cores):
        xs = x_full[cix * SEQ_PER_CORE:(cix + 1) * SEQ_PER_CORE]
        m = {"xin": fm_of(xs)}
        m.update(consts)
        m.update(winp)
        in_maps.append(m)
    res = run_bass_kernel_spmd(ctx.nc, in_maps, core_ids=list(range(n_cores)))
    outs = [fm_back(r["xout"]) for r in res.results]
    return np.concatenate(outs, axis=0)


def build_program_fused():
    nc = bass.Bass("TRN2", target_bir_lowering=False, detect_race_conditions=False)
    ctx = Ctx(nc, pool_kb=205)
    for i in range(8):
        ctx.P(f"bank{i}", [128, 512], F32)
    xin = ctx.din("xin", [SEQ_PER_CORE, NCH, 128, T], F32)
    xout = ctx.dout("xout", [SEQ_PER_CORE, NCH, 128, T], F32)
    xa = nc.dram_tensor("xbounceA", [SEQ_PER_CORE, NCH, 128, T], F32,
                        kind="Internal").ap()
    xb = nc.dram_tensor("xbounceB", [SEQ_PER_CORE, NCH, 128, T], F32,
                        kind="Internal").ap()
    setup_common(ctx)
    ctx.pool_reset()
    build_l0_attn(ctx, xin, xa)
    ctx.pool_reset()
    build_l0_ffn(ctx, xa, xb)
    ctx.pool_reset()
    build_l1_attn(ctx, xb, xa)
    ctx.pool_reset()
    build_l1_ffn(ctx, xa, xout)
    ctx.pool_reset()
    ctx.prog.emit()
    return ctx


def run_fused(inputs, n_cores=NCORES):
    ctx = _get_ctx("fused", build_program_fused)
    x = np.ascontiguousarray(np.asarray(inputs["L0_attn_x"], np.float32))
    host = {}
    host.update(common_consts())
    host.update(l0_attn_host_inputs(inputs))
    host.update(l0_ffn_host_inputs(inputs))
    host.update(l1_attn_host_inputs(inputs))
    host.update(l1_ffn_host_inputs(inputs))
    in_maps = []
    for cix in range(n_cores):
        xs = x[cix * SEQ_PER_CORE:(cix + 1) * SEQ_PER_CORE]
        m = {"xin": fm_of(xs)}
        m.update(host)
        in_maps.append(m)
    res = run_bass_kernel_spmd(ctx.nc, in_maps, core_ids=list(range(n_cores)))
    outs = [fm_back(r["xout"]) for r in res.results]
    return np.concatenate(outs, axis=0)


_CTX_CACHE = {}


def _get_ctx(name, builder):
    if name not in _CTX_CACHE:
        _CTX_CACHE[name] = builder()
    return _CTX_CACHE[name]


def _run_block(name, builder, host_inputs, x_full, n_cores=NCORES):
    ctx = _get_ctx(name, builder)
    consts = common_consts()
    in_maps = []
    for cix in range(n_cores):
        xs = x_full[cix * SEQ_PER_CORE:(cix + 1) * SEQ_PER_CORE]
        m = {"xin": fm_of(xs)}
        m.update(consts)
        m.update(host_inputs)
        in_maps.append(m)
    res = run_bass_kernel_spmd(ctx.nc, in_maps, core_ids=list(range(n_cores)))
    outs = [fm_back(r["xout"]) for r in res.results]
    return np.concatenate(outs, axis=0)


def kernel(**inputs):
    return run_fused(inputs)


def kernel_unfused(**inputs):
    x = np.ascontiguousarray(np.asarray(inputs["L0_attn_x"], np.float32))
    x = _run_block("b1", build_program_l0attn, l0_attn_host_inputs(inputs), x)
    x = _run_block("b2", build_program_l0ffn, l0_ffn_host_inputs(inputs), x)
    x = _run_block("b3", build_program_l1attn, l1_attn_host_inputs(inputs), x)
    x = _run_block("b4", build_program_l1ffn, l1_ffn_host_inputs(inputs), x)
    return x
